# version 06
# baseline (speedup 1.0000x reference)
import numpy as np
import ml_dtypes
from contextlib import ExitStack

import concourse.bass as bass
import concourse.mybir as mybir
from concourse.bass_utils import run_bass_kernel_spmd
from concourse.alu_op_type import AluOpType as OP

F32 = mybir.dt.float32
BF16 = mybir.dt.bfloat16
I32 = mybir.dt.int32
I8 = mybir.dt.int8
AF = mybir.ActivationFunctionType
AX = mybir.AxisListType

D = 1024
DC = 8
HP = 8
H0 = 3072
JC0 = H0 // 128
H1 = 4096
JC1 = H1 // 128
C127 = float(np.float32(1.0 / 127.0))
CD127 = float(np.float32(1.0 / (127.0 * 127.0)))
EXP_SHIFT = -20.0
RG = [[0, 1], [2, 3], [4, 5], [6, 7]]

R_M, R_R, R_VAR, R_SD = 0, 1, 2, 3
R_REC, R_RECO = 0, 1
R_S2, R_S2M = 2, 3
R_S1, R_S1M = 4, 8
R_WM = 2
R_HALO = 1
R_SC = 0


def build_nc(S):
    T = S // 2
    NB = T // 256
    NSL = T // 512
    TT0 = 512
    NT0 = T // TT0
    HN = 2 * NB
    assert HN <= 32

    nc = bass.Bass("TRN2", target_bir_lowering=False)
    DT = nc.dram_tensor

    xT_in = DT("xT", [D, T], F32, kind="ExternalInput").ap()
    masks_in = DT("masks", [128, 1024], F32, kind="ExternalInput").ap()
    ident_in = DT("ident", [128, 128], F32, kind="ExternalInput").ap()
    hsel_in = DT("hsel", [128, 2], F32, kind="ExternalInput").ap()
    wq_in = DT("wqT", [D, D], F32, kind="ExternalInput").ap()
    wk_in = DT("wkT", [D, D], F32, kind="ExternalInput").ap()
    wv_in = DT("wvT", [D, D], F32, kind="ExternalInput").ap()
    wo_in = DT("woT", [D, D], F32, kind="ExternalInput").ap()
    wg_in = DT("wgT", [D, H0], I8, kind="ExternalInput").ap()
    wu_in = DT("wuT", [D, H0], I8, kind="ExternalInput").ap()
    wd_in = DT("wdT", [H0, D], I8, kind="ExternalInput").ap()
    wqkv_in = DT("wqkvT", [D, 3 * D], F32, kind="ExternalInput").ap()
    wpj_in = DT("wprojT", [D, D], F32, kind="ExternalInput").ap()
    bpj_in = DT("bproj", [D, 1], F32, kind="ExternalInput").ap()
    w1_in = DT("w1T", [D, H1], F32, kind="ExternalInput").ap()
    w2_in = DT("w2T", [H1, D], F32, kind="ExternalInput").ap()
    cw_in = DT("convw", [H1, 3], F32, kind="ExternalInput").ap()
    cb_in = DT("convb", [H1, 1], F32, kind="ExternalInput").ap()
    out_d = DT("outT", [D, T], F32, kind="ExternalOutput").ap()

    KE = HP * 128 * T
    VE = (T // 128) * 128 * D
    ccin = [DT(f"cc{L}i", [KE + VE], BF16, kind="Internal").ap() for L in (0, 1)]
    ccout = [DT(f"cc{L}o", [2, KE + VE], BF16, kind="Internal").ap() for L in (0, 1)]
    hai = DT("hai", [D, HN], F32, kind="Internal").ap()
    hao = DT("hao", [2, D, HN], F32, kind="Internal").ap()
    qsp = DT("qsp", [HP, 128, T], BF16, kind="Internal").ap()
    asp = DT("asp", [HP, 128, T], BF16, kind="Internal").ap()
    t1d = DT("t1d", [DC, 128, H1], BF16, kind="Internal").ap()
    t2d = DT("t2d", [JC1, 128, D], BF16, kind="Internal").ap()

    es = ExitStack()
    SB = lambda n, sh, d: es.enter_context(nc.sbuf_tensor(n, sh, d))

    x_sb = SB("x", [128, DC * T], F32)
    xn_sb = SB("xn", [128, DC * T], BF16)
    rows = SB("rows", [128, 2048], F32)
    ROW = lambda i, c0, w: rows[32 * (i % 3):32 * (i % 3) + 1,
                                (i // 3) * 512 + c0:(i // 3) * 512 + c0 + w]
    bb1 = SB("bb1", [128, 512], F32)
    bb2 = SB("bb2", [128, 512], F32)
    fsc = SB("fsc", [128, 1024], F32)
    warena = SB("warena", [128, DC * 512], BF16)
    wff_w = SB("wff_w", [128, 2 * 1024], BF16)
    kvst = SB("kvst", [128, 2 * 512], BF16)
    katt = SB("katt", [128, S], BF16)
    vatt = SB("vatt", [128, (S // 128) * 130], BF16)
    qatt = SB("qatt", [128, T], BF16)
    p_sb = SB("p_sb", [128, 2048], BF16)
    attst = SB("attst", [128, T], BF16)
    attb = SB("attb", [128, DC * 256], BF16)
    masks_sb = SB("masksb", [128, 1024], BF16)
    ident_sb = SB("identb", [128, 128], F32)
    hsel_sb = SB("hselb", [128, 2], F32)
    ones_r = SB("ones_r", [128, 128], F32)
    onesk = SB("onesk", [128, 1], F32)
    mxc = SB("mxc", [128, 64], F32)
    hpool = SB("hpool", [128, JC0 * TT0], BF16)
    xq1_sb = SB("xq1", [128, DC * 258], BF16)
    htile = SB("htile", [128, 258], F32)
    t1buf = SB("t1buf", [128, 2 * DC * 128], BF16)
    t2buf = SB("t2buf", [128, 16 * 128], BF16)
    wqf = SB("wqf", [128, 256], F32)
    tq_sb = SB("tq", [128, 256], BF16)
    wabs = SB("wabs", [128, 4], F32)
    thr_sb = SB("thr", [128, 8], F32)
    i32a = SB("i32a", [128, 256], I32)
    sc258 = SB("sc258", [128, 258], F32)
    s2mb = SB("s2mb", [128, 256], F32)
    run_sb = SB("run_sb", [128, 256], F32)
    cvw_sb = SB("cvw", [128, 3 * JC1], F32)
    cvb_sb = SB("cvb", [128, JC1], F32)
    bpj_sb = SB("bpjb", [128, DC], F32)
    cbias = SB("cbias", [128, 4], F32)
    hx_sb = SB("hx", [128, 2 * DC * NB], F32)
    hcand = SB("hcand", [128, 2 * DC * NB], F32)
    hq_sb = SB("hq", [128, 2 * DC * NB], BF16)

    ps = es.enter_context(nc.psum_tensor("ps_all", [128, 3584], F32))
    pstr = es.enter_context(nc.psum_tensor("ps_tr", [128, 128], F32))

    DMA_FAMS = ["YX", "YC", "YS0", "YS1", "YH", "YW", "YT", "YA", "YB",
                "YP", "YO", "Y1a", "Y1b", "Y2", "GM", "GA", "GF", "GU",
                "GD0", "GD1", "CC"]
    sems, cnt = {}, {k: 0 for k in (["T", "V", "S"] + DMA_FAMS)}
    for k in cnt:
        sems[k] = es.enter_context(nc.semaphore("sem" + k))
    last_wait = {}

    blk = es.enter_context(nc.Block())

    class _RecInstr:
        __slots__ = ("incs",)

        def __init__(self):
            self.incs = []

        def then_inc(self, sem, n):
            self.incs.append((sem, n))
            return self

    class _Rec:
        def __init__(self, tape):
            self._tape = tape

        def __getattr__(self, m):
            tape = self._tape

            def call(*a, **k):
                ri = _RecInstr()
                tape.append((m, a, k, ri))
                return ri
            return call

    tapes = {k: [] for k in "TVSYG"}
    ENG = {k: _Rec(tapes[k]) for k in "TVSYG"}
    t, v, s, y, g = (ENG[k] for k in "TVSYG")

    def _replay(tape):
        def f(eng):
            for m, a, k, ri in tape:
                instr = getattr(eng, m)(*a, **k)
                for sem, n in ri.incs:
                    instr.then_inc(sem, n)
        return f

    def inc(tag, instr, n=1):
        instr.then_inc(sems[tag], n)
        cnt[tag] += n
        return (tag, cnt[tag])

    def ck(tag):
        return (tag, cnt[tag])

    def W(consumer, *cks):
        lw = last_wait.setdefault(consumer, {})
        best = {}
        for c in cks:
            if c is None:
                continue
            tag, val = c
            if val > best.get(tag, 0):
                best[tag] = val
        for tag, val in best.items():
            if val <= 0 or lw.get(tag, 0) >= val:
                continue
            assert val <= cnt[tag], f"forward wait {tag}:{val} > {cnt[tag]}"
            ENG[consumer].wait_ge(sems[tag], val)
            lw[tag] = val

    for dc in range(DC):
        inc("YX", y.dma_start(out=x_sb[:, dc * T:(dc + 1) * T],
                              in_=xT_in[dc * 128:(dc + 1) * 128, :]), 16)
    ck_x0 = ck("YX")
    inc("YC", y.dma_start(out=ident_sb[:], in_=ident_in), 16)
    inc("YC", y.dma_start(out=hsel_sb[:], in_=hsel_in), 16)
    inc("YC", y.dma_start(
        out=cvw_sb[:].rearrange("p (c k) -> p c k", c=JC1),
        in_=cw_in.rearrange("(c p) k -> p c k", p=128),
        allow_slow_non_contiguous=True), 16)
    inc("YC", y.dma_start(
        out=cvb_sb[:].rearrange("p (c o) -> p c o", c=JC1),
        in_=cb_in.rearrange("(c p) o -> p c o", p=128),
        allow_slow_non_contiguous=True), 16)
    inc("YC", y.dma_start(
        out=bpj_sb[:].rearrange("p (c o) -> p c o", c=DC),
        in_=bpj_in.rearrange("(c p) o -> p c o", p=128),
        allow_slow_non_contiguous=True), 16)
    ck_consts = ck("YC")
    ck_masks = inc("GM", g.dma_start(out=masks_sb[:], in_=masks_in), 16)

    W("V", ck_consts)
    v.memset(ones_r[:], 1.0)
    v.memset(onesk[:], 1.0)
    v.memset(cbias[:, 0:1], 1e-5)
    v.memset(cbias[:, 1:2], 1.0)
    ck_ones = inc("V", v.memset(cbias[:, 2:3], EXP_SHIFT))

    def bcast_row(slot, c0, n, dst_sb, dst_off, pbank, deps=()):
        b = 32 * (slot % 3)
        W("T", ck_ones, ck("V"), *deps)
        mm = t.matmul(ps[:, pbank * 512:pbank * 512 + n], ones_r[b:b + 1, 0:128],
                      ROW(slot, c0, n), start=True, stop=True)
        ckt = inc("T", mm)
        W("V", ckt)
        cp = v.tensor_copy(out=dst_sb[:, dst_off:dst_off + n],
                           in_=ps[:, pbank * 512:pbank * 512 + n])
        v.drain()
        return inc("V", cp)

    def wload(fam, dram_ap, row0, rows_, col0, ncols, dst, dst_off):
        ins = g.dma_start(
            out=dst[:, dst_off:dst_off + (rows_ // 128) * ncols].rearrange(
                "p (c n) -> p c n", c=rows_ // 128),
            in_=dram_ap[row0:row0 + rows_, col0:col0 + ncols].rearrange(
                "(c p) n -> p c n", p=128))
        return inc(fam, ins, 16)

    def ln_slice(xs_aps, w_, dep):
        mm2 = None
        W("T", ck("V"))
        for dc in range(DC):
            xap = xs_aps[dc][0]
            off = (dc % 2) * 512
            W("V", dep, ck("T"))
            sq = v.tensor_mul(out=fsc[:, off:off + w_], in0=xap, in1=xap)
            v.drain()
            cksq = inc("V", sq)
            W("T", ck_ones, cksq)
            t.matmul(ps[0:1, 3 * 512:3 * 512 + w_], onesk[:], xap,
                     start=(dc == 0), stop=(dc == DC - 1))
            mm2 = t.matmul(ps[0:1, 4 * 512:4 * 512 + w_], onesk[:],
                           fsc[:, off:off + w_],
                           start=(dc == 0), stop=(dc == DC - 1))
            inc("T", mm2)
        ck_st = ck("T")
        W("V", ck_st)
        v.tensor_scalar(out=ROW(R_M, 0, w_),
                        in0=ps[0:1, 3 * 512:3 * 512 + w_],
                        scalar1=1.0 / D, scalar2=None, op0=OP.mult)
        v.drain()
        v.tensor_mul(out=ROW(R_VAR, 0, w_), in0=ROW(R_M, 0, w_),
                     in1=ROW(R_M, 0, w_))
        v.drain()
        va = v.scalar_tensor_tensor(out=ROW(R_VAR, 0, w_),
                                    in0=ps[0:1, 4 * 512:4 * 512 + w_],
                                    scalar=1.0 / D, in1=ROW(R_VAR, 0, w_),
                                    op0=OP.mult, op1=OP.subtract)
        v.drain()
        ck_var = inc("V", va)
        W("S", ck_var)
        sq2 = s.activation(out=ROW(R_SD, 0, w_),
                           in_=ROW(R_VAR, 0, w_), func=AF.Sqrt,
                           bias=cbias[0:1, 0:1])
        ck_sd = inc("S", sq2)
        W("V", ck_sd)
        rc = v.reciprocal(out=ROW(R_R, 0, w_), in_=ROW(R_SD, 0, w_))
        v.drain()
        ck_rows = inc("V", rc)
        ck_mb = bcast_row(R_M, 0, w_, bb1, 0, 5, deps=(ck_rows,))
        ck_rb = bcast_row(R_R, 0, w_, bb2, 0, 6, deps=(ck_rows,))
        mu = None
        for dc in range(DC):
            xap, oap = xs_aps[dc]
            W("V", ck_mb, ck_rb)
            v.tensor_sub(out=fsc[:, 0:w_], in0=xap, in1=bb1[:, 0:w_])
            v.drain()
            mu = v.tensor_mul(out=oap, in0=fsc[:, 0:w_], in1=bb2[:, 0:w_])
            v.drain()
        return inc("V", mu)

    def emit_layernorm(dep):
        outs = []
        for sl in range(NSL):
            c0 = sl * 512
            xs = [(x_sb[:, dc * T + c0:dc * T + c0 + 512],
                   xn_sb[:, dc * T + c0:dc * T + c0 + 512]) for dc in range(DC)]
            outs.append(ln_slice(xs, 512, dep))
        return outs[-1]

    proj_state = {"dma": [], "slot": 0, "evac": []}

    def proj_evac(ck_t, pb, dst_ap):
        sl = proj_state["slot"]
        so = (sl % 2) * 512
        W("V", ck_t)
        if sl >= 2:
            W("V", proj_state["dma"][sl - 2])
        cp = v.tensor_copy(out=kvst[:, so:so + 512],
                           in_=ps[:, pb * 512:pb * 512 + 512])
        v.drain()
        ck_v = inc("V", cp)
        proj_state["evac"].append(ck_v)
        W("Y", ck_v, ("YA", cnt["YA"]))
        dd = y.dma_start(out=dst_ap, in_=kvst[:, so:so + 512])
        proj_state["dma"].append(inc("YS0" if sl % 2 == 0 else "YS1", dd, 16))
        proj_state["slot"] += 1
        pump_side(2)

    def emit_proj_w(w_dram, wcol0, dep_xn, dst_fn):
        for half in range(2):
            W("G", ck("T"))
            ck_w = wload("GA", w_dram, 0, D, wcol0 + half * 512, 512, warena, 0)
            for a2 in range(4):
                a = half * 4 + a2
                for b in range(NSL):
                    pb = 5 + (proj_state["slot"] % 2)
                    mm = None
                    ev = proj_state["evac"]
                    for dc in range(DC):
                        W("T", ck_w, dep_xn,
                          ev[-2] if len(ev) >= 2 else None)
                        mm = t.matmul(
                            ps[:, pb * 512:pb * 512 + 512],
                            warena[:, dc * 512 + a2 * 128:dc * 512 + a2 * 128 + 128],
                            xn_sb[:, dc * T + b * 512:dc * T + b * 512 + 512],
                            start=(dc == 0), stop=(dc == DC - 1))
                    proj_evac(inc("T", mm), pb, dst_fn(a, b))

    def emit_proj_v(w_dram, wcol0, dep_xn, dst_fn):
        for b in range(2):
            W("G", ck("T"))
            ck_w = wload("GA", w_dram, 0, D, wcol0 + b * 512, 512, warena, 0)
            for a in range(T // 128):
                pb = 5 + (proj_state["slot"] % 2)
                mm = None
                ev = proj_state["evac"]
                for dc in range(DC):
                    W("T", ck_w, dep_xn,
                      ev[-2] if len(ev) >= 2 else None)
                    mm = t.matmul(
                        ps[:, pb * 512:pb * 512 + 512],
                        xn_sb[:, dc * T + a * 128:dc * T + a * 128 + 128],
                        warena[:, dc * 512:dc * 512 + 512],
                        start=(dc == 0), stop=(dc == DC - 1))
                proj_evac(inc("T", mm), pb, dst_fn(a, b))

    def token_scales(fold_ap, mcol_ap, dep):
        W("T", dep, ck_consts, ck("V"))
        tr = t.transpose(pstr[:, :], fold_ap, ident_sb[:])
        ck_tr = inc("T", tr)
        W("V", ck_tr)
        v.reduce_max(out=mxc[:, 34:35], in_=pstr[:, :], axis=AX.X)
        v.drain()
        v.tensor_scalar(out=mxc[:, 35:36], in0=mxc[:, 34:35], scalar1=1e-5,
                        scalar2=None, op0=OP.max)
        v.drain()
        v.reciprocal(out=mxc[:, 36:37], in_=mxc[:, 35:36])
        v.drain()
        v.tensor_scalar(out=mxc[:, 0:1], in0=mxc[:, 36:37], scalar1=127.0,
                        scalar2=None, op0=OP.mult)
        v.drain()
        sm = v.tensor_scalar(out=mxc[:, 32:33], in0=mxc[:, 35:36], scalar1=mcol_ap,
                             scalar2=C127, op0=OP.mult, op1=OP.mult)
        v.drain()
        ck_mx = inc("V", sm)
        W("T", ck_mx)
        tr2 = t.transpose(pstr[0:64, :], mxc[:, :], ident_sb[:])
        return inc("T", tr2)

    def wquant_gen(dram_ap, rows_, ncol, col, dst_dram):
        prev_v = None
        first = True
        ck_acc = None
        for c in range(rows_ // 128):
            for p0 in range(0, ncol, 256):
                W("Y", prev_v)
                ld = inc("YW", y.dma_start(
                    out=wqf[:], in_=dram_ap[c * 128:(c + 1) * 128, p0:p0 + 256]), 16)
                W("V", ld)
                v.reduce_sum(out=wabs[:, 2:3], in_=wqf[:], axis=AX.X,
                             apply_absolute_value=True)
                v.drain()
                if first:
                    cp = v.tensor_copy(out=wabs[:, 0:1], in_=wabs[:, 2:3])
                    first = False
                else:
                    cp = v.tensor_add(out=wabs[:, 0:1], in0=wabs[:, 0:1],
                                      in1=wabs[:, 2:3])
                v.drain()
                ck_acc = inc("V", cp)
                prev_v = ck_acc
                yield
        W("T", ck_acc, ck_ones)
        mm = t.matmul(ps[0:1, 4 * 512:4 * 512 + 1], onesk[:], wabs[:, 0:1],
                      start=True, stop=True)
        ck_t = inc("T", mm)
        W("V", ck_t)
        v.tensor_scalar(out=ROW(R_WM, col, 1), in0=ps[0:1, 4 * 512:4 * 512 + 1],
                        scalar1=1.0 / (rows_ * ncol), scalar2=None, op0=OP.mult)
        v.drain()
        th = v.tensor_scalar(out=ROW(R_WM, col + 2, 1), in0=ROW(R_WM, col, 1),
                             scalar1=0.5, scalar2=None, op0=OP.mult)
        ck_m = inc("V", th)
        W("T", ck_m)
        wb = 32 * (R_WM % 3)
        mm = t.matmul(ps[:, 4 * 512:4 * 512 + 1], ones_r[wb:wb + 1, 0:128],
                      ROW(R_WM, col + 2, 1), start=True, stop=True)
        ck_t = inc("T", mm)
        W("V", ck_t)
        v.tensor_copy(out=thr_sb[:, 2 * col:2 * col + 1],
                      in_=ps[:, 4 * 512:4 * 512 + 1])
        v.drain()
        v.tensor_scalar(out=thr_sb[:, 2 * col + 1:2 * col + 2],
                        in0=thr_sb[:, 2 * col:2 * col + 1], scalar1=-1.0,
                        scalar2=None, op0=OP.mult)
        v.drain()
        th2 = v.tensor_scalar(out=thr_sb[:, 4 + col:5 + col],
                              in0=thr_sb[:, 2 * col:2 * col + 1], scalar1=2.0,
                              scalar2=None, op0=OP.mult)
        v.drain()
        ck_thr = inc("V", th2)
        yield
        for c in range(rows_ // 128):
            for p0 in range(0, ncol, 256):
                W("Y", prev_v)
                ld = inc("YW", y.dma_start(
                    out=wqf[:], in_=dram_ap[c * 128:(c + 1) * 128, p0:p0 + 256]), 16)
                W("V", ld, ck_thr)
                if tern_dma:
                    W("V", tern_dma[-1])
                v.tensor_scalar(out=run_sb[:], in0=wqf[:],
                                scalar1=thr_sb[:, 2 * col + 1:2 * col + 2],
                                scalar2=None, op0=OP.is_lt)
                v.drain()
                v.tensor_scalar(out=tq_sb[:], in0=wqf[:],
                                scalar1=thr_sb[:, 2 * col:2 * col + 1],
                                scalar2=None, op0=OP.is_gt)
                v.drain()
                sb2 = v.tensor_sub(out=tq_sb[:], in0=tq_sb[:], in1=run_sb[:])
                v.drain()
                ck_q = inc("V", sb2)
                prev_v = ck_q
                W("Y", ck_q)
                dd = y.dma_start(out=dst_dram[c][:, p0:p0 + 256], in_=tq_sb[:])
                tern_dma.append(inc("YT", dd, 16))
                yield

    side_gens = []

    def pump_side(n=1):
        for _ in range(n):
            while side_gens:
                try:
                    next(side_gens[0])
                    break
                except StopIteration:
                    side_gens.pop(0)
            else:
                return

    def drain_side():
        while side_gens:
            pump_side()

    tern_dma = []


    def emit_attention(L, ck_ag):
        cc = ccout[L]
        st = {"v0": ck("V")}
        prev_tail = None
        for hp in range(HP):
            W("Y", ck_ag, prev_tail, ck("T") if hp else None,
              ("YS0", cnt["YS0"]), ("YS1", cnt["YS1"]))
            kld = []
            for r in (0, 1):
                srck = cc[r][0:KE].rearrange("(h p n) -> h p n", h=HP, p=128)[hp]
                dd = y.dma_start(
                    out=katt[:].rearrange("p (j q) -> p j q",
                                          q=512)[:, :, 256 * r:256 * r + 256],
                    in_=srck.rearrange("p (j n) -> p j n", n=256))
                kld.append(inc("YA", dd, 16))
                srcv = cc[r][KE:KE + VE].rearrange("(c p n) -> c p n",
                                                   c=T // 128, p=128)
                for u in range(2):
                    for hd in range(2):
                        off = 260 * r + 130 * u + 65 * hd
                        dd = y.dma_start(
                            out=vatt[:].rearrange(
                                "p (j q) -> p j q", q=520)[:, :, off:off + 64],
                            in_=srcv[u::2, :, hp * 128 + hd * 64:
                                     hp * 128 + hd * 64 + 64].rearrange(
                                "c p n -> p c n"))
                        kld.append(inc("YA", dd, 16))
            dd = y.dma_start(out=qatt[:], in_=qsp[hp])
            kld.append(inc("YA", dd, 16))
            W("V", *kld, prev_tail)
            ckone = inc("V", v.memset(
                vatt[:].rearrange("p (c f) -> p c f", f=130)[:, :, 64:130:65], 1.0))
            for j in range(NB):
                q0 = j * 256
                for bt in range(j + 1):
                    for hd in range(2):
                        bko = hd * 1024
                        po = hd * 1024
                        mm = None
                        for i in range(4):
                            kb = 4 * bt + i
                            W("T", *kld, ckone, st.get(("exp", hd)),
                              st.get("avfree") if bt == 0 else None)
                            t.ldweights(
                                katt[64 * hd:64 * hd + 64, kb * 128:kb * 128 + 128],
                                tile_position=(64 * hd, 0))
                            mm = t.matmul(
                                ps[:, bko + i * 256:bko + i * 256 + 256],
                                katt[64 * hd:64 * hd + 64, kb * 128:kb * 128 + 128],
                                qatt[64 * hd:64 * hd + 64, q0:q0 + 256],
                                start=True, stop=True, tile_position=(64 * hd, 0))
                        ck_sc = inc("T", mm)
                        if L == 0:
                            W("S", ck_sc, st.get(("pfree", hd)), st["v0"])
                            e2 = s.activation(out=p_sb[:, po:po + 1024],
                                              in_=ps[:, bko:bko + 1024],
                                              func=AF.Exp, scale=0.125,
                                              bias=cbias[:, 2:3])
                            ck_e = inc("S", e2)
                        else:
                            W("S", ck_sc, st.get("sqv"), st["v0"])
                            s.activation(out=fsc[:], in_=ps[:, bko:bko + 1024],
                                         func=AF.Exp, scale=0.125)
                            s.drain()
                            e2 = s.activation(out=fsc[:], in_=fsc[:],
                                              func=AF.Ln, bias=cbias[:, 1:2])
                            ck_ln = inc("S", e2)
                            W("V", ck_ln, st.get(("pfree", hd)))
                            sq = v.tensor_mul(out=p_sb[:, po:po + 1024],
                                              in0=fsc[:], in1=fsc[:])
                            v.drain()
                            ck_e = inc("V", sq)
                            st["sqv"] = ck_e
                        st[("exp", hd)] = ck_e
                        if bt == j:
                            W("V", ck_e, ck_masks)
                            mk = v.tensor_mul(out=p_sb[:, po:po + 1024],
                                              in0=p_sb[:, po:po + 1024],
                                              in1=masks_sb[:])
                            v.drain()
                            ck_e = inc("V", mk)
                            st[("exp", hd)] = ck_e
                        mm = None
                        for i in range(4):
                            kb = 4 * bt + i
                            W("T", ck_e,
                              st.get("avfree") if bt == 0 else None)
                            mm = t.matmul(
                                ps[0:65, (4 + hd) * 512:(4 + hd) * 512 + 256],
                                vatt[:, kb * 130 + hd * 65:kb * 130 + hd * 65 + 65],
                                p_sb[:, po + i * 256:po + i * 256 + 256],
                                start=(bt == 0 and i == 0),
                                stop=(bt == j and i == 3))
                        st[("pfree", hd)] = inc("T", mm)
                ck_av = st[("pfree", 1)]
                W("V", ck_av)
                v.tensor_copy(out=ROW(R_REC, 0, 256),
                              in_=ps[64:65, 2048:2304])
                v.drain()
                v.tensor_copy(out=ROW(R_REC, 256, 256),
                              in_=ps[64:65, 2560:2816])
                v.drain()
                if L == 1:
                    v.tensor_scalar(out=ROW(R_REC, 0, 512),
                                    in0=ROW(R_REC, 0, 512),
                                    scalar1=1e-9, scalar2=None, op0=OP.add)
                    v.drain()
                rc = v.reciprocal(out=ROW(R_RECO, 0, 512),
                                  in_=ROW(R_REC, 0, 512))
                v.drain()
                ck_rc = inc("V", rc)
                W("T", ck_rc)
                rb = 32 * (R_RECO % 3)
                t.matmul(ps[0:64, 3072:3328], ones_r[rb:rb + 1, 0:64],
                         ROW(R_RECO, 0, 256), start=True, stop=True,
                         tile_position=(rb, 0))
                mm = t.matmul(ps[64:128, 3072:3328], ones_r[rb:rb + 1, 0:64],
                              ROW(R_RECO, 256, 256), start=True,
                              stop=True, tile_position=(rb, 64))
                ck_bc = inc("T", mm)
                W("V", ck_bc)
                v.tensor_copy(out=attst[0:64, q0:q0 + 256], in_=ps[0:64, 2048:2304])
                v.drain()
                v.tensor_copy(out=attst[64:128, q0:q0 + 256], in_=ps[0:64, 2560:2816])
                v.drain()
                v.tensor_mul(out=attst[0:64, q0:q0 + 256],
                             in0=attst[0:64, q0:q0 + 256], in1=ps[0:64, 3072:3328])
                v.drain()
                mu = v.tensor_mul(out=attst[64:128, q0:q0 + 256],
                                  in0=attst[64:128, q0:q0 + 256],
                                  in1=ps[64:128, 3072:3328])
                v.drain()
                st["avread"] = inc("V", mu)
                st["avfree"] = st["avread"]
            W("Y", st["avread"], ("YB", cnt["YB"]))
            dd = y.dma_start(out=asp[hp], in_=attst[:])
            prev_tail = inc("YP", dd, 16)
        return prev_tail

    def emit_oproj(L, w_dram, dep_att):
        last = None
        for half in range(2):
            W("G", ck("T"))
            ck_w = wload("GA", w_dram, 0, D, half * 512, 512, warena, 0)
            for sl in range(T // 256):
                abl = []
                W("Y", dep_att, ck("T") if (half or sl) else None)
                for c in range(DC):
                    dd = y.dma_start(out=attb[:, c * 256:(c + 1) * 256],
                                     in_=asp[c][:, sl * 256:sl * 256 + 256])
                    abl.append(inc("YB", dd, 16))
                for m2 in range(4):
                    mc = half * 4 + m2
                    pb = 5 + ((sl * 4 + m2) % 2)
                    mm = None
                    for c in range(DC):
                        W("T", ck_w, *abl, last)
                        mm = t.matmul(
                            ps[:, pb * 512:pb * 512 + 256],
                            warena[:, c * 512 + m2 * 128:c * 512 + m2 * 128 + 128],
                            attb[:, c * 256:(c + 1) * 256],
                            start=(c == 0), stop=(c == DC - 1))
                    ck_t = inc("T", mm)
                    xsl = x_sb[:, mc * T + sl * 256:mc * T + sl * 256 + 256]
                    W("V", ck_t)
                    if L == 0:
                        ad = v.tensor_add(out=xsl, in0=xsl,
                                          in1=ps[:, pb * 512:pb * 512 + 256])
                    else:
                        ad = v.scalar_tensor_tensor(
                            out=xsl, in0=ps[:, pb * 512:pb * 512 + 256],
                            scalar=bpj_sb[:, mc:mc + 1], in1=xsl,
                            op0=OP.add, op1=OP.add)
                    v.drain()
                    last = inc("V", ad)
        return last

    cck = [ccin[L][0:KE].rearrange("(h p n) -> h p n", h=HP, p=128)
           for L in (0, 1)]
    ccv = [ccin[L][KE:KE + VE].rearrange("(c p n) -> c p n", c=T // 128, p=128)
           for L in (0, 1)]

    ck_xn = emit_layernorm(ck_x0)
    emit_proj_w(wk_in, 0, ck_xn, lambda a, b: cck[0][a][:, b * 512:(b + 1) * 512])
    emit_proj_v(wv_in, 0, ck_xn, lambda a, b: ccv[0][a][:, b * 512:(b + 1) * 512])
    W("G", ("YS0", cnt["YS0"]), ("YS1", cnt["YS1"]))
    ck_ag0 = inc("CC", g.collective_compute(
        "AllGather", OP.bypass, replica_groups=RG, ins=[ccin[0]], outs=[ccout[0]]))
    emit_proj_w(wq_in, 0, ck_xn, lambda a, b: qsp[a][:, b * 512:(b + 1) * 512])

    side_gens.append(wquant_gen(w1_in, D, H1, 0, t1d))
    side_gens.append(wquant_gen(w2_in, H1, D, 1, t2d))

    fs = {}
    for tt in range(NT0):
        t0 = tt * TT0
        for jh in range(JC0):
            if tt == 0 and jh == 0:
                fs["wg"] = wload("GF", wg_in, 0, D, 0, 128, wff_w, 0)
            mm = None
            for dc in range(DC):
                W("T", fs["wg"], ck_xn, fs.get("h"), fs.get("si"),
                  fs.get("hused"))
                mm = t.matmul(ps[:, 0:512],
                              wff_w[:, dc * 128:dc * 128 + 128],
                              xn_sb[:, dc * T + t0:dc * T + t0 + 512],
                              start=(dc == 0), stop=(dc == DC - 1))
            ck_g = inc("T", mm)
            if tt == 0 and jh == 0:
                fs["wu"] = wload("GU", wu_in, 0, D, 0, 128, wff_w, 1024)
            for dc in range(DC):
                W("T", fs["wu"])
                mm = t.matmul(ps[:, 512:1024],
                              wff_w[:, 1024 + dc * 128:1024 + dc * 128 + 128],
                              xn_sb[:, dc * T + t0:dc * T + t0 + 512],
                              start=(dc == 0), stop=(dc == DC - 1))
            ck_gu = inc("T", mm)
            pump_side(1)
            nj, ntt = (jh + 1, tt) if jh + 1 < JC0 else (0, tt + 1)
            if ntt < NT0:
                W("G", ck_g)
                fs["wg"] = wload("GF", wg_in, 0, D, nj * 128, 128, wff_w, 0)
                W("G", ck_gu)
                fs["wu"] = wload("GU", wu_in, 0, D, nj * 128, 128, wff_w, 1024)
            W("S", ck_gu, fs.get("h"))
            si = s.activation(out=fsc[:, 0:512], in_=ps[:, 0:512],
                              func=AF.Silu, scale=C127)
            ck_si = inc("S", si)
            fs["si"] = ck_si
            W("V", ck_si)
            hh = v.tensor_mul(out=hpool[:, jh * 512:(jh + 1) * 512],
                              in0=fsc[:, 0:512], in1=ps[:, 512:1024])
            v.drain()
            fs["h"] = inc("V", hh)
        ck_h = fs["h"]
        for mh in range(2):
            mm = None
            for jc in range(JC0):
                so = (jc % 2) * 512
                if tt == 0 and mh == 0 and jc == 0:
                    W("G", ("YS0", cnt["YS0"]), ("YS1", cnt["YS1"]))
                    fs["wd0"] = wload("GD0", wd_in, 0, 128, 0, 512, kvst, 0)
                ck_wd = fs["wd0" if jc % 2 == 0 else "wd1"]
                for m2 in range(4):
                    W("T", ck_wd, ck_h, fs.get("xadd"))
                    mm = t.matmul(
                        ps[:, m2 * 512:m2 * 512 + 512],
                        kvst[:, so + m2 * 128:so + m2 * 128 + 128],
                        hpool[:, jc * TT0:jc * TT0 + 512],
                        start=(jc == 0), stop=(jc == JC0 - 1))
                inc("T", mm)
                nj, nmh = (jc + 1, mh) if jc + 1 < JC0 else (0, mh + 1)
                ntt2 = tt if nmh < 2 else tt + 1
                if nmh == 2:
                    nmh = 0
                if ntt2 < NT0:
                    W("G", ck("T"))
                    fs["wd0" if nj % 2 == 0 else "wd1"] = wload(
                        "GD0" if nj % 2 == 0 else "GD1", wd_in, nj * 128, 128,
                        nmh * 512, 512, kvst, (nj % 2) * 512)
            ck_dn = ck("T")
            ad = None
            for m2 in range(4):
                mc = mh * 4 + m2
                xsl = x_sb[:, mc * T + t0:mc * T + t0 + 512]
                W("V", ck_dn)
                ad = v.scalar_tensor_tensor(
                    out=xsl, in0=ps[:, m2 * 512:m2 * 512 + 512],
                    scalar=CD127, in1=xsl, op0=OP.mult, op1=OP.add)
                v.drain()
            fs["xadd"] = inc("V", ad)
        fs["hused"] = fs["xadd"]

    drain_side()
    ck_attn0 = emit_attention(0, ck_ag0)
    ck_x1 = emit_oproj(0, wo_in, ck_attn0)

    W("Y", ck_x1)
    x4d = x_sb[:].rearrange("p (c j n) -> p c j n", c=DC, j=NB)
    hai3 = hai.rearrange("(c p) n -> p c n", p=128)
    for u in range(2):
        for dcq in range(DC):
            hx = y.dma_start(
                out=hai3[:, dcq, u * NB:(u + 1) * NB],
                in_=x4d[:, dcq, :, 255 * u],
                allow_slow_non_contiguous=True)
            inc("YH", hx, 16)
    ck_hai = ck("YH")
    W("G", ck_hai)
    ck_ag1a = inc("CC", g.collective_compute(
        "AllGather", OP.bypass, replica_groups=RG, ins=[hai], outs=[hao]))

    ck_xn = emit_layernorm(ck_x1)
    emit_proj_w(wqkv_in, 1024, ck_xn,
                lambda a, b: cck[1][a][:, b * 512:(b + 1) * 512])
    emit_proj_v(wqkv_in, 2048, ck_xn,
                lambda a, b: ccv[1][a][:, b * 512:(b + 1) * 512])
    W("G", ("YS0", cnt["YS0"]), ("YS1", cnt["YS1"]))
    ck_ag1 = inc("CC", g.collective_compute(
        "AllGather", OP.bypass, replica_groups=RG, ins=[ccin[1]], outs=[ccout[1]]))
    emit_proj_w(wqkv_in, 0, ck_xn, lambda a, b: qsp[a][:, b * 512:(b + 1) * 512])

    W("Y", ck_ag1a)
    c0 = y.dma_start(
        out=hcand[:, 0:DC * NB].rearrange("p (c j) -> p c j", c=DC)[:, :, 1:NB],
        in_=hao[1].rearrange("(c p) n -> p c n", p=128)[:, :, NB:2 * NB - 1],
        allow_slow_non_contiguous=True)
    inc("YH", c0, 16)
    c1 = y.dma_start(
        out=hcand[:, DC * NB:2 * DC * NB].rearrange("p (c j) -> p c j", c=DC),
        in_=hao[0].rearrange("(c p) n -> p c n", p=128)[:, :, NB:2 * NB],
        allow_slow_non_contiguous=True)
    ck_hc = inc("YH", c1, 16)
    W("V", ck_hc, ck_consts)
    v.memset(hcand[:, 0:DC * NB].rearrange("p (c j) -> p c j", c=DC)[:, :, 0:1], 0.0)
    v.drain()
    hx3 = hx_sb[:].rearrange("p (c n) -> p c n", n=HN)
    hc3a = hcand[:, 0:DC * NB].rearrange("p (c j) -> p c j", c=DC)
    hc3b = hcand[:, DC * NB:2 * DC * NB].rearrange("p (c j) -> p c j", c=DC)
    v.tensor_scalar(out=hx3[:, :, 0:NB], in0=hc3a,
                    scalar1=hsel_sb[:, 0:1], scalar2=None, op0=OP.mult)
    v.drain()
    hl = v.scalar_tensor_tensor(out=hx3[:, :, 0:NB], in0=hc3b,
                                scalar=hsel_sb[:, 1:2], in1=hx3[:, :, 0:NB],
                                op0=OP.mult, op1=OP.add)
    v.drain()
    ck_hl = inc("V", hl)
    W("Y", ck_hl)
    c0 = y.dma_start(
        out=hcand[:, 0:DC * NB].rearrange("p (c j) -> p c j", c=DC),
        in_=hao[1].rearrange("(c p) n -> p c n", p=128)[:, :, 0:NB],
        allow_slow_non_contiguous=True)
    inc("YH", c0, 16)
    c1 = y.dma_start(
        out=hcand[:, DC * NB:2 * DC * NB].rearrange(
            "p (c j) -> p c j", c=DC)[:, :, 0:NB - 1],
        in_=hao[0].rearrange("(c p) n -> p c n", p=128)[:, :, 1:NB],
        allow_slow_non_contiguous=True)
    ck_hc2 = inc("YH", c1, 16)
    W("V", ck_hc2)
    v.memset(hcand[:, DC * NB:2 * DC * NB].rearrange(
        "p (c j) -> p c j", c=DC)[:, :, NB - 1:NB], 0.0)
    v.drain()
    v.tensor_scalar(out=hx3[:, :, NB:HN], in0=hc3a,
                    scalar1=hsel_sb[:, 0:1], scalar2=None, op0=OP.mult)
    v.drain()
    hr = v.scalar_tensor_tensor(out=hx3[:, :, NB:HN], in0=hc3b,
                                scalar=hsel_sb[:, 1:2],
                                in1=hx3[:, :, NB:HN],
                                op0=OP.mult, op1=OP.add)
    v.drain()
    ck_hx = inc("V", hr)

    def hx_ap(tens, dc):
        return tens[:, dc * HN:(dc + 1) * HN]

    for side in range(2):
        o = side * NB
        hxs = [(hx_ap(hx_sb, dc)[:, o:o + NB], hx_ap(hcand, dc)[:, o:o + NB])
               for dc in range(DC)]
        ck_hxn = ln_slice(hxs, NB, ck_hx)
    am = None
    for dc in range(DC):
        W("V", ck_hxn)
        if dc == 0:
            v.tensor_copy(out=run_sb[:, 0:HN], in_=hx_ap(hcand, 0))
            v.drain()
            am = v.tensor_copy(out=s2mb[:, 0:HN], in_=hx_ap(hcand, 0))
        else:
            v.tensor_tensor(out=run_sb[:, 0:HN], in0=run_sb[:, 0:HN],
                            in1=hx_ap(hcand, dc), op=OP.max)
            v.drain()
            am = v.tensor_tensor(out=s2mb[:, 0:HN], in0=s2mb[:, 0:HN],
                                 in1=hx_ap(hcand, dc), op=OP.min)
        v.drain()
    v.tensor_scalar(out=s2mb[:, 0:HN], in0=s2mb[:, 0:HN], scalar1=-1.0,
                    scalar2=None, op0=OP.mult)
    v.drain()
    am = v.tensor_tensor(out=run_sb[:, 0:HN], in0=run_sb[:, 0:HN],
                         in1=s2mb[:, 0:HN], op=OP.max)
    v.drain()
    inc("V", am)
    W("V", ck("V"))
    zz = v.memset(run_sb[:, HN:128], 0.0)
    v.drain()
    ck_hfold = inc("V", zz)
    ck_trb = token_scales(run_sb[:, 0:128], thr_sb[:, 4:5], ck_hfold)
    W("V", ck_trb)
    v.tensor_copy(out=ROW(R_HALO, 64, HN), in_=pstr[0:1, 0:HN])
    v.drain()
    hsm = v.tensor_copy(out=ROW(R_HALO, 96, HN),
                        in_=pstr[32:33, 0:HN])
    v.drain()
    ck_hsm = inc("V", hsm)
    ck_hqb = bcast_row(R_HALO, 64, HN, bb1, 0, 5, deps=(ck_hsm,))
    mu = None
    for dc in range(DC):
        W("V", ck_hqb)
        v.tensor_mul(out=fsc[:, dc * HN:(dc + 1) * HN], in0=hx_ap(hcand, dc),
                     in1=bb1[:, 0:HN])
        v.drain()
    v.tensor_copy(out=i32a[:, 0:DC * HN], in_=fsc[:, 0:DC * HN])
    v.drain()
    mu = v.tensor_copy(out=hq_sb[:, 0:DC * HN], in_=i32a[:, 0:DC * HN])
    v.drain()
    ck_hq = inc("V", mu)

    for sl in range(NSL):
        for dc in range(DC):
            W("V", ck_xn)
            xs_ = xn_sb[:, dc * T + sl * 512:dc * T + sl * 512 + 512]
            if dc == 0:
                v.tensor_copy(out=bb2[:, 0:512], in_=xs_)
                v.drain()
                am = v.tensor_copy(out=bb1[:, 0:512], in_=xs_)
            else:
                v.tensor_tensor(out=bb2[:, 0:512], in0=bb2[:, 0:512],
                                in1=xs_, op=OP.max)
                v.drain()
                am = v.tensor_tensor(out=bb1[:, 0:512], in0=bb1[:, 0:512],
                                     in1=xs_, op=OP.min)
            v.drain()
        v.tensor_scalar(out=bb1[:, 0:512], in0=bb1[:, 0:512], scalar1=-1.0,
                        scalar2=None, op0=OP.mult)
        v.drain()
        am = v.tensor_tensor(out=bb2[:, 0:512], in0=bb2[:, 0:512],
                             in1=bb1[:, 0:512], op=OP.max)
        v.drain()
        ck_fold = inc("V", am)
        for i in range(4):
            ck_trb = token_scales(bb2[:, i * 128:(i + 1) * 128],
                                  thr_sb[:, 4:5], ck_fold)
            W("V", ck_trb)
            v.tensor_copy(out=ROW(R_S1 + sl, i * 128, 128),
                          in_=pstr[0:1, 0:128])
            v.drain()
            cp = v.tensor_copy(out=ROW(R_S1M + sl, i * 128, 128),
                               in_=pstr[32:33, 0:128])
            v.drain()
            inc("V", cp)
    ck_s1 = ck("V")

    l1 = {}
    for tt in range(NB):
        tb = tt * 256
        sl_, off_ = tb // 512, tb % 512
        ck_s1b = bcast_row(R_S1 + sl_, off_, 256, bb1, 0, 4, deps=(ck_s1,))
        xq = None
        for dc in range(DC):
            W("V", ck_s1b, ck_hq, l1.get("w1done"))
            v.tensor_mul(out=fsc[:, 0:256],
                         in0=xn_sb[:, dc * T + tb:dc * T + tb + 256],
                         in1=bb1[:, 0:256])
            v.drain()
            v.tensor_copy(out=i32a[:, 0:256], in_=fsc[:, 0:256])
            v.drain()
            v.tensor_copy(out=xq1_sb[:, dc * 258 + 1:dc * 258 + 257],
                          in_=i32a[:, 0:256])
            v.drain()
            v.tensor_copy(out=xq1_sb[:, dc * 258:dc * 258 + 1],
                          in_=hq_sb[:, dc * HN + tt:dc * HN + tt + 1])
            v.drain()
            xq = v.tensor_copy(out=xq1_sb[:, dc * 258 + 257:dc * 258 + 258],
                               in_=hq_sb[:, dc * HN + NB + tt:dc * HN + NB + tt + 1])
            v.drain()
        ck_xq1 = inc("V", xq)
        W("V", ck_s1, ck_hsm)
        v.tensor_copy(out=ROW(R_SC, 1, 256),
                      in_=ROW(R_S1M + sl_, off_, 256))
        v.drain()
        v.tensor_copy(out=ROW(R_SC, 0, 1),
                      in_=ROW(R_HALO, 96 + tt, 1))
        v.drain()
        hsr = v.tensor_copy(out=ROW(R_SC, 257, 1),
                            in_=ROW(R_HALO, 96 + NB + tt, 1))
        v.drain()
        ck_hsr = inc("V", hsr)
        ck_scb = bcast_row(R_SC, 0, 258, sc258, 0, 4, deps=(ck_hsr,))
        for jc in range(JC1):
            tslot = (jc % 2) * DC * 128
            W("Y", ("YT", cnt["YT"]), ck("T"))
            tl = y.dma_start(
                out=t1buf[:, tslot:tslot + DC * 128].rearrange(
                    "p (c n) -> p c n", c=DC),
                in_=t1d[:, :, jc * 128:(jc + 1) * 128])
            ck_t1 = inc("Y1a" if jc % 2 == 0 else "Y1b", tl, 16)
            pb = 5 + (jc % 2)
            mm = None
            for dc in range(DC):
                W("T", ck_t1, ck_xq1, l1.get(("hev", jc - 2)))
                mm = t.matmul(ps[:, pb * 512:pb * 512 + 258],
                              t1buf[:, tslot + dc * 128:tslot + dc * 128 + 128],
                              xq1_sb[:, dc * 258:(dc + 1) * 258],
                              start=(dc == 0), stop=(dc == DC - 1))
            ck_w1 = inc("T", mm)
            hb = 0
            W("V", ck_w1, ck_scb)
            he = v.tensor_mul(out=htile[:, hb:hb + 258],
                              in0=ps[:, pb * 512:pb * 512 + 258], in1=sc258[:])
            v.drain()
            l1[("hev", jc)] = inc("V", he)
            v.tensor_scalar(out=fsc[:, 0:256], in0=htile[:, hb:hb + 256],
                            scalar1=cvw_sb[:, jc * 3:jc * 3 + 1], scalar2=None,
                            op0=OP.mult)
            v.drain()
            v.scalar_tensor_tensor(out=fsc[:, 0:256], in0=htile[:, hb + 1:hb + 257],
                                   scalar=cvw_sb[:, jc * 3 + 1:jc * 3 + 2],
                                   in1=fsc[:, 0:256], op0=OP.mult, op1=OP.add)
            v.drain()
            cv3 = v.scalar_tensor_tensor(out=fsc[:, 0:256],
                                         in0=htile[:, hb + 2:hb + 258],
                                         scalar=cvw_sb[:, jc * 3 + 2:jc * 3 + 3],
                                         in1=fsc[:, 0:256], op0=OP.mult, op1=OP.add)
            v.drain()
            ck_cv = inc("V", cv3)
            W("S", ck_cv)
            si = s.activation(out=hpool[:, jc * 256:(jc + 1) * 256],
                              in_=fsc[:, 0:256], func=AF.Silu,
                              bias=cvb_sb[:, jc:jc + 1])
            ck_si = inc("S", si)
            W("V", ck_si)
            if jc == 0:
                v.tensor_copy(out=run_sb[:, 0:256], in_=hpool[:, 0:256])
                v.drain()
                am = v.tensor_copy(out=s2mb[:], in_=hpool[:, 0:256])
            else:
                v.tensor_tensor(out=run_sb[:, 0:256], in0=run_sb[:, 0:256],
                                in1=hpool[:, jc * 256:(jc + 1) * 256], op=OP.max)
                v.drain()
                am = v.tensor_tensor(out=s2mb[:], in0=s2mb[:],
                                     in1=hpool[:, jc * 256:(jc + 1) * 256],
                                     op=OP.min)
            v.drain()
            inc("V", am)
        W("V", ck("V"))
        v.tensor_scalar(out=s2mb[:], in0=s2mb[:], scalar1=-1.0,
                        scalar2=None, op0=OP.mult)
        v.drain()
        am = v.tensor_tensor(out=run_sb[:], in0=run_sb[:], in1=s2mb[:],
                             op=OP.max)
        v.drain()
        inc("V", am)
        l1["w1done"] = ck("V")
        ck_run = ck("V")
        for half in range(2):
            ck_trb = token_scales(run_sb[:, half * 128:(half + 1) * 128]
                                  if half else run_sb[:, 0:128],
                                  thr_sb[:, 5:6], ck_run)
            W("V", ck_trb)
            v.tensor_copy(out=ROW(R_S2, half * 128, 128),
                          in_=pstr[0:1, 0:128])
            v.drain()
            cp = v.tensor_copy(out=ROW(R_S2M, half * 128, 128),
                               in_=pstr[32:33, 0:128])
            v.drain()
            inc("V", cp)
        ck_s2 = ck("V")
        ck_s2b = bcast_row(R_S2, 0, 256, run_sb, 0, 2, deps=(ck_s2,))
        ck_s2mb = bcast_row(R_S2M, 0, 256, s2mb, 0, 2, deps=(ck_s2,))
        xq2 = None
        for jc in range(JC1):
            W("V", ck_s2b)
            v.tensor_mul(out=fsc[:, 0:256], in0=hpool[:, jc * 256:(jc + 1) * 256],
                         in1=run_sb[:])
            v.drain()
            v.tensor_copy(out=i32a[:, 0:256], in_=fsc[:, 0:256])
            v.drain()
            xq2 = v.tensor_copy(out=hpool[:, jc * 256:(jc + 1) * 256],
                                in_=i32a[:, 0:256])
            v.drain()
        ck_xq2 = inc("V", xq2)
        for mc in range(DC):
            pb = mc % 2
            mm = None
            for jh2 in range(2):
                W("Y", ("YT", cnt["YT"]), ck("T"))
                tl = y.dma_start(
                    out=t2buf[:].rearrange("p (c n) -> p c n", c=16),
                    in_=t2d[jh2 * 16:(jh2 + 1) * 16, :, mc * 128:(mc + 1) * 128])
                ck_t2 = inc("Y2", tl, 16)
                for jc2 in range(16):
                    jc = jh2 * 16 + jc2
                    W("T", ck_t2, ck_xq2, l1.get(("xadd", mc - 2)))
                    mm = t.matmul(ps[:, pb * 512:pb * 512 + 256],
                                  t2buf[:, jc2 * 128:jc2 * 128 + 128],
                                  hpool[:, jc * 256:(jc + 1) * 256],
                                  start=(jc == 0), stop=(jc == JC1 - 1))
                inc("T", mm)
            ck_w2 = ck("T")
            xsl = x_sb[:, mc * T + tb:mc * T + tb + 256]
            W("V", ck_w2, ck_s2mb)
            v.tensor_mul(out=fsc[:, 0:256], in0=ps[:, pb * 512:pb * 512 + 256],
                         in1=s2mb[:])
            v.drain()
            ad = v.tensor_add(out=xsl, in0=xsl, in1=fsc[:, 0:256])
            v.drain()
            l1[("xadd", mc)] = inc("V", ad)
        l1["w1done"] = ck("V")

    ck_attn1 = emit_attention(1, ck_ag1)
    ck_x2 = emit_oproj(1, wpj_in, ck_attn1)

    W("Y", ck_x2)
    for dc in range(DC):
        inc("YO", y.dma_start(out=out_d[dc * 128:(dc + 1) * 128, :],
                              in_=x_sb[:, dc * T:(dc + 1) * T]), 16)
    y.wait_ge(sems["YO"], cnt["YO"])

    blk.tensor(_replay(tapes["T"]))
    blk.vector(_replay(tapes["V"]))
    blk.scalar(_replay(tapes["S"]))
    blk.sync(_replay(tapes["Y"]))
    blk.gpsimd(_replay(tapes["G"]))

    es.close()
    return nc


def _masks(par):
    kk = np.arange(128)[:, None]
    qq = np.arange(256)[None, :]
    tri0 = (kk <= qq).astype(np.float32)
    tri1 = (kk + 128 <= qq).astype(np.float32)
    one = np.ones((128, 256), np.float32)
    zero = np.zeros((128, 256), np.float32)
    ms = [tri0, tri1, zero, zero] if par == 0 else [one, one, tri0, tri1]
    return np.concatenate(ms, axis=1)


def _stage(inputs, S):
    x = np.asarray(inputs["L0_attn_residual"])
    B = x.shape[0]
    NBG = S // 256
    C = lambda a: np.ascontiguousarray(a)

    wq = C(np.asarray(inputs["L0_attn_W_Q"]).reshape(D, D).T)
    wk = C(np.asarray(inputs["L0_attn_W_K"]).reshape(D, D).T)
    wv = C(np.asarray(inputs["L0_attn_W_V"]).reshape(D, D).T)
    wo = C(np.asarray(inputs["L0_attn_W_O"]).reshape(D, D).T)
    wg = C(np.asarray(inputs["L0_ffn_w_gate"]).T.astype(np.int8))
    wu = C(np.asarray(inputs["L0_ffn_w_up"]).T.astype(np.int8))
    wd = C(np.asarray(inputs["L0_ffn_w_down"]).T.astype(np.int8))
    wqkv = C(np.asarray(inputs["L1_attn_w_qkv"]).T)
    wpj = C(np.asarray(inputs["L1_attn_w_proj"]).T)
    bpj = C(np.asarray(inputs["L1_attn_b_proj"]).reshape(D, 1))
    w1 = C(np.asarray(inputs["L1_ffn_w1"]).T)
    w2 = C(np.asarray(inputs["L1_ffn_w2"]).T)
    cw = C(np.asarray(inputs["L1_ffn_conv_w"]).reshape(H1, 3))
    cb = C(np.asarray(inputs["L1_ffn_conv_b"]).reshape(H1, 1))
    ident = np.eye(128, dtype=np.float32)

    in_maps = []
    for core in range(2 * B):
        b, par = core // 2, core % 2
        blocks = x[b].reshape(NBG, 256, D)[par::2]
        xT = C(blocks.reshape(S // 2, D).T)
        hsel = np.zeros((128, 2), np.float32)
        hsel[:, 0 if par == 0 else 1] = 1.0
        in_maps.append({
            "xT": xT, "masks": _masks(par), "ident": ident, "hsel": hsel,
            "wqT": wq, "wkT": wk, "wvT": wv, "woT": wo,
            "wgT": wg, "wuT": wu, "wdT": wd,
            "wqkvT": wqkv, "wprojT": wpj, "bproj": bpj,
            "w1T": w1, "w2T": w2, "convw": cw, "convb": cb,
        })
    return in_maps


def kernel(**inputs):
    x = np.asarray(inputs["L0_attn_residual"])
    B, S, _ = x.shape
    nc = build_nc(S)
    in_maps = _stage(inputs, S)
    res = run_bass_kernel_spmd(nc, in_maps, core_ids=list(range(2 * B)))
    NBG = S // 256
    out = np.empty((B, S, D), np.float32)
    for core in range(2 * B):
        b, par = core // 2, core % 2
        oT = np.asarray(res.results[core]["outT"])
        out[b].reshape(NBG, 256, D)[par::2] = oT.T.reshape(NBG // 2, 256, D)
    return out


# version 08
# speedup vs baseline: 1.1710x; 1.1710x over previous
import numpy as np
import ml_dtypes
from contextlib import ExitStack

import concourse.bass as bass
import concourse.mybir as mybir
from concourse.bass_utils import run_bass_kernel_spmd
from concourse.alu_op_type import AluOpType as OP

F32 = mybir.dt.float32
BF16 = mybir.dt.bfloat16
I32 = mybir.dt.int32
I8 = mybir.dt.int8
AF = mybir.ActivationFunctionType
AX = mybir.AxisListType

D = 1024
DC = 8
HP = 8
H0 = 3072
JC0 = H0 // 128
H1 = 4096
JC1 = H1 // 128
C127 = float(np.float32(1.0 / 127.0))
CD127 = float(np.float32(1.0 / (127.0 * 127.0)))
EXP_SHIFT = -20.0
RG = [[0, 1], [2, 3], [4, 5], [6, 7]]

R_M, R_R, R_VAR, R_SD = 0, 1, 2, 3
R_REC, R_RECO = 0, 1
R_S2, R_S2M = 2, 3
R_S1, R_S1M = 4, 8
R_WM = 2
R_HALO = 1
R_SC = 0


def build_nc(S):
    T = S // 2
    NB = T // 256
    NSL = T // 512
    TT0 = 512
    NT0 = T // TT0
    HN = 2 * NB
    assert HN <= 32

    nc = bass.Bass("TRN2", target_bir_lowering=False)
    DT = nc.dram_tensor

    xT_in = DT("xT", [D, T], F32, kind="ExternalInput").ap()
    masks_in = DT("masks", [128, 1024], F32, kind="ExternalInput").ap()
    ident_in = DT("ident", [128, 128], F32, kind="ExternalInput").ap()
    hsel_in = DT("hsel", [128, 2], F32, kind="ExternalInput").ap()
    wq_in = DT("wqT", [D, D], F32, kind="ExternalInput").ap()
    wk_in = DT("wkT", [D, D], F32, kind="ExternalInput").ap()
    wv_in = DT("wvT", [D, D], F32, kind="ExternalInput").ap()
    wo_in = DT("woT", [D, D], F32, kind="ExternalInput").ap()
    wg_in = DT("wgT", [D, H0], I8, kind="ExternalInput").ap()
    wu_in = DT("wuT", [D, H0], I8, kind="ExternalInput").ap()
    wd_in = DT("wdT", [H0, D], I8, kind="ExternalInput").ap()
    wqkv_in = DT("wqkvT", [D, 3 * D], F32, kind="ExternalInput").ap()
    wpj_in = DT("wprojT", [D, D], F32, kind="ExternalInput").ap()
    bpj_in = DT("bproj", [D, 1], F32, kind="ExternalInput").ap()
    w1_in = DT("w1T", [D, H1], F32, kind="ExternalInput").ap()
    w2_in = DT("w2T", [H1, D], F32, kind="ExternalInput").ap()
    cw_in = DT("convw", [H1, 3], F32, kind="ExternalInput").ap()
    cb_in = DT("convb", [H1, 1], F32, kind="ExternalInput").ap()
    out_d = DT("outT", [D, T], F32, kind="ExternalOutput").ap()

    KE = HP * 128 * T
    VE = (T // 128) * 128 * D
    ccin = [DT(f"cc{L}i", [KE + VE], BF16, kind="Internal").ap() for L in (0, 1)]
    ccout = [DT(f"cc{L}o", [2, KE + VE], BF16, kind="Internal").ap() for L in (0, 1)]
    hai = DT("hai", [D, HN], F32, kind="Internal").ap()
    hao = DT("hao", [2, D, HN], F32, kind="Internal").ap()
    qsp = DT("qsp", [HP, 128, T], BF16, kind="Internal").ap()
    asp = DT("asp", [HP, 128, T], BF16, kind="Internal").ap()
    t1d = DT("t1d", [DC, 128, H1], BF16, kind="Internal").ap()
    t2d = DT("t2d", [JC1, 128, D], BF16, kind="Internal").ap()

    es = ExitStack()
    SB = lambda n, sh, d: es.enter_context(nc.sbuf_tensor(n, sh, d))

    x_sb = SB("x", [128, DC * T], F32)
    xn_sb = SB("xn", [128, DC * T], BF16)
    rows = SB("rows", [128, 2048], F32)
    ROW = lambda i, c0, w: rows[32 * (i % 3):32 * (i % 3) + 1,
                                (i // 3) * 512 + c0:(i // 3) * 512 + c0 + w]
    bb1 = SB("bb1", [128, 512], F32)
    bb2 = SB("bb2", [128, 512], F32)
    fsc = SB("fsc", [128, 1024], F32)
    warena = SB("warena", [128, DC * 512], BF16)
    wff_w = SB("wff_w", [128, 2 * 1024], BF16)
    kvst = SB("kvst", [128, 2 * 512], BF16)
    katt = SB("katt", [128, S], BF16)
    vatt = SB("vatt", [128, (S // 128) * 130], BF16)
    qatt = SB("qatt", [128, T], BF16)
    p_sb = SB("p_sb", [128, 2048], BF16)
    attst = SB("attst", [128, T], BF16)
    attb = SB("attb", [128, DC * 256], BF16)
    masks_sb = SB("masksb", [128, 1024], BF16)
    ident_sb = SB("identb", [128, 128], F32)
    hsel_sb = SB("hselb", [128, 2], F32)
    ones_r = SB("ones_r", [128, 128], F32)
    onesk = SB("onesk", [128, 1], F32)
    mxc = SB("mxc", [128, 64], F32)
    hpool = SB("hpool", [128, JC0 * TT0], BF16)
    xq1_sb = SB("xq1", [128, DC * 258], BF16)
    htile = SB("htile", [128, 258], F32)
    t1buf = SB("t1buf", [128, 2 * DC * 128], BF16)
    t2buf = SB("t2buf", [128, 16 * 128], BF16)
    wqf = SB("wqf", [128, 256], BF16)
    tq_sb = SB("tq", [128, 256], BF16)
    wabs = SB("wabs", [128, 4], F32)
    thr_sb = SB("thr", [128, 8], F32)
    i32a = SB("i32a", [128, 256], I32)
    sc258 = SB("sc258", [128, 258], F32)
    s2mb = SB("s2mb", [128, 256], F32)
    run_sb = SB("run_sb", [128, 256], F32)
    cvw_sb = SB("cvw", [128, 3 * JC1], F32)
    cvb_sb = SB("cvb", [128, JC1], F32)
    bpj_sb = SB("bpjb", [128, DC], F32)
    cbias = SB("cbias", [128, 4], F32)
    hx_sb = SB("hx", [128, 2 * DC * NB], F32)
    hcand = SB("hcand", [128, 2 * DC * NB], F32)
    hq_sb = SB("hq", [128, 2 * DC * NB], BF16)

    ps = es.enter_context(nc.psum_tensor("ps_all", [128, 3584], F32))
    pstr = es.enter_context(nc.psum_tensor("ps_tr", [128, 128], F32))

    DMA_FAMS = ["YX", "YC", "YS0", "YS1", "YH", "YW", "YT", "YA", "YB",
                "YP", "YO", "Y1a", "Y1b", "Y2", "GM", "GA", "GF", "GU",
                "GD0", "GD1", "GW", "CC"]
    sems, cnt = {}, {k: 0 for k in (["T", "V", "S"] + DMA_FAMS)}
    for k in cnt:
        sems[k] = es.enter_context(nc.semaphore("sem" + k))
    last_wait = {}

    blk = es.enter_context(nc.Block())

    class _RecInstr:
        __slots__ = ("incs",)

        def __init__(self):
            self.incs = []

        def then_inc(self, sem, n):
            self.incs.append((sem, n))
            return self

    class _Rec:
        def __init__(self, tape):
            self._tape = tape

        def __getattr__(self, m):
            tape = self._tape

            def call(*a, **k):
                ri = _RecInstr()
                tape.append((m, a, k, ri))
                return ri
            return call

    tapes = {k: [] for k in "TVSYG"}
    ENG = {k: _Rec(tapes[k]) for k in "TVSYG"}
    t, v, s, y, g = (ENG[k] for k in "TVSYG")

    import os as _os
    _SIM = bool(_os.environ.get("BASS_SIM_FP64"))

    def _replay(tape):
        def f(eng):
            for m, a, k, ri in tape:
                instr = getattr(eng, m)(*a, **k)
                for sem, n in ri.incs:
                    instr.then_inc(sem, n)
        return f

    def inc(tag, instr, n=1):
        instr.then_inc(sems[tag], n)
        cnt[tag] += n
        return (tag, cnt[tag])

    def ck(tag):
        return (tag, cnt[tag])

    def W(consumer, *cks):
        lw = last_wait.setdefault(consumer, {})
        best = {}
        for c in cks:
            if c is None:
                continue
            tag, val = c
            if val > best.get(tag, 0):
                best[tag] = val
        for tag, val in best.items():
            if val <= 0 or lw.get(tag, 0) >= val:
                continue
            assert val <= cnt[tag], f"forward wait {tag}:{val} > {cnt[tag]}"
            ENG[consumer].wait_ge(sems[tag], val)
            lw[tag] = val

    for dc in range(DC):
        inc("YX", y.dma_start(out=x_sb[:, dc * T:(dc + 1) * T],
                              in_=xT_in[dc * 128:(dc + 1) * 128, :]), 16)
    ck_x0 = ck("YX")
    inc("YC", y.dma_start(out=ident_sb[:], in_=ident_in), 16)
    inc("YC", y.dma_start(out=hsel_sb[:], in_=hsel_in), 16)
    inc("YC", y.dma_start(
        out=cvw_sb[:].rearrange("p (c k) -> p c k", c=JC1),
        in_=cw_in.rearrange("(c p) k -> p c k", p=128),
        allow_slow_non_contiguous=True), 16)
    inc("YC", y.dma_start(
        out=cvb_sb[:].rearrange("p (c o) -> p c o", c=JC1),
        in_=cb_in.rearrange("(c p) o -> p c o", p=128),
        allow_slow_non_contiguous=True), 16)
    inc("YC", y.dma_start(
        out=bpj_sb[:].rearrange("p (c o) -> p c o", c=DC),
        in_=bpj_in.rearrange("(c p) o -> p c o", p=128),
        allow_slow_non_contiguous=True), 16)
    ck_consts = ck("YC")
    ck_masks = inc("GM", g.dma_start(out=masks_sb[:], in_=masks_in), 16)

    W("V", ck_consts)
    v.memset(ones_r[:], 1.0)
    v.memset(onesk[:], 1.0)
    v.memset(cbias[:, 0:1], 1e-5)
    v.memset(cbias[:, 1:2], 1.0)
    ck_ones = inc("V", v.memset(cbias[:, 2:3], EXP_SHIFT))

    def bcast_row(slot, c0, n, dst_sb, dst_off, pbank, deps=()):
        b = 32 * (slot % 3)
        W("T", ck_ones, ck("V"), *deps)
        mm = t.matmul(ps[:, pbank * 512:pbank * 512 + n], ones_r[b:b + 1, 0:128],
                      ROW(slot, c0, n), start=True, stop=True)
        ckt = inc("T", mm)
        W("V", ckt)
        cp = v.tensor_copy(out=dst_sb[:, dst_off:dst_off + n],
                           in_=ps[:, pbank * 512:pbank * 512 + n])
        v.drain()
        return inc("V", cp)

    def wload(fam, dram_ap, row0, rows_, col0, ncols, dst, dst_off):
        ins = g.dma_start(
            out=dst[:, dst_off:dst_off + (rows_ // 128) * ncols].rearrange(
                "p (c n) -> p c n", c=rows_ // 128),
            in_=dram_ap[row0:row0 + rows_, col0:col0 + ncols].rearrange(
                "(c p) n -> p c n", p=128))
        return inc(fam, ins, 16)

    def ln_slice(xs_aps, w_, dep):
        mm2 = None
        W("T", ck("V"))
        for dc in range(DC):
            xap = xs_aps[dc][0]
            off = (dc % 2) * 512
            W("V", dep, ck("T"))
            sq = v.tensor_mul(out=fsc[:, off:off + w_], in0=xap, in1=xap)
            v.drain()
            cksq = inc("V", sq)
            W("T", ck_ones, cksq)
            t.matmul(ps[0:1, 3 * 512:3 * 512 + w_], onesk[:], xap,
                     start=(dc == 0), stop=(dc == DC - 1))
            mm2 = t.matmul(ps[0:1, 4 * 512:4 * 512 + w_], onesk[:],
                           fsc[:, off:off + w_],
                           start=(dc == 0), stop=(dc == DC - 1))
            inc("T", mm2)
        ck_st = ck("T")
        W("V", ck_st)
        v.tensor_scalar(out=ROW(R_M, 0, w_),
                        in0=ps[0:1, 3 * 512:3 * 512 + w_],
                        scalar1=1.0 / D, scalar2=None, op0=OP.mult)
        v.drain()
        v.tensor_mul(out=ROW(R_VAR, 0, w_), in0=ROW(R_M, 0, w_),
                     in1=ROW(R_M, 0, w_))
        v.drain()
        va = v.scalar_tensor_tensor(out=ROW(R_VAR, 0, w_),
                                    in0=ps[0:1, 4 * 512:4 * 512 + w_],
                                    scalar=1.0 / D, in1=ROW(R_VAR, 0, w_),
                                    op0=OP.mult, op1=OP.subtract)
        v.drain()
        ck_var = inc("V", va)
        W("S", ck_var)
        sq2 = s.activation(out=ROW(R_SD, 0, w_),
                           in_=ROW(R_VAR, 0, w_), func=AF.Sqrt,
                           bias=cbias[0:1, 0:1])
        ck_sd = inc("S", sq2)
        W("V", ck_sd)
        rc = v.reciprocal(out=ROW(R_R, 0, w_), in_=ROW(R_SD, 0, w_))
        v.drain()
        ck_rows = inc("V", rc)
        ck_mb = bcast_row(R_M, 0, w_, bb1, 0, 5, deps=(ck_rows,))
        ck_rb = bcast_row(R_R, 0, w_, bb2, 0, 6, deps=(ck_rows,))
        mu = None
        for dc in range(DC):
            xap, oap = xs_aps[dc]
            W("V", ck_mb, ck_rb)
            v.tensor_sub(out=fsc[:, 0:w_], in0=xap, in1=bb1[:, 0:w_])
            v.drain()
            mu = v.tensor_mul(out=oap, in0=fsc[:, 0:w_], in1=bb2[:, 0:w_])
            v.drain()
        return inc("V", mu)

    def emit_layernorm(dep):
        outs = []
        for sl in range(NSL):
            c0 = sl * 512
            xs = [(x_sb[:, dc * T + c0:dc * T + c0 + 512],
                   xn_sb[:, dc * T + c0:dc * T + c0 + 512]) for dc in range(DC)]
            outs.append(ln_slice(xs, 512, dep))
        return outs[-1]

    proj_state = {"dma": [], "slot": 0, "evac": []}

    def proj_evac(ck_t, pb, dst_ap):
        sl = proj_state["slot"]
        so = (sl % 2) * 512
        W("V", ck_t)
        if sl >= 2:
            W("V", proj_state["dma"][sl - 2])
        cp = v.tensor_copy(out=kvst[:, so:so + 512],
                           in_=ps[:, pb * 512:pb * 512 + 512])
        v.drain()
        ck_v = inc("V", cp)
        proj_state["evac"].append(ck_v)
        W("Y", ck_v, ("YA", cnt["YA"]))
        dd = y.dma_start(out=dst_ap, in_=kvst[:, so:so + 512])
        proj_state["dma"].append(inc("YS0" if sl % 2 == 0 else "YS1", dd, 16))
        proj_state["slot"] += 1
        pump_side(2)

    def emit_proj_w(w_dram, wcol0, dep_xn, dst_fn):
        for half in range(2):
            W("G", ck("T"))
            ck_w = wload("GA", w_dram, 0, D, wcol0 + half * 512, 512, warena, 0)
            for a2 in range(4):
                a = half * 4 + a2
                for b in range(NSL):
                    pb = 5 + (proj_state["slot"] % 2)
                    mm = None
                    ev = proj_state["evac"]
                    for dc in range(DC):
                        W("T", ck_w, dep_xn,
                          ev[-2] if len(ev) >= 2 else None)
                        mm = t.matmul(
                            ps[:, pb * 512:pb * 512 + 512],
                            warena[:, dc * 512 + a2 * 128:dc * 512 + a2 * 128 + 128],
                            xn_sb[:, dc * T + b * 512:dc * T + b * 512 + 512],
                            start=(dc == 0), stop=(dc == DC - 1))
                    proj_evac(inc("T", mm), pb, dst_fn(a, b))

    def emit_proj_v(w_dram, wcol0, dep_xn, dst_fn):
        for b in range(2):
            W("G", ck("T"))
            ck_w = wload("GA", w_dram, 0, D, wcol0 + b * 512, 512, warena, 0)
            for a in range(T // 128):
                pb = 5 + (proj_state["slot"] % 2)
                mm = None
                ev = proj_state["evac"]
                for dc in range(DC):
                    W("T", ck_w, dep_xn,
                      ev[-2] if len(ev) >= 2 else None)
                    mm = t.matmul(
                        ps[:, pb * 512:pb * 512 + 512],
                        xn_sb[:, dc * T + a * 128:dc * T + a * 128 + 128],
                        warena[:, dc * 512:dc * 512 + 512],
                        start=(dc == 0), stop=(dc == DC - 1))
                proj_evac(inc("T", mm), pb, dst_fn(a, b))

    def token_scales(fold_ap, mcol_ap, dep):
        W("T", dep, ck_consts, ck("V"))
        tr = t.transpose(pstr[:, :], fold_ap, ident_sb[:])
        ck_tr = inc("T", tr)
        W("V", ck_tr)
        v.reduce_max(out=mxc[:, 34:35], in_=pstr[:, :], axis=AX.X)
        v.drain()
        v.tensor_scalar(out=mxc[:, 35:36], in0=mxc[:, 34:35], scalar1=1e-5,
                        scalar2=None, op0=OP.max)
        v.drain()
        v.reciprocal(out=mxc[:, 36:37], in_=mxc[:, 35:36])
        v.drain()
        v.tensor_scalar(out=mxc[:, 0:1], in0=mxc[:, 36:37], scalar1=127.0,
                        scalar2=None, op0=OP.mult)
        v.drain()
        sm = v.tensor_scalar(out=mxc[:, 32:33], in0=mxc[:, 35:36], scalar1=mcol_ap,
                             scalar2=C127, op0=OP.mult, op1=OP.mult)
        v.drain()
        ck_mx = inc("V", sm)
        W("T", ck_mx)
        tr2 = t.transpose(pstr[0:64, :], mxc[:, :], ident_sb[:])
        return inc("T", tr2)

    def wquant_gen(dram_ap, rows_, ncol, col, dst_dram):
        prev_v = None
        first = True
        started = False
        ck_acc = None
        for c in range(rows_ // 128):
            for p0 in range(0, ncol, 256):
                if not started:
                    W("G", ("GW", cnt["GW"]), ("V", cnt["V"]))
                    started = True
                W("G", prev_v)
                ld = inc("GW", g.dma_start(
                    out=wqf[:], in_=dram_ap[c * 128:(c + 1) * 128, p0:p0 + 256]), 16)
                W("V", ld)
                v.reduce_sum(out=wabs[:, 2:3], in_=wqf[:], axis=AX.X,
                             apply_absolute_value=True)
                v.drain()
                if first:
                    cp = v.tensor_copy(out=wabs[:, 0:1], in_=wabs[:, 2:3])
                    first = False
                else:
                    cp = v.tensor_add(out=wabs[:, 0:1], in0=wabs[:, 0:1],
                                      in1=wabs[:, 2:3])
                v.drain()
                ck_acc = inc("V", cp)
                prev_v = ck_acc
                yield
        W("T", ck_acc, ck_ones)
        mm = t.matmul(pstr[0:1, 0:1], onesk[:], wabs[:, 0:1],
                      start=True, stop=True)
        ck_t = inc("T", mm)
        W("V", ck_t)
        v.tensor_scalar(out=ROW(R_WM, col, 1), in0=pstr[0:1, 0:1],
                        scalar1=1.0 / (rows_ * ncol), scalar2=None, op0=OP.mult)
        v.drain()
        th = v.tensor_scalar(out=ROW(R_WM, col + 2, 1), in0=ROW(R_WM, col, 1),
                             scalar1=0.5, scalar2=None, op0=OP.mult)
        ck_m = inc("V", th)
        W("T", ck_m)
        wb = 32 * (R_WM % 3)
        mm = t.matmul(pstr[:, 0:1], ones_r[wb:wb + 1, 0:128],
                      ROW(R_WM, col + 2, 1), start=True, stop=True)
        ck_t = inc("T", mm)
        W("V", ck_t)
        v.tensor_copy(out=thr_sb[:, 2 * col:2 * col + 1],
                      in_=pstr[:, 0:1])
        v.drain()
        v.tensor_scalar(out=thr_sb[:, 2 * col + 1:2 * col + 2],
                        in0=thr_sb[:, 2 * col:2 * col + 1], scalar1=-1.0,
                        scalar2=None, op0=OP.mult)
        v.drain()
        th2 = v.tensor_scalar(out=thr_sb[:, 4 + col:5 + col],
                              in0=thr_sb[:, 2 * col:2 * col + 1], scalar1=2.0,
                              scalar2=None, op0=OP.mult)
        v.drain()
        ck_thr = inc("V", th2)
        yield
        for c in range(rows_ // 128):
            for p0 in range(0, ncol, 256):
                W("G", prev_v)
                ld = inc("GW", g.dma_start(
                    out=wqf[:], in_=dram_ap[c * 128:(c + 1) * 128, p0:p0 + 256]), 16)
                W("V", ld, ck_thr)
                if tern_dma:
                    W("V", tern_dma[-1])
                v.tensor_scalar(out=run_sb[:], in0=wqf[:],
                                scalar1=thr_sb[:, 2 * col + 1:2 * col + 2],
                                scalar2=None, op0=OP.is_lt)
                v.drain()
                v.tensor_scalar(out=tq_sb[:], in0=wqf[:],
                                scalar1=thr_sb[:, 2 * col:2 * col + 1],
                                scalar2=None, op0=OP.is_gt)
                v.drain()
                sb2 = v.tensor_sub(out=tq_sb[:], in0=tq_sb[:], in1=run_sb[:])
                v.drain()
                ck_q = inc("V", sb2)
                prev_v = ck_q
                W("Y", ck_q)
                dd = y.dma_start(out=dst_dram[c][:, p0:p0 + 256], in_=tq_sb[:])
                tern_dma.append(inc("YT", dd, 16))
                yield

    side_gens = []

    def pump_side(n=1):
        for _ in range(n):
            while side_gens:
                try:
                    next(side_gens[0])
                    break
                except StopIteration:
                    side_gens.pop(0)
            else:
                return

    def drain_side():
        while side_gens:
            pump_side()

    tern_dma = []


    def emit_attention(L, ck_ag):
        cc = ccout[L]
        st = {"v0": ck("V")}
        prev_tail = None
        for hp in range(HP):
            W("Y", ck_ag, prev_tail, ck("T") if hp else None,
              ("YS0", cnt["YS0"]), ("YS1", cnt["YS1"]))
            kld = []
            for r in (0, 1):
                srck = cc[r][0:KE].rearrange("(h p n) -> h p n", h=HP, p=128)[hp]
                dd = y.dma_start(
                    out=katt[:].rearrange("p (j q) -> p j q",
                                          q=512)[:, :, 256 * r:256 * r + 256],
                    in_=srck.rearrange("p (j n) -> p j n", n=256))
                kld.append(inc("YA", dd, 16))
                srcv = cc[r][KE:KE + VE].rearrange("(c p n) -> c p n",
                                                   c=T // 128, p=128)
                for u in range(2):
                    for hd in range(2):
                        off = 260 * r + 130 * u + 65 * hd
                        dd = y.dma_start(
                            out=vatt[:].rearrange(
                                "p (j q) -> p j q", q=520)[:, :, off:off + 64],
                            in_=srcv[u::2, :, hp * 128 + hd * 64:
                                     hp * 128 + hd * 64 + 64].rearrange(
                                "c p n -> p c n"))
                        kld.append(inc("YA", dd, 16))
            dd = y.dma_start(out=qatt[:], in_=qsp[hp])
            kld.append(inc("YA", dd, 16))
            W("V", *kld, prev_tail)
            ckone = inc("V", v.memset(
                vatt[:].rearrange("p (c f) -> p c f", f=130)[:, :, 64:130:65], 1.0))
            for j in range(NB):
                q0 = j * 256
                for bt in range(j + 1):
                    for hd in range(2):
                        bko = hd * 1024
                        po = hd * 1024
                        mm = None
                        for i in range(4):
                            kb = 4 * bt + i
                            W("T", *kld, ckone, st.get(("exp", hd)),
                              st.get("avfree") if bt == 0 else None)
                            t.ldweights(
                                katt[64 * hd:64 * hd + 64, kb * 128:kb * 128 + 128],
                                tile_position=(64 * hd, 0))
                            mm = t.matmul(
                                ps[:, bko + i * 256:bko + i * 256 + 256],
                                katt[64 * hd:64 * hd + 64, kb * 128:kb * 128 + 128],
                                qatt[64 * hd:64 * hd + 64, q0:q0 + 256],
                                start=True, stop=True, tile_position=(64 * hd, 0))
                        ck_sc = inc("T", mm)
                        if L == 0:
                            W("S", ck_sc, st.get(("pfree", hd)), st["v0"])
                            e2 = s.activation(out=p_sb[:, po:po + 1024],
                                              in_=ps[:, bko:bko + 1024],
                                              func=AF.Exp, scale=0.125,
                                              bias=cbias[:, 2:3])
                            ck_e = inc("S", e2)
                        else:
                            W("S", ck_sc, st.get(("pfree", hd)), st["v0"])
                            s.activation(out=p_sb[:, po:po + 1024],
                                         in_=ps[:, bko:bko + 1024],
                                         func=AF.Exp, scale=0.125)
                            s.drain()
                            e2 = s.activation(out=p_sb[:, po:po + 1024],
                                              in_=p_sb[:, po:po + 1024],
                                              func=AF.Ln, bias=cbias[:, 1:2])
                            ck_ln = inc("S", e2)
                            W("V", ck_ln)
                            sq = v.tensor_mul(out=p_sb[:, po:po + 1024],
                                              in0=p_sb[:, po:po + 1024],
                                              in1=p_sb[:, po:po + 1024])
                            v.drain()
                            ck_e = inc("V", sq)
                        st[("exp", hd)] = ck_e
                        if bt == j:
                            W("V", ck_e, ck_masks)
                            mk = v.tensor_mul(out=p_sb[:, po:po + 1024],
                                              in0=p_sb[:, po:po + 1024],
                                              in1=masks_sb[:])
                            v.drain()
                            ck_e = inc("V", mk)
                            st[("exp", hd)] = ck_e
                        mm = None
                        for i in range(4):
                            kb = 4 * bt + i
                            W("T", ck_e,
                              st.get("avfree") if bt == 0 else None)
                            mm = t.matmul(
                                ps[0:65, (4 + hd) * 512:(4 + hd) * 512 + 256],
                                vatt[:, kb * 130 + hd * 65:kb * 130 + hd * 65 + 65],
                                p_sb[:, po + i * 256:po + i * 256 + 256],
                                start=(bt == 0 and i == 0),
                                stop=(bt == j and i == 3))
                        st[("pfree", hd)] = inc("T", mm)
                        pump_side(1)
                ck_av = st[("pfree", 1)]
                W("V", ck_av)
                v.tensor_copy(out=ROW(R_REC, 0, 256),
                              in_=ps[64:65, 2048:2304])
                v.drain()
                v.tensor_copy(out=ROW(R_REC, 256, 256),
                              in_=ps[64:65, 2560:2816])
                v.drain()
                if L == 1:
                    v.tensor_scalar(out=ROW(R_REC, 0, 512),
                                    in0=ROW(R_REC, 0, 512),
                                    scalar1=1e-9, scalar2=None, op0=OP.add)
                    v.drain()
                rc = v.reciprocal(out=ROW(R_RECO, 0, 512),
                                  in_=ROW(R_REC, 0, 512))
                v.drain()
                ck_rc = inc("V", rc)
                W("T", ck_rc)
                rb = 32 * (R_RECO % 3)
                t.matmul(ps[0:64, 3072:3328], ones_r[rb:rb + 1, 0:64],
                         ROW(R_RECO, 0, 256), start=True, stop=True,
                         tile_position=(rb, 0))
                mm = t.matmul(ps[64:128, 3072:3328], ones_r[rb:rb + 1, 0:64],
                              ROW(R_RECO, 256, 256), start=True,
                              stop=True, tile_position=(rb, 64))
                ck_bc = inc("T", mm)
                if L == 0:
                    W("S", ck_av)
                    s.copy(out=attst[0:64, q0:q0 + 256], in_=ps[0:64, 2048:2304])
                    s.drain()
                    cpk = s.copy(out=attst[64:128, q0:q0 + 256],
                                 in_=ps[0:64, 2560:2816])
                    s.drain()
                    ck_cp = inc("S", cpk)
                else:
                    W("V", ck_av)
                    v.tensor_copy(out=attst[0:64, q0:q0 + 256],
                                  in_=ps[0:64, 2048:2304])
                    v.drain()
                    cpk = v.tensor_copy(out=attst[64:128, q0:q0 + 256],
                                        in_=ps[0:64, 2560:2816])
                    v.drain()
                    ck_cp = inc("V", cpk)
                W("V", ck_bc, ck_cp)
                v.tensor_mul(out=attst[0:64, q0:q0 + 256],
                             in0=attst[0:64, q0:q0 + 256], in1=ps[0:64, 3072:3328])
                v.drain()
                mu = v.tensor_mul(out=attst[64:128, q0:q0 + 256],
                                  in0=attst[64:128, q0:q0 + 256],
                                  in1=ps[64:128, 3072:3328])
                v.drain()
                st["avread"] = inc("V", mu)
                st["avfree"] = st["avread"]
            W("Y", st["avread"], ("YB", cnt["YB"]))
            dd = y.dma_start(out=asp[hp], in_=attst[:])
            prev_tail = inc("YP", dd, 16)
        return prev_tail

    def emit_oproj(L, w_dram, dep_att):
        last = None
        for half in range(2):
            W("G", ck("T"))
            ck_w = wload("GA", w_dram, 0, D, half * 512, 512, warena, 0)
            for sl in range(T // 256):
                W("Y", dep_att, ck("T") if (half or sl) else None)
                dd = y.dma_start(
                    out=attb[:].rearrange("p (c n) -> p c n", n=256),
                    in_=asp[:, :, sl * 256:sl * 256 + 256].rearrange(
                        "c p n -> p c n"))
                abl = [inc("YB", dd, 16)]
                for m2 in range(4):
                    mc = half * 4 + m2
                    pb = 5 + ((sl * 4 + m2) % 2)
                    mm = None
                    for c in range(DC):
                        W("T", ck_w, *abl, last)
                        mm = t.matmul(
                            ps[:, pb * 512:pb * 512 + 256],
                            warena[:, c * 512 + m2 * 128:c * 512 + m2 * 128 + 128],
                            attb[:, c * 256:(c + 1) * 256],
                            start=(c == 0), stop=(c == DC - 1))
                    ck_t = inc("T", mm)
                    xsl = x_sb[:, mc * T + sl * 256:mc * T + sl * 256 + 256]
                    W("V", ck_t)
                    if L == 0:
                        ad = v.tensor_add(out=xsl, in0=xsl,
                                          in1=ps[:, pb * 512:pb * 512 + 256])
                    else:
                        ad = v.scalar_tensor_tensor(
                            out=xsl, in0=ps[:, pb * 512:pb * 512 + 256],
                            scalar=bpj_sb[:, mc:mc + 1], in1=xsl,
                            op0=OP.add, op1=OP.add)
                    v.drain()
                    last = inc("V", ad)
        return last

    cck = [ccin[L][0:KE].rearrange("(h p n) -> h p n", h=HP, p=128)
           for L in (0, 1)]
    ccv = [ccin[L][KE:KE + VE].rearrange("(c p n) -> c p n", c=T // 128, p=128)
           for L in (0, 1)]

    ck_xn = emit_layernorm(ck_x0)
    emit_proj_w(wk_in, 0, ck_xn, lambda a, b: cck[0][a][:, b * 512:(b + 1) * 512])
    emit_proj_v(wv_in, 0, ck_xn, lambda a, b: ccv[0][a][:, b * 512:(b + 1) * 512])
    W("G", ("YS0", cnt["YS0"]), ("YS1", cnt["YS1"]))
    ck_ag0 = inc("CC", g.collective_compute(
        "AllGather", OP.bypass, replica_groups=RG, ins=[ccin[0]], outs=[ccout[0]]))
    emit_proj_w(wq_in, 0, ck_xn, lambda a, b: qsp[a][:, b * 512:(b + 1) * 512])

    side_gens.append(wquant_gen(w1_in, D, H1, 0, t1d))
    side_gens.append(wquant_gen(w2_in, H1, D, 1, t2d))

    fs = {}
    for tt in range(NT0):
        t0 = tt * TT0
        for jh in range(JC0):
            if tt == 0 and jh == 0:
                fs["wg"] = wload("GF", wg_in, 0, D, 0, 128, wff_w, 0)
            mm = None
            for dc in range(DC):
                W("T", fs["wg"], ck_xn, fs.get("h"), fs.get("si"),
                  fs.get("hused"))
                mm = t.matmul(ps[:, 0:512],
                              wff_w[:, dc * 128:dc * 128 + 128],
                              xn_sb[:, dc * T + t0:dc * T + t0 + 512],
                              start=(dc == 0), stop=(dc == DC - 1))
            ck_g = inc("T", mm)
            if tt == 0 and jh == 0:
                fs["wu"] = wload("GU", wu_in, 0, D, 0, 128, wff_w, 1024)
            for dc in range(DC):
                W("T", fs["wu"])
                mm = t.matmul(ps[:, 512:1024],
                              wff_w[:, 1024 + dc * 128:1024 + dc * 128 + 128],
                              xn_sb[:, dc * T + t0:dc * T + t0 + 512],
                              start=(dc == 0), stop=(dc == DC - 1))
            ck_gu = inc("T", mm)
            pump_side(1)
            nj, ntt = (jh + 1, tt) if jh + 1 < JC0 else (0, tt + 1)
            if ntt < NT0:
                W("G", ck_g)
                fs["wg"] = wload("GF", wg_in, 0, D, nj * 128, 128, wff_w, 0)
                W("G", ck_gu)
                fs["wu"] = wload("GU", wu_in, 0, D, nj * 128, 128, wff_w, 1024)
            W("S", ck_gu, fs.get("h"))
            si = s.activation(out=fsc[:, 0:512], in_=ps[:, 0:512],
                              func=AF.Silu, scale=C127)
            ck_si = inc("S", si)
            fs["si"] = ck_si
            W("V", ck_si)
            hh = v.tensor_mul(out=hpool[:, jh * 512:(jh + 1) * 512],
                              in0=fsc[:, 0:512], in1=ps[:, 512:1024])
            v.drain()
            fs["h"] = inc("V", hh)
        ck_h = fs["h"]
        for mh in range(2):
            mm = None
            for jc in range(JC0):
                so = (jc % 2) * 512
                if tt == 0 and mh == 0 and jc == 0:
                    W("G", ("YS0", cnt["YS0"]), ("YS1", cnt["YS1"]))
                    fs["wd0"] = wload("GD0", wd_in, 0, 128, 0, 512, kvst, 0)
                ck_wd = fs["wd0" if jc % 2 == 0 else "wd1"]
                for m2 in range(4):
                    W("T", ck_wd, ck_h, fs.get("xadd"))
                    mm = t.matmul(
                        ps[:, m2 * 512:m2 * 512 + 512],
                        kvst[:, so + m2 * 128:so + m2 * 128 + 128],
                        hpool[:, jc * TT0:jc * TT0 + 512],
                        start=(jc == 0), stop=(jc == JC0 - 1))
                inc("T", mm)
                nj, nmh = (jc + 1, mh) if jc + 1 < JC0 else (0, mh + 1)
                ntt2 = tt if nmh < 2 else tt + 1
                if nmh == 2:
                    nmh = 0
                if ntt2 < NT0:
                    W("G", ck("T"))
                    fs["wd0" if nj % 2 == 0 else "wd1"] = wload(
                        "GD0" if nj % 2 == 0 else "GD1", wd_in, nj * 128, 128,
                        nmh * 512, 512, kvst, (nj % 2) * 512)
            ck_dn = ck("T")
            ad = None
            for m2 in range(4):
                mc = mh * 4 + m2
                xsl = x_sb[:, mc * T + t0:mc * T + t0 + 512]
                W("V", ck_dn)
                ad = v.scalar_tensor_tensor(
                    out=xsl, in0=ps[:, m2 * 512:m2 * 512 + 512],
                    scalar=CD127, in1=xsl, op0=OP.mult, op1=OP.add)
                v.drain()
            fs["xadd"] = inc("V", ad)
        fs["hused"] = fs["xadd"]

    ck_attn0 = emit_attention(0, ck_ag0)
    drain_side()
    ck_x1 = emit_oproj(0, wo_in, ck_attn0)

    W("Y", ck_x1)
    x4d = x_sb[:].rearrange("p (c j n) -> p c j n", c=DC, j=NB)
    hai3 = hai.rearrange("(c p) n -> p c n", p=128)
    for u in range(2):
        for dcq in range(DC):
            hx = y.dma_start(
                out=hai3[:, dcq, u * NB:(u + 1) * NB],
                in_=x4d[:, dcq, :, 255 * u],
                allow_slow_non_contiguous=True)
            inc("YH", hx, 16)
    ck_hai = ck("YH")
    W("G", ck_hai)
    ck_ag1a = inc("CC", g.collective_compute(
        "AllGather", OP.bypass, replica_groups=RG, ins=[hai], outs=[hao]))

    ck_xn = emit_layernorm(ck_x1)
    emit_proj_w(wqkv_in, 1024, ck_xn,
                lambda a, b: cck[1][a][:, b * 512:(b + 1) * 512])
    emit_proj_v(wqkv_in, 2048, ck_xn,
                lambda a, b: ccv[1][a][:, b * 512:(b + 1) * 512])
    W("G", ("YS0", cnt["YS0"]), ("YS1", cnt["YS1"]))
    ck_ag1 = inc("CC", g.collective_compute(
        "AllGather", OP.bypass, replica_groups=RG, ins=[ccin[1]], outs=[ccout[1]]))
    emit_proj_w(wqkv_in, 0, ck_xn, lambda a, b: qsp[a][:, b * 512:(b + 1) * 512])

    W("Y", ck_ag1a)
    c0 = y.dma_start(
        out=hcand[:, 0:DC * NB].rearrange("p (c j) -> p c j", c=DC)[:, :, 1:NB],
        in_=hao[1].rearrange("(c p) n -> p c n", p=128)[:, :, NB:2 * NB - 1],
        allow_slow_non_contiguous=True)
    inc("YH", c0, 16)
    c1 = y.dma_start(
        out=hcand[:, DC * NB:2 * DC * NB].rearrange("p (c j) -> p c j", c=DC),
        in_=hao[0].rearrange("(c p) n -> p c n", p=128)[:, :, NB:2 * NB],
        allow_slow_non_contiguous=True)
    ck_hc = inc("YH", c1, 16)
    W("V", ck_hc, ck_consts)
    v.memset(hcand[:, 0:DC * NB].rearrange("p (c j) -> p c j", c=DC)[:, :, 0:1], 0.0)
    v.drain()
    hx3 = hx_sb[:].rearrange("p (c n) -> p c n", n=HN)
    hc3a = hcand[:, 0:DC * NB].rearrange("p (c j) -> p c j", c=DC)
    hc3b = hcand[:, DC * NB:2 * DC * NB].rearrange("p (c j) -> p c j", c=DC)
    v.tensor_scalar(out=hx3[:, :, 0:NB], in0=hc3a,
                    scalar1=hsel_sb[:, 0:1], scalar2=None, op0=OP.mult)
    v.drain()
    hl = v.scalar_tensor_tensor(out=hx3[:, :, 0:NB], in0=hc3b,
                                scalar=hsel_sb[:, 1:2], in1=hx3[:, :, 0:NB],
                                op0=OP.mult, op1=OP.add)
    v.drain()
    ck_hl = inc("V", hl)
    W("Y", ck_hl)
    c0 = y.dma_start(
        out=hcand[:, 0:DC * NB].rearrange("p (c j) -> p c j", c=DC),
        in_=hao[1].rearrange("(c p) n -> p c n", p=128)[:, :, 0:NB],
        allow_slow_non_contiguous=True)
    inc("YH", c0, 16)
    c1 = y.dma_start(
        out=hcand[:, DC * NB:2 * DC * NB].rearrange(
            "p (c j) -> p c j", c=DC)[:, :, 0:NB - 1],
        in_=hao[0].rearrange("(c p) n -> p c n", p=128)[:, :, 1:NB],
        allow_slow_non_contiguous=True)
    ck_hc2 = inc("YH", c1, 16)
    W("V", ck_hc2)
    v.memset(hcand[:, DC * NB:2 * DC * NB].rearrange(
        "p (c j) -> p c j", c=DC)[:, :, NB - 1:NB], 0.0)
    v.drain()
    v.tensor_scalar(out=hx3[:, :, NB:HN], in0=hc3a,
                    scalar1=hsel_sb[:, 0:1], scalar2=None, op0=OP.mult)
    v.drain()
    hr = v.scalar_tensor_tensor(out=hx3[:, :, NB:HN], in0=hc3b,
                                scalar=hsel_sb[:, 1:2],
                                in1=hx3[:, :, NB:HN],
                                op0=OP.mult, op1=OP.add)
    v.drain()
    ck_hx = inc("V", hr)

    def hx_ap(tens, dc):
        return tens[:, dc * HN:(dc + 1) * HN]

    for side in range(2):
        o = side * NB
        hxs = [(hx_ap(hx_sb, dc)[:, o:o + NB], hx_ap(hcand, dc)[:, o:o + NB])
               for dc in range(DC)]
        ck_hxn = ln_slice(hxs, NB, ck_hx)
    am = None
    for dc in range(DC):
        W("V", ck_hxn)
        if dc == 0:
            v.tensor_copy(out=run_sb[:, 0:HN], in_=hx_ap(hcand, 0))
            v.drain()
            am = v.tensor_copy(out=s2mb[:, 0:HN], in_=hx_ap(hcand, 0))
        else:
            v.tensor_tensor(out=run_sb[:, 0:HN], in0=run_sb[:, 0:HN],
                            in1=hx_ap(hcand, dc), op=OP.max)
            v.drain()
            am = v.tensor_tensor(out=s2mb[:, 0:HN], in0=s2mb[:, 0:HN],
                                 in1=hx_ap(hcand, dc), op=OP.min)
        v.drain()
    v.tensor_scalar(out=s2mb[:, 0:HN], in0=s2mb[:, 0:HN], scalar1=-1.0,
                    scalar2=None, op0=OP.mult)
    v.drain()
    am = v.tensor_tensor(out=run_sb[:, 0:HN], in0=run_sb[:, 0:HN],
                         in1=s2mb[:, 0:HN], op=OP.max)
    v.drain()
    inc("V", am)
    W("V", ck("V"))
    zz = v.memset(run_sb[:, HN:128], 0.0)
    v.drain()
    ck_hfold = inc("V", zz)
    ck_trb = token_scales(run_sb[:, 0:128], thr_sb[:, 4:5], ck_hfold)
    W("V", ck_trb)
    v.tensor_copy(out=ROW(R_HALO, 64, HN), in_=pstr[0:1, 0:HN])
    v.drain()
    hsm = v.tensor_copy(out=ROW(R_HALO, 96, HN),
                        in_=pstr[32:33, 0:HN])
    v.drain()
    ck_hsm = inc("V", hsm)
    ck_hqb = bcast_row(R_HALO, 64, HN, bb1, 0, 5, deps=(ck_hsm,))
    mu = None
    for dc in range(DC):
        W("V", ck_hqb)
        v.tensor_mul(out=i32a[:, dc * HN:(dc + 1) * HN], in0=hx_ap(hcand, dc),
                     in1=bb1[:, 0:HN])
        v.drain()
    mu = v.tensor_copy(out=hq_sb[:, 0:DC * HN], in_=i32a[:, 0:DC * HN])
    v.drain()
    ck_hq = inc("V", mu)

    for sl in range(NSL):
        for dc in range(DC):
            W("V", ck_xn)
            xs_ = xn_sb[:, dc * T + sl * 512:dc * T + sl * 512 + 512]
            if dc == 0:
                v.tensor_copy(out=bb2[:, 0:512], in_=xs_)
                v.drain()
                am = v.tensor_copy(out=bb1[:, 0:512], in_=xs_)
            else:
                v.tensor_tensor(out=bb2[:, 0:512], in0=bb2[:, 0:512],
                                in1=xs_, op=OP.max)
                v.drain()
                am = v.tensor_tensor(out=bb1[:, 0:512], in0=bb1[:, 0:512],
                                     in1=xs_, op=OP.min)
            v.drain()
        v.tensor_scalar(out=bb1[:, 0:512], in0=bb1[:, 0:512], scalar1=-1.0,
                        scalar2=None, op0=OP.mult)
        v.drain()
        am = v.tensor_tensor(out=bb2[:, 0:512], in0=bb2[:, 0:512],
                             in1=bb1[:, 0:512], op=OP.max)
        v.drain()
        ck_fold = inc("V", am)
        for i in range(4):
            ck_trb = token_scales(bb2[:, i * 128:(i + 1) * 128],
                                  thr_sb[:, 4:5], ck_fold)
            W("V", ck_trb)
            v.tensor_copy(out=ROW(R_S1 + sl, i * 128, 128),
                          in_=pstr[0:1, 0:128])
            v.drain()
            cp = v.tensor_copy(out=ROW(R_S1M + sl, i * 128, 128),
                               in_=pstr[32:33, 0:128])
            v.drain()
            inc("V", cp)
    ck_s1 = ck("V")

    l1 = {}
    for tt in range(NB):
        tb = tt * 256
        sl_, off_ = tb // 512, tb % 512
        ck_s1b = bcast_row(R_S1 + sl_, off_, 256, bb1, 0, 4, deps=(ck_s1,))
        xq = None
        for dc in range(DC):
            W("V", ck_s1b, ck_hq, l1.get("w1done"))
            v.tensor_mul(out=fsc[:, 0:256],
                         in0=xn_sb[:, dc * T + tb:dc * T + tb + 256],
                         in1=bb1[:, 0:256])
            v.drain()
            v.tensor_copy(out=i32a[:, 0:256], in_=fsc[:, 0:256])
            v.drain()
            v.tensor_copy(out=xq1_sb[:, dc * 258 + 1:dc * 258 + 257],
                          in_=i32a[:, 0:256])
            v.drain()
            v.tensor_copy(out=xq1_sb[:, dc * 258:dc * 258 + 1],
                          in_=hq_sb[:, dc * HN + tt:dc * HN + tt + 1])
            v.drain()
            xq = v.tensor_copy(out=xq1_sb[:, dc * 258 + 257:dc * 258 + 258],
                               in_=hq_sb[:, dc * HN + NB + tt:dc * HN + NB + tt + 1])
            v.drain()
        ck_xq1 = inc("V", xq)
        W("V", ck_s1, ck_hsm)
        v.tensor_copy(out=ROW(R_SC, 1, 256),
                      in_=ROW(R_S1M + sl_, off_, 256))
        v.drain()
        v.tensor_copy(out=ROW(R_SC, 0, 1),
                      in_=ROW(R_HALO, 96 + tt, 1))
        v.drain()
        hsr = v.tensor_copy(out=ROW(R_SC, 257, 1),
                            in_=ROW(R_HALO, 96 + NB + tt, 1))
        v.drain()
        ck_hsr = inc("V", hsr)
        ck_scb = bcast_row(R_SC, 0, 258, sc258, 0, 4, deps=(ck_hsr,))
        for jc in range(JC1):
            tslot = (jc % 2) * DC * 128
            W("Y", ("YT", cnt["YT"]), ck("T"))
            tl = y.dma_start(
                out=t1buf[:, tslot:tslot + DC * 128].rearrange(
                    "p (c n) -> p c n", c=DC),
                in_=t1d[:, :, jc * 128:(jc + 1) * 128])
            ck_t1 = inc("Y1a" if jc % 2 == 0 else "Y1b", tl, 16)
            pb = 5 + (jc % 2)
            mm = None
            for dc in range(DC):
                W("T", ck_t1, ck_xq1, l1.get(("hev", jc - 2)))
                mm = t.matmul(ps[:, pb * 512:pb * 512 + 258],
                              t1buf[:, tslot + dc * 128:tslot + dc * 128 + 128],
                              xq1_sb[:, dc * 258:(dc + 1) * 258],
                              start=(dc == 0), stop=(dc == DC - 1))
            ck_w1 = inc("T", mm)
            hb = 0
            W("V", ck_w1, ck_scb)
            he = v.tensor_mul(out=htile[:, hb:hb + 258],
                              in0=ps[:, pb * 512:pb * 512 + 258], in1=sc258[:])
            v.drain()
            l1[("hev", jc)] = inc("V", he)
            v.tensor_scalar(out=fsc[:, 0:256], in0=htile[:, hb:hb + 256],
                            scalar1=cvw_sb[:, jc * 3:jc * 3 + 1], scalar2=None,
                            op0=OP.mult)
            v.drain()
            v.scalar_tensor_tensor(out=fsc[:, 0:256], in0=htile[:, hb + 1:hb + 257],
                                   scalar=cvw_sb[:, jc * 3 + 1:jc * 3 + 2],
                                   in1=fsc[:, 0:256], op0=OP.mult, op1=OP.add)
            v.drain()
            cv3 = v.scalar_tensor_tensor(out=fsc[:, 0:256],
                                         in0=htile[:, hb + 2:hb + 258],
                                         scalar=cvw_sb[:, jc * 3 + 2:jc * 3 + 3],
                                         in1=fsc[:, 0:256], op0=OP.mult, op1=OP.add)
            v.drain()
            ck_cv = inc("V", cv3)
            W("S", ck_cv)
            si = s.activation(out=hpool[:, jc * 256:(jc + 1) * 256],
                              in_=fsc[:, 0:256], func=AF.Silu,
                              bias=cvb_sb[:, jc:jc + 1])
            ck_si = inc("S", si)
            W("V", ck_si)
            if jc == 0:
                v.tensor_copy(out=run_sb[:, 0:256], in_=hpool[:, 0:256])
                v.drain()
                am = v.tensor_copy(out=s2mb[:], in_=hpool[:, 0:256])
            else:
                v.tensor_tensor(out=run_sb[:, 0:256], in0=run_sb[:, 0:256],
                                in1=hpool[:, jc * 256:(jc + 1) * 256], op=OP.max)
                v.drain()
                am = v.tensor_tensor(out=s2mb[:], in0=s2mb[:],
                                     in1=hpool[:, jc * 256:(jc + 1) * 256],
                                     op=OP.min)
            v.drain()
            inc("V", am)
        W("V", ck("V"))
        v.tensor_scalar(out=s2mb[:], in0=s2mb[:], scalar1=-1.0,
                        scalar2=None, op0=OP.mult)
        v.drain()
        am = v.tensor_tensor(out=run_sb[:], in0=run_sb[:], in1=s2mb[:],
                             op=OP.max)
        v.drain()
        inc("V", am)
        l1["w1done"] = ck("V")
        ck_run = ck("V")
        for half in range(2):
            ck_trb = token_scales(run_sb[:, half * 128:(half + 1) * 128]
                                  if half else run_sb[:, 0:128],
                                  thr_sb[:, 5:6], ck_run)
            W("V", ck_trb)
            v.tensor_copy(out=ROW(R_S2, half * 128, 128),
                          in_=pstr[0:1, 0:128])
            v.drain()
            cp = v.tensor_copy(out=ROW(R_S2M, half * 128, 128),
                               in_=pstr[32:33, 0:128])
            v.drain()
            inc("V", cp)
        ck_s2 = ck("V")
        ck_s2b = bcast_row(R_S2, 0, 256, run_sb, 0, 2, deps=(ck_s2,))
        ck_s2mb = bcast_row(R_S2M, 0, 256, s2mb, 0, 2, deps=(ck_s2,))
        xq2 = None
        for jc in range(JC1):
            W("V", ck_s2b)
            v.tensor_mul(out=i32a[:, 0:256], in0=hpool[:, jc * 256:(jc + 1) * 256],
                         in1=run_sb[:])
            v.drain()
            xq2 = v.tensor_copy(out=hpool[:, jc * 256:(jc + 1) * 256],
                                in_=i32a[:, 0:256])
            v.drain()
        ck_xq2 = inc("V", xq2)
        for mc in range(DC):
            pb = mc % 2
            mm = None
            for jh2 in range(2):
                W("Y", ("YT", cnt["YT"]), ck("T"))
                tl = y.dma_start(
                    out=t2buf[:].rearrange("p (c n) -> p c n", c=16),
                    in_=t2d[jh2 * 16:(jh2 + 1) * 16, :, mc * 128:(mc + 1) * 128])
                ck_t2 = inc("Y2", tl, 16)
                for jc2 in range(16):
                    jc = jh2 * 16 + jc2
                    W("T", ck_t2, ck_xq2, l1.get(("xadd", mc - 2)))
                    mm = t.matmul(ps[:, pb * 512:pb * 512 + 256],
                                  t2buf[:, jc2 * 128:jc2 * 128 + 128],
                                  hpool[:, jc * 256:(jc + 1) * 256],
                                  start=(jc == 0), stop=(jc == JC1 - 1))
                inc("T", mm)
            ck_w2 = ck("T")
            xsl = x_sb[:, mc * T + tb:mc * T + tb + 256]
            W("V", ck_w2, ck_s2mb)
            v.tensor_mul(out=fsc[:, 0:256], in0=ps[:, pb * 512:pb * 512 + 256],
                         in1=s2mb[:])
            v.drain()
            ad = v.tensor_add(out=xsl, in0=xsl, in1=fsc[:, 0:256])
            v.drain()
            l1[("xadd", mc)] = inc("V", ad)
        l1["w1done"] = ck("V")

    ck_attn1 = emit_attention(1, ck_ag1)
    ck_x2 = emit_oproj(1, wpj_in, ck_attn1)

    W("Y", ck_x2)
    for dc in range(DC):
        inc("YO", y.dma_start(out=out_d[dc * 128:(dc + 1) * 128, :],
                              in_=x_sb[:, dc * T:(dc + 1) * T]), 16)
    y.wait_ge(sems["YO"], cnt["YO"])

    blk.tensor(_replay(tapes["T"]))
    blk.vector(_replay(tapes["V"]))
    blk.scalar(_replay(tapes["S"]))
    blk.sync(_replay(tapes["Y"]))
    blk.gpsimd(_replay(tapes["G"]))

    es.close()
    return nc


def _masks(par):
    kk = np.arange(128)[:, None]
    qq = np.arange(256)[None, :]
    tri0 = (kk <= qq).astype(np.float32)
    tri1 = (kk + 128 <= qq).astype(np.float32)
    one = np.ones((128, 256), np.float32)
    zero = np.zeros((128, 256), np.float32)
    ms = [tri0, tri1, zero, zero] if par == 0 else [one, one, tri0, tri1]
    return np.concatenate(ms, axis=1)


def _stage(inputs, S):
    x = np.asarray(inputs["L0_attn_residual"])
    B = x.shape[0]
    NBG = S // 256
    C = lambda a: np.ascontiguousarray(a)

    wq = C(np.asarray(inputs["L0_attn_W_Q"]).reshape(D, D).T)
    wk = C(np.asarray(inputs["L0_attn_W_K"]).reshape(D, D).T)
    wv = C(np.asarray(inputs["L0_attn_W_V"]).reshape(D, D).T)
    wo = C(np.asarray(inputs["L0_attn_W_O"]).reshape(D, D).T)
    wg = C(np.asarray(inputs["L0_ffn_w_gate"]).T.astype(np.int8))
    wu = C(np.asarray(inputs["L0_ffn_w_up"]).T.astype(np.int8))
    wd = C(np.asarray(inputs["L0_ffn_w_down"]).T.astype(np.int8))
    wqkv = C(np.asarray(inputs["L1_attn_w_qkv"]).T)
    wpj = C(np.asarray(inputs["L1_attn_w_proj"]).T)
    bpj = C(np.asarray(inputs["L1_attn_b_proj"]).reshape(D, 1))
    w1 = C(np.asarray(inputs["L1_ffn_w1"]).T)
    w2 = C(np.asarray(inputs["L1_ffn_w2"]).T)
    cw = C(np.asarray(inputs["L1_ffn_conv_w"]).reshape(H1, 3))
    cb = C(np.asarray(inputs["L1_ffn_conv_b"]).reshape(H1, 1))
    ident = np.eye(128, dtype=np.float32)

    in_maps = []
    for core in range(2 * B):
        b, par = core // 2, core % 2
        blocks = x[b].reshape(NBG, 256, D)[par::2]
        xT = C(blocks.reshape(S // 2, D).T)
        hsel = np.zeros((128, 2), np.float32)
        hsel[:, 0 if par == 0 else 1] = 1.0
        in_maps.append({
            "xT": xT, "masks": _masks(par), "ident": ident, "hsel": hsel,
            "wqT": wq, "wkT": wk, "wvT": wv, "woT": wo,
            "wgT": wg, "wuT": wu, "wdT": wd,
            "wqkvT": wqkv, "wprojT": wpj, "bproj": bpj,
            "w1T": w1, "w2T": w2, "convw": cw, "convb": cb,
        })
    return in_maps


def kernel(**inputs):
    x = np.asarray(inputs["L0_attn_residual"])
    B, S, _ = x.shape
    nc = build_nc(S)
    in_maps = _stage(inputs, S)
    res = run_bass_kernel_spmd(nc, in_maps, core_ids=list(range(2 * B)))
    NBG = S // 256
    out = np.empty((B, S, D), np.float32)
    for core in range(2 * B):
        b, par = core // 2, core % 2
        oT = np.asarray(res.results[core]["outT"])
        out[b].reshape(NBG, 256, D)[par::2] = oT.T.reshape(NBG // 2, 256, D)
    return out


# version 09
# speedup vs baseline: 1.1826x; 1.0099x over previous
import numpy as np
import ml_dtypes
from contextlib import ExitStack

import concourse.bass as bass
import concourse.mybir as mybir
from concourse.bass_utils import run_bass_kernel_spmd
from concourse.alu_op_type import AluOpType as OP

F32 = mybir.dt.float32
BF16 = mybir.dt.bfloat16
I32 = mybir.dt.int32
I8 = mybir.dt.int8
AF = mybir.ActivationFunctionType
AX = mybir.AxisListType

D = 1024
DC = 8
HP = 8
H0 = 3072
JC0 = H0 // 128
H1 = 4096
JC1 = H1 // 128
C127 = float(np.float32(1.0 / 127.0))
CD127 = float(np.float32(1.0 / (127.0 * 127.0)))
EXP_SHIFT = -20.0
RG = [[0, 1], [2, 3], [4, 5], [6, 7]]

R_M, R_R, R_VAR, R_SD = 0, 1, 2, 3
R_REC, R_RECO = 0, 1
R_S2, R_S2M = 2, 3
R_S1, R_S1M = 4, 8
R_WM = 2
R_HALO = 1
R_SC = 0


def build_nc(S):
    T = S // 2
    NB = T // 256
    NSL = T // 512
    TT0 = 512
    NT0 = T // TT0
    HN = 2 * NB
    assert HN <= 32

    nc = bass.Bass("TRN2", target_bir_lowering=False)
    DT = nc.dram_tensor

    xT_in = DT("xT", [D, T], F32, kind="ExternalInput").ap()
    masks_in = DT("masks", [128, 1024], F32, kind="ExternalInput").ap()
    ident_in = DT("ident", [128, 128], F32, kind="ExternalInput").ap()
    hsel_in = DT("hsel", [128, 2], F32, kind="ExternalInput").ap()
    wq_in = DT("wqT", [D, D], F32, kind="ExternalInput").ap()
    wk_in = DT("wkT", [D, D], F32, kind="ExternalInput").ap()
    wv_in = DT("wvT", [D, D], F32, kind="ExternalInput").ap()
    wo_in = DT("woT", [D, D], F32, kind="ExternalInput").ap()
    wg_in = DT("wgT", [D, H0], I8, kind="ExternalInput").ap()
    wu_in = DT("wuT", [D, H0], I8, kind="ExternalInput").ap()
    wd_in = DT("wdT", [H0, D], I8, kind="ExternalInput").ap()
    wqkv_in = DT("wqkvT", [D, 3 * D], F32, kind="ExternalInput").ap()
    wpj_in = DT("wprojT", [D, D], F32, kind="ExternalInput").ap()
    bpj_in = DT("bproj", [D, 1], F32, kind="ExternalInput").ap()
    w1_in = DT("w1T", [D, H1], F32, kind="ExternalInput").ap()
    w2_in = DT("w2T", [H1, D], F32, kind="ExternalInput").ap()
    cw_in = DT("convw", [H1, 3], F32, kind="ExternalInput").ap()
    cb_in = DT("convb", [H1, 1], F32, kind="ExternalInput").ap()
    out_d = DT("outT", [D, T], F32, kind="ExternalOutput").ap()

    KE = HP * 128 * T
    VE = (T // 128) * 128 * D
    ccin = [DT(f"cc{L}i", [KE + VE], BF16, kind="Internal").ap() for L in (0, 1)]
    ccout = [DT(f"cc{L}o", [2, KE + VE], BF16, kind="Internal").ap() for L in (0, 1)]
    hai = DT("hai", [D, HN], F32, kind="Internal").ap()
    hao = DT("hao", [2, D, HN], F32, kind="Internal").ap()
    qsp = DT("qsp", [HP, 128, T], BF16, kind="Internal").ap()
    asp = DT("asp", [HP, 128, T], BF16, kind="Internal").ap()
    t1d = DT("t1d", [DC, 128, H1], BF16, kind="Internal").ap()
    t2d = DT("t2d", [JC1, 128, D], BF16, kind="Internal").ap()

    es = ExitStack()
    SB = lambda n, sh, d: es.enter_context(nc.sbuf_tensor(n, sh, d))

    x_sb = SB("x", [128, DC * T], F32)
    xn_sb = SB("xn", [128, DC * T], BF16)
    rows = SB("rows", [128, 2048], F32)
    ROW = lambda i, c0, w: rows[32 * (i % 3):32 * (i % 3) + 1,
                                (i // 3) * 512 + c0:(i // 3) * 512 + c0 + w]
    bb1 = SB("bb1", [128, 512], F32)
    bb2 = SB("bb2", [128, 512], F32)
    fsc = SB("fsc", [128, 1024], F32)
    warena = SB("warena", [128, DC * 512], BF16)
    wff_w = SB("wff_w", [128, 2 * 1024], BF16)
    kvst = SB("kvst", [128, 2 * 512], BF16)
    katt = SB("katt", [128, S], BF16)
    vatt = SB("vatt", [128, (S // 128) * 130], BF16)
    qatt = SB("qatt", [128, T], BF16)
    p_sb = SB("p_sb", [128, 2048], BF16)
    attst = SB("attst", [128, T], BF16)
    attb = SB("attb", [128, DC * 256], BF16)
    masks_sb = SB("masksb", [128, 1024], BF16)
    ident_sb = SB("identb", [128, 128], F32)
    hsel_sb = SB("hselb", [128, 2], F32)
    ones_r = SB("ones_r", [128, 128], F32)
    onesk = SB("onesk", [128, 1], F32)
    mxc = SB("mxc", [128, 64], F32)
    hpool = SB("hpool", [128, JC0 * TT0], BF16)
    xq1_sb = SB("xq1", [128, DC * 258], BF16)
    t1buf = SB("t1buf", [128, 2 * DC * 128], BF16)
    t2buf = SB("t2buf", [128, 16 * 128], BF16)
    wqf = SB("wqf", [128, 256], BF16)
    tq_sb = SB("tq", [128, 256], BF16)
    wabs = SB("wabs", [128, 4], F32)
    thr_sb = SB("thr", [128, 8], F32)
    i32a = SB("i32a", [128, 256], I32)
    s2mb = SB("s2mb", [128, 256], F32)
    run_sb = SB("run_sb", [128, 256], F32)
    cvw_sb = SB("cvw", [128, 3 * JC1], F32)
    cvb_sb = SB("cvb", [128, JC1], F32)
    bpj_sb = SB("bpjb", [128, DC], F32)
    cbias = SB("cbias", [128, 4], F32)
    hx_sb = SB("hx", [128, 2 * DC * NB], F32)
    hcand = SB("hcand", [128, 2 * DC * NB], F32)
    hq_sb = SB("hq", [128, 2 * DC * NB], BF16)

    ps = es.enter_context(nc.psum_tensor("ps_all", [128, 3584], F32))
    pstr = es.enter_context(nc.psum_tensor("ps_tr", [128, 128], F32))

    DMA_FAMS = ["YX", "YC", "YS0", "YS1", "YH", "YW", "YT", "YA", "YB",
                "YP", "YO", "Y1a", "Y1b", "Y2", "GM", "GA", "GF", "GU",
                "GD0", "GD1", "GW", "CC"]
    sems, cnt = {}, {k: 0 for k in (["T", "V", "S"] + DMA_FAMS)}
    for k in cnt:
        sems[k] = es.enter_context(nc.semaphore("sem" + k))
    last_wait = {}

    blk = es.enter_context(nc.Block())

    class _RecInstr:
        __slots__ = ("incs",)

        def __init__(self):
            self.incs = []

        def then_inc(self, sem, n):
            self.incs.append((sem, n))
            return self

    class _Rec:
        def __init__(self, tape):
            self._tape = tape

        def __getattr__(self, m):
            tape = self._tape

            def call(*a, **k):
                ri = _RecInstr()
                tape.append((m, a, k, ri))
                return ri
            return call

    tapes = {k: [] for k in "TVSYG"}
    ENG = {k: _Rec(tapes[k]) for k in "TVSYG"}
    t, v, s, y, g = (ENG[k] for k in "TVSYG")

    import os as _os
    _SIM = bool(_os.environ.get("BASS_SIM_FP64"))

    def _replay(tape):
        def f(eng):
            for m, a, k, ri in tape:
                instr = getattr(eng, m)(*a, **k)
                for sem, n in ri.incs:
                    instr.then_inc(sem, n)
        return f

    def inc(tag, instr, n=1):
        instr.then_inc(sems[tag], n)
        cnt[tag] += n
        return (tag, cnt[tag])

    def ck(tag):
        return (tag, cnt[tag])

    def W(consumer, *cks):
        lw = last_wait.setdefault(consumer, {})
        best = {}
        for c in cks:
            if c is None:
                continue
            tag, val = c
            if val > best.get(tag, 0):
                best[tag] = val
        for tag, val in best.items():
            if val <= 0 or lw.get(tag, 0) >= val:
                continue
            assert val <= cnt[tag], f"forward wait {tag}:{val} > {cnt[tag]}"
            ENG[consumer].wait_ge(sems[tag], val)
            lw[tag] = val

    for dc in range(DC):
        inc("YX", y.dma_start(out=x_sb[:, dc * T:(dc + 1) * T],
                              in_=xT_in[dc * 128:(dc + 1) * 128, :]), 16)
    ck_x0 = ck("YX")
    inc("YC", y.dma_start(out=ident_sb[:], in_=ident_in), 16)
    inc("YC", y.dma_start(out=hsel_sb[:], in_=hsel_in), 16)
    inc("YC", y.dma_start(
        out=cvw_sb[:].rearrange("p (c k) -> p c k", c=JC1),
        in_=cw_in.rearrange("(c p) k -> p c k", p=128),
        allow_slow_non_contiguous=True), 16)
    inc("YC", y.dma_start(
        out=cvb_sb[:].rearrange("p (c o) -> p c o", c=JC1),
        in_=cb_in.rearrange("(c p) o -> p c o", p=128),
        allow_slow_non_contiguous=True), 16)
    inc("YC", y.dma_start(
        out=bpj_sb[:].rearrange("p (c o) -> p c o", c=DC),
        in_=bpj_in.rearrange("(c p) o -> p c o", p=128),
        allow_slow_non_contiguous=True), 16)
    ck_consts = ck("YC")
    ck_masks = inc("GM", g.dma_start(out=masks_sb[:], in_=masks_in), 16)

    W("V", ck_consts)
    v.memset(ones_r[:], 1.0)
    v.memset(onesk[:], 1.0)
    v.memset(cbias[:, 0:1], 1e-5)
    v.memset(cbias[:, 1:2], 1.0)
    ck_ones = inc("V", v.memset(cbias[:, 2:3], EXP_SHIFT))

    def bcast_row(slot, c0, n, dst_sb, dst_off, pbank, deps=()):
        b = 32 * (slot % 3)
        W("T", ck_ones, ck("V"), *deps)
        mm = t.matmul(ps[:, pbank * 512:pbank * 512 + n], ones_r[b:b + 1, 0:128],
                      ROW(slot, c0, n), start=True, stop=True)
        ckt = inc("T", mm)
        W("V", ckt)
        cp = v.tensor_copy(out=dst_sb[:, dst_off:dst_off + n],
                           in_=ps[:, pbank * 512:pbank * 512 + n])
        v.drain()
        return inc("V", cp)

    def wload(fam, dram_ap, row0, rows_, col0, ncols, dst, dst_off):
        ins = g.dma_start(
            out=dst[:, dst_off:dst_off + (rows_ // 128) * ncols].rearrange(
                "p (c n) -> p c n", c=rows_ // 128),
            in_=dram_ap[row0:row0 + rows_, col0:col0 + ncols].rearrange(
                "(c p) n -> p c n", p=128))
        return inc(fam, ins, 16)

    def ln_slice(xs_aps, w_, dep):
        mm2 = None
        W("T", ck("V"))
        for dc in range(DC):
            xap = xs_aps[dc][0]
            off = (dc % 2) * 512
            W("V", dep, ck("T"))
            sq = v.tensor_mul(out=fsc[:, off:off + w_], in0=xap, in1=xap)
            v.drain()
            cksq = inc("V", sq)
            W("T", ck_ones, cksq)
            t.matmul(ps[0:1, 3 * 512:3 * 512 + w_], onesk[:], xap,
                     start=(dc == 0), stop=(dc == DC - 1))
            mm2 = t.matmul(ps[0:1, 4 * 512:4 * 512 + w_], onesk[:],
                           fsc[:, off:off + w_],
                           start=(dc == 0), stop=(dc == DC - 1))
            inc("T", mm2)
        ck_st = ck("T")
        W("V", ck_st)
        v.tensor_scalar(out=ROW(R_M, 0, w_),
                        in0=ps[0:1, 3 * 512:3 * 512 + w_],
                        scalar1=1.0 / D, scalar2=None, op0=OP.mult)
        v.drain()
        v.tensor_mul(out=ROW(R_VAR, 0, w_), in0=ROW(R_M, 0, w_),
                     in1=ROW(R_M, 0, w_))
        v.drain()
        va = v.scalar_tensor_tensor(out=ROW(R_VAR, 0, w_),
                                    in0=ps[0:1, 4 * 512:4 * 512 + w_],
                                    scalar=1.0 / D, in1=ROW(R_VAR, 0, w_),
                                    op0=OP.mult, op1=OP.subtract)
        v.drain()
        ck_var = inc("V", va)
        W("S", ck_var)
        sq2 = s.activation(out=ROW(R_SD, 0, w_),
                           in_=ROW(R_VAR, 0, w_), func=AF.Sqrt,
                           bias=cbias[0:1, 0:1])
        ck_sd = inc("S", sq2)
        W("V", ck_sd)
        rc = v.reciprocal(out=ROW(R_R, 0, w_), in_=ROW(R_SD, 0, w_))
        v.drain()
        ck_rows = inc("V", rc)
        ck_mb = bcast_row(R_M, 0, w_, bb1, 0, 5, deps=(ck_rows,))
        ck_rb = bcast_row(R_R, 0, w_, bb2, 0, 6, deps=(ck_rows,))
        mu = None
        for dc in range(DC):
            xap, oap = xs_aps[dc]
            W("V", ck_mb, ck_rb)
            v.tensor_sub(out=fsc[:, 0:w_], in0=xap, in1=bb1[:, 0:w_])
            v.drain()
            mu = v.tensor_mul(out=oap, in0=fsc[:, 0:w_], in1=bb2[:, 0:w_])
            v.drain()
        return inc("V", mu)

    def emit_layernorm(dep):
        outs = []
        for sl in range(NSL):
            c0 = sl * 512
            xs = [(x_sb[:, dc * T + c0:dc * T + c0 + 512],
                   xn_sb[:, dc * T + c0:dc * T + c0 + 512]) for dc in range(DC)]
            outs.append(ln_slice(xs, 512, dep))
        return outs[-1]

    proj_state = {"dma": [], "slot": 0, "evac": []}

    def proj_evac(ck_t, pb, dst_ap):
        sl = proj_state["slot"]
        so = (sl % 2) * 512
        W("S", ck_t)
        if sl >= 2:
            W("S", proj_state["dma"][sl - 2])
        cp = s.copy(out=kvst[:, so:so + 512],
                    in_=ps[:, pb * 512:pb * 512 + 512])
        s.drain()
        ck_v = inc("S", cp)
        proj_state["evac"].append(ck_v)
        W("Y", ck_v, ("YA", cnt["YA"]))
        dd = y.dma_start(out=dst_ap, in_=kvst[:, so:so + 512])
        proj_state["dma"].append(inc("YS0" if sl % 2 == 0 else "YS1", dd, 16))
        proj_state["slot"] += 1
        pump_side(2)

    def emit_proj_w(w_dram, wcol0, dep_xn, dst_fn):
        for half in range(2):
            W("G", ck("T"))
            ck_w = wload("GA", w_dram, 0, D, wcol0 + half * 512, 512, warena, 0)
            for a2 in range(4):
                a = half * 4 + a2
                for b in range(NSL):
                    pb = 5 + (proj_state["slot"] % 2)
                    mm = None
                    ev = proj_state["evac"]
                    for dc in range(DC):
                        W("T", ck_w, dep_xn,
                          ev[-2] if len(ev) >= 2 else None)
                        mm = t.matmul(
                            ps[:, pb * 512:pb * 512 + 512],
                            warena[:, dc * 512 + a2 * 128:dc * 512 + a2 * 128 + 128],
                            xn_sb[:, dc * T + b * 512:dc * T + b * 512 + 512],
                            start=(dc == 0), stop=(dc == DC - 1))
                    proj_evac(inc("T", mm), pb, dst_fn(a, b))

    def emit_proj_v(w_dram, wcol0, dep_xn, dst_fn):
        for b in range(2):
            W("G", ck("T"))
            ck_w = wload("GA", w_dram, 0, D, wcol0 + b * 512, 512, warena, 0)
            for a in range(T // 128):
                pb = 5 + (proj_state["slot"] % 2)
                mm = None
                ev = proj_state["evac"]
                for dc in range(DC):
                    W("T", ck_w, dep_xn,
                      ev[-2] if len(ev) >= 2 else None)
                    mm = t.matmul(
                        ps[:, pb * 512:pb * 512 + 512],
                        xn_sb[:, dc * T + a * 128:dc * T + a * 128 + 128],
                        warena[:, dc * 512:dc * 512 + 512],
                        start=(dc == 0), stop=(dc == DC - 1))
                proj_evac(inc("T", mm), pb, dst_fn(a, b))

    def token_scales(fold_ap, mcol_ap, dep):
        W("T", dep, ck_consts, ck("V"))
        tr = t.transpose(pstr[:, :], fold_ap, ident_sb[:])
        ck_tr = inc("T", tr)
        W("V", ck_tr)
        v.reduce_max(out=mxc[:, 34:35], in_=pstr[:, :], axis=AX.X)
        v.drain()
        v.tensor_scalar(out=mxc[:, 35:36], in0=mxc[:, 34:35], scalar1=1e-5,
                        scalar2=None, op0=OP.max)
        v.drain()
        v.reciprocal(out=mxc[:, 36:37], in_=mxc[:, 35:36])
        v.drain()
        v.tensor_scalar(out=mxc[:, 0:1], in0=mxc[:, 36:37], scalar1=127.0,
                        scalar2=None, op0=OP.mult)
        v.drain()
        sm = v.tensor_scalar(out=mxc[:, 32:33], in0=mxc[:, 35:36], scalar1=mcol_ap,
                             scalar2=C127, op0=OP.mult, op1=OP.mult)
        v.drain()
        ck_mx = inc("V", sm)
        W("T", ck_mx)
        tr2 = t.transpose(pstr[0:64, :], mxc[:, :], ident_sb[:])
        return inc("T", tr2)

    def wquant_gen(dram_ap, rows_, ncol, col, dst_dram):
        prev_v = None
        first = True
        started = False
        ck_acc = None
        for c in range(rows_ // 128):
            for p0 in range(0, ncol, 256):
                if not started:
                    W("G", ("GW", cnt["GW"]), ("V", cnt["V"]))
                    started = True
                W("G", prev_v)
                ld = inc("GW", g.dma_start(
                    out=wqf[:], in_=dram_ap[c * 128:(c + 1) * 128, p0:p0 + 256]), 16)
                W("V", ld)
                v.reduce_sum(out=wabs[:, 2:3], in_=wqf[:], axis=AX.X,
                             apply_absolute_value=True)
                v.drain()
                if first:
                    cp = v.tensor_copy(out=wabs[:, 0:1], in_=wabs[:, 2:3])
                    first = False
                else:
                    cp = v.tensor_add(out=wabs[:, 0:1], in0=wabs[:, 0:1],
                                      in1=wabs[:, 2:3])
                v.drain()
                ck_acc = inc("V", cp)
                prev_v = ck_acc
                yield
        W("T", ck_acc, ck_ones)
        mm = t.matmul(pstr[0:1, 0:1], onesk[:], wabs[:, 0:1],
                      start=True, stop=True)
        ck_t = inc("T", mm)
        W("V", ck_t)
        v.tensor_scalar(out=ROW(R_WM, col, 1), in0=pstr[0:1, 0:1],
                        scalar1=1.0 / (rows_ * ncol), scalar2=None, op0=OP.mult)
        v.drain()
        th = v.tensor_scalar(out=ROW(R_WM, col + 2, 1), in0=ROW(R_WM, col, 1),
                             scalar1=0.5, scalar2=None, op0=OP.mult)
        ck_m = inc("V", th)
        W("T", ck_m)
        wb = 32 * (R_WM % 3)
        mm = t.matmul(pstr[:, 0:1], ones_r[wb:wb + 1, 0:128],
                      ROW(R_WM, col + 2, 1), start=True, stop=True)
        ck_t = inc("T", mm)
        W("V", ck_t)
        v.tensor_copy(out=thr_sb[:, 2 * col:2 * col + 1],
                      in_=pstr[:, 0:1])
        v.drain()
        v.tensor_scalar(out=thr_sb[:, 2 * col + 1:2 * col + 2],
                        in0=thr_sb[:, 2 * col:2 * col + 1], scalar1=-1.0,
                        scalar2=None, op0=OP.mult)
        v.drain()
        th2 = v.tensor_scalar(out=thr_sb[:, 4 + col:5 + col],
                              in0=thr_sb[:, 2 * col:2 * col + 1], scalar1=2.0,
                              scalar2=None, op0=OP.mult)
        v.drain()
        ck_thr = inc("V", th2)
        yield
        for c in range(rows_ // 128):
            for p0 in range(0, ncol, 256):
                W("G", prev_v)
                ld = inc("GW", g.dma_start(
                    out=wqf[:], in_=dram_ap[c * 128:(c + 1) * 128, p0:p0 + 256]), 16)
                W("V", ld, ck_thr)
                if tern_dma:
                    W("V", tern_dma[-1])
                v.tensor_scalar(out=run_sb[:], in0=wqf[:],
                                scalar1=thr_sb[:, 2 * col + 1:2 * col + 2],
                                scalar2=None, op0=OP.is_lt)
                v.drain()
                v.tensor_scalar(out=tq_sb[:], in0=wqf[:],
                                scalar1=thr_sb[:, 2 * col:2 * col + 1],
                                scalar2=None, op0=OP.is_gt)
                v.drain()
                sb2 = v.tensor_sub(out=tq_sb[:], in0=tq_sb[:], in1=run_sb[:])
                v.drain()
                ck_q = inc("V", sb2)
                prev_v = ck_q
                W("Y", ck_q)
                dd = y.dma_start(out=dst_dram[c][:, p0:p0 + 256], in_=tq_sb[:])
                tern_dma.append(inc("YT", dd, 16))
                yield

    side_gens = []

    def pump_side(n=1):
        for _ in range(n):
            while side_gens:
                try:
                    next(side_gens[0])
                    break
                except StopIteration:
                    side_gens.pop(0)
            else:
                return

    def drain_side():
        while side_gens:
            pump_side()

    tern_dma = []


    def emit_attention(L, ck_ag):
        cc = ccout[L]
        st = {"v0": ck("V")}
        prev_tail = None
        for hp in range(HP):
            W("Y", ck_ag, prev_tail, ck("T") if hp else None,
              ("YS0", cnt["YS0"]), ("YS1", cnt["YS1"]))
            kld = []
            for r in (0, 1):
                srck = cc[r][0:KE].rearrange("(h p n) -> h p n", h=HP, p=128)[hp]
                dd = y.dma_start(
                    out=katt[:].rearrange("p (j q) -> p j q",
                                          q=512)[:, :, 256 * r:256 * r + 256],
                    in_=srck.rearrange("p (j n) -> p j n", n=256))
                kld.append(inc("YA", dd, 16))
                srcv = cc[r][KE:KE + VE].rearrange("(c p n) -> c p n",
                                                   c=T // 128, p=128)
                for u in range(2):
                    for hd in range(2):
                        off = 260 * r + 130 * u + 65 * hd
                        dd = y.dma_start(
                            out=vatt[:].rearrange(
                                "p (j q) -> p j q", q=520)[:, :, off:off + 64],
                            in_=srcv[u::2, :, hp * 128 + hd * 64:
                                     hp * 128 + hd * 64 + 64].rearrange(
                                "c p n -> p c n"))
                        kld.append(inc("YA", dd, 16))
            dd = y.dma_start(out=qatt[:], in_=qsp[hp])
            kld.append(inc("YA", dd, 16))
            W("V", *kld, prev_tail)
            ckone = inc("V", v.memset(
                vatt[:].rearrange("p (c f) -> p c f", f=130)[:, :, 64:130:65], 1.0))
            for j in range(NB):
                q0 = j * 256
                for bt in range(j + 1):
                    for hd in range(2):
                        bko = hd * 1024
                        po = hd * 1024
                        mm = None
                        for i in range(4):
                            kb = 4 * bt + i
                            W("T", *kld, ckone, st.get(("exp", hd)),
                              st.get("avfree") if bt == 0 else None)
                            mm = t.matmul(
                                ps[:, bko + i * 256:bko + i * 256 + 256],
                                katt[64 * hd:64 * hd + 64, kb * 128:kb * 128 + 128],
                                qatt[64 * hd:64 * hd + 64, q0:q0 + 256],
                                start=True, stop=True, tile_position=(64 * hd, 0))
                        ck_sc = inc("T", mm)
                        if L == 0:
                            W("S", ck_sc, st.get(("pfree", hd)), st["v0"])
                            e2 = s.activation(out=p_sb[:, po:po + 1024],
                                              in_=ps[:, bko:bko + 1024],
                                              func=AF.Exp, scale=0.125,
                                              bias=cbias[:, 2:3])
                            ck_e = inc("S", e2)
                        else:
                            W("S", ck_sc, st.get(("pfree", hd)), st["v0"])
                            s.activation(out=p_sb[:, po:po + 1024],
                                         in_=ps[:, bko:bko + 1024],
                                         func=AF.Exp, scale=0.125)
                            s.drain()
                            e2 = s.activation(out=p_sb[:, po:po + 1024],
                                              in_=p_sb[:, po:po + 1024],
                                              func=AF.Ln, bias=cbias[:, 1:2])
                            ck_ln = inc("S", e2)
                            W("V", ck_ln)
                            sq = v.tensor_mul(out=p_sb[:, po:po + 1024],
                                              in0=p_sb[:, po:po + 1024],
                                              in1=p_sb[:, po:po + 1024])
                            v.drain()
                            ck_e = inc("V", sq)
                        st[("exp", hd)] = ck_e
                        if bt == j:
                            W("V", ck_e, ck_masks)
                            mk = v.tensor_mul(out=p_sb[:, po:po + 1024],
                                              in0=p_sb[:, po:po + 1024],
                                              in1=masks_sb[:])
                            v.drain()
                            ck_e = inc("V", mk)
                            st[("exp", hd)] = ck_e
                        mm = None
                        for i in range(4):
                            kb = 4 * bt + i
                            W("T", ck_e,
                              st.get("avfree") if bt == 0 else None)
                            mm = t.matmul(
                                ps[0:65, (4 + hd) * 512:(4 + hd) * 512 + 256],
                                vatt[:, kb * 130 + hd * 65:kb * 130 + hd * 65 + 65],
                                p_sb[:, po + i * 256:po + i * 256 + 256],
                                start=(bt == 0 and i == 0),
                                stop=(bt == j and i == 3))
                        st[("pfree", hd)] = inc("T", mm)
                        pump_side(1)
                ck_av = st[("pfree", 1)]
                W("V", ck_av)
                v.tensor_copy(out=ROW(R_REC, 0, 256),
                              in_=ps[64:65, 2048:2304])
                v.drain()
                v.tensor_copy(out=ROW(R_REC, 256, 256),
                              in_=ps[64:65, 2560:2816])
                v.drain()
                if L == 1:
                    v.tensor_scalar(out=ROW(R_REC, 0, 512),
                                    in0=ROW(R_REC, 0, 512),
                                    scalar1=1e-9, scalar2=None, op0=OP.add)
                    v.drain()
                rc = v.reciprocal(out=ROW(R_RECO, 0, 512),
                                  in_=ROW(R_REC, 0, 512))
                v.drain()
                ck_rc = inc("V", rc)
                W("T", ck_rc)
                rb = 32 * (R_RECO % 3)
                t.matmul(ps[0:64, 3072:3328], ones_r[rb:rb + 1, 0:64],
                         ROW(R_RECO, 0, 256), start=True, stop=True,
                         tile_position=(rb, 0))
                mm = t.matmul(ps[64:128, 3072:3328], ones_r[rb:rb + 1, 0:64],
                              ROW(R_RECO, 256, 256), start=True,
                              stop=True, tile_position=(rb, 64))
                ck_bc = inc("T", mm)
                if L == 0:
                    W("S", ck_av)
                    s.copy(out=attst[0:64, q0:q0 + 256], in_=ps[0:64, 2048:2304])
                    s.drain()
                    cpk = s.copy(out=attst[64:128, q0:q0 + 256],
                                 in_=ps[0:64, 2560:2816])
                    s.drain()
                    ck_cp = inc("S", cpk)
                else:
                    W("V", ck_av)
                    v.tensor_copy(out=attst[0:64, q0:q0 + 256],
                                  in_=ps[0:64, 2048:2304])
                    v.drain()
                    cpk = v.tensor_copy(out=attst[64:128, q0:q0 + 256],
                                        in_=ps[0:64, 2560:2816])
                    v.drain()
                    ck_cp = inc("V", cpk)
                W("V", ck_bc, ck_cp)
                v.tensor_mul(out=attst[0:64, q0:q0 + 256],
                             in0=attst[0:64, q0:q0 + 256], in1=ps[0:64, 3072:3328])
                v.drain()
                mu = v.tensor_mul(out=attst[64:128, q0:q0 + 256],
                                  in0=attst[64:128, q0:q0 + 256],
                                  in1=ps[64:128, 3072:3328])
                v.drain()
                st["avread"] = inc("V", mu)
                st["avfree"] = st["avread"]
            W("Y", st["avread"], ("YB", cnt["YB"]))
            dd = y.dma_start(out=asp[hp], in_=attst[:])
            prev_tail = inc("YP", dd, 16)
        return prev_tail

    def emit_oproj(L, w_dram, dep_att):
        last = None
        for half in range(2):
            W("G", ck("T"))
            ck_w = wload("GA", w_dram, 0, D, half * 512, 512, warena, 0)
            for sl in range(T // 256):
                W("Y", dep_att, ck("T") if (half or sl) else None)
                dd = y.dma_start(
                    out=attb[:].rearrange("p (c n) -> p c n", n=256),
                    in_=asp[:, :, sl * 256:sl * 256 + 256].rearrange(
                        "c p n -> p c n"))
                abl = [inc("YB", dd, 16)]
                for m2 in range(4):
                    mc = half * 4 + m2
                    pb = 5 + ((sl * 4 + m2) % 2)
                    mm = None
                    for c in range(DC):
                        W("T", ck_w, *abl, last)
                        mm = t.matmul(
                            ps[:, pb * 512:pb * 512 + 256],
                            warena[:, c * 512 + m2 * 128:c * 512 + m2 * 128 + 128],
                            attb[:, c * 256:(c + 1) * 256],
                            start=(c == 0), stop=(c == DC - 1))
                    ck_t = inc("T", mm)
                    xsl = x_sb[:, mc * T + sl * 256:mc * T + sl * 256 + 256]
                    W("V", ck_t)
                    if L == 0:
                        ad = v.tensor_add(out=xsl, in0=xsl,
                                          in1=ps[:, pb * 512:pb * 512 + 256])
                    else:
                        ad = v.scalar_tensor_tensor(
                            out=xsl, in0=ps[:, pb * 512:pb * 512 + 256],
                            scalar=bpj_sb[:, mc:mc + 1], in1=xsl,
                            op0=OP.add, op1=OP.add)
                    v.drain()
                    last = inc("V", ad)
        return last

    cck = [ccin[L][0:KE].rearrange("(h p n) -> h p n", h=HP, p=128)
           for L in (0, 1)]
    ccv = [ccin[L][KE:KE + VE].rearrange("(c p n) -> c p n", c=T // 128, p=128)
           for L in (0, 1)]

    ck_xn = emit_layernorm(ck_x0)
    emit_proj_w(wk_in, 0, ck_xn, lambda a, b: cck[0][a][:, b * 512:(b + 1) * 512])
    emit_proj_v(wv_in, 0, ck_xn, lambda a, b: ccv[0][a][:, b * 512:(b + 1) * 512])
    W("G", ("YS0", cnt["YS0"]), ("YS1", cnt["YS1"]))
    ck_ag0 = inc("CC", g.collective_compute(
        "AllGather", OP.bypass, replica_groups=RG, ins=[ccin[0]], outs=[ccout[0]]))
    emit_proj_w(wq_in, 0, ck_xn, lambda a, b: qsp[a][:, b * 512:(b + 1) * 512])

    side_gens.append(wquant_gen(w1_in, D, H1, 0, t1d))
    side_gens.append(wquant_gen(w2_in, H1, D, 1, t2d))

    fs = {}
    for tt in range(NT0):
        t0 = tt * TT0
        for jh in range(JC0):
            if tt == 0 and jh == 0:
                fs["wg"] = wload("GF", wg_in, 0, D, 0, 128, wff_w, 0)
            mm = None
            for dc in range(DC):
                W("T", fs["wg"], ck_xn, fs.get("h"), fs.get("si"),
                  fs.get("hused"))
                mm = t.matmul(ps[:, 0:512],
                              wff_w[:, dc * 128:dc * 128 + 128],
                              xn_sb[:, dc * T + t0:dc * T + t0 + 512],
                              start=(dc == 0), stop=(dc == DC - 1))
            ck_g = inc("T", mm)
            if tt == 0 and jh == 0:
                fs["wu"] = wload("GU", wu_in, 0, D, 0, 128, wff_w, 1024)
            for dc in range(DC):
                W("T", fs["wu"])
                mm = t.matmul(ps[:, 512:1024],
                              wff_w[:, 1024 + dc * 128:1024 + dc * 128 + 128],
                              xn_sb[:, dc * T + t0:dc * T + t0 + 512],
                              start=(dc == 0), stop=(dc == DC - 1))
            ck_gu = inc("T", mm)
            pump_side(1)
            nj, ntt = (jh + 1, tt) if jh + 1 < JC0 else (0, tt + 1)
            if ntt < NT0:
                W("G", ck_g)
                fs["wg"] = wload("GF", wg_in, 0, D, nj * 128, 128, wff_w, 0)
                W("G", ck_gu)
                fs["wu"] = wload("GU", wu_in, 0, D, nj * 128, 128, wff_w, 1024)
            W("S", ck_gu, fs.get("h"))
            si = s.activation(out=fsc[:, 0:512], in_=ps[:, 0:512],
                              func=AF.Silu, scale=C127)
            ck_si = inc("S", si)
            fs["si"] = ck_si
            W("V", ck_si)
            hh = v.tensor_mul(out=hpool[:, jh * 512:(jh + 1) * 512],
                              in0=fsc[:, 0:512], in1=ps[:, 512:1024])
            v.drain()
            fs["h"] = inc("V", hh)
        ck_h = fs["h"]
        for mh in range(2):
            mm = None
            for jc in range(JC0):
                so = (jc % 2) * 512
                if tt == 0 and mh == 0 and jc == 0:
                    W("G", ("YS0", cnt["YS0"]), ("YS1", cnt["YS1"]))
                    fs["wd0"] = wload("GD0", wd_in, 0, 128, 0, 512, kvst, 0)
                ck_wd = fs["wd0" if jc % 2 == 0 else "wd1"]
                for m2 in range(4):
                    W("T", ck_wd, ck_h, fs.get("xadd"))
                    mm = t.matmul(
                        ps[:, m2 * 512:m2 * 512 + 512],
                        kvst[:, so + m2 * 128:so + m2 * 128 + 128],
                        hpool[:, jc * TT0:jc * TT0 + 512],
                        start=(jc == 0), stop=(jc == JC0 - 1))
                inc("T", mm)
                nj, nmh = (jc + 1, mh) if jc + 1 < JC0 else (0, mh + 1)
                ntt2 = tt if nmh < 2 else tt + 1
                if nmh == 2:
                    nmh = 0
                if ntt2 < NT0:
                    W("G", ck("T"))
                    fs["wd0" if nj % 2 == 0 else "wd1"] = wload(
                        "GD0" if nj % 2 == 0 else "GD1", wd_in, nj * 128, 128,
                        nmh * 512, 512, kvst, (nj % 2) * 512)
            ck_dn = ck("T")
            ad = None
            for m2 in range(4):
                mc = mh * 4 + m2
                xsl = x_sb[:, mc * T + t0:mc * T + t0 + 512]
                W("V", ck_dn)
                ad = v.scalar_tensor_tensor(
                    out=xsl, in0=ps[:, m2 * 512:m2 * 512 + 512],
                    scalar=CD127, in1=xsl, op0=OP.mult, op1=OP.add)
                v.drain()
            fs["xadd"] = inc("V", ad)
        fs["hused"] = fs["xadd"]

    ck_attn0 = emit_attention(0, ck_ag0)
    drain_side()
    ck_x1 = emit_oproj(0, wo_in, ck_attn0)

    W("Y", ck_x1)
    x4d = x_sb[:].rearrange("p (c j n) -> p c j n", c=DC, j=NB)
    hai3 = hai.rearrange("(c p) n -> p c n", p=128)
    for u in range(2):
        for dcq in range(DC):
            hx = y.dma_start(
                out=hai3[:, dcq, u * NB:(u + 1) * NB],
                in_=x4d[:, dcq, :, 255 * u],
                allow_slow_non_contiguous=True)
            inc("YH", hx, 16)
    ck_hai = ck("YH")
    W("G", ck_hai)
    ck_ag1a = inc("CC", g.collective_compute(
        "AllGather", OP.bypass, replica_groups=RG, ins=[hai], outs=[hao]))

    ck_xn = emit_layernorm(ck_x1)
    emit_proj_w(wqkv_in, 1024, ck_xn,
                lambda a, b: cck[1][a][:, b * 512:(b + 1) * 512])
    emit_proj_v(wqkv_in, 2048, ck_xn,
                lambda a, b: ccv[1][a][:, b * 512:(b + 1) * 512])
    W("G", ("YS0", cnt["YS0"]), ("YS1", cnt["YS1"]))
    ck_ag1 = inc("CC", g.collective_compute(
        "AllGather", OP.bypass, replica_groups=RG, ins=[ccin[1]], outs=[ccout[1]]))
    emit_proj_w(wqkv_in, 0, ck_xn, lambda a, b: qsp[a][:, b * 512:(b + 1) * 512])

    W("Y", ck_ag1a)
    c0 = y.dma_start(
        out=hcand[:, 0:DC * NB].rearrange("p (c j) -> p c j", c=DC)[:, :, 1:NB],
        in_=hao[1].rearrange("(c p) n -> p c n", p=128)[:, :, NB:2 * NB - 1],
        allow_slow_non_contiguous=True)
    inc("YH", c0, 16)
    c1 = y.dma_start(
        out=hcand[:, DC * NB:2 * DC * NB].rearrange("p (c j) -> p c j", c=DC),
        in_=hao[0].rearrange("(c p) n -> p c n", p=128)[:, :, NB:2 * NB],
        allow_slow_non_contiguous=True)
    ck_hc = inc("YH", c1, 16)
    W("V", ck_hc, ck_consts)
    v.memset(hcand[:, 0:DC * NB].rearrange("p (c j) -> p c j", c=DC)[:, :, 0:1], 0.0)
    v.drain()
    hx3 = hx_sb[:].rearrange("p (c n) -> p c n", n=HN)
    hc3a = hcand[:, 0:DC * NB].rearrange("p (c j) -> p c j", c=DC)
    hc3b = hcand[:, DC * NB:2 * DC * NB].rearrange("p (c j) -> p c j", c=DC)
    v.tensor_scalar(out=hx3[:, :, 0:NB], in0=hc3a,
                    scalar1=hsel_sb[:, 0:1], scalar2=None, op0=OP.mult)
    v.drain()
    hl = v.scalar_tensor_tensor(out=hx3[:, :, 0:NB], in0=hc3b,
                                scalar=hsel_sb[:, 1:2], in1=hx3[:, :, 0:NB],
                                op0=OP.mult, op1=OP.add)
    v.drain()
    ck_hl = inc("V", hl)
    W("Y", ck_hl)
    c0 = y.dma_start(
        out=hcand[:, 0:DC * NB].rearrange("p (c j) -> p c j", c=DC),
        in_=hao[1].rearrange("(c p) n -> p c n", p=128)[:, :, 0:NB],
        allow_slow_non_contiguous=True)
    inc("YH", c0, 16)
    c1 = y.dma_start(
        out=hcand[:, DC * NB:2 * DC * NB].rearrange(
            "p (c j) -> p c j", c=DC)[:, :, 0:NB - 1],
        in_=hao[0].rearrange("(c p) n -> p c n", p=128)[:, :, 1:NB],
        allow_slow_non_contiguous=True)
    ck_hc2 = inc("YH", c1, 16)
    W("V", ck_hc2)
    v.memset(hcand[:, DC * NB:2 * DC * NB].rearrange(
        "p (c j) -> p c j", c=DC)[:, :, NB - 1:NB], 0.0)
    v.drain()
    v.tensor_scalar(out=hx3[:, :, NB:HN], in0=hc3a,
                    scalar1=hsel_sb[:, 0:1], scalar2=None, op0=OP.mult)
    v.drain()
    hr = v.scalar_tensor_tensor(out=hx3[:, :, NB:HN], in0=hc3b,
                                scalar=hsel_sb[:, 1:2],
                                in1=hx3[:, :, NB:HN],
                                op0=OP.mult, op1=OP.add)
    v.drain()
    ck_hx = inc("V", hr)

    def hx_ap(tens, dc):
        return tens[:, dc * HN:(dc + 1) * HN]

    for side in range(2):
        o = side * NB
        hxs = [(hx_ap(hx_sb, dc)[:, o:o + NB], hx_ap(hcand, dc)[:, o:o + NB])
               for dc in range(DC)]
        ck_hxn = ln_slice(hxs, NB, ck_hx)
    am = None
    for dc in range(DC):
        W("V", ck_hxn)
        if dc == 0:
            v.tensor_copy(out=run_sb[:, 0:HN], in_=hx_ap(hcand, 0))
            v.drain()
            am = v.tensor_copy(out=s2mb[:, 0:HN], in_=hx_ap(hcand, 0))
        else:
            v.tensor_tensor(out=run_sb[:, 0:HN], in0=run_sb[:, 0:HN],
                            in1=hx_ap(hcand, dc), op=OP.max)
            v.drain()
            am = v.tensor_tensor(out=s2mb[:, 0:HN], in0=s2mb[:, 0:HN],
                                 in1=hx_ap(hcand, dc), op=OP.min)
        v.drain()
    v.tensor_scalar(out=s2mb[:, 0:HN], in0=s2mb[:, 0:HN], scalar1=-1.0,
                    scalar2=None, op0=OP.mult)
    v.drain()
    am = v.tensor_tensor(out=run_sb[:, 0:HN], in0=run_sb[:, 0:HN],
                         in1=s2mb[:, 0:HN], op=OP.max)
    v.drain()
    inc("V", am)
    W("V", ck("V"))
    zz = v.memset(run_sb[:, HN:128], 0.0)
    v.drain()
    ck_hfold = inc("V", zz)
    ck_trb = token_scales(run_sb[:, 0:128], thr_sb[:, 4:5], ck_hfold)
    W("V", ck_trb)
    v.tensor_copy(out=ROW(R_HALO, 64, HN), in_=pstr[0:1, 0:HN])
    v.drain()
    hsm = v.tensor_copy(out=ROW(R_HALO, 96, HN),
                        in_=pstr[32:33, 0:HN])
    v.drain()
    ck_hsm = inc("V", hsm)
    ck_hqb = bcast_row(R_HALO, 64, HN, bb1, 0, 5, deps=(ck_hsm,))
    mu = None
    for dc in range(DC):
        W("V", ck_hqb)
        v.tensor_mul(out=i32a[:, dc * HN:(dc + 1) * HN], in0=hx_ap(hcand, dc),
                     in1=bb1[:, 0:HN])
        v.drain()
    mu = v.tensor_copy(out=hq_sb[:, 0:DC * HN], in_=i32a[:, 0:DC * HN])
    v.drain()
    ck_hq = inc("V", mu)

    for sl in range(NSL):
        for dc in range(DC):
            W("V", ck_xn)
            xs_ = xn_sb[:, dc * T + sl * 512:dc * T + sl * 512 + 512]
            if dc == 0:
                v.tensor_copy(out=bb2[:, 0:512], in_=xs_)
                v.drain()
                am = v.tensor_copy(out=bb1[:, 0:512], in_=xs_)
            else:
                v.tensor_tensor(out=bb2[:, 0:512], in0=bb2[:, 0:512],
                                in1=xs_, op=OP.max)
                v.drain()
                am = v.tensor_tensor(out=bb1[:, 0:512], in0=bb1[:, 0:512],
                                     in1=xs_, op=OP.min)
            v.drain()
        v.tensor_scalar(out=bb1[:, 0:512], in0=bb1[:, 0:512], scalar1=-1.0,
                        scalar2=None, op0=OP.mult)
        v.drain()
        am = v.tensor_tensor(out=bb2[:, 0:512], in0=bb2[:, 0:512],
                             in1=bb1[:, 0:512], op=OP.max)
        v.drain()
        ck_fold = inc("V", am)
        for i in range(4):
            ck_trb = token_scales(bb2[:, i * 128:(i + 1) * 128],
                                  thr_sb[:, 4:5], ck_fold)
            W("V", ck_trb)
            v.tensor_copy(out=ROW(R_S1 + sl, i * 128, 128),
                          in_=pstr[0:1, 0:128])
            v.drain()
            cp = v.tensor_copy(out=ROW(R_S1M + sl, i * 128, 128),
                               in_=pstr[32:33, 0:128])
            v.drain()
            inc("V", cp)
    ck_s1 = ck("V")

    l1 = {}
    for tt in range(NB):
        tb = tt * 256
        sl_, off_ = tb // 512, tb % 512
        ck_s1b = bcast_row(R_S1 + sl_, off_, 256, bb1, 0, 4, deps=(ck_s1,))
        ck_s1mb = bcast_row(R_S1M + sl_, off_, 256, s2mb, 0, 3, deps=(ck_s1,))
        xq = None
        for dc in range(DC):
            W("V", ck_s1b, ck_s1mb, ck_hq, l1.get("w1done"))
            v.tensor_mul(out=i32a[:, 0:256],
                         in0=xn_sb[:, dc * T + tb:dc * T + tb + 256],
                         in1=bb1[:, 0:256])
            v.drain()
            v.tensor_mul(out=xq1_sb[:, dc * 258 + 1:dc * 258 + 257],
                         in0=i32a[:, 0:256], in1=s2mb[:])
            v.drain()
        x3 = xq1_sb[:].rearrange("p (c n) -> p c n", n=258)
        h3 = hq_sb[:].rearrange("p (c n) -> p c n", n=HN)
        v.tensor_copy(out=x3[:, :, 0:1], in_=h3[:, :, tt:tt + 1])
        v.drain()
        xq = v.tensor_copy(out=x3[:, :, 257:258], in_=h3[:, :, NB + tt:NB + tt + 1])
        v.drain()
        ck_xq1 = inc("V", xq)
        for jc in range(JC1):
            tslot = (jc % 2) * DC * 128
            W("Y", ("YT", cnt["YT"]), ck("T"))
            tl = y.dma_start(
                out=t1buf[:, tslot:tslot + DC * 128].rearrange(
                    "p (c n) -> p c n", c=DC),
                in_=t1d[:, :, jc * 128:(jc + 1) * 128])
            ck_t1 = inc("Y1a" if jc % 2 == 0 else "Y1b", tl, 16)
            pb = 5 + (jc % 2)
            mm = None
            for dc in range(DC):
                W("T", ck_t1, ck_xq1, l1.get(("hev", jc - 2)))
                mm = t.matmul(ps[:, pb * 512:pb * 512 + 258],
                              t1buf[:, tslot + dc * 128:tslot + dc * 128 + 128],
                              xq1_sb[:, dc * 258:(dc + 1) * 258],
                              start=(dc == 0), stop=(dc == DC - 1))
            ck_w1 = inc("T", mm)
            pso = pb * 512
            W("V", ck_w1)
            v.tensor_scalar(out=fsc[:, 0:256], in0=ps[:, pso:pso + 256],
                            scalar1=cvw_sb[:, jc * 3:jc * 3 + 1], scalar2=None,
                            op0=OP.mult)
            v.drain()
            v.scalar_tensor_tensor(out=fsc[:, 0:256], in0=ps[:, pso + 1:pso + 257],
                                   scalar=cvw_sb[:, jc * 3 + 1:jc * 3 + 2],
                                   in1=fsc[:, 0:256], op0=OP.mult, op1=OP.add)
            v.drain()
            cv3 = v.scalar_tensor_tensor(out=fsc[:, 0:256],
                                         in0=ps[:, pso + 2:pso + 258],
                                         scalar=cvw_sb[:, jc * 3 + 2:jc * 3 + 3],
                                         in1=fsc[:, 0:256], op0=OP.mult, op1=OP.add)
            v.drain()
            ck_cv = inc("V", cv3)
            l1[("hev", jc)] = ck_cv
            W("S", ck_cv)
            si = s.activation(out=hpool[:, jc * 256:(jc + 1) * 256],
                              in_=fsc[:, 0:256], func=AF.Silu,
                              bias=cvb_sb[:, jc:jc + 1])
            ck_si = inc("S", si)
            W("V", ck_si)
            if jc == 0:
                v.tensor_copy(out=run_sb[:, 0:256], in_=hpool[:, 0:256])
                v.drain()
                am = v.tensor_copy(out=s2mb[:], in_=hpool[:, 0:256])
            else:
                v.tensor_tensor(out=run_sb[:, 0:256], in0=run_sb[:, 0:256],
                                in1=hpool[:, jc * 256:(jc + 1) * 256], op=OP.max)
                v.drain()
                am = v.tensor_tensor(out=s2mb[:], in0=s2mb[:],
                                     in1=hpool[:, jc * 256:(jc + 1) * 256],
                                     op=OP.min)
            v.drain()
            inc("V", am)
        W("V", ck("V"))
        v.tensor_scalar(out=s2mb[:], in0=s2mb[:], scalar1=-1.0,
                        scalar2=None, op0=OP.mult)
        v.drain()
        am = v.tensor_tensor(out=run_sb[:], in0=run_sb[:], in1=s2mb[:],
                             op=OP.max)
        v.drain()
        inc("V", am)
        l1["w1done"] = ck("V")
        ck_run = ck("V")
        for half in range(2):
            ck_trb = token_scales(run_sb[:, half * 128:(half + 1) * 128]
                                  if half else run_sb[:, 0:128],
                                  thr_sb[:, 5:6], ck_run)
            W("V", ck_trb)
            v.tensor_copy(out=ROW(R_S2, half * 128, 128),
                          in_=pstr[0:1, 0:128])
            v.drain()
            cp = v.tensor_copy(out=ROW(R_S2M, half * 128, 128),
                               in_=pstr[32:33, 0:128])
            v.drain()
            inc("V", cp)
        ck_s2 = ck("V")
        ck_s2b = bcast_row(R_S2, 0, 256, run_sb, 0, 2, deps=(ck_s2,))
        ck_s2mb = bcast_row(R_S2M, 0, 256, s2mb, 0, 2, deps=(ck_s2,))
        xq2 = None
        for jc in range(JC1):
            W("V", ck_s2b, ck_s2mb)
            v.tensor_mul(out=i32a[:, 0:256], in0=hpool[:, jc * 256:(jc + 1) * 256],
                         in1=run_sb[:])
            v.drain()
            xq2 = v.tensor_mul(out=hpool[:, jc * 256:(jc + 1) * 256],
                               in0=i32a[:, 0:256], in1=s2mb[:])
            v.drain()
        ck_xq2 = inc("V", xq2)
        for mc in range(DC):
            pb = mc % 2
            mm = None
            for jh2 in range(2):
                W("Y", ("YT", cnt["YT"]), ck("T"))
                tl = y.dma_start(
                    out=t2buf[:].rearrange("p (c n) -> p c n", c=16),
                    in_=t2d[jh2 * 16:(jh2 + 1) * 16, :, mc * 128:(mc + 1) * 128])
                ck_t2 = inc("Y2", tl, 16)
                for jc2 in range(16):
                    jc = jh2 * 16 + jc2
                    W("T", ck_t2, ck_xq2, l1.get(("xadd", mc - 2)))
                    mm = t.matmul(ps[:, pb * 512:pb * 512 + 256],
                                  t2buf[:, jc2 * 128:jc2 * 128 + 128],
                                  hpool[:, jc * 256:(jc + 1) * 256],
                                  start=(jc == 0), stop=(jc == JC1 - 1))
                inc("T", mm)
            ck_w2 = ck("T")
            xsl = x_sb[:, mc * T + tb:mc * T + tb + 256]
            W("V", ck_w2)
            ad = v.tensor_add(out=xsl, in0=xsl,
                              in1=ps[:, pb * 512:pb * 512 + 256])
            v.drain()
            l1[("xadd", mc)] = inc("V", ad)
        l1["w1done"] = ck("V")

    ck_attn1 = emit_attention(1, ck_ag1)
    ck_x2 = emit_oproj(1, wpj_in, ck_attn1)

    W("Y", ck_x2)
    for dc in range(DC):
        inc("YO", y.dma_start(out=out_d[dc * 128:(dc + 1) * 128, :],
                              in_=x_sb[:, dc * T:(dc + 1) * T]), 16)
    y.wait_ge(sems["YO"], cnt["YO"])

    blk.tensor(_replay(tapes["T"]))
    blk.vector(_replay(tapes["V"]))
    blk.scalar(_replay(tapes["S"]))
    blk.sync(_replay(tapes["Y"]))
    blk.gpsimd(_replay(tapes["G"]))

    es.close()
    return nc


def _masks(par):
    kk = np.arange(128)[:, None]
    qq = np.arange(256)[None, :]
    tri0 = (kk <= qq).astype(np.float32)
    tri1 = (kk + 128 <= qq).astype(np.float32)
    one = np.ones((128, 256), np.float32)
    zero = np.zeros((128, 256), np.float32)
    ms = [tri0, tri1, zero, zero] if par == 0 else [one, one, tri0, tri1]
    return np.concatenate(ms, axis=1)


def _stage(inputs, S):
    x = np.asarray(inputs["L0_attn_residual"])
    B = x.shape[0]
    NBG = S // 256
    C = lambda a: np.ascontiguousarray(a)

    wq = C(np.asarray(inputs["L0_attn_W_Q"]).reshape(D, D).T)
    wk = C(np.asarray(inputs["L0_attn_W_K"]).reshape(D, D).T)
    wv = C(np.asarray(inputs["L0_attn_W_V"]).reshape(D, D).T)
    wo = C(np.asarray(inputs["L0_attn_W_O"]).reshape(D, D).T)
    wg = C(np.asarray(inputs["L0_ffn_w_gate"]).T.astype(np.int8))
    wu = C(np.asarray(inputs["L0_ffn_w_up"]).T.astype(np.int8))
    wd = C(np.asarray(inputs["L0_ffn_w_down"]).T.astype(np.int8))
    wqkv = C(np.asarray(inputs["L1_attn_w_qkv"]).T)
    wpj = C(np.asarray(inputs["L1_attn_w_proj"]).T)
    bpj = C(np.asarray(inputs["L1_attn_b_proj"]).reshape(D, 1))
    w1 = C(np.asarray(inputs["L1_ffn_w1"]).T)
    w2 = C(np.asarray(inputs["L1_ffn_w2"]).T)
    cw = C(np.asarray(inputs["L1_ffn_conv_w"]).reshape(H1, 3))
    cb = C(np.asarray(inputs["L1_ffn_conv_b"]).reshape(H1, 1))
    ident = np.eye(128, dtype=np.float32)

    in_maps = []
    for core in range(2 * B):
        b, par = core // 2, core % 2
        blocks = x[b].reshape(NBG, 256, D)[par::2]
        xT = C(blocks.reshape(S // 2, D).T)
        hsel = np.zeros((128, 2), np.float32)
        hsel[:, 0 if par == 0 else 1] = 1.0
        in_maps.append({
            "xT": xT, "masks": _masks(par), "ident": ident, "hsel": hsel,
            "wqT": wq, "wkT": wk, "wvT": wv, "woT": wo,
            "wgT": wg, "wuT": wu, "wdT": wd,
            "wqkvT": wqkv, "wprojT": wpj, "bproj": bpj,
            "w1T": w1, "w2T": w2, "convw": cw, "convb": cb,
        })
    return in_maps


def kernel(**inputs):
    x = np.asarray(inputs["L0_attn_residual"])
    B, S, _ = x.shape
    nc = build_nc(S)
    in_maps = _stage(inputs, S)
    res = run_bass_kernel_spmd(nc, in_maps, core_ids=list(range(2 * B)))
    NBG = S // 256
    out = np.empty((B, S, D), np.float32)
    for core in range(2 * B):
        b, par = core // 2, core % 2
        oT = np.asarray(res.results[core]["outT"])
        out[b].reshape(NBG, 256, D)[par::2] = oT.T.reshape(NBG // 2, 256, D)
    return out


# version 10
# speedup vs baseline: 1.2100x; 1.0232x over previous
import numpy as np
import ml_dtypes
from contextlib import ExitStack

import concourse.bass as bass
import concourse.mybir as mybir
from concourse.bass_utils import run_bass_kernel_spmd
from concourse.alu_op_type import AluOpType as OP

F32 = mybir.dt.float32
BF16 = mybir.dt.bfloat16
I32 = mybir.dt.int32
I8 = mybir.dt.int8
AF = mybir.ActivationFunctionType
AX = mybir.AxisListType

D = 1024
DC = 8
HP = 8
H0 = 3072
JC0 = H0 // 128
H1 = 4096
JC1 = H1 // 128
C127 = float(np.float32(1.0 / 127.0))
CD127 = float(np.float32(1.0 / (127.0 * 127.0)))
EXP_SHIFT = -20.0
RG = [[0, 1], [2, 3], [4, 5], [6, 7]]

R_M, R_R, R_VAR, R_SD = 0, 1, 2, 3
R_REC, R_RECO = 0, 1
R_S2, R_S2M = 2, 3
R_S1, R_S1M = 4, 8
R_WM = 2
R_HALO = 1
R_SC = 0


def build_nc(S):
    T = S // 2
    NB = T // 256
    NSL = T // 512
    TT0 = 512
    NT0 = T // TT0
    HN = 2 * NB
    assert HN <= 32

    nc = bass.Bass("TRN2", target_bir_lowering=False)
    DT = nc.dram_tensor

    xT_in = DT("xT", [D, T], F32, kind="ExternalInput").ap()
    masks_in = DT("masks", [128, 1024], F32, kind="ExternalInput").ap()
    ident_in = DT("ident", [128, 128], F32, kind="ExternalInput").ap()
    hsel_in = DT("hsel", [128, 2], F32, kind="ExternalInput").ap()
    wq_in = DT("wqT", [D, D], F32, kind="ExternalInput").ap()
    wk_in = DT("wkT", [D, D], F32, kind="ExternalInput").ap()
    wv_in = DT("wvT", [D, D], F32, kind="ExternalInput").ap()
    wo_in = DT("woT", [D, D], F32, kind="ExternalInput").ap()
    wg_in = DT("wgT", [D, H0], I8, kind="ExternalInput").ap()
    wu_in = DT("wuT", [D, H0], I8, kind="ExternalInput").ap()
    wd_in = DT("wdT", [H0, D], I8, kind="ExternalInput").ap()
    wqkv_in = DT("wqkvT", [D, 3 * D], F32, kind="ExternalInput").ap()
    wpj_in = DT("wprojT", [D, D], F32, kind="ExternalInput").ap()
    bpj_in = DT("bproj", [D, 1], F32, kind="ExternalInput").ap()
    w1_in = DT("w1T", [D, H1], F32, kind="ExternalInput").ap()
    w2_in = DT("w2T", [H1, D], F32, kind="ExternalInput").ap()
    cw_in = DT("convw", [H1, 3], F32, kind="ExternalInput").ap()
    cb_in = DT("convb", [H1, 1], F32, kind="ExternalInput").ap()
    out_d = DT("outT", [D, T], F32, kind="ExternalOutput").ap()

    KE = HP * 128 * T
    VE = (T // 128) * 128 * D
    ccin = [DT(f"cc{L}i", [KE + VE], BF16, kind="Internal").ap() for L in (0, 1)]
    ccout = [DT(f"cc{L}o", [2, KE + VE], BF16, kind="Internal").ap() for L in (0, 1)]
    hai = DT("hai", [D, HN], F32, kind="Internal").ap()
    hao = DT("hao", [2, D, HN], F32, kind="Internal").ap()
    qsp = DT("qsp", [HP, 128, T], BF16, kind="Internal").ap()
    asp = DT("asp", [HP, 128, T], BF16, kind="Internal").ap()
    t1d = DT("t1d", [DC, 128, H1], BF16, kind="Internal").ap()
    t2d = DT("t2d", [JC1, 128, D], BF16, kind="Internal").ap()

    es = ExitStack()
    SB = lambda n, sh, d: es.enter_context(nc.sbuf_tensor(n, sh, d))

    x_sb = SB("x", [128, DC * T], F32)
    xn_sb = SB("xn", [128, DC * T], BF16)
    rows = SB("rows", [128, 2048], F32)
    ROW = lambda i, c0, w: rows[32 * (i % 3):32 * (i % 3) + 1,
                                (i // 3) * 512 + c0:(i // 3) * 512 + c0 + w]
    bb1 = SB("bb1", [128, 512], F32)
    bb2 = SB("bb2", [128, 512], F32)
    fsc = SB("fsc", [128, 1024], F32)
    warena = SB("warena", [128, DC * 512], BF16)
    wff_w = SB("wff_w", [128, 2 * 1024], BF16)
    kvst = SB("kvst", [128, 2 * 512], BF16)
    katt = SB("katt", [128, S], BF16)
    vatt = SB("vatt", [128, (S // 128) * 130], BF16)
    qatt = SB("qatt", [128, T], BF16)
    p_sb = SB("p_sb", [128, 2048], BF16)
    attst = SB("attst", [128, T], BF16)
    attb = SB("attb", [128, DC * 256], BF16)
    masks_sb = SB("masksb", [128, 1024], BF16)
    ident_sb = SB("identb", [128, 128], F32)
    hsel_sb = SB("hselb", [128, 2], F32)
    ones_r = SB("ones_r", [128, 128], F32)
    onesk = SB("onesk", [128, 1], F32)
    mxc = SB("mxc", [128, 64], F32)
    hpool = SB("hpool", [128, JC0 * TT0], BF16)
    xq1_sb = SB("xq1", [128, DC * 258], BF16)
    t1buf = SB("t1buf", [128, 2 * DC * 128], BF16)
    t2buf = SB("t2buf", [128, 16 * 128], BF16)
    wqf = SB("wqf", [128, 512], BF16)
    tq_sb = SB("tq", [128, 512], BF16)
    wabs = SB("wabs", [128, 4], F32)
    thr_sb = SB("thr", [128, 8], F32)
    i32a = SB("i32a", [128, 256], I32)
    s2mb = SB("s2mb", [128, 256], F32)
    run_sb = SB("run_sb", [128, 256], F32)
    cvw_sb = SB("cvw", [128, 3 * JC1], F32)
    cvb_sb = SB("cvb", [128, JC1], F32)
    bpj_sb = SB("bpjb", [128, DC], F32)
    cbias = SB("cbias", [128, 4], F32)
    hx_sb = SB("hx", [128, 2 * DC * NB], F32)
    hcand = SB("hcand", [128, 2 * DC * NB], F32)
    hq_sb = SB("hq", [128, 2 * DC * NB], BF16)

    ps = es.enter_context(nc.psum_tensor("ps_all", [128, 3584], F32))
    pstr = es.enter_context(nc.psum_tensor("ps_tr", [128, 512], F32))

    DMA_FAMS = ["YX", "YC", "YS0", "YS1", "YH", "YW", "YT0", "YT1", "YA",
                "YB", "YP", "YO", "Y1a", "Y1b", "Y2", "GM", "GA", "GF", "GU",
                "GD0", "GD1", "GW0", "GW1", "CC"]
    sems, cnt = {}, {k: 0 for k in (["T", "V", "S"] + DMA_FAMS)}
    for k in cnt:
        sems[k] = es.enter_context(nc.semaphore("sem" + k))
    last_wait = {}

    blk = es.enter_context(nc.Block())

    class _RecInstr:
        __slots__ = ("incs",)

        def __init__(self):
            self.incs = []

        def then_inc(self, sem, n):
            self.incs.append((sem, n))
            return self

    class _Rec:
        def __init__(self, tape):
            self._tape = tape

        def __getattr__(self, m):
            tape = self._tape

            def call(*a, **k):
                ri = _RecInstr()
                tape.append((m, a, k, ri))
                return ri
            return call

    tapes = {k: [] for k in "TVSYG"}
    ENG = {k: _Rec(tapes[k]) for k in "TVSYG"}
    t, v, s, y, g = (ENG[k] for k in "TVSYG")

    import os as _os
    _SIM = bool(_os.environ.get("BASS_SIM_FP64"))

    def _replay(tape):
        def f(eng):
            for m, a, k, ri in tape:
                instr = getattr(eng, m)(*a, **k)
                for sem, n in ri.incs:
                    instr.then_inc(sem, n)
        return f

    def inc(tag, instr, n=1):
        instr.then_inc(sems[tag], n)
        cnt[tag] += n
        return (tag, cnt[tag])

    def ck(tag):
        return (tag, cnt[tag])

    def W(consumer, *cks):
        lw = last_wait.setdefault(consumer, {})
        best = {}
        for c in cks:
            if c is None:
                continue
            tag, val = c
            if val > best.get(tag, 0):
                best[tag] = val
        for tag, val in best.items():
            if val <= 0 or lw.get(tag, 0) >= val:
                continue
            assert val <= cnt[tag], f"forward wait {tag}:{val} > {cnt[tag]}"
            ENG[consumer].wait_ge(sems[tag], val)
            lw[tag] = val

    for dc in range(DC):
        inc("YX", y.dma_start(out=x_sb[:, dc * T:(dc + 1) * T],
                              in_=xT_in[dc * 128:(dc + 1) * 128, :]), 16)
    ck_x0 = ck("YX")
    inc("YC", y.dma_start(out=ident_sb[:], in_=ident_in), 16)
    inc("YC", y.dma_start(out=hsel_sb[:], in_=hsel_in), 16)
    inc("YC", y.dma_start(
        out=cvw_sb[:].rearrange("p (c k) -> p c k", c=JC1),
        in_=cw_in.rearrange("(c p) k -> p c k", p=128),
        allow_slow_non_contiguous=True), 16)
    inc("YC", y.dma_start(
        out=cvb_sb[:].rearrange("p (c o) -> p c o", c=JC1),
        in_=cb_in.rearrange("(c p) o -> p c o", p=128),
        allow_slow_non_contiguous=True), 16)
    inc("YC", y.dma_start(
        out=bpj_sb[:].rearrange("p (c o) -> p c o", c=DC),
        in_=bpj_in.rearrange("(c p) o -> p c o", p=128),
        allow_slow_non_contiguous=True), 16)
    ck_consts = ck("YC")
    ck_masks = inc("GM", g.dma_start(out=masks_sb[:], in_=masks_in), 16)

    W("V", ck_consts)
    v.memset(ones_r[:], 1.0)
    v.memset(onesk[:], 1.0)
    v.memset(cbias[:, 0:1], 1e-5)
    v.memset(cbias[:, 1:2], 1.0)
    ck_ones = inc("V", v.memset(cbias[:, 2:3], EXP_SHIFT))

    def bcast_row(slot, c0, n, dst_sb, dst_off, pbank, deps=(), recip=False):
        b = 32 * (slot % 3)
        W("T", ck_ones, ck("V"), *deps)
        mm = t.matmul(ps[:, pbank * 512:pbank * 512 + n], ones_r[b:b + 1, 0:128],
                      ROW(slot, c0, n), start=True, stop=True)
        ckt = inc("T", mm)
        W("V", ckt)
        if recip:
            cp = v.reciprocal(out=dst_sb[:, dst_off:dst_off + n],
                              in_=ps[:, pbank * 512:pbank * 512 + n])
        else:
            cp = v.tensor_copy(out=dst_sb[:, dst_off:dst_off + n],
                               in_=ps[:, pbank * 512:pbank * 512 + n])
        v.drain()
        return inc("V", cp)

    def wload(fam, dram_ap, row0, rows_, col0, ncols, dst, dst_off):
        ins = g.dma_start(
            out=dst[:, dst_off:dst_off + (rows_ // 128) * ncols].rearrange(
                "p (c n) -> p c n", c=rows_ // 128),
            in_=dram_ap[row0:row0 + rows_, col0:col0 + ncols].rearrange(
                "(c p) n -> p c n", p=128))
        return inc(fam, ins, 16)

    def ln_slice(xs_aps, w_, dep):
        mm2 = None
        W("T", ck("V"))
        for dc in range(DC):
            xap = xs_aps[dc][0]
            off = (dc % 2) * 512
            W("V", dep, ck("T"))
            sq = v.tensor_mul(out=fsc[:, off:off + w_], in0=xap, in1=xap)
            v.drain()
            cksq = inc("V", sq)
            W("T", ck_ones, cksq)
            t.matmul(ps[0:1, 3 * 512:3 * 512 + w_], onesk[:], xap,
                     start=(dc == 0), stop=(dc == DC - 1))
            mm2 = t.matmul(ps[0:1, 4 * 512:4 * 512 + w_], onesk[:],
                           fsc[:, off:off + w_],
                           start=(dc == 0), stop=(dc == DC - 1))
            inc("T", mm2)
        ck_st = ck("T")
        W("V", ck_st)
        v.tensor_scalar(out=ROW(R_M, 0, w_),
                        in0=ps[0:1, 3 * 512:3 * 512 + w_],
                        scalar1=1.0 / D, scalar2=None, op0=OP.mult)
        v.drain()
        v.tensor_mul(out=ROW(R_VAR, 0, w_), in0=ROW(R_M, 0, w_),
                     in1=ROW(R_M, 0, w_))
        v.drain()
        va = v.scalar_tensor_tensor(out=ROW(R_VAR, 0, w_),
                                    in0=ps[0:1, 4 * 512:4 * 512 + w_],
                                    scalar=1.0 / D, in1=ROW(R_VAR, 0, w_),
                                    op0=OP.mult, op1=OP.subtract)
        v.drain()
        ck_var = inc("V", va)
        W("S", ck_var)
        sq2 = s.activation(out=ROW(R_SD, 0, w_),
                           in_=ROW(R_VAR, 0, w_), func=AF.Sqrt,
                           bias=cbias[0:1, 0:1])
        ck_sd = inc("S", sq2)
        ck_mb = bcast_row(R_M, 0, w_, bb1, 0, 5, deps=(ck_sd,))
        ck_rb = bcast_row(R_SD, 0, w_, bb2, 0, 6, deps=(ck_sd,), recip=True)
        mu = None
        for dc in range(DC):
            xap, oap = xs_aps[dc]
            W("V", ck_mb, ck_rb)
            v.tensor_sub(out=fsc[:, 0:w_], in0=xap, in1=bb1[:, 0:w_])
            v.drain()
            mu = v.tensor_mul(out=oap, in0=fsc[:, 0:w_], in1=bb2[:, 0:w_])
            v.drain()
        return inc("V", mu)

    def emit_layernorm(dep):
        outs = []
        for sl in range(NSL):
            c0 = sl * 512
            xs = [(x_sb[:, dc * T + c0:dc * T + c0 + 512],
                   xn_sb[:, dc * T + c0:dc * T + c0 + 512]) for dc in range(DC)]
            outs.append(ln_slice(xs, 512, dep))
        return outs[-1]

    proj_state = {"dma": [], "slot": 0, "evac": []}

    def proj_evac(ck_t, pb, dst_ap):
        sl = proj_state["slot"]
        so = (sl % 2) * 512
        W("S", ck_t)
        if sl >= 2:
            W("S", proj_state["dma"][sl - 2])
        cp = s.copy(out=kvst[:, so:so + 512],
                    in_=ps[:, pb * 512:pb * 512 + 512])
        s.drain()
        ck_v = inc("S", cp)
        proj_state["evac"].append(ck_v)
        W("Y", ck_v, ("YA", cnt["YA"]))
        dd = y.dma_start(out=dst_ap, in_=kvst[:, so:so + 512])
        proj_state["dma"].append(inc("YS0" if sl % 2 == 0 else "YS1", dd, 16))
        proj_state["slot"] += 1
        pump_side(2)

    def emit_proj_w(w_dram, wcol0, dep_xn, dst_fn):
        for half in range(2):
            W("G", ck("T"))
            ck_w = wload("GA", w_dram, 0, D, wcol0 + half * 512, 512, warena, 0)
            for a2 in range(4):
                a = half * 4 + a2
                for b in range(NSL):
                    pb = 5 + (proj_state["slot"] % 2)
                    mm = None
                    ev = proj_state["evac"]
                    for dc in range(DC):
                        W("T", ck_w, dep_xn,
                          ev[-2] if len(ev) >= 2 else None)
                        mm = t.matmul(
                            ps[:, pb * 512:pb * 512 + 512],
                            warena[:, dc * 512 + a2 * 128:dc * 512 + a2 * 128 + 128],
                            xn_sb[:, dc * T + b * 512:dc * T + b * 512 + 512],
                            start=(dc == 0), stop=(dc == DC - 1))
                    proj_evac(inc("T", mm), pb, dst_fn(a, b))

    def emit_proj_v(w_dram, wcol0, dep_xn, dst_fn):
        for b in range(2):
            W("G", ck("T"))
            ck_w = wload("GA", w_dram, 0, D, wcol0 + b * 512, 512, warena, 0)
            for a in range(T // 128):
                pb = 5 + (proj_state["slot"] % 2)
                mm = None
                ev = proj_state["evac"]
                for dc in range(DC):
                    W("T", ck_w, dep_xn,
                      ev[-2] if len(ev) >= 2 else None)
                    mm = t.matmul(
                        ps[:, pb * 512:pb * 512 + 512],
                        xn_sb[:, dc * T + a * 128:dc * T + a * 128 + 128],
                        warena[:, dc * 512:dc * 512 + 512],
                        start=(dc == 0), stop=(dc == DC - 1))
                proj_evac(inc("T", mm), pb, dst_fn(a, b))

    def token_scales(fold_ap, mcol_ap, dep):
        W("T", dep, ck_consts, ck("V"))
        tr = t.transpose(pstr[:, 0:128], fold_ap, ident_sb[:])
        ck_tr = inc("T", tr)
        W("V", ck_tr)
        v.reduce_max(out=mxc[:, 34:35], in_=pstr[:, 0:128], axis=AX.X)
        v.drain()
        v.tensor_scalar(out=mxc[:, 35:36], in0=mxc[:, 34:35], scalar1=1e-5,
                        scalar2=None, op0=OP.max)
        v.drain()
        v.reciprocal(out=mxc[:, 36:37], in_=mxc[:, 35:36])
        v.drain()
        v.tensor_scalar(out=mxc[:, 0:1], in0=mxc[:, 36:37], scalar1=127.0,
                        scalar2=None, op0=OP.mult)
        v.drain()
        sm = v.tensor_scalar(out=mxc[:, 32:33], in0=mxc[:, 35:36], scalar1=mcol_ap,
                             scalar2=C127, op0=OP.mult, op1=OP.mult)
        v.drain()
        ck_mx = inc("V", sm)
        W("T", ck_mx)
        tr2 = t.transpose(pstr[0:64, 0:128], mxc[:, :], ident_sb[:])
        return inc("T", tr2)

    def wquant_gen(dram_ap, rows_, ncol, col, dst_dram):
        pieces = [(c, p0) for c in range(rows_ // 128)
                  for p0 in range(0, ncol, 256)]
        vdone, lds = {}, {}

        def issue(i, dep0):
            c, p0 = pieces[i]
            sl = i % 2
            W("G", vdone.get(i - 2, dep0 if i < 2 else None))
            return inc("GW0" if sl == 0 else "GW1", g.dma_start(
                out=wqf[:, sl * 256:sl * 256 + 256],
                in_=dram_ap[c * 128:(c + 1) * 128, p0:p0 + 256]), 16)

        dep0 = ("V", cnt["V"])
        lds[0] = issue(0, dep0)
        first = True
        ck_acc = None
        for i in range(len(pieces)):
            if i + 1 < len(pieces):
                lds[i + 1] = issue(i + 1, dep0)
            sl = i % 2
            W("V", lds[i])
            v.reduce_sum(out=wabs[:, 2:3], in_=wqf[:, sl * 256:sl * 256 + 256],
                         axis=AX.X, apply_absolute_value=True)
            v.drain()
            if first:
                cp = v.tensor_copy(out=wabs[:, 0:1], in_=wabs[:, 2:3])
                first = False
            else:
                cp = v.tensor_add(out=wabs[:, 0:1], in0=wabs[:, 0:1],
                                  in1=wabs[:, 2:3])
            v.drain()
            ck_acc = inc("V", cp)
            vdone[i] = ck_acc
            yield
        W("T", ck_acc, ck_ones)
        mm = t.matmul(pstr[0:1, 0:1], onesk[:], wabs[:, 0:1],
                      start=True, stop=True)
        ck_t = inc("T", mm)
        W("V", ck_t)
        v.tensor_scalar(out=ROW(R_WM, col, 1), in0=pstr[0:1, 0:1],
                        scalar1=1.0 / (rows_ * ncol), scalar2=None, op0=OP.mult)
        v.drain()
        th = v.tensor_scalar(out=ROW(R_WM, col + 2, 1), in0=ROW(R_WM, col, 1),
                             scalar1=0.5, scalar2=None, op0=OP.mult)
        ck_m = inc("V", th)
        W("T", ck_m)
        wb = 32 * (R_WM % 3)
        mm = t.matmul(pstr[:, 0:1], ones_r[wb:wb + 1, 0:128],
                      ROW(R_WM, col + 2, 1), start=True, stop=True)
        ck_t = inc("T", mm)
        W("V", ck_t)
        v.tensor_copy(out=thr_sb[:, 2 * col:2 * col + 1],
                      in_=pstr[:, 0:1])
        v.drain()
        v.tensor_scalar(out=thr_sb[:, 2 * col + 1:2 * col + 2],
                        in0=thr_sb[:, 2 * col:2 * col + 1], scalar1=-1.0,
                        scalar2=None, op0=OP.mult)
        v.drain()
        th2 = v.tensor_scalar(out=thr_sb[:, 4 + col:5 + col],
                              in0=thr_sb[:, 2 * col:2 * col + 1], scalar1=2.0,
                              scalar2=None, op0=OP.mult)
        v.drain()
        ck_thr = inc("V", th2)
        yield
        qdone, ytd, lds2 = {}, {}, {}
        yt_prev = (("YT0", cnt["YT0"]), ("YT1", cnt["YT1"]))

        def issue2(i):
            c, p0 = pieces[i]
            sl = i % 2
            W("G", qdone.get(i - 2, ck_acc if i < 2 else None))
            return inc("GW0" if sl == 0 else "GW1", g.dma_start(
                out=wqf[:, sl * 256:sl * 256 + 256],
                in_=dram_ap[c * 128:(c + 1) * 128, p0:p0 + 256]), 16)

        lds2[0] = issue2(0)
        for i in range(len(pieces)):
            c, p0 = pieces[i]
            if i + 1 < len(pieces):
                lds2[i + 1] = issue2(i + 1)
            sl = i % 2
            so = sl * 256
            W("V", lds2[i], ck_thr,
              *(yt_prev if i < 2 else (ytd.get(i - 2),)))
            v.tensor_scalar(out=run_sb[:], in0=wqf[:, so:so + 256],
                            scalar1=thr_sb[:, 2 * col + 1:2 * col + 2],
                            scalar2=None, op0=OP.is_lt)
            v.drain()
            v.tensor_scalar(out=tq_sb[:, so:so + 256], in0=wqf[:, so:so + 256],
                            scalar1=thr_sb[:, 2 * col:2 * col + 1],
                            scalar2=None, op0=OP.is_gt)
            v.drain()
            sb2 = v.tensor_sub(out=tq_sb[:, so:so + 256],
                               in0=tq_sb[:, so:so + 256], in1=run_sb[:])
            v.drain()
            ck_q = inc("V", sb2)
            qdone[i] = ck_q
            W("Y", ck_q)
            dd = y.dma_start(out=dst_dram[c][:, p0:p0 + 256],
                             in_=tq_sb[:, so:so + 256])
            ytd[i] = inc("YT0" if sl == 0 else "YT1", dd, 16)
            yield

    side_gens = []

    def pump_side(n=1):
        for _ in range(n):
            while side_gens:
                try:
                    next(side_gens[0])
                    break
                except StopIteration:
                    side_gens.pop(0)
            else:
                return

    def drain_side():
        while side_gens:
            pump_side()


    def emit_attention(L, ck_ag):
        cc = ccout[L]
        st = {"v0": ck("V")}
        prev_tail = None
        for hp in range(HP):
            W("Y", ck_ag, prev_tail, ck("T") if hp else None,
              ("YS0", cnt["YS0"]), ("YS1", cnt["YS1"]))
            kld = []
            for r in (0, 1):
                srck = cc[r][0:KE].rearrange("(h p n) -> h p n", h=HP, p=128)[hp]
                dd = y.dma_start(
                    out=katt[:].rearrange("p (j q) -> p j q",
                                          q=512)[:, :, 256 * r:256 * r + 256],
                    in_=srck.rearrange("p (j n) -> p j n", n=256))
                kld.append(inc("YA", dd, 16))
                srcv = cc[r][KE:KE + VE].rearrange("(c p n) -> c p n",
                                                   c=T // 128, p=128)
                for u in range(2):
                    for hd in range(2):
                        off = 260 * r + 130 * u + 65 * hd
                        dd = y.dma_start(
                            out=vatt[:].rearrange(
                                "p (j q) -> p j q", q=520)[:, :, off:off + 64],
                            in_=srcv[u::2, :, hp * 128 + hd * 64:
                                     hp * 128 + hd * 64 + 64].rearrange(
                                "c p n -> p c n"))
                        kld.append(inc("YA", dd, 16))
            dd = y.dma_start(out=qatt[:], in_=qsp[hp])
            kld.append(inc("YA", dd, 16))
            W("V", *kld, prev_tail)
            ckone = inc("V", v.memset(
                vatt[:].rearrange("p (c f) -> p c f", f=130)[:, :, 64:130:65], 1.0))
            for j in range(NB):
                q0 = j * 256
                for bt in range(j + 1):
                    for hd in range(2):
                        bko = hd * 1024
                        po = hd * 1024
                        mm = None
                        for i in range(4):
                            kb = 4 * bt + i
                            W("T", *kld, ckone, st.get(("exp", hd)),
                              st.get("avfree") if bt == 0 else None)
                            mm = t.matmul(
                                ps[:, bko + i * 256:bko + i * 256 + 256],
                                katt[64 * hd:64 * hd + 64, kb * 128:kb * 128 + 128],
                                qatt[64 * hd:64 * hd + 64, q0:q0 + 256],
                                start=True, stop=True, tile_position=(64 * hd, 0))
                        ck_sc = inc("T", mm)
                        if L == 0:
                            W("S", ck_sc, st.get(("pfree", hd)), st["v0"])
                            e2 = s.activation(out=p_sb[:, po:po + 1024],
                                              in_=ps[:, bko:bko + 1024],
                                              func=AF.Exp, scale=0.125,
                                              bias=cbias[:, 2:3])
                            ck_e = inc("S", e2)
                        else:
                            W("S", ck_sc, st.get(("pfree", hd)), st["v0"])
                            s.activation(out=p_sb[:, po:po + 1024],
                                         in_=ps[:, bko:bko + 1024],
                                         func=AF.Exp, scale=0.125)
                            s.drain()
                            e2 = s.activation(out=p_sb[:, po:po + 1024],
                                              in_=p_sb[:, po:po + 1024],
                                              func=AF.Ln, bias=cbias[:, 1:2])
                            ck_ln = inc("S", e2)
                            W("V", ck_ln)
                            sq = v.tensor_mul(out=p_sb[:, po:po + 1024],
                                              in0=p_sb[:, po:po + 1024],
                                              in1=p_sb[:, po:po + 1024])
                            v.drain()
                            ck_e = inc("V", sq)
                        st[("exp", hd)] = ck_e
                        if bt == j:
                            W("V", ck_e, ck_masks)
                            mk = v.tensor_mul(out=p_sb[:, po:po + 1024],
                                              in0=p_sb[:, po:po + 1024],
                                              in1=masks_sb[:])
                            v.drain()
                            ck_e = inc("V", mk)
                            st[("exp", hd)] = ck_e
                        mm = None
                        for i in range(4):
                            kb = 4 * bt + i
                            W("T", ck_e,
                              st.get("avfree") if bt == 0 else None)
                            mm = t.matmul(
                                ps[0:65, (4 + hd) * 512:(4 + hd) * 512 + 256],
                                vatt[:, kb * 130 + hd * 65:kb * 130 + hd * 65 + 65],
                                p_sb[:, po + i * 256:po + i * 256 + 256],
                                start=(bt == 0 and i == 0),
                                stop=(bt == j and i == 3))
                        st[("pfree", hd)] = inc("T", mm)
                        pump_side(1)
                ck_av = st[("pfree", 1)]
                W("V", ck_av)
                if L == 1:
                    v.tensor_scalar(out=ROW(R_REC, 0, 256),
                                    in0=ps[64:65, 2048:2304], scalar1=1e-9,
                                    scalar2=None, op0=OP.add)
                    v.drain()
                    cps = v.tensor_scalar(out=ROW(R_REC, 256, 256),
                                          in0=ps[64:65, 2560:2816],
                                          scalar1=1e-9, scalar2=None,
                                          op0=OP.add)
                else:
                    v.tensor_copy(out=ROW(R_REC, 0, 256),
                                  in_=ps[64:65, 2048:2304])
                    v.drain()
                    cps = v.tensor_copy(out=ROW(R_REC, 256, 256),
                                        in_=ps[64:65, 2560:2816])
                v.drain()
                ck_rc = inc("V", cps)
                W("T", ck_rc)
                rb = 32 * (R_REC % 3)
                t.matmul(pstr[0:64, 256:512], ones_r[rb:rb + 1, 0:64],
                         ROW(R_REC, 0, 256), start=True, stop=True,
                         tile_position=(rb, 0))
                mm = t.matmul(pstr[64:128, 256:512], ones_r[rb:rb + 1, 0:64],
                              ROW(R_REC, 256, 256), start=True,
                              stop=True, tile_position=(rb, 64))
                ck_bc = inc("T", mm)
                if L == 0:
                    W("S", ck_av)
                    s.copy(out=attst[0:64, q0:q0 + 256], in_=ps[0:64, 2048:2304])
                    s.drain()
                    cpk = s.copy(out=attst[64:128, q0:q0 + 256],
                                 in_=ps[0:64, 2560:2816])
                    s.drain()
                    ck_cp = inc("S", cpk)
                else:
                    W("V", ck_av)
                    v.tensor_copy(out=attst[0:64, q0:q0 + 256],
                                  in_=ps[0:64, 2048:2304])
                    v.drain()
                    cpk = v.tensor_copy(out=attst[64:128, q0:q0 + 256],
                                        in_=ps[0:64, 2560:2816])
                    v.drain()
                    ck_cp = inc("V", cpk)
                W("V", ck_bc, ck_cp)
                v.reciprocal(out=fsc[:, 512:768], in_=pstr[:, 256:512])
                v.drain()
                v.tensor_mul(out=attst[0:64, q0:q0 + 256],
                             in0=attst[0:64, q0:q0 + 256],
                             in1=fsc[0:64, 512:768])
                v.drain()
                mu = v.tensor_mul(out=attst[64:128, q0:q0 + 256],
                                  in0=attst[64:128, q0:q0 + 256],
                                  in1=fsc[64:128, 512:768])
                v.drain()
                st["avread"] = inc("V", mu)
                st["avfree"] = st["avread"]
            W("Y", st["avread"], ("YB", cnt["YB"]))
            dd = y.dma_start(out=asp[hp], in_=attst[:])
            prev_tail = inc("YP", dd, 16)
        return prev_tail

    def emit_oproj(L, w_dram, dep_att):
        last = None
        for half in range(2):
            W("G", ck("T"))
            ck_w = wload("GA", w_dram, 0, D, half * 512, 512, warena, 0)
            for sl in range(T // 256):
                W("Y", dep_att, ck("T") if (half or sl) else None)
                dd = y.dma_start(
                    out=attb[:].rearrange("p (c n) -> p c n", n=256),
                    in_=asp[:, :, sl * 256:sl * 256 + 256].rearrange(
                        "c p n -> p c n"))
                abl = [inc("YB", dd, 16)]
                for m2 in range(4):
                    mc = half * 4 + m2
                    pb = 5 + ((sl * 4 + m2) % 2)
                    mm = None
                    for c in range(DC):
                        W("T", ck_w, *abl, last)
                        mm = t.matmul(
                            ps[:, pb * 512:pb * 512 + 256],
                            warena[:, c * 512 + m2 * 128:c * 512 + m2 * 128 + 128],
                            attb[:, c * 256:(c + 1) * 256],
                            start=(c == 0), stop=(c == DC - 1))
                    ck_t = inc("T", mm)
                    xsl = x_sb[:, mc * T + sl * 256:mc * T + sl * 256 + 256]
                    W("V", ck_t)
                    if L == 0:
                        ad = v.tensor_add(out=xsl, in0=xsl,
                                          in1=ps[:, pb * 512:pb * 512 + 256])
                    else:
                        ad = v.scalar_tensor_tensor(
                            out=xsl, in0=ps[:, pb * 512:pb * 512 + 256],
                            scalar=bpj_sb[:, mc:mc + 1], in1=xsl,
                            op0=OP.add, op1=OP.add)
                    v.drain()
                    last = inc("V", ad)
                    pump_side(1)
        return last

    cck = [ccin[L][0:KE].rearrange("(h p n) -> h p n", h=HP, p=128)
           for L in (0, 1)]
    ccv = [ccin[L][KE:KE + VE].rearrange("(c p n) -> c p n", c=T // 128, p=128)
           for L in (0, 1)]

    ck_xn = emit_layernorm(ck_x0)
    side_gens.append(wquant_gen(w1_in, D, H1, 0, t1d))
    side_gens.append(wquant_gen(w2_in, H1, D, 1, t2d))
    emit_proj_w(wk_in, 0, ck_xn, lambda a, b: cck[0][a][:, b * 512:(b + 1) * 512])
    emit_proj_v(wv_in, 0, ck_xn, lambda a, b: ccv[0][a][:, b * 512:(b + 1) * 512])
    W("G", ("YS0", cnt["YS0"]), ("YS1", cnt["YS1"]))
    ck_ag0 = inc("CC", g.collective_compute(
        "AllGather", OP.bypass, replica_groups=RG, ins=[ccin[0]], outs=[ccout[0]]))
    emit_proj_w(wq_in, 0, ck_xn, lambda a, b: qsp[a][:, b * 512:(b + 1) * 512])

    fs = {}
    for tt in range(NT0):
        t0 = tt * TT0
        for jh in range(JC0):
            if tt == 0 and jh == 0:
                fs["wg"] = wload("GF", wg_in, 0, D, 0, 128, wff_w, 0)
            mm = None
            for dc in range(DC):
                W("T", fs["wg"], ck_xn, fs.get("h"), fs.get("si"),
                  fs.get("hused"))
                mm = t.matmul(ps[:, 0:512],
                              wff_w[:, dc * 128:dc * 128 + 128],
                              xn_sb[:, dc * T + t0:dc * T + t0 + 512],
                              start=(dc == 0), stop=(dc == DC - 1))
            ck_g = inc("T", mm)
            if tt == 0 and jh == 0:
                fs["wu"] = wload("GU", wu_in, 0, D, 0, 128, wff_w, 1024)
            for dc in range(DC):
                W("T", fs["wu"])
                mm = t.matmul(ps[:, 512:1024],
                              wff_w[:, 1024 + dc * 128:1024 + dc * 128 + 128],
                              xn_sb[:, dc * T + t0:dc * T + t0 + 512],
                              start=(dc == 0), stop=(dc == DC - 1))
            ck_gu = inc("T", mm)
            pump_side(1)
            nj, ntt = (jh + 1, tt) if jh + 1 < JC0 else (0, tt + 1)
            if ntt < NT0:
                W("G", ck_g)
                fs["wg"] = wload("GF", wg_in, 0, D, nj * 128, 128, wff_w, 0)
                W("G", ck_gu)
                fs["wu"] = wload("GU", wu_in, 0, D, nj * 128, 128, wff_w, 1024)
            W("S", ck_gu, fs.get("h"))
            si = s.activation(out=fsc[:, 0:512], in_=ps[:, 0:512],
                              func=AF.Silu, scale=C127)
            ck_si = inc("S", si)
            fs["si"] = ck_si
            W("V", ck_si)
            hh = v.tensor_mul(out=hpool[:, jh * 512:(jh + 1) * 512],
                              in0=fsc[:, 0:512], in1=ps[:, 512:1024])
            v.drain()
            fs["h"] = inc("V", hh)
        ck_h = fs["h"]
        for mh in range(2):
            mm = None
            for jc in range(JC0):
                so = (jc % 2) * 512
                if tt == 0 and mh == 0 and jc == 0:
                    W("G", ("YS0", cnt["YS0"]), ("YS1", cnt["YS1"]))
                    fs["wd0"] = wload("GD0", wd_in, 0, 128, 0, 512, kvst, 0)
                ck_wd = fs["wd0" if jc % 2 == 0 else "wd1"]
                for m2 in range(4):
                    W("T", ck_wd, ck_h, fs.get("xadd"))
                    mm = t.matmul(
                        ps[:, m2 * 512:m2 * 512 + 512],
                        kvst[:, so + m2 * 128:so + m2 * 128 + 128],
                        hpool[:, jc * TT0:jc * TT0 + 512],
                        start=(jc == 0), stop=(jc == JC0 - 1))
                inc("T", mm)
                nj, nmh = (jc + 1, mh) if jc + 1 < JC0 else (0, mh + 1)
                ntt2 = tt if nmh < 2 else tt + 1
                if nmh == 2:
                    nmh = 0
                if ntt2 < NT0:
                    W("G", ck("T"))
                    fs["wd0" if nj % 2 == 0 else "wd1"] = wload(
                        "GD0" if nj % 2 == 0 else "GD1", wd_in, nj * 128, 128,
                        nmh * 512, 512, kvst, (nj % 2) * 512)
            ck_dn = ck("T")
            ad = None
            for m2 in range(4):
                mc = mh * 4 + m2
                xsl = x_sb[:, mc * T + t0:mc * T + t0 + 512]
                W("V", ck_dn)
                ad = v.scalar_tensor_tensor(
                    out=xsl, in0=ps[:, m2 * 512:m2 * 512 + 512],
                    scalar=CD127, in1=xsl, op0=OP.mult, op1=OP.add)
                v.drain()
            fs["xadd"] = inc("V", ad)
        fs["hused"] = fs["xadd"]

    ck_attn0 = emit_attention(0, ck_ag0)
    ck_x1 = emit_oproj(0, wo_in, ck_attn0)

    W("Y", ck_x1)
    x4d = x_sb[:].rearrange("p (c j n) -> p c j n", c=DC, j=NB)
    hai3 = hai.rearrange("(c p) n -> p c n", p=128)
    for u in range(2):
        for dcq in range(DC):
            hx = y.dma_start(
                out=hai3[:, dcq, u * NB:(u + 1) * NB],
                in_=x4d[:, dcq, :, 255 * u],
                allow_slow_non_contiguous=True)
            inc("YH", hx, 16)
    ck_hai = ck("YH")
    W("G", ck_hai)
    ck_ag1a = inc("CC", g.collective_compute(
        "AllGather", OP.bypass, replica_groups=RG, ins=[hai], outs=[hao]))

    ck_xn = emit_layernorm(ck_x1)
    emit_proj_w(wqkv_in, 1024, ck_xn,
                lambda a, b: cck[1][a][:, b * 512:(b + 1) * 512])
    emit_proj_v(wqkv_in, 2048, ck_xn,
                lambda a, b: ccv[1][a][:, b * 512:(b + 1) * 512])
    W("G", ("YS0", cnt["YS0"]), ("YS1", cnt["YS1"]))
    ck_ag1 = inc("CC", g.collective_compute(
        "AllGather", OP.bypass, replica_groups=RG, ins=[ccin[1]], outs=[ccout[1]]))
    emit_proj_w(wqkv_in, 0, ck_xn, lambda a, b: qsp[a][:, b * 512:(b + 1) * 512])

    W("Y", ck_ag1a)
    c0 = y.dma_start(
        out=hcand[:, 0:DC * NB].rearrange("p (c j) -> p c j", c=DC)[:, :, 1:NB],
        in_=hao[1].rearrange("(c p) n -> p c n", p=128)[:, :, NB:2 * NB - 1],
        allow_slow_non_contiguous=True)
    inc("YH", c0, 16)
    c1 = y.dma_start(
        out=hcand[:, DC * NB:2 * DC * NB].rearrange("p (c j) -> p c j", c=DC),
        in_=hao[0].rearrange("(c p) n -> p c n", p=128)[:, :, NB:2 * NB],
        allow_slow_non_contiguous=True)
    ck_hc = inc("YH", c1, 16)
    W("V", ck_hc, ck_consts)
    v.memset(hcand[:, 0:DC * NB].rearrange("p (c j) -> p c j", c=DC)[:, :, 0:1], 0.0)
    v.drain()
    hx3 = hx_sb[:].rearrange("p (c n) -> p c n", n=HN)
    hc3a = hcand[:, 0:DC * NB].rearrange("p (c j) -> p c j", c=DC)
    hc3b = hcand[:, DC * NB:2 * DC * NB].rearrange("p (c j) -> p c j", c=DC)
    v.tensor_scalar(out=hx3[:, :, 0:NB], in0=hc3a,
                    scalar1=hsel_sb[:, 0:1], scalar2=None, op0=OP.mult)
    v.drain()
    hl = v.scalar_tensor_tensor(out=hx3[:, :, 0:NB], in0=hc3b,
                                scalar=hsel_sb[:, 1:2], in1=hx3[:, :, 0:NB],
                                op0=OP.mult, op1=OP.add)
    v.drain()
    ck_hl = inc("V", hl)
    W("Y", ck_hl)
    c0 = y.dma_start(
        out=hcand[:, 0:DC * NB].rearrange("p (c j) -> p c j", c=DC),
        in_=hao[1].rearrange("(c p) n -> p c n", p=128)[:, :, 0:NB],
        allow_slow_non_contiguous=True)
    inc("YH", c0, 16)
    c1 = y.dma_start(
        out=hcand[:, DC * NB:2 * DC * NB].rearrange(
            "p (c j) -> p c j", c=DC)[:, :, 0:NB - 1],
        in_=hao[0].rearrange("(c p) n -> p c n", p=128)[:, :, 1:NB],
        allow_slow_non_contiguous=True)
    ck_hc2 = inc("YH", c1, 16)
    W("V", ck_hc2)
    v.memset(hcand[:, DC * NB:2 * DC * NB].rearrange(
        "p (c j) -> p c j", c=DC)[:, :, NB - 1:NB], 0.0)
    v.drain()
    v.tensor_scalar(out=hx3[:, :, NB:HN], in0=hc3a,
                    scalar1=hsel_sb[:, 0:1], scalar2=None, op0=OP.mult)
    v.drain()
    hr = v.scalar_tensor_tensor(out=hx3[:, :, NB:HN], in0=hc3b,
                                scalar=hsel_sb[:, 1:2],
                                in1=hx3[:, :, NB:HN],
                                op0=OP.mult, op1=OP.add)
    v.drain()
    ck_hx = inc("V", hr)

    def hx_ap(tens, dc):
        return tens[:, dc * HN:(dc + 1) * HN]

    for side in range(2):
        o = side * NB
        hxs = [(hx_ap(hx_sb, dc)[:, o:o + NB], hx_ap(hcand, dc)[:, o:o + NB])
               for dc in range(DC)]
        ck_hxn = ln_slice(hxs, NB, ck_hx)
    am = None
    for dc in range(DC):
        W("V", ck_hxn)
        if dc == 0:
            v.tensor_copy(out=run_sb[:, 0:HN], in_=hx_ap(hcand, 0))
            v.drain()
            am = v.tensor_copy(out=s2mb[:, 0:HN], in_=hx_ap(hcand, 0))
        else:
            v.tensor_tensor(out=run_sb[:, 0:HN], in0=run_sb[:, 0:HN],
                            in1=hx_ap(hcand, dc), op=OP.max)
            v.drain()
            am = v.tensor_tensor(out=s2mb[:, 0:HN], in0=s2mb[:, 0:HN],
                                 in1=hx_ap(hcand, dc), op=OP.min)
        v.drain()
    v.tensor_scalar(out=s2mb[:, 0:HN], in0=s2mb[:, 0:HN], scalar1=-1.0,
                    scalar2=None, op0=OP.mult)
    v.drain()
    am = v.tensor_tensor(out=run_sb[:, 0:HN], in0=run_sb[:, 0:HN],
                         in1=s2mb[:, 0:HN], op=OP.max)
    v.drain()
    inc("V", am)
    W("V", ck("V"))
    zz = v.memset(run_sb[:, HN:128], 0.0)
    v.drain()
    ck_hfold = inc("V", zz)
    ck_trb = token_scales(run_sb[:, 0:128], thr_sb[:, 4:5], ck_hfold)
    W("V", ck_trb)
    v.tensor_copy(out=ROW(R_HALO, 64, HN), in_=pstr[0:1, 0:HN])
    v.drain()
    hsm = v.tensor_copy(out=ROW(R_HALO, 96, HN),
                        in_=pstr[32:33, 0:HN])
    v.drain()
    ck_hsm = inc("V", hsm)
    ck_hqb = bcast_row(R_HALO, 64, HN, bb1, 0, 5, deps=(ck_hsm,))
    mu = None
    for dc in range(DC):
        W("V", ck_hqb)
        v.tensor_mul(out=i32a[:, dc * HN:(dc + 1) * HN], in0=hx_ap(hcand, dc),
                     in1=bb1[:, 0:HN])
        v.drain()
    mu = v.tensor_copy(out=hq_sb[:, 0:DC * HN], in_=i32a[:, 0:DC * HN])
    v.drain()
    ck_hq = inc("V", mu)

    for sl in range(NSL):
        for dc in range(DC):
            W("V", ck_xn)
            xs_ = xn_sb[:, dc * T + sl * 512:dc * T + sl * 512 + 512]
            if dc == 0:
                v.tensor_copy(out=bb2[:, 0:512], in_=xs_)
                v.drain()
                am = v.tensor_copy(out=bb1[:, 0:512], in_=xs_)
            else:
                v.tensor_tensor(out=bb2[:, 0:512], in0=bb2[:, 0:512],
                                in1=xs_, op=OP.max)
                v.drain()
                am = v.tensor_tensor(out=bb1[:, 0:512], in0=bb1[:, 0:512],
                                     in1=xs_, op=OP.min)
            v.drain()
        v.tensor_scalar(out=bb1[:, 0:512], in0=bb1[:, 0:512], scalar1=-1.0,
                        scalar2=None, op0=OP.mult)
        v.drain()
        am = v.tensor_tensor(out=bb2[:, 0:512], in0=bb2[:, 0:512],
                             in1=bb1[:, 0:512], op=OP.max)
        v.drain()
        ck_fold = inc("V", am)
        for i in range(4):
            ck_trb = token_scales(bb2[:, i * 128:(i + 1) * 128],
                                  thr_sb[:, 4:5], ck_fold)
            W("V", ck_trb)
            v.tensor_copy(out=ROW(R_S1 + sl, i * 128, 128),
                          in_=pstr[0:1, 0:128])
            v.drain()
            cp = v.tensor_copy(out=ROW(R_S1M + sl, i * 128, 128),
                               in_=pstr[32:33, 0:128])
            v.drain()
            inc("V", cp)
    ck_s1 = ck("V")
    drain_side()

    l1 = {}
    for tt in range(NB):
        tb = tt * 256
        sl_, off_ = tb // 512, tb % 512
        ck_s1b = bcast_row(R_S1 + sl_, off_, 256, bb1, 0, 4, deps=(ck_s1,))
        ck_s1mb = bcast_row(R_S1M + sl_, off_, 256, s2mb, 0, 3, deps=(ck_s1,))
        xq = None
        for dc in range(DC):
            W("V", ck_s1b, ck_s1mb, ck_hq, l1.get("w1done"))
            v.tensor_mul(out=i32a[:, 0:256],
                         in0=xn_sb[:, dc * T + tb:dc * T + tb + 256],
                         in1=bb1[:, 0:256])
            v.drain()
            v.tensor_mul(out=xq1_sb[:, dc * 258 + 1:dc * 258 + 257],
                         in0=i32a[:, 0:256], in1=s2mb[:])
            v.drain()
        x3 = xq1_sb[:].rearrange("p (c n) -> p c n", n=258)
        h3 = hq_sb[:].rearrange("p (c n) -> p c n", n=HN)
        v.tensor_copy(out=x3[:, :, 0:1], in_=h3[:, :, tt:tt + 1])
        v.drain()
        xq = v.tensor_copy(out=x3[:, :, 257:258], in_=h3[:, :, NB + tt:NB + tt + 1])
        v.drain()
        ck_xq1 = inc("V", xq)
        for jc in range(JC1):
            tslot = (jc % 2) * DC * 128
            W("Y", ("YT0", cnt["YT0"]), ("YT1", cnt["YT1"]), ck("T"))
            tl = y.dma_start(
                out=t1buf[:, tslot:tslot + DC * 128].rearrange(
                    "p (c n) -> p c n", c=DC),
                in_=t1d[:, :, jc * 128:(jc + 1) * 128])
            ck_t1 = inc("Y1a" if jc % 2 == 0 else "Y1b", tl, 16)
            pb = 5 + (jc % 2)
            mm = None
            for dc in range(DC):
                W("T", ck_t1, ck_xq1, l1.get(("hev", jc - 2)))
                mm = t.matmul(ps[:, pb * 512:pb * 512 + 258],
                              t1buf[:, tslot + dc * 128:tslot + dc * 128 + 128],
                              xq1_sb[:, dc * 258:(dc + 1) * 258],
                              start=(dc == 0), stop=(dc == DC - 1))
            ck_w1 = inc("T", mm)
            pso = pb * 512
            W("V", ck_w1)
            v.tensor_scalar(out=fsc[:, 0:256], in0=ps[:, pso:pso + 256],
                            scalar1=cvw_sb[:, jc * 3:jc * 3 + 1], scalar2=None,
                            op0=OP.mult)
            v.drain()
            v.scalar_tensor_tensor(out=fsc[:, 0:256], in0=ps[:, pso + 1:pso + 257],
                                   scalar=cvw_sb[:, jc * 3 + 1:jc * 3 + 2],
                                   in1=fsc[:, 0:256], op0=OP.mult, op1=OP.add)
            v.drain()
            cv3 = v.scalar_tensor_tensor(out=fsc[:, 0:256],
                                         in0=ps[:, pso + 2:pso + 258],
                                         scalar=cvw_sb[:, jc * 3 + 2:jc * 3 + 3],
                                         in1=fsc[:, 0:256], op0=OP.mult, op1=OP.add)
            v.drain()
            ck_cv = inc("V", cv3)
            l1[("hev", jc)] = ck_cv
            W("S", ck_cv)
            si = s.activation(out=hpool[:, jc * 256:(jc + 1) * 256],
                              in_=fsc[:, 0:256], func=AF.Silu,
                              bias=cvb_sb[:, jc:jc + 1])
            ck_si = inc("S", si)
            W("V", ck_si)
            if jc == 0:
                v.tensor_copy(out=run_sb[:, 0:256], in_=hpool[:, 0:256])
                v.drain()
                am = v.tensor_copy(out=s2mb[:], in_=hpool[:, 0:256])
            else:
                v.tensor_tensor(out=run_sb[:, 0:256], in0=run_sb[:, 0:256],
                                in1=hpool[:, jc * 256:(jc + 1) * 256], op=OP.max)
                v.drain()
                am = v.tensor_tensor(out=s2mb[:], in0=s2mb[:],
                                     in1=hpool[:, jc * 256:(jc + 1) * 256],
                                     op=OP.min)
            v.drain()
            inc("V", am)
        W("V", ck("V"))
        v.tensor_scalar(out=s2mb[:], in0=s2mb[:], scalar1=-1.0,
                        scalar2=None, op0=OP.mult)
        v.drain()
        am = v.tensor_tensor(out=run_sb[:], in0=run_sb[:], in1=s2mb[:],
                             op=OP.max)
        v.drain()
        inc("V", am)
        l1["w1done"] = ck("V")
        ck_run = ck("V")
        for half in range(2):
            ck_trb = token_scales(run_sb[:, half * 128:(half + 1) * 128]
                                  if half else run_sb[:, 0:128],
                                  thr_sb[:, 5:6], ck_run)
            W("V", ck_trb)
            v.tensor_copy(out=ROW(R_S2, half * 128, 128),
                          in_=pstr[0:1, 0:128])
            v.drain()
            cp = v.tensor_copy(out=ROW(R_S2M, half * 128, 128),
                               in_=pstr[32:33, 0:128])
            v.drain()
            inc("V", cp)
        ck_s2 = ck("V")
        ck_s2b = bcast_row(R_S2, 0, 256, run_sb, 0, 2, deps=(ck_s2,))
        ck_s2mb = bcast_row(R_S2M, 0, 256, s2mb, 0, 2, deps=(ck_s2,))
        xq2 = None
        for jc in range(JC1):
            W("V", ck_s2b, ck_s2mb)
            v.tensor_mul(out=i32a[:, 0:256], in0=hpool[:, jc * 256:(jc + 1) * 256],
                         in1=run_sb[:])
            v.drain()
            xq2 = v.tensor_mul(out=hpool[:, jc * 256:(jc + 1) * 256],
                               in0=i32a[:, 0:256], in1=s2mb[:])
            v.drain()
        ck_xq2 = inc("V", xq2)
        for mc in range(DC):
            pb = mc % 2
            mm = None
            for jh2 in range(2):
                W("Y", ("YT0", cnt["YT0"]), ("YT1", cnt["YT1"]), ck("T"))
                tl = y.dma_start(
                    out=t2buf[:].rearrange("p (c n) -> p c n", c=16),
                    in_=t2d[jh2 * 16:(jh2 + 1) * 16, :, mc * 128:(mc + 1) * 128])
                ck_t2 = inc("Y2", tl, 16)
                for jc2 in range(16):
                    jc = jh2 * 16 + jc2
                    W("T", ck_t2, ck_xq2, l1.get(("xadd", mc - 2)))
                    mm = t.matmul(ps[:, pb * 512:pb * 512 + 256],
                                  t2buf[:, jc2 * 128:jc2 * 128 + 128],
                                  hpool[:, jc * 256:(jc + 1) * 256],
                                  start=(jc == 0), stop=(jc == JC1 - 1))
                inc("T", mm)
            ck_w2 = ck("T")
            xsl = x_sb[:, mc * T + tb:mc * T + tb + 256]
            W("V", ck_w2)
            ad = v.tensor_add(out=xsl, in0=xsl,
                              in1=ps[:, pb * 512:pb * 512 + 256])
            v.drain()
            l1[("xadd", mc)] = inc("V", ad)
        l1["w1done"] = ck("V")

    ck_attn1 = emit_attention(1, ck_ag1)
    ck_x2 = emit_oproj(1, wpj_in, ck_attn1)

    W("Y", ck_x2)
    for dc in range(DC):
        inc("YO", y.dma_start(out=out_d[dc * 128:(dc + 1) * 128, :],
                              in_=x_sb[:, dc * T:(dc + 1) * T]), 16)
    y.wait_ge(sems["YO"], cnt["YO"])

    blk.tensor(_replay(tapes["T"]))
    blk.vector(_replay(tapes["V"]))
    blk.scalar(_replay(tapes["S"]))
    blk.sync(_replay(tapes["Y"]))
    blk.gpsimd(_replay(tapes["G"]))

    es.close()
    return nc


def _masks(par):
    kk = np.arange(128)[:, None]
    qq = np.arange(256)[None, :]
    tri0 = (kk <= qq).astype(np.float32)
    tri1 = (kk + 128 <= qq).astype(np.float32)
    one = np.ones((128, 256), np.float32)
    zero = np.zeros((128, 256), np.float32)
    ms = [tri0, tri1, zero, zero] if par == 0 else [one, one, tri0, tri1]
    return np.concatenate(ms, axis=1)


def _stage(inputs, S):
    x = np.asarray(inputs["L0_attn_residual"])
    B = x.shape[0]
    NBG = S // 256
    C = lambda a: np.ascontiguousarray(a)

    wq = C(np.asarray(inputs["L0_attn_W_Q"]).reshape(D, D).T)
    wk = C(np.asarray(inputs["L0_attn_W_K"]).reshape(D, D).T)
    wv = C(np.asarray(inputs["L0_attn_W_V"]).reshape(D, D).T)
    wo = C(np.asarray(inputs["L0_attn_W_O"]).reshape(D, D).T)
    wg = C(np.asarray(inputs["L0_ffn_w_gate"]).T.astype(np.int8))
    wu = C(np.asarray(inputs["L0_ffn_w_up"]).T.astype(np.int8))
    wd = C(np.asarray(inputs["L0_ffn_w_down"]).T.astype(np.int8))
    wqkv = C(np.asarray(inputs["L1_attn_w_qkv"]).T)
    wpj = C(np.asarray(inputs["L1_attn_w_proj"]).T)
    bpj = C(np.asarray(inputs["L1_attn_b_proj"]).reshape(D, 1))
    w1 = C(np.asarray(inputs["L1_ffn_w1"]).T)
    w2 = C(np.asarray(inputs["L1_ffn_w2"]).T)
    cw = C(np.asarray(inputs["L1_ffn_conv_w"]).reshape(H1, 3))
    cb = C(np.asarray(inputs["L1_ffn_conv_b"]).reshape(H1, 1))
    ident = np.eye(128, dtype=np.float32)

    in_maps = []
    for core in range(2 * B):
        b, par = core // 2, core % 2
        blocks = x[b].reshape(NBG, 256, D)[par::2]
        xT = C(blocks.reshape(S // 2, D).T)
        hsel = np.zeros((128, 2), np.float32)
        hsel[:, 0 if par == 0 else 1] = 1.0
        in_maps.append({
            "xT": xT, "masks": _masks(par), "ident": ident, "hsel": hsel,
            "wqT": wq, "wkT": wk, "wvT": wv, "woT": wo,
            "wgT": wg, "wuT": wu, "wdT": wd,
            "wqkvT": wqkv, "wprojT": wpj, "bproj": bpj,
            "w1T": w1, "w2T": w2, "convw": cw, "convb": cb,
        })
    return in_maps


def kernel(**inputs):
    x = np.asarray(inputs["L0_attn_residual"])
    B, S, _ = x.shape
    nc = build_nc(S)
    in_maps = _stage(inputs, S)
    res = run_bass_kernel_spmd(nc, in_maps, core_ids=list(range(2 * B)))
    NBG = S // 256
    out = np.empty((B, S, D), np.float32)
    for core in range(2 * B):
        b, par = core // 2, core % 2
        oT = np.asarray(res.results[core]["outT"])
        out[b].reshape(NBG, 256, D)[par::2] = oT.T.reshape(NBG // 2, 256, D)
    return out


# version 11
# speedup vs baseline: 1.4080x; 1.1636x over previous
import numpy as np
import ml_dtypes
from contextlib import ExitStack

import concourse.bass as bass
import concourse.mybir as mybir
from concourse.bass_utils import run_bass_kernel_spmd
from concourse.alu_op_type import AluOpType as OP

F32 = mybir.dt.float32
BF16 = mybir.dt.bfloat16
I32 = mybir.dt.int32
I8 = mybir.dt.int8
AF = mybir.ActivationFunctionType
AX = mybir.AxisListType

D = 1024
DC = 8
HP = 8
H0 = 3072
JC0 = H0 // 128
H1 = 4096
JC1 = H1 // 128
C127 = float(np.float32(1.0 / 127.0))
CD127 = float(np.float32(1.0 / (127.0 * 127.0)))
EXP_SHIFT = -20.0
RG = [[0, 1], [2, 3], [4, 5], [6, 7]]

R_M, R_R, R_VAR, R_SD = 0, 1, 2, 3
R_REC, R_RECO = 0, 1
R_S2, R_S2M = 2, 3
R_S1, R_S1M = 4, 8
R_WM = 2
R_HALO = 1
R_SC = 0


def build_nc(S):
    T = S // 2
    NB = T // 256
    NSL = T // 512
    TT0 = 512
    NT0 = T // TT0
    HN = 2 * NB
    assert HN <= 32

    nc = bass.Bass("TRN2", target_bir_lowering=False)
    DT = nc.dram_tensor

    xT_in = DT("xT", [D, T], F32, kind="ExternalInput").ap()
    masks_in = DT("masks", [128, 1024], F32, kind="ExternalInput").ap()
    ident_in = DT("ident", [128, 128], F32, kind="ExternalInput").ap()
    hsel_in = DT("hsel", [128, 2], F32, kind="ExternalInput").ap()
    wq_in = DT("wqT", [D, D], F32, kind="ExternalInput").ap()
    wk_in = DT("wkT", [D, D], F32, kind="ExternalInput").ap()
    wv_in = DT("wvT", [D, D], F32, kind="ExternalInput").ap()
    wo_in = DT("woT", [D, D], F32, kind="ExternalInput").ap()
    wg_in = DT("wgT", [D, H0], I8, kind="ExternalInput").ap()
    wu_in = DT("wuT", [D, H0], I8, kind="ExternalInput").ap()
    wd_in = DT("wdT", [H0, D], I8, kind="ExternalInput").ap()
    wqkv_in = DT("wqkvT", [D, 3 * D], F32, kind="ExternalInput").ap()
    wpj_in = DT("wprojT", [D, D], F32, kind="ExternalInput").ap()
    bpj_in = DT("bproj", [D, 1], F32, kind="ExternalInput").ap()
    w1_in = DT("w1T", [D, H1], F32, kind="ExternalInput").ap()
    w2_in = DT("w2T", [H1, D], F32, kind="ExternalInput").ap()
    cw_in = DT("convw", [H1, 3], F32, kind="ExternalInput").ap()
    cb_in = DT("convb", [H1, 1], F32, kind="ExternalInput").ap()
    out_d = DT("outT", [D, T], F32, kind="ExternalOutput").ap()

    KE = HP * 128 * T
    VE = (T // 128) * 128 * D
    ccin = [DT(f"cc{L}i", [KE + VE], BF16, kind="Internal").ap() for L in (0, 1)]
    ccout = [DT(f"cc{L}o", [2, KE + VE], BF16, kind="Internal").ap() for L in (0, 1)]
    hai = DT("hai", [D, HN], F32, kind="Internal").ap()
    hao = DT("hao", [2, D, HN], F32, kind="Internal").ap()
    qsp = DT("qsp", [HP, 128, T], BF16, kind="Internal").ap()
    asp = DT("asp", [HP, 128, T], BF16, kind="Internal").ap()
    t1d = DT("t1d", [DC, 128, H1], BF16, kind="Internal").ap()
    t2d = DT("t2d", [JC1, 128, D], BF16, kind="Internal").ap()

    es = ExitStack()
    SB = lambda n, sh, d: es.enter_context(nc.sbuf_tensor(n, sh, d))

    x_sb = SB("x", [128, DC * T], F32)
    xn_sb = SB("xn", [128, DC * T], BF16)
    rows = SB("rows", [128, 2048], F32)
    ROW = lambda i, c0, w: rows[32 * (i % 3):32 * (i % 3) + 1,
                                (i // 3) * 512 + c0:(i // 3) * 512 + c0 + w]
    bb1 = SB("bb1", [128, 512], F32)
    bb2 = SB("bb2", [128, 512], F32)
    fsc = SB("fsc", [128, 1024], F32)
    warena = SB("warena", [128, DC * 512], BF16)
    wff_w = SB("wff_w", [128, 2 * 1024], BF16)
    kvst = SB("kvst", [128, 2 * 512], BF16)
    katt = SB("katt", [128, S], BF16)
    vatt = SB("vatt", [128, (S // 128) * 130], BF16)
    qatt = SB("qatt", [128, T], BF16)
    p_sb = SB("p_sb", [128, 2048], BF16)
    attst = SB("attst", [128, T], BF16)
    attb = SB("attb", [128, DC * 256], BF16)
    masks_sb = SB("masksb", [128, 1024], BF16)
    ident_sb = SB("identb", [128, 128], F32)
    hsel_sb = SB("hselb", [128, 2], F32)
    ones_r = SB("ones_r", [128, 128], F32)
    onesk = SB("onesk", [128, 1], F32)
    mxc = SB("mxc", [128, 64], F32)
    hpool = SB("hpool", [128, JC0 * TT0], BF16)
    xq1_sb = SB("xq1", [128, DC * 258], BF16)
    t1buf = SB("t1buf", [128, 2 * DC * 128], BF16)
    t2buf = SB("t2buf", [128, 16 * 128], BF16)
    wqf = SB("wqf", [128, 512], BF16)
    tq_sb = SB("tq", [128, 512], BF16)
    wabs = SB("wabs", [128, 4], F32)
    thr_sb = SB("thr", [128, 8], F32)
    i32a = SB("i32a", [128, 256], I32)
    s2mb = SB("s2mb", [128, 256], F32)
    run_sb = SB("run_sb", [128, 256], F32)
    cvw_sb = SB("cvw", [128, 3 * JC1], F32)
    cvb_sb = SB("cvb", [128, JC1], F32)
    bpj_sb = SB("bpjb", [128, DC], F32)
    cbias = SB("cbias", [128, 4], F32)
    hx_sb = SB("hx", [128, 2 * DC * NB], F32)
    hcand = SB("hcand", [128, 2 * DC * NB], F32)
    hq_sb = SB("hq", [128, 2 * DC * NB], BF16)

    ps = es.enter_context(nc.psum_tensor("ps_all", [128, 3584], F32))
    pstr = es.enter_context(nc.psum_tensor("ps_tr", [128, 512], F32))

    DMA_FAMS = ["YX", "YC", "YS0", "YS1", "YH", "YW", "YT0", "YT1", "YA",
                "YB", "YP", "YO", "Y1a", "Y1b", "Y2", "GM", "GA", "GF", "GU",
                "GD0", "GD1", "GW0", "GW1", "CC"]
    sems, cnt = {}, {k: 0 for k in (["T", "V", "S"] + DMA_FAMS)}
    for k in cnt:
        sems[k] = es.enter_context(nc.semaphore("sem" + k))
    last_wait = {}

    blk = es.enter_context(nc.Block())

    class _RecInstr:
        __slots__ = ("incs",)

        def __init__(self):
            self.incs = []

        def then_inc(self, sem, n):
            self.incs.append((sem, n))
            return self

    class _Rec:
        def __init__(self, tape):
            self._tape = tape

        def __getattr__(self, m):
            tape = self._tape

            def call(*a, **k):
                ri = _RecInstr()
                tape.append((m, a, k, ri))
                return ri
            return call

    tapes = {k: [] for k in "TVSYG"}
    ENG = {k: _Rec(tapes[k]) for k in "TVSYG"}
    t, v, s, y, g = (ENG[k] for k in "TVSYG")

    import os as _os
    _SIM = bool(_os.environ.get("BASS_SIM_FP64"))

    def _replay(tape):
        def f(eng):
            for m, a, k, ri in tape:
                instr = getattr(eng, m)(*a, **k)
                for sem, n in ri.incs:
                    instr.then_inc(sem, n)
        return f

    def inc(tag, instr, n=1):
        instr.then_inc(sems[tag], n)
        cnt[tag] += n
        return (tag, cnt[tag])

    def ck(tag):
        return (tag, cnt[tag])

    def W(consumer, *cks):
        lw = last_wait.setdefault(consumer, {})
        best = {}
        for c in cks:
            if c is None:
                continue
            tag, val = c
            if val > best.get(tag, 0):
                best[tag] = val
        for tag, val in best.items():
            if val <= 0 or lw.get(tag, 0) >= val:
                continue
            assert val <= cnt[tag], f"forward wait {tag}:{val} > {cnt[tag]}"
            ENG[consumer].wait_ge(sems[tag], val)
            lw[tag] = val

    for dc in range(DC):
        inc("YX", y.dma_start(out=x_sb[:, dc * T:(dc + 1) * T],
                              in_=xT_in[dc * 128:(dc + 1) * 128, :]), 16)
    ck_x0 = ck("YX")
    inc("YC", y.dma_start(out=ident_sb[:], in_=ident_in), 16)
    inc("YC", y.dma_start(out=hsel_sb[:], in_=hsel_in), 16)
    inc("YC", y.dma_start(
        out=cvw_sb[:].rearrange("p (c k) -> p c k", c=JC1),
        in_=cw_in.rearrange("(c p) k -> p c k", p=128),
        allow_slow_non_contiguous=True), 16)
    inc("YC", y.dma_start(
        out=cvb_sb[:].rearrange("p (c o) -> p c o", c=JC1),
        in_=cb_in.rearrange("(c p) o -> p c o", p=128),
        allow_slow_non_contiguous=True), 16)
    inc("YC", y.dma_start(
        out=bpj_sb[:].rearrange("p (c o) -> p c o", c=DC),
        in_=bpj_in.rearrange("(c p) o -> p c o", p=128),
        allow_slow_non_contiguous=True), 16)
    ck_consts = ck("YC")
    ck_masks = inc("GM", g.dma_start(out=masks_sb[:], in_=masks_in), 16)

    W("V", ck_consts)
    v.memset(ones_r[:], 1.0)
    v.memset(onesk[:], 1.0)
    v.memset(cbias[:, 0:1], 1e-5)
    v.memset(cbias[:, 1:2], 1.0)
    ck_ones = inc("V", v.memset(cbias[:, 2:3], EXP_SHIFT))

    def bcast_row(slot, c0, n, dst_sb, dst_off, pbank, deps=(), recip=False):
        b = 32 * (slot % 3)
        W("T", ck_ones, ck("V"), *deps)
        mm = t.matmul(ps[:, pbank * 512:pbank * 512 + n], ones_r[b:b + 1, 0:128],
                      ROW(slot, c0, n), start=True, stop=True)
        ckt = inc("T", mm)
        W("V", ckt)
        if recip:
            cp = v.reciprocal(out=dst_sb[:, dst_off:dst_off + n],
                              in_=ps[:, pbank * 512:pbank * 512 + n])
        else:
            cp = v.tensor_copy(out=dst_sb[:, dst_off:dst_off + n],
                               in_=ps[:, pbank * 512:pbank * 512 + n])
        v.drain()
        return inc("V", cp)

    def wload(fam, dram_ap, row0, rows_, col0, ncols, dst, dst_off):
        ins = g.dma_start(
            out=dst[:, dst_off:dst_off + (rows_ // 128) * ncols].rearrange(
                "p (c n) -> p c n", c=rows_ // 128),
            in_=dram_ap[row0:row0 + rows_, col0:col0 + ncols].rearrange(
                "(c p) n -> p c n", p=128))
        return inc(fam, ins, 16)

    def ln_slice(xs_aps, w_, dep):
        mm2 = None
        W("T", ck("V"))
        for dc in range(DC):
            xap = xs_aps[dc][0]
            off = (dc % 2) * 512
            W("V", dep, ck("T"))
            sq = v.tensor_mul(out=fsc[:, off:off + w_], in0=xap, in1=xap)
            v.drain()
            cksq = inc("V", sq)
            W("T", ck_ones, cksq)
            t.matmul(ps[0:1, 3 * 512:3 * 512 + w_], onesk[:], xap,
                     start=(dc == 0), stop=(dc == DC - 1))
            mm2 = t.matmul(ps[0:1, 4 * 512:4 * 512 + w_], onesk[:],
                           fsc[:, off:off + w_],
                           start=(dc == 0), stop=(dc == DC - 1))
            inc("T", mm2)
        ck_st = ck("T")
        W("V", ck_st)
        v.tensor_scalar(out=ROW(R_M, 0, w_),
                        in0=ps[0:1, 3 * 512:3 * 512 + w_],
                        scalar1=1.0 / D, scalar2=None, op0=OP.mult)
        v.drain()
        v.tensor_mul(out=ROW(R_VAR, 0, w_), in0=ROW(R_M, 0, w_),
                     in1=ROW(R_M, 0, w_))
        v.drain()
        va = v.scalar_tensor_tensor(out=ROW(R_VAR, 0, w_),
                                    in0=ps[0:1, 4 * 512:4 * 512 + w_],
                                    scalar=1.0 / D, in1=ROW(R_VAR, 0, w_),
                                    op0=OP.mult, op1=OP.subtract)
        v.drain()
        ck_var = inc("V", va)
        W("S", ck_var)
        sq2 = s.activation(out=ROW(R_SD, 0, w_),
                           in_=ROW(R_VAR, 0, w_), func=AF.Sqrt,
                           bias=cbias[0:1, 0:1])
        ck_sd = inc("S", sq2)
        ck_mb = bcast_row(R_M, 0, w_, bb1, 0, 5, deps=(ck_sd,))
        ck_rb = bcast_row(R_SD, 0, w_, bb2, 0, 6, deps=(ck_sd,), recip=True)
        mu = None
        for dc in range(DC):
            xap, oap = xs_aps[dc]
            W("V", ck_mb, ck_rb)
            v.tensor_sub(out=fsc[:, 0:w_], in0=xap, in1=bb1[:, 0:w_])
            v.drain()
            mu = v.tensor_mul(out=oap, in0=fsc[:, 0:w_], in1=bb2[:, 0:w_])
            v.drain()
        return inc("V", mu)

    def emit_layernorm(dep):
        outs = []
        for sl in range(NSL):
            c0 = sl * 512
            xs = [(x_sb[:, dc * T + c0:dc * T + c0 + 512],
                   xn_sb[:, dc * T + c0:dc * T + c0 + 512]) for dc in range(DC)]
            outs.append(ln_slice(xs, 512, dep))
        return outs[-1]

    proj_state = {"dma": [], "slot": 0, "evac": []}

    def proj_evac(ck_t, pb, dst_ap):
        sl = proj_state["slot"]
        so = (sl % 2) * 512
        W("S", ck_t)
        if sl >= 2:
            W("S", proj_state["dma"][sl - 2])
        cp = s.copy(out=kvst[:, so:so + 512],
                    in_=ps[:, pb * 512:pb * 512 + 512])
        s.drain()
        ck_v = inc("S", cp)
        proj_state["evac"].append(ck_v)
        W("Y", ck_v, ("YA", cnt["YA"]))
        dd = y.dma_start(out=dst_ap, in_=kvst[:, so:so + 512])
        proj_state["dma"].append(inc("YS0" if sl % 2 == 0 else "YS1", dd, 16))
        proj_state["slot"] += 1
        pump_side(2)

    def emit_proj_w(w_dram, wcol0, dep_xn, dst_fn):
        for half in range(2):
            W("G", ck("T"))
            ck_w = wload("GA", w_dram, 0, D, wcol0 + half * 512, 512, warena, 0)
            for a2 in range(4):
                a = half * 4 + a2
                for b in range(NSL):
                    pb = 5 + (proj_state["slot"] % 2)
                    mm = None
                    ev = proj_state["evac"]
                    for dc in range(DC):
                        W("T", ck_w, dep_xn,
                          ev[-2] if len(ev) >= 2 else None)
                        mm = t.matmul(
                            ps[:, pb * 512:pb * 512 + 512],
                            warena[:, dc * 512 + a2 * 128:dc * 512 + a2 * 128 + 128],
                            xn_sb[:, dc * T + b * 512:dc * T + b * 512 + 512],
                            start=(dc == 0), stop=(dc == DC - 1))
                    proj_evac(inc("T", mm), pb, dst_fn(a, b))

    def emit_proj_v(w_dram, wcol0, dep_xn, dst_fn):
        for b in range(2):
            W("G", ck("T"))
            ck_w = wload("GA", w_dram, 0, D, wcol0 + b * 512, 512, warena, 0)
            for a in range(T // 128):
                pb = 5 + (proj_state["slot"] % 2)
                mm = None
                ev = proj_state["evac"]
                for dc in range(DC):
                    W("T", ck_w, dep_xn,
                      ev[-2] if len(ev) >= 2 else None)
                    mm = t.matmul(
                        ps[:, pb * 512:pb * 512 + 512],
                        xn_sb[:, dc * T + a * 128:dc * T + a * 128 + 128],
                        warena[:, dc * 512:dc * 512 + 512],
                        start=(dc == 0), stop=(dc == DC - 1))
                proj_evac(inc("T", mm), pb, dst_fn(a, b))

    def token_scales(fold_ap, mcol_ap, dep):
        W("T", dep, ck_consts, ck("V"))
        tr = t.transpose(pstr[:, 0:128], fold_ap, ident_sb[:])
        ck_tr = inc("T", tr)
        W("V", ck_tr)
        v.reduce_max(out=mxc[:, 34:35], in_=pstr[:, 0:128], axis=AX.X)
        v.drain()
        v.tensor_scalar(out=mxc[:, 35:36], in0=mxc[:, 34:35], scalar1=1e-5,
                        scalar2=None, op0=OP.max)
        v.drain()
        v.reciprocal(out=mxc[:, 36:37], in_=mxc[:, 35:36])
        v.drain()
        v.tensor_scalar(out=mxc[:, 0:1], in0=mxc[:, 36:37], scalar1=127.0,
                        scalar2=None, op0=OP.mult)
        v.drain()
        sm = v.tensor_scalar(out=mxc[:, 32:33], in0=mxc[:, 35:36], scalar1=mcol_ap,
                             scalar2=C127, op0=OP.mult, op1=OP.mult)
        v.drain()
        ck_mx = inc("V", sm)
        W("T", ck_mx)
        tr2 = t.transpose(pstr[0:64, 0:128], mxc[:, :], ident_sb[:])
        return inc("T", tr2)

    def wquant_gen(dram_ap, rows_, ncol, col, dst_dram):
        pieces = [(c, p0) for c in range(rows_ // 128)
                  for p0 in range(0, ncol, 256)]
        vdone, lds = {}, {}

        def issue(i, dep0):
            c, p0 = pieces[i]
            sl = i % 2
            W("G", vdone.get(i - 2, dep0 if i < 2 else None))
            return inc("GW0" if sl == 0 else "GW1", g.dma_start(
                out=wqf[:, sl * 256:sl * 256 + 256],
                in_=dram_ap[c * 128:(c + 1) * 128, p0:p0 + 256]), 16)

        dep0 = ("V", cnt["V"])
        lds[0] = issue(0, dep0)
        first = True
        ck_acc = None
        for i in range(len(pieces)):
            if i + 1 < len(pieces):
                lds[i + 1] = issue(i + 1, dep0)
            sl = i % 2
            W("V", lds[i])
            v.reduce_sum(out=wabs[:, 2:3], in_=wqf[:, sl * 256:sl * 256 + 256],
                         axis=AX.X, apply_absolute_value=True)
            v.drain()
            if first:
                cp = v.tensor_copy(out=wabs[:, 0:1], in_=wabs[:, 2:3])
                first = False
            else:
                cp = v.tensor_add(out=wabs[:, 0:1], in0=wabs[:, 0:1],
                                  in1=wabs[:, 2:3])
            v.drain()
            ck_acc = inc("V", cp)
            vdone[i] = ck_acc
            yield
        W("T", ck_acc, ck_ones)
        mm = t.matmul(pstr[0:1, 0:1], onesk[:], wabs[:, 0:1],
                      start=True, stop=True)
        ck_t = inc("T", mm)
        W("V", ck_t)
        v.tensor_scalar(out=ROW(R_WM, col, 1), in0=pstr[0:1, 0:1],
                        scalar1=1.0 / (rows_ * ncol), scalar2=None, op0=OP.mult)
        v.drain()
        th = v.tensor_scalar(out=ROW(R_WM, col + 2, 1), in0=ROW(R_WM, col, 1),
                             scalar1=0.5, scalar2=None, op0=OP.mult)
        ck_m = inc("V", th)
        W("T", ck_m)
        wb = 32 * (R_WM % 3)
        mm = t.matmul(pstr[:, 0:1], ones_r[wb:wb + 1, 0:128],
                      ROW(R_WM, col + 2, 1), start=True, stop=True)
        ck_t = inc("T", mm)
        W("V", ck_t)
        v.tensor_copy(out=thr_sb[:, 2 * col:2 * col + 1],
                      in_=pstr[:, 0:1])
        v.drain()
        v.tensor_scalar(out=thr_sb[:, 2 * col + 1:2 * col + 2],
                        in0=thr_sb[:, 2 * col:2 * col + 1], scalar1=-1.0,
                        scalar2=None, op0=OP.mult)
        v.drain()
        th2 = v.tensor_scalar(out=thr_sb[:, 4 + col:5 + col],
                              in0=thr_sb[:, 2 * col:2 * col + 1], scalar1=2.0,
                              scalar2=None, op0=OP.mult)
        v.drain()
        ck_thr = inc("V", th2)
        yield
        qdone, ytd, lds2 = {}, {}, {}
        yt_prev = (("YT0", cnt["YT0"]), ("YT1", cnt["YT1"]))

        def issue2(i):
            c, p0 = pieces[i]
            sl = i % 2
            W("G", qdone.get(i - 2, ck_acc if i < 2 else None))
            return inc("GW0" if sl == 0 else "GW1", g.dma_start(
                out=wqf[:, sl * 256:sl * 256 + 256],
                in_=dram_ap[c * 128:(c + 1) * 128, p0:p0 + 256]), 16)

        lds2[0] = issue2(0)
        for i in range(len(pieces)):
            c, p0 = pieces[i]
            if i + 1 < len(pieces):
                lds2[i + 1] = issue2(i + 1)
            sl = i % 2
            so = sl * 256
            W("V", lds2[i], ck_thr,
              *(yt_prev if i < 2 else (ytd.get(i - 2),)))
            v.tensor_scalar(out=run_sb[:], in0=wqf[:, so:so + 256],
                            scalar1=thr_sb[:, 2 * col + 1:2 * col + 2],
                            scalar2=None, op0=OP.is_lt)
            v.drain()
            v.tensor_scalar(out=tq_sb[:, so:so + 256], in0=wqf[:, so:so + 256],
                            scalar1=thr_sb[:, 2 * col:2 * col + 1],
                            scalar2=None, op0=OP.is_gt)
            v.drain()
            sb2 = v.tensor_sub(out=tq_sb[:, so:so + 256],
                               in0=tq_sb[:, so:so + 256], in1=run_sb[:])
            v.drain()
            ck_q = inc("V", sb2)
            qdone[i] = ck_q
            W("Y", ck_q)
            dd = y.dma_start(out=dst_dram[c][:, p0:p0 + 256],
                             in_=tq_sb[:, so:so + 256])
            ytd[i] = inc("YT0" if sl == 0 else "YT1", dd, 16)
            yield

    side_gens = []

    def pump_side(n=1):
        for _ in range(n):
            while side_gens:
                try:
                    next(side_gens[0])
                    break
                except StopIteration:
                    side_gens.pop(0)
            else:
                return

    def drain_side():
        while side_gens:
            pump_side()


    def emit_attention(L, ck_ag):
        cc = ccout[L]
        st = {"v0": ck("V")}
        prev_tail = None
        for hp in range(HP):
            W("Y", ck_ag, prev_tail, ck("T") if hp else None,
              ("YS0", cnt["YS0"]), ("YS1", cnt["YS1"]))
            kld = []
            for r in (0, 1):
                srck = cc[r][0:KE].rearrange("(h p n) -> h p n", h=HP, p=128)[hp]
                dd = y.dma_start(
                    out=katt[:].rearrange("p (j q) -> p j q",
                                          q=512)[:, :, 256 * r:256 * r + 256],
                    in_=srck.rearrange("p (j n) -> p j n", n=256))
                kld.append(inc("YA", dd, 16))
                srcv = cc[r][KE:KE + VE].rearrange("(c p n) -> c p n",
                                                   c=T // 128, p=128)
                for u in range(2):
                    for hd in range(2):
                        off = 260 * r + 130 * u + 65 * hd
                        dd = y.dma_start(
                            out=vatt[:].rearrange(
                                "p (j q) -> p j q", q=520)[:, :, off:off + 64],
                            in_=srcv[u::2, :, hp * 128 + hd * 64:
                                     hp * 128 + hd * 64 + 64].rearrange(
                                "c p n -> p c n"))
                        kld.append(inc("YA", dd, 16))
            dd = y.dma_start(out=qatt[:], in_=qsp[hp])
            kld.append(inc("YA", dd, 16))
            W("V", *kld, prev_tail)
            ckone = inc("V", v.memset(
                vatt[:].rearrange("p (c f) -> p c f", f=130)[:, :, 64:130:65], 1.0))
            for j in range(NB):
                q0 = j * 256
                for bt in range(j + 1):
                    ck_sc = {}
                    for hd in range(2):
                        bko = hd * 1024
                        mm = None
                        for i in range(4):
                            kb = 4 * bt + i
                            W("T", *kld, ckone, st.get(("exp", hd)),
                              st.get("avfree") if bt == 0 else None)
                            mm = t.matmul(
                                ps[:, bko + i * 256:bko + i * 256 + 256],
                                katt[64 * hd:64 * hd + 64, kb * 128:kb * 128 + 128],
                                qatt[64 * hd:64 * hd + 64, q0:q0 + 256],
                                start=True, stop=True, tile_position=(64 * hd, 0))
                        ck_sc[hd] = inc("T", mm)
                    for hd in range(2):
                        bko = hd * 1024
                        po = hd * 1024
                        if L == 0:
                            W("S", ck_sc[hd], st.get(("pfree", hd)), st["v0"])
                            e2 = s.activation(out=p_sb[:, po:po + 1024],
                                              in_=ps[:, bko:bko + 1024],
                                              func=AF.Exp, scale=0.125,
                                              bias=cbias[:, 2:3])
                            ck_e = inc("S", e2)
                        else:
                            W("S", ck_sc[hd], st.get(("pfree", hd)), st["v0"])
                            s.activation(out=p_sb[:, po:po + 1024],
                                         in_=ps[:, bko:bko + 1024],
                                         func=AF.Exp, scale=0.125)
                            s.drain()
                            e2 = s.activation(out=p_sb[:, po:po + 1024],
                                              in_=p_sb[:, po:po + 1024],
                                              func=AF.Ln, bias=cbias[:, 1:2])
                            ck_ln = inc("S", e2)
                            W("V", ck_ln)
                            sq = v.tensor_mul(out=p_sb[:, po:po + 1024],
                                              in0=p_sb[:, po:po + 1024],
                                              in1=p_sb[:, po:po + 1024])
                            v.drain()
                            ck_e = inc("V", sq)
                        st[("exp", hd)] = ck_e
                        if bt == j:
                            W("V", ck_e, ck_masks)
                            mk = v.tensor_mul(out=p_sb[:, po:po + 1024],
                                              in0=p_sb[:, po:po + 1024],
                                              in1=masks_sb[:])
                            v.drain()
                            ck_e = inc("V", mk)
                            st[("exp", hd)] = ck_e
                        mm = None
                        for i in range(4):
                            kb = 4 * bt + i
                            W("T", st[("exp", hd)],
                              st.get("avfree") if bt == 0 else None)
                            mm = t.matmul(
                                ps[0:65, (4 + hd) * 512:(4 + hd) * 512 + 256],
                                vatt[:, kb * 130 + hd * 65:kb * 130 + hd * 65 + 65],
                                p_sb[:, po + i * 256:po + i * 256 + 256],
                                start=(bt == 0 and i == 0),
                                stop=(bt == j and i == 3))
                        st[("pfree", hd)] = inc("T", mm)
                    pump_side(1)
                ck_av = st[("pfree", 1)]
                W("V", ck_av)
                if L == 1:
                    v.tensor_scalar(out=ROW(R_REC, 0, 256),
                                    in0=ps[64:65, 2048:2304], scalar1=1e-9,
                                    scalar2=None, op0=OP.add)
                    v.drain()
                    cps = v.tensor_scalar(out=ROW(R_REC, 256, 256),
                                          in0=ps[64:65, 2560:2816],
                                          scalar1=1e-9, scalar2=None,
                                          op0=OP.add)
                else:
                    v.tensor_copy(out=ROW(R_REC, 0, 256),
                                  in_=ps[64:65, 2048:2304])
                    v.drain()
                    cps = v.tensor_copy(out=ROW(R_REC, 256, 256),
                                        in_=ps[64:65, 2560:2816])
                v.drain()
                ck_rc = inc("V", cps)
                W("T", ck_rc)
                rb = 32 * (R_REC % 3)
                t.matmul(pstr[0:64, 256:512], ones_r[rb:rb + 1, 0:64],
                         ROW(R_REC, 0, 256), start=True, stop=True,
                         tile_position=(rb, 0))
                mm = t.matmul(pstr[64:128, 256:512], ones_r[rb:rb + 1, 0:64],
                              ROW(R_REC, 256, 256), start=True,
                              stop=True, tile_position=(rb, 64))
                ck_bc = inc("T", mm)
                if L == 0:
                    W("S", ck_av)
                    s.copy(out=attst[0:64, q0:q0 + 256], in_=ps[0:64, 2048:2304])
                    s.drain()
                    cpk = s.copy(out=attst[64:128, q0:q0 + 256],
                                 in_=ps[0:64, 2560:2816])
                    s.drain()
                    ck_cp = inc("S", cpk)
                else:
                    W("V", ck_av)
                    v.tensor_copy(out=attst[0:64, q0:q0 + 256],
                                  in_=ps[0:64, 2048:2304])
                    v.drain()
                    cpk = v.tensor_copy(out=attst[64:128, q0:q0 + 256],
                                        in_=ps[0:64, 2560:2816])
                    v.drain()
                    ck_cp = inc("V", cpk)
                W("V", ck_bc, ck_cp)
                v.reciprocal(out=fsc[:, 512:768], in_=pstr[:, 256:512])
                v.drain()
                v.tensor_mul(out=attst[0:64, q0:q0 + 256],
                             in0=attst[0:64, q0:q0 + 256],
                             in1=fsc[0:64, 512:768])
                v.drain()
                mu = v.tensor_mul(out=attst[64:128, q0:q0 + 256],
                                  in0=attst[64:128, q0:q0 + 256],
                                  in1=fsc[64:128, 512:768])
                v.drain()
                st["avread"] = inc("V", mu)
                st["avfree"] = st["avread"]
            W("Y", st["avread"], ("YB", cnt["YB"]))
            dd = y.dma_start(out=asp[hp], in_=attst[:])
            prev_tail = inc("YP", dd, 16)
        return prev_tail

    def emit_oproj(L, w_dram, dep_att):
        last = None
        for half in range(2):
            W("G", ck("T"))
            ck_w = wload("GA", w_dram, 0, D, half * 512, 512, warena, 0)
            for sl in range(T // 256):
                W("Y", dep_att, ck("T") if (half or sl) else None)
                dd = y.dma_start(
                    out=attb[:].rearrange("p (c n) -> p c n", n=256),
                    in_=asp[:, :, sl * 256:sl * 256 + 256].rearrange(
                        "c p n -> p c n"))
                abl = [inc("YB", dd, 16)]
                for m2 in range(4):
                    mc = half * 4 + m2
                    pb = 5 + ((sl * 4 + m2) % 2)
                    mm = None
                    for c in range(DC):
                        W("T", ck_w, *abl, last)
                        mm = t.matmul(
                            ps[:, pb * 512:pb * 512 + 256],
                            warena[:, c * 512 + m2 * 128:c * 512 + m2 * 128 + 128],
                            attb[:, c * 256:(c + 1) * 256],
                            start=(c == 0), stop=(c == DC - 1))
                    ck_t = inc("T", mm)
                    xsl = x_sb[:, mc * T + sl * 256:mc * T + sl * 256 + 256]
                    W("V", ck_t)
                    if L == 0:
                        ad = v.tensor_add(out=xsl, in0=xsl,
                                          in1=ps[:, pb * 512:pb * 512 + 256])
                    else:
                        ad = v.scalar_tensor_tensor(
                            out=xsl, in0=ps[:, pb * 512:pb * 512 + 256],
                            scalar=bpj_sb[:, mc:mc + 1], in1=xsl,
                            op0=OP.add, op1=OP.add)
                    v.drain()
                    last = inc("V", ad)
                    pump_side(1)
        return last

    cck = [ccin[L][0:KE].rearrange("(h p n) -> h p n", h=HP, p=128)
           for L in (0, 1)]
    ccv = [ccin[L][KE:KE + VE].rearrange("(c p n) -> c p n", c=T // 128, p=128)
           for L in (0, 1)]

    ck_xn = emit_layernorm(ck_x0)
    side_gens.append(wquant_gen(w1_in, D, H1, 0, t1d))
    side_gens.append(wquant_gen(w2_in, H1, D, 1, t2d))
    emit_proj_w(wk_in, 0, ck_xn, lambda a, b: cck[0][a][:, b * 512:(b + 1) * 512])
    emit_proj_v(wv_in, 0, ck_xn, lambda a, b: ccv[0][a][:, b * 512:(b + 1) * 512])
    W("G", ("YS0", cnt["YS0"]), ("YS1", cnt["YS1"]))
    ck_ag0 = inc("CC", g.collective_compute(
        "AllGather", OP.bypass, replica_groups=RG, ins=[ccin[0]], outs=[ccout[0]]))
    emit_proj_w(wq_in, 0, ck_xn, lambda a, b: qsp[a][:, b * 512:(b + 1) * 512])

    fs = {}
    for tt in range(NT0):
        t0 = tt * TT0
        for jh in range(JC0):
            if tt == 0 and jh == 0:
                fs["wg"] = wload("GF", wg_in, 0, D, 0, 128, wff_w, 0)
            mm = None
            for dc in range(DC):
                W("T", fs["wg"], ck_xn, fs.get("h"), fs.get("si"),
                  fs.get("hused"))
                mm = t.matmul(ps[:, 0:512],
                              wff_w[:, dc * 128:dc * 128 + 128],
                              xn_sb[:, dc * T + t0:dc * T + t0 + 512],
                              start=(dc == 0), stop=(dc == DC - 1))
            ck_g = inc("T", mm)
            if tt == 0 and jh == 0:
                fs["wu"] = wload("GU", wu_in, 0, D, 0, 128, wff_w, 1024)
            for dc in range(DC):
                W("T", fs["wu"])
                mm = t.matmul(ps[:, 512:1024],
                              wff_w[:, 1024 + dc * 128:1024 + dc * 128 + 128],
                              xn_sb[:, dc * T + t0:dc * T + t0 + 512],
                              start=(dc == 0), stop=(dc == DC - 1))
            ck_gu = inc("T", mm)
            pump_side(1)
            nj, ntt = (jh + 1, tt) if jh + 1 < JC0 else (0, tt + 1)
            if ntt < NT0:
                W("G", ck_g)
                fs["wg"] = wload("GF", wg_in, 0, D, nj * 128, 128, wff_w, 0)
                W("G", ck_gu)
                fs["wu"] = wload("GU", wu_in, 0, D, nj * 128, 128, wff_w, 1024)
            W("S", ck_gu, fs.get("h"))
            si = s.activation(out=fsc[:, 0:512], in_=ps[:, 0:512],
                              func=AF.Silu, scale=C127)
            ck_si = inc("S", si)
            fs["si"] = ck_si
            W("V", ck_si)
            hh = v.tensor_mul(out=hpool[:, jh * 512:(jh + 1) * 512],
                              in0=fsc[:, 0:512], in1=ps[:, 512:1024])
            v.drain()
            fs["h"] = inc("V", hh)
        ck_h = fs["h"]
        for mh in range(2):
            mm = None
            for jc in range(JC0):
                so = (jc % 2) * 512
                if tt == 0 and mh == 0 and jc == 0:
                    W("G", ("YS0", cnt["YS0"]), ("YS1", cnt["YS1"]))
                    fs["wd0"] = wload("GD0", wd_in, 0, 128, 0, 512, kvst, 0)
                ck_wd = fs["wd0" if jc % 2 == 0 else "wd1"]
                for m2 in range(4):
                    W("T", ck_wd, ck_h, fs.get("xadd"))
                    mm = t.matmul(
                        ps[:, m2 * 512:m2 * 512 + 512],
                        kvst[:, so + m2 * 128:so + m2 * 128 + 128],
                        hpool[:, jc * TT0:jc * TT0 + 512],
                        start=(jc == 0), stop=(jc == JC0 - 1))
                inc("T", mm)
                nj, nmh = (jc + 1, mh) if jc + 1 < JC0 else (0, mh + 1)
                ntt2 = tt if nmh < 2 else tt + 1
                if nmh == 2:
                    nmh = 0
                if ntt2 < NT0:
                    W("G", ck("T"))
                    fs["wd0" if nj % 2 == 0 else "wd1"] = wload(
                        "GD0" if nj % 2 == 0 else "GD1", wd_in, nj * 128, 128,
                        nmh * 512, 512, kvst, (nj % 2) * 512)
            ck_dn = ck("T")
            ad = None
            for m2 in range(4):
                mc = mh * 4 + m2
                xsl = x_sb[:, mc * T + t0:mc * T + t0 + 512]
                W("V", ck_dn)
                ad = v.scalar_tensor_tensor(
                    out=xsl, in0=ps[:, m2 * 512:m2 * 512 + 512],
                    scalar=CD127, in1=xsl, op0=OP.mult, op1=OP.add)
                v.drain()
            fs["xadd"] = inc("V", ad)
        fs["hused"] = fs["xadd"]

    ck_attn0 = emit_attention(0, ck_ag0)
    ck_x1 = emit_oproj(0, wo_in, ck_attn0)

    W("Y", ck_x1)
    x4d = x_sb[:].rearrange("p (c j n) -> p c j n", c=DC, j=NB)
    hai3 = hai.rearrange("(c p) n -> p c n", p=128)
    for u in range(2):
        for dcq in range(DC):
            hx = y.dma_start(
                out=hai3[:, dcq, u * NB:(u + 1) * NB],
                in_=x4d[:, dcq, :, 255 * u],
                allow_slow_non_contiguous=True)
            inc("YH", hx, 16)
    ck_hai = ck("YH")
    W("G", ck_hai)
    ck_ag1a = inc("CC", g.collective_compute(
        "AllGather", OP.bypass, replica_groups=RG, ins=[hai], outs=[hao]))

    ck_xn = emit_layernorm(ck_x1)
    emit_proj_w(wqkv_in, 1024, ck_xn,
                lambda a, b: cck[1][a][:, b * 512:(b + 1) * 512])
    emit_proj_v(wqkv_in, 2048, ck_xn,
                lambda a, b: ccv[1][a][:, b * 512:(b + 1) * 512])
    W("G", ("YS0", cnt["YS0"]), ("YS1", cnt["YS1"]))
    ck_ag1 = inc("CC", g.collective_compute(
        "AllGather", OP.bypass, replica_groups=RG, ins=[ccin[1]], outs=[ccout[1]]))
    emit_proj_w(wqkv_in, 0, ck_xn, lambda a, b: qsp[a][:, b * 512:(b + 1) * 512])

    W("Y", ck_ag1a)
    c0 = y.dma_start(
        out=hcand[:, 0:DC * NB].rearrange("p (c j) -> p c j", c=DC)[:, :, 1:NB],
        in_=hao[1].rearrange("(c p) n -> p c n", p=128)[:, :, NB:2 * NB - 1],
        allow_slow_non_contiguous=True)
    inc("YH", c0, 16)
    c1 = y.dma_start(
        out=hcand[:, DC * NB:2 * DC * NB].rearrange("p (c j) -> p c j", c=DC),
        in_=hao[0].rearrange("(c p) n -> p c n", p=128)[:, :, NB:2 * NB],
        allow_slow_non_contiguous=True)
    ck_hc = inc("YH", c1, 16)
    W("V", ck_hc, ck_consts)
    v.memset(hcand[:, 0:DC * NB].rearrange("p (c j) -> p c j", c=DC)[:, :, 0:1], 0.0)
    v.drain()
    hx3 = hx_sb[:].rearrange("p (c n) -> p c n", n=HN)
    hc3a = hcand[:, 0:DC * NB].rearrange("p (c j) -> p c j", c=DC)
    hc3b = hcand[:, DC * NB:2 * DC * NB].rearrange("p (c j) -> p c j", c=DC)
    v.tensor_scalar(out=hx3[:, :, 0:NB], in0=hc3a,
                    scalar1=hsel_sb[:, 0:1], scalar2=None, op0=OP.mult)
    v.drain()
    hl = v.scalar_tensor_tensor(out=hx3[:, :, 0:NB], in0=hc3b,
                                scalar=hsel_sb[:, 1:2], in1=hx3[:, :, 0:NB],
                                op0=OP.mult, op1=OP.add)
    v.drain()
    ck_hl = inc("V", hl)
    W("Y", ck_hl)
    c0 = y.dma_start(
        out=hcand[:, 0:DC * NB].rearrange("p (c j) -> p c j", c=DC),
        in_=hao[1].rearrange("(c p) n -> p c n", p=128)[:, :, 0:NB],
        allow_slow_non_contiguous=True)
    inc("YH", c0, 16)
    c1 = y.dma_start(
        out=hcand[:, DC * NB:2 * DC * NB].rearrange(
            "p (c j) -> p c j", c=DC)[:, :, 0:NB - 1],
        in_=hao[0].rearrange("(c p) n -> p c n", p=128)[:, :, 1:NB],
        allow_slow_non_contiguous=True)
    ck_hc2 = inc("YH", c1, 16)
    W("V", ck_hc2)
    v.memset(hcand[:, DC * NB:2 * DC * NB].rearrange(
        "p (c j) -> p c j", c=DC)[:, :, NB - 1:NB], 0.0)
    v.drain()
    v.tensor_scalar(out=hx3[:, :, NB:HN], in0=hc3a,
                    scalar1=hsel_sb[:, 0:1], scalar2=None, op0=OP.mult)
    v.drain()
    hr = v.scalar_tensor_tensor(out=hx3[:, :, NB:HN], in0=hc3b,
                                scalar=hsel_sb[:, 1:2],
                                in1=hx3[:, :, NB:HN],
                                op0=OP.mult, op1=OP.add)
    v.drain()
    ck_hx = inc("V", hr)

    def hx_ap(tens, dc):
        return tens[:, dc * HN:(dc + 1) * HN]

    for side in range(2):
        o = side * NB
        hxs = [(hx_ap(hx_sb, dc)[:, o:o + NB], hx_ap(hcand, dc)[:, o:o + NB])
               for dc in range(DC)]
        ck_hxn = ln_slice(hxs, NB, ck_hx)
    am = None
    for dc in range(DC):
        W("V", ck_hxn)
        if dc == 0:
            v.tensor_copy(out=run_sb[:, 0:HN], in_=hx_ap(hcand, 0))
            v.drain()
            am = v.tensor_copy(out=s2mb[:, 0:HN], in_=hx_ap(hcand, 0))
        else:
            v.tensor_tensor(out=run_sb[:, 0:HN], in0=run_sb[:, 0:HN],
                            in1=hx_ap(hcand, dc), op=OP.max)
            v.drain()
            am = v.tensor_tensor(out=s2mb[:, 0:HN], in0=s2mb[:, 0:HN],
                                 in1=hx_ap(hcand, dc), op=OP.min)
        v.drain()
    v.tensor_scalar(out=s2mb[:, 0:HN], in0=s2mb[:, 0:HN], scalar1=-1.0,
                    scalar2=None, op0=OP.mult)
    v.drain()
    am = v.tensor_tensor(out=run_sb[:, 0:HN], in0=run_sb[:, 0:HN],
                         in1=s2mb[:, 0:HN], op=OP.max)
    v.drain()
    inc("V", am)
    W("V", ck("V"))
    zz = v.memset(run_sb[:, HN:128], 0.0)
    v.drain()
    ck_hfold = inc("V", zz)
    ck_trb = token_scales(run_sb[:, 0:128], thr_sb[:, 4:5], ck_hfold)
    W("V", ck_trb)
    v.tensor_copy(out=ROW(R_HALO, 64, HN), in_=pstr[0:1, 0:HN])
    v.drain()
    hsm = v.tensor_copy(out=ROW(R_HALO, 96, HN),
                        in_=pstr[32:33, 0:HN])
    v.drain()
    ck_hsm = inc("V", hsm)
    ck_hqb = bcast_row(R_HALO, 64, HN, bb1, 0, 5, deps=(ck_hsm,))
    mu = None
    for dc in range(DC):
        W("V", ck_hqb)
        v.tensor_mul(out=i32a[:, dc * HN:(dc + 1) * HN], in0=hx_ap(hcand, dc),
                     in1=bb1[:, 0:HN])
        v.drain()
    mu = v.tensor_copy(out=hq_sb[:, 0:DC * HN], in_=i32a[:, 0:DC * HN])
    v.drain()
    ck_hq = inc("V", mu)

    for sl in range(NSL):
        for dc in range(DC):
            W("V", ck_xn)
            xs_ = xn_sb[:, dc * T + sl * 512:dc * T + sl * 512 + 512]
            if dc == 0:
                v.tensor_copy(out=bb2[:, 0:512], in_=xs_)
                v.drain()
                am = v.tensor_copy(out=bb1[:, 0:512], in_=xs_)
            else:
                v.tensor_tensor(out=bb2[:, 0:512], in0=bb2[:, 0:512],
                                in1=xs_, op=OP.max)
                v.drain()
                am = v.tensor_tensor(out=bb1[:, 0:512], in0=bb1[:, 0:512],
                                     in1=xs_, op=OP.min)
            v.drain()
        v.tensor_scalar(out=bb1[:, 0:512], in0=bb1[:, 0:512], scalar1=-1.0,
                        scalar2=None, op0=OP.mult)
        v.drain()
        am = v.tensor_tensor(out=bb2[:, 0:512], in0=bb2[:, 0:512],
                             in1=bb1[:, 0:512], op=OP.max)
        v.drain()
        ck_fold = inc("V", am)
        for i in range(4):
            ck_trb = token_scales(bb2[:, i * 128:(i + 1) * 128],
                                  thr_sb[:, 4:5], ck_fold)
            W("V", ck_trb)
            v.tensor_copy(out=ROW(R_S1 + sl, i * 128, 128),
                          in_=pstr[0:1, 0:128])
            v.drain()
            cp = v.tensor_copy(out=ROW(R_S1M + sl, i * 128, 128),
                               in_=pstr[32:33, 0:128])
            v.drain()
            inc("V", cp)
    ck_s1 = ck("V")
    drain_side()

    l1 = {}
    for tt in range(NB):
        tb = tt * 256
        sl_, off_ = tb // 512, tb % 512
        ck_s1b = bcast_row(R_S1 + sl_, off_, 256, bb1, 0, 4, deps=(ck_s1,))
        ck_s1mb = bcast_row(R_S1M + sl_, off_, 256, s2mb, 0, 3, deps=(ck_s1,))
        xq = None
        for dc in range(DC):
            W("V", ck_s1b, ck_s1mb, ck_hq, l1.get("w1done"))
            v.tensor_mul(out=i32a[:, 0:256],
                         in0=xn_sb[:, dc * T + tb:dc * T + tb + 256],
                         in1=bb1[:, 0:256])
            v.drain()
            v.tensor_mul(out=xq1_sb[:, dc * 258 + 1:dc * 258 + 257],
                         in0=i32a[:, 0:256], in1=s2mb[:])
            v.drain()
        x3 = xq1_sb[:].rearrange("p (c n) -> p c n", n=258)
        h3 = hq_sb[:].rearrange("p (c n) -> p c n", n=HN)
        v.tensor_copy(out=x3[:, :, 0:1], in_=h3[:, :, tt:tt + 1])
        v.drain()
        xq = v.tensor_copy(out=x3[:, :, 257:258], in_=h3[:, :, NB + tt:NB + tt + 1])
        v.drain()
        ck_xq1 = inc("V", xq)
        for jc in range(JC1):
            tslot = (jc % 2) * DC * 128
            W("Y", ("YT0", cnt["YT0"]), ("YT1", cnt["YT1"]), ck("T"))
            tl = y.dma_start(
                out=t1buf[:, tslot:tslot + DC * 128].rearrange(
                    "p (c n) -> p c n", c=DC),
                in_=t1d[:, :, jc * 128:(jc + 1) * 128])
            ck_t1 = inc("Y1a" if jc % 2 == 0 else "Y1b", tl, 16)
            pb = 5 + (jc % 2)
            mm = None
            for dc in range(DC):
                W("T", ck_t1, ck_xq1, l1.get(("hev", jc - 2)))
                mm = t.matmul(ps[:, pb * 512:pb * 512 + 258],
                              t1buf[:, tslot + dc * 128:tslot + dc * 128 + 128],
                              xq1_sb[:, dc * 258:(dc + 1) * 258],
                              start=(dc == 0), stop=(dc == DC - 1))
            ck_w1 = inc("T", mm)
            pso = pb * 512
            W("V", ck_w1)
            v.tensor_scalar(out=fsc[:, 0:256], in0=ps[:, pso:pso + 256],
                            scalar1=cvw_sb[:, jc * 3:jc * 3 + 1], scalar2=None,
                            op0=OP.mult)
            v.drain()
            v.scalar_tensor_tensor(out=fsc[:, 0:256], in0=ps[:, pso + 1:pso + 257],
                                   scalar=cvw_sb[:, jc * 3 + 1:jc * 3 + 2],
                                   in1=fsc[:, 0:256], op0=OP.mult, op1=OP.add)
            v.drain()
            cv3 = v.scalar_tensor_tensor(out=fsc[:, 0:256],
                                         in0=ps[:, pso + 2:pso + 258],
                                         scalar=cvw_sb[:, jc * 3 + 2:jc * 3 + 3],
                                         in1=fsc[:, 0:256], op0=OP.mult, op1=OP.add)
            v.drain()
            ck_cv = inc("V", cv3)
            l1[("hev", jc)] = ck_cv
            W("S", ck_cv)
            si = s.activation(out=hpool[:, jc * 256:(jc + 1) * 256],
                              in_=fsc[:, 0:256], func=AF.Silu,
                              bias=cvb_sb[:, jc:jc + 1])
            ck_si = inc("S", si)
            W("V", ck_si)
            if jc == 0:
                v.tensor_copy(out=run_sb[:, 0:256], in_=hpool[:, 0:256])
                v.drain()
                am = v.tensor_copy(out=s2mb[:], in_=hpool[:, 0:256])
            else:
                v.tensor_tensor(out=run_sb[:, 0:256], in0=run_sb[:, 0:256],
                                in1=hpool[:, jc * 256:(jc + 1) * 256], op=OP.max)
                v.drain()
                am = v.tensor_tensor(out=s2mb[:], in0=s2mb[:],
                                     in1=hpool[:, jc * 256:(jc + 1) * 256],
                                     op=OP.min)
            v.drain()
            inc("V", am)
        W("V", ck("V"))
        v.tensor_scalar(out=s2mb[:], in0=s2mb[:], scalar1=-1.0,
                        scalar2=None, op0=OP.mult)
        v.drain()
        am = v.tensor_tensor(out=run_sb[:], in0=run_sb[:], in1=s2mb[:],
                             op=OP.max)
        v.drain()
        inc("V", am)
        l1["w1done"] = ck("V")
        ck_run = ck("V")
        for half in range(2):
            ck_trb = token_scales(run_sb[:, half * 128:(half + 1) * 128]
                                  if half else run_sb[:, 0:128],
                                  thr_sb[:, 5:6], ck_run)
            W("V", ck_trb)
            v.tensor_copy(out=ROW(R_S2, half * 128, 128),
                          in_=pstr[0:1, 0:128])
            v.drain()
            cp = v.tensor_copy(out=ROW(R_S2M, half * 128, 128),
                               in_=pstr[32:33, 0:128])
            v.drain()
            inc("V", cp)
        ck_s2 = ck("V")
        ck_s2b = bcast_row(R_S2, 0, 256, run_sb, 0, 2, deps=(ck_s2,))
        ck_s2mb = bcast_row(R_S2M, 0, 256, s2mb, 0, 2, deps=(ck_s2,))
        xq2 = None
        for jc in range(JC1):
            W("V", ck_s2b, ck_s2mb)
            v.tensor_mul(out=i32a[:, 0:256], in0=hpool[:, jc * 256:(jc + 1) * 256],
                         in1=run_sb[:])
            v.drain()
            xq2 = v.tensor_mul(out=hpool[:, jc * 256:(jc + 1) * 256],
                               in0=i32a[:, 0:256], in1=s2mb[:])
            v.drain()
        ck_xq2 = inc("V", xq2)
        for mc in range(DC):
            pb = mc % 2
            mm = None
            for jh2 in range(2):
                W("Y", ("YT0", cnt["YT0"]), ("YT1", cnt["YT1"]), ck("T"))
                tl = y.dma_start(
                    out=t2buf[:].rearrange("p (c n) -> p c n", c=16),
                    in_=t2d[jh2 * 16:(jh2 + 1) * 16, :, mc * 128:(mc + 1) * 128])
                ck_t2 = inc("Y2", tl, 16)
                for jc2 in range(16):
                    jc = jh2 * 16 + jc2
                    W("T", ck_t2, ck_xq2, l1.get(("xadd", mc - 2)))
                    mm = t.matmul(ps[:, pb * 512:pb * 512 + 256],
                                  t2buf[:, jc2 * 128:jc2 * 128 + 128],
                                  hpool[:, jc * 256:(jc + 1) * 256],
                                  start=(jc == 0), stop=(jc == JC1 - 1))
                inc("T", mm)
            ck_w2 = ck("T")
            xsl = x_sb[:, mc * T + tb:mc * T + tb + 256]
            W("V", ck_w2)
            ad = v.tensor_add(out=xsl, in0=xsl,
                              in1=ps[:, pb * 512:pb * 512 + 256])
            v.drain()
            l1[("xadd", mc)] = inc("V", ad)
        l1["w1done"] = ck("V")

    ck_attn1 = emit_attention(1, ck_ag1)
    ck_x2 = emit_oproj(1, wpj_in, ck_attn1)

    W("Y", ck_x2)
    for dc in range(DC):
        inc("YO", y.dma_start(out=out_d[dc * 128:(dc + 1) * 128, :],
                              in_=x_sb[:, dc * T:(dc + 1) * T]), 16)
    y.wait_ge(sems["YO"], cnt["YO"])

    blk.tensor(_replay(tapes["T"]))
    blk.vector(_replay(tapes["V"]))
    blk.scalar(_replay(tapes["S"]))
    blk.sync(_replay(tapes["Y"]))
    blk.gpsimd(_replay(tapes["G"]))

    es.close()
    return nc


def _masks(par):
    kk = np.arange(128)[:, None]
    qq = np.arange(256)[None, :]
    tri0 = (kk <= qq).astype(np.float32)
    tri1 = (kk + 128 <= qq).astype(np.float32)
    one = np.ones((128, 256), np.float32)
    zero = np.zeros((128, 256), np.float32)
    ms = [tri0, tri1, zero, zero] if par == 0 else [one, one, tri0, tri1]
    return np.concatenate(ms, axis=1)


def _stage(inputs, S):
    x = np.asarray(inputs["L0_attn_residual"])
    B = x.shape[0]
    NBG = S // 256
    C = lambda a: np.ascontiguousarray(a)

    wq = C(np.asarray(inputs["L0_attn_W_Q"]).reshape(D, D).T)
    wk = C(np.asarray(inputs["L0_attn_W_K"]).reshape(D, D).T)
    wv = C(np.asarray(inputs["L0_attn_W_V"]).reshape(D, D).T)
    wo = C(np.asarray(inputs["L0_attn_W_O"]).reshape(D, D).T)
    wg = C(np.asarray(inputs["L0_ffn_w_gate"]).T.astype(np.int8))
    wu = C(np.asarray(inputs["L0_ffn_w_up"]).T.astype(np.int8))
    wd = C(np.asarray(inputs["L0_ffn_w_down"]).T.astype(np.int8))
    wqkv = C(np.asarray(inputs["L1_attn_w_qkv"]).T)
    wpj = C(np.asarray(inputs["L1_attn_w_proj"]).T)
    bpj = C(np.asarray(inputs["L1_attn_b_proj"]).reshape(D, 1))
    w1 = C(np.asarray(inputs["L1_ffn_w1"]).T)
    w2 = C(np.asarray(inputs["L1_ffn_w2"]).T)
    cw = C(np.asarray(inputs["L1_ffn_conv_w"]).reshape(H1, 3))
    cb = C(np.asarray(inputs["L1_ffn_conv_b"]).reshape(H1, 1))
    ident = np.eye(128, dtype=np.float32)

    in_maps = []
    for core in range(2 * B):
        b, par = core // 2, core % 2
        blocks = x[b].reshape(NBG, 256, D)[par::2]
        xT = C(blocks.reshape(S // 2, D).T)
        hsel = np.zeros((128, 2), np.float32)
        hsel[:, 0 if par == 0 else 1] = 1.0
        in_maps.append({
            "xT": xT, "masks": _masks(par), "ident": ident, "hsel": hsel,
            "wqT": wq, "wkT": wk, "wvT": wv, "woT": wo,
            "wgT": wg, "wuT": wu, "wdT": wd,
            "wqkvT": wqkv, "wprojT": wpj, "bproj": bpj,
            "w1T": w1, "w2T": w2, "convw": cw, "convb": cb,
        })
    return in_maps


def kernel(**inputs):
    x = np.asarray(inputs["L0_attn_residual"])
    B, S, _ = x.shape
    nc = build_nc(S)
    in_maps = _stage(inputs, S)
    res = run_bass_kernel_spmd(nc, in_maps, core_ids=list(range(2 * B)))
    NBG = S // 256
    out = np.empty((B, S, D), np.float32)
    for core in range(2 * B):
        b, par = core // 2, core % 2
        oT = np.asarray(res.results[core]["outT"])
        out[b].reshape(NBG, 256, D)[par::2] = oT.T.reshape(NBG // 2, 256, D)
    return out


# version 13
# speedup vs baseline: 1.4588x; 1.0361x over previous
import numpy as np
import ml_dtypes
from contextlib import ExitStack

import concourse.bass as bass
import concourse.mybir as mybir
from concourse.bass_utils import run_bass_kernel_spmd
from concourse.alu_op_type import AluOpType as OP

F32 = mybir.dt.float32
BF16 = mybir.dt.bfloat16
I32 = mybir.dt.int32
I8 = mybir.dt.int8
AF = mybir.ActivationFunctionType
AX = mybir.AxisListType

D = 1024
DC = 8
HP = 8
H0 = 3072
JC0 = H0 // 128
H1 = 4096
JC1 = H1 // 128
C127 = float(np.float32(1.0 / 127.0))
CD127 = float(np.float32(1.0 / (127.0 * 127.0)))
EXP_SHIFT = -20.0
RG = [[0, 1], [2, 3], [4, 5], [6, 7]]

R_M, R_R, R_VAR, R_SD = 0, 1, 2, 3
R_REC, R_RECO = 0, 1
R_S2, R_S2M = 2, 3
R_S1, R_S1M = 4, 8
R_WM = 2
R_HALO = 1
R_SC = 0


def build_nc(S):
    T = S // 2
    NB = T // 256
    NSL = T // 512
    TT0 = 512
    NT0 = T // TT0
    HN = 2 * NB
    assert HN <= 32

    nc = bass.Bass("TRN2", target_bir_lowering=False)
    DT = nc.dram_tensor

    xT_in = DT("xT", [D, T], F32, kind="ExternalInput").ap()
    masks_in = DT("masks", [128, 1024], F32, kind="ExternalInput").ap()
    ident_in = DT("ident", [128, 128], F32, kind="ExternalInput").ap()
    hsel_in = DT("hsel", [128, 2], F32, kind="ExternalInput").ap()
    wq_in = DT("wqT", [D, D], F32, kind="ExternalInput").ap()
    wk_in = DT("wkT", [D, D], F32, kind="ExternalInput").ap()
    wv_in = DT("wvT", [D, D], F32, kind="ExternalInput").ap()
    wo_in = DT("woT", [D, D], F32, kind="ExternalInput").ap()
    wg_in = DT("wgT", [D, H0], I8, kind="ExternalInput").ap()
    wu_in = DT("wuT", [D, H0], I8, kind="ExternalInput").ap()
    wd_in = DT("wdT", [H0, D], I8, kind="ExternalInput").ap()
    wqkv_in = DT("wqkvT", [D, 3 * D], F32, kind="ExternalInput").ap()
    wpj_in = DT("wprojT", [D, D], F32, kind="ExternalInput").ap()
    bpj_in = DT("bproj", [D, 1], F32, kind="ExternalInput").ap()
    w1_in = DT("w1T", [D, H1], F32, kind="ExternalInput").ap()
    w2_in = DT("w2T", [H1, D], F32, kind="ExternalInput").ap()
    cw_in = DT("convw", [H1, 3], F32, kind="ExternalInput").ap()
    cb_in = DT("convb", [H1, 1], F32, kind="ExternalInput").ap()
    out_d = DT("outT", [D, T], F32, kind="ExternalOutput").ap()

    KE = HP * 128 * T
    VE = (T // 128) * 128 * D
    ccin = [DT(f"cc{L}i", [KE + VE], BF16, kind="Internal").ap() for L in (0, 1)]
    ccout = [DT(f"cc{L}o", [2, KE + VE], BF16, kind="Internal").ap() for L in (0, 1)]
    hai = DT("hai", [D, HN], F32, kind="Internal").ap()
    hao = DT("hao", [2, D, HN], F32, kind="Internal").ap()
    qsp = DT("qsp", [HP, 128, T], BF16, kind="Internal").ap()
    asp = DT("asp", [HP, 128, T], BF16, kind="Internal").ap()
    t1d = DT("t1d", [DC, 128, H1], BF16, kind="Internal").ap()
    t2d = DT("t2d", [JC1, 128, D], BF16, kind="Internal").ap()

    es = ExitStack()
    SB = lambda n, sh, d: es.enter_context(nc.sbuf_tensor(n, sh, d))

    x_sb = SB("x", [128, DC * T], F32)
    xn_sb = SB("xn", [128, DC * T], BF16)
    rows = SB("rows", [128, 2048], F32)
    ROW = lambda i, c0, w: rows[32 * (i % 3):32 * (i % 3) + 1,
                                (i // 3) * 512 + c0:(i // 3) * 512 + c0 + w]
    bb1 = SB("bb1", [128, 512], F32)
    bb2 = SB("bb2", [128, 512], F32)
    fsc = SB("fsc", [128, 1024], F32)
    warena = SB("warena", [128, DC * 512], BF16)
    wff_w = SB("wff_w", [128, 2 * 1024], BF16)
    kvst = SB("kvst", [128, 2 * 512], BF16)
    katt = SB("katt", [128, S], BF16)
    vatt = SB("vatt", [128, (S // 128) * 130], BF16)
    qatt = SB("qatt", [128, T], BF16)
    p_sb = SB("p_sb", [128, 2048], BF16)
    attst = SB("attst", [128, T], BF16)
    attb = SB("attb", [128, DC * 256], BF16)
    masks_sb = SB("masksb", [128, 1024], BF16)
    ident_sb = SB("identb", [128, 128], F32)
    hsel_sb = SB("hselb", [128, 2], F32)
    ones_r = SB("ones_r", [128, 128], F32)
    onesk = SB("onesk", [128, 1], F32)
    mxc = SB("mxc", [128, 64], F32)
    hpool = SB("hpool", [128, JC0 * TT0], BF16)
    xq1_sb = SB("xq1", [128, DC * 258], BF16)
    t1buf = SB("t1buf", [128, 2 * DC * 128], BF16)
    t2buf = SB("t2buf", [128, 16 * 128], BF16)
    wqf = SB("wqf", [128, 512], BF16)
    tq_sb = SB("tq", [128, 512], BF16)
    wabs = SB("wabs", [128, 4], F32)
    thr_sb = SB("thr", [128, 8], F32)
    i32a = SB("i32a", [128, 256], I32)
    s2mb = SB("s2mb", [128, 256], F32)
    run_sb = SB("run_sb", [128, 256], F32)
    cvw_sb = SB("cvw", [128, 3 * JC1], F32)
    cvb_sb = SB("cvb", [128, JC1], F32)
    bpj_sb = SB("bpjb", [128, DC], F32)
    cbias = SB("cbias", [128, 4], F32)
    hx_sb = SB("hx", [128, 2 * DC * NB], F32)
    hcand = SB("hcand", [128, 2 * DC * NB], F32)
    hq_sb = SB("hq", [128, 2 * DC * NB], BF16)

    ps = es.enter_context(nc.psum_tensor("ps_all", [128, 3584], F32))
    pstr = es.enter_context(nc.psum_tensor("ps_tr", [128, 512], F32))

    DMA_FAMS = ["YX", "YC", "YS0", "YS1", "YH", "YW", "YT0", "YT1", "YA",
                "YB", "YP", "YO", "Y1a", "Y1b", "Y2", "Y2b", "GM", "GA", "GF",
                "GU", "GD0", "GD1", "GW0", "GW1", "CC"]
    sems, cnt = {}, {k: 0 for k in (["T", "V", "S"] + DMA_FAMS)}
    for k in cnt:
        sems[k] = es.enter_context(nc.semaphore("sem" + k))
    last_wait = {}

    blk = es.enter_context(nc.Block())

    class _RecInstr:
        __slots__ = ("incs",)

        def __init__(self):
            self.incs = []

        def then_inc(self, sem, n):
            self.incs.append((sem, n))
            return self

    class _Rec:
        def __init__(self, tape):
            self._tape = tape

        def __getattr__(self, m):
            tape = self._tape

            def call(*a, **k):
                ri = _RecInstr()
                tape.append((m, a, k, ri))
                return ri
            return call

    tapes = {k: [] for k in "TVSYG"}
    ENG = {k: _Rec(tapes[k]) for k in "TVSYG"}
    t, v, s, y, g = (ENG[k] for k in "TVSYG")

    import os as _os
    _SIM = bool(_os.environ.get("BASS_SIM_FP64"))

    def _replay(tape):
        def f(eng):
            for m, a, k, ri in tape:
                instr = getattr(eng, m)(*a, **k)
                for sem, n in ri.incs:
                    instr.then_inc(sem, n)
        return f

    def inc(tag, instr, n=1):
        instr.then_inc(sems[tag], n)
        cnt[tag] += n
        return (tag, cnt[tag])

    def ck(tag):
        return (tag, cnt[tag])

    def W(consumer, *cks):
        lw = last_wait.setdefault(consumer, {})
        best = {}
        for c in cks:
            if c is None:
                continue
            tag, val = c
            if val > best.get(tag, 0):
                best[tag] = val
        for tag, val in best.items():
            if val <= 0 or lw.get(tag, 0) >= val:
                continue
            assert val <= cnt[tag], f"forward wait {tag}:{val} > {cnt[tag]}"
            ENG[consumer].wait_ge(sems[tag], val)
            lw[tag] = val

    for dc in range(DC):
        inc("YX", y.dma_start(out=x_sb[:, dc * T:(dc + 1) * T],
                              in_=xT_in[dc * 128:(dc + 1) * 128, :]), 16)
    ck_x0 = ck("YX")
    inc("YC", y.dma_start(out=ident_sb[:], in_=ident_in), 16)
    inc("YC", y.dma_start(out=hsel_sb[:], in_=hsel_in), 16)
    inc("YC", y.dma_start(
        out=cvw_sb[:].rearrange("p (c k) -> p c k", c=JC1),
        in_=cw_in.rearrange("(c p) k -> p c k", p=128),
        allow_slow_non_contiguous=True), 16)
    inc("YC", y.dma_start(
        out=cvb_sb[:].rearrange("p (c o) -> p c o", c=JC1),
        in_=cb_in.rearrange("(c p) o -> p c o", p=128),
        allow_slow_non_contiguous=True), 16)
    inc("YC", y.dma_start(
        out=bpj_sb[:].rearrange("p (c o) -> p c o", c=DC),
        in_=bpj_in.rearrange("(c p) o -> p c o", p=128),
        allow_slow_non_contiguous=True), 16)
    ck_consts = ck("YC")
    ck_masks = inc("GM", g.dma_start(out=masks_sb[:], in_=masks_in), 16)

    W("V", ck_consts)
    v.memset(ones_r[:], 1.0)
    v.memset(onesk[:], 1.0)
    v.memset(cbias[:, 0:1], 1e-5)
    v.memset(cbias[:, 1:2], 1.0)
    ck_ones = inc("V", v.memset(cbias[:, 2:3], EXP_SHIFT))

    def bcast_row(slot, c0, n, dst_sb, dst_off, pbank, deps=(), recip=False):
        b = 32 * (slot % 3)
        pap = (pstr[:, 128:128 + n] if pbank == "P"
               else ps[:, pbank * 512:pbank * 512 + n])
        W("T", ck_ones, ck("V"), *deps)
        mm = t.matmul(pap, ones_r[b:b + 1, 0:128],
                      ROW(slot, c0, n), start=True, stop=True)
        ckt = inc("T", mm)
        W("V", ckt)
        if recip:
            cp = v.reciprocal(out=dst_sb[:, dst_off:dst_off + n], in_=pap)
        else:
            cp = v.tensor_copy(out=dst_sb[:, dst_off:dst_off + n], in_=pap)
        v.drain()
        return inc("V", cp)

    def wload(fam, dram_ap, row0, rows_, col0, ncols, dst, dst_off):
        ins = g.dma_start(
            out=dst[:, dst_off:dst_off + (rows_ // 128) * ncols].rearrange(
                "p (c n) -> p c n", c=rows_ // 128),
            in_=dram_ap[row0:row0 + rows_, col0:col0 + ncols].rearrange(
                "(c p) n -> p c n", p=128))
        return inc(fam, ins, 16)

    def ln_slice(xs_aps, w_, dep):
        mm2 = None
        W("T", ck("V"))
        for dc in range(DC):
            xap = xs_aps[dc][0]
            off = (dc % 2) * 512
            W("V", dep, ck("T"))
            sq = v.tensor_mul(out=fsc[:, off:off + w_], in0=xap, in1=xap)
            v.drain()
            cksq = inc("V", sq)
            W("T", ck_ones, cksq)
            t.matmul(ps[0:1, 3 * 512:3 * 512 + w_], onesk[:], xap,
                     start=(dc == 0), stop=(dc == DC - 1))
            mm2 = t.matmul(ps[0:1, 4 * 512:4 * 512 + w_], onesk[:],
                           fsc[:, off:off + w_],
                           start=(dc == 0), stop=(dc == DC - 1))
            inc("T", mm2)
        ck_st = ck("T")
        W("V", ck_st)
        v.tensor_scalar(out=ROW(R_M, 0, w_),
                        in0=ps[0:1, 3 * 512:3 * 512 + w_],
                        scalar1=1.0 / D, scalar2=None, op0=OP.mult)
        v.drain()
        v.tensor_mul(out=ROW(R_VAR, 0, w_), in0=ROW(R_M, 0, w_),
                     in1=ROW(R_M, 0, w_))
        v.drain()
        va = v.scalar_tensor_tensor(out=ROW(R_VAR, 0, w_),
                                    in0=ps[0:1, 4 * 512:4 * 512 + w_],
                                    scalar=1.0 / D, in1=ROW(R_VAR, 0, w_),
                                    op0=OP.mult, op1=OP.subtract)
        v.drain()
        ck_var = inc("V", va)
        W("S", ck_var)
        sq2 = s.activation(out=ROW(R_SD, 0, w_),
                           in_=ROW(R_VAR, 0, w_), func=AF.Sqrt,
                           bias=cbias[0:1, 0:1])
        ck_sd = inc("S", sq2)
        ck_mb = bcast_row(R_M, 0, w_, bb1, 0, 5, deps=(ck_sd,))
        ck_rb = bcast_row(R_SD, 0, w_, bb2, 0, 6, deps=(ck_sd,), recip=True)
        mu = None
        for dc in range(DC):
            xap, oap = xs_aps[dc]
            W("V", ck_mb, ck_rb)
            v.tensor_sub(out=fsc[:, 0:w_], in0=xap, in1=bb1[:, 0:w_])
            v.drain()
            mu = v.tensor_mul(out=oap, in0=fsc[:, 0:w_], in1=bb2[:, 0:w_])
            v.drain()
        return inc("V", mu)

    def emit_layernorm(dep):
        outs = []
        for sl in range(NSL):
            c0 = sl * 512
            xs = [(x_sb[:, dc * T + c0:dc * T + c0 + 512],
                   xn_sb[:, dc * T + c0:dc * T + c0 + 512]) for dc in range(DC)]
            outs.append(ln_slice(xs, 512, dep))
        return outs[-1]

    proj_state = {"dma": [], "slot": 0, "evac": []}

    def proj_evac(ck_t, pb, dst_ap):
        sl = proj_state["slot"]
        so = (sl % 2) * 512
        W("S", ck_t)
        if sl >= 2:
            W("S", proj_state["dma"][sl - 2])
        cp = s.copy(out=kvst[:, so:so + 512],
                    in_=ps[:, pb * 512:pb * 512 + 512])
        s.drain()
        ck_v = inc("S", cp)
        proj_state["evac"].append(ck_v)
        W("Y", ck_v, ("YA", cnt["YA"]))
        dd = y.dma_start(out=dst_ap, in_=kvst[:, so:so + 512])
        proj_state["dma"].append(inc("YS0" if sl % 2 == 0 else "YS1", dd, 16))
        proj_state["slot"] += 1
        pump_side(2)

    def emit_proj_w(w_dram, wcol0, dep_xn, dst_fn):
        for half in range(2):
            W("G", ck("T"))
            ck_w = wload("GA", w_dram, 0, D, wcol0 + half * 512, 512, warena, 0)
            for a2 in range(4):
                a = half * 4 + a2
                for b in range(NSL):
                    pb = 5 + (proj_state["slot"] % 2)
                    mm = None
                    ev = proj_state["evac"]
                    for dc in range(DC):
                        W("T", ck_w, dep_xn,
                          ev[-2] if len(ev) >= 2 else None)
                        mm = t.matmul(
                            ps[:, pb * 512:pb * 512 + 512],
                            warena[:, dc * 512 + a2 * 128:dc * 512 + a2 * 128 + 128],
                            xn_sb[:, dc * T + b * 512:dc * T + b * 512 + 512],
                            start=(dc == 0), stop=(dc == DC - 1))
                    proj_evac(inc("T", mm), pb, dst_fn(a, b))

    def emit_proj_v(w_dram, wcol0, dep_xn, dst_fn):
        for b in range(2):
            W("G", ck("T"))
            ck_w = wload("GA", w_dram, 0, D, wcol0 + b * 512, 512, warena, 0)
            for a in range(T // 128):
                pb = 5 + (proj_state["slot"] % 2)
                mm = None
                ev = proj_state["evac"]
                for dc in range(DC):
                    W("T", ck_w, dep_xn,
                      ev[-2] if len(ev) >= 2 else None)
                    mm = t.matmul(
                        ps[:, pb * 512:pb * 512 + 512],
                        xn_sb[:, dc * T + a * 128:dc * T + a * 128 + 128],
                        warena[:, dc * 512:dc * 512 + 512],
                        start=(dc == 0), stop=(dc == DC - 1))
                proj_evac(inc("T", mm), pb, dst_fn(a, b))

    def token_scales(fold_ap, mcol_ap, dep):
        W("T", dep, ck_consts, ck("V"))
        tr = t.transpose(pstr[:, 0:128], fold_ap, ident_sb[:])
        ck_tr = inc("T", tr)
        W("V", ck_tr)
        v.reduce_max(out=mxc[:, 34:35], in_=pstr[:, 0:128], axis=AX.X)
        v.drain()
        v.tensor_scalar(out=mxc[:, 35:36], in0=mxc[:, 34:35], scalar1=1e-5,
                        scalar2=None, op0=OP.max)
        v.drain()
        v.reciprocal(out=mxc[:, 36:37], in_=mxc[:, 35:36])
        v.drain()
        v.tensor_scalar(out=mxc[:, 0:1], in0=mxc[:, 36:37], scalar1=127.0,
                        scalar2=None, op0=OP.mult)
        v.drain()
        sm = v.tensor_scalar(out=mxc[:, 32:33], in0=mxc[:, 35:36], scalar1=mcol_ap,
                             scalar2=C127, op0=OP.mult, op1=OP.mult)
        v.drain()
        ck_mx = inc("V", sm)
        W("T", ck_mx)
        tr2 = t.transpose(pstr[0:64, 0:128], mxc[:, :], ident_sb[:])
        return inc("T", tr2)

    def wquant_gen(dram_ap, rows_, ncol, col, dst_dram):
        pieces = [(c, p0) for c in range(rows_ // 128)
                  for p0 in range(0, ncol, 256)]
        vdone, lds = {}, {}

        def issue(i, dep0):
            c, p0 = pieces[i]
            sl = i % 2
            W("G", vdone.get(i - 2, dep0 if i < 2 else None))
            return inc("GW0" if sl == 0 else "GW1", g.dma_start(
                out=wqf[:, sl * 256:sl * 256 + 256],
                in_=dram_ap[c * 128:(c + 1) * 128, p0:p0 + 256]), 16)

        dep0 = ("V", cnt["V"])
        lds[0] = issue(0, dep0)
        first = True
        ck_acc = None
        for i in range(len(pieces)):
            if i + 1 < len(pieces):
                lds[i + 1] = issue(i + 1, dep0)
            sl = i % 2
            W("V", lds[i])
            v.reduce_sum(out=wabs[:, 2:3], in_=wqf[:, sl * 256:sl * 256 + 256],
                         axis=AX.X, apply_absolute_value=True)
            v.drain()
            if first:
                cp = v.tensor_copy(out=wabs[:, 0:1], in_=wabs[:, 2:3])
                first = False
            else:
                cp = v.tensor_add(out=wabs[:, 0:1], in0=wabs[:, 0:1],
                                  in1=wabs[:, 2:3])
            v.drain()
            ck_acc = inc("V", cp)
            vdone[i] = ck_acc
            yield
        W("T", ck_acc, ck_ones)
        mm = t.matmul(pstr[0:1, 0:1], onesk[:], wabs[:, 0:1],
                      start=True, stop=True)
        ck_t = inc("T", mm)
        W("V", ck_t)
        v.tensor_scalar(out=ROW(R_WM, col, 1), in0=pstr[0:1, 0:1],
                        scalar1=1.0 / (rows_ * ncol), scalar2=None, op0=OP.mult)
        v.drain()
        th = v.tensor_scalar(out=ROW(R_WM, col + 2, 1), in0=ROW(R_WM, col, 1),
                             scalar1=0.5, scalar2=None, op0=OP.mult)
        ck_m = inc("V", th)
        W("T", ck_m)
        wb = 32 * (R_WM % 3)
        mm = t.matmul(pstr[:, 0:1], ones_r[wb:wb + 1, 0:128],
                      ROW(R_WM, col + 2, 1), start=True, stop=True)
        ck_t = inc("T", mm)
        W("V", ck_t)
        v.tensor_copy(out=thr_sb[:, 2 * col:2 * col + 1],
                      in_=pstr[:, 0:1])
        v.drain()
        v.tensor_scalar(out=thr_sb[:, 2 * col + 1:2 * col + 2],
                        in0=thr_sb[:, 2 * col:2 * col + 1], scalar1=-1.0,
                        scalar2=None, op0=OP.mult)
        v.drain()
        th2 = v.tensor_scalar(out=thr_sb[:, 4 + col:5 + col],
                              in0=thr_sb[:, 2 * col:2 * col + 1], scalar1=2.0,
                              scalar2=None, op0=OP.mult)
        v.drain()
        ck_thr = inc("V", th2)
        yield
        qdone, ytd, lds2 = {}, {}, {}
        yt_prev = (("YT0", cnt["YT0"]), ("YT1", cnt["YT1"]))

        def issue2(i):
            c, p0 = pieces[i]
            sl = i % 2
            W("G", qdone.get(i - 2, ck_acc if i < 2 else None))
            return inc("GW0" if sl == 0 else "GW1", g.dma_start(
                out=wqf[:, sl * 256:sl * 256 + 256],
                in_=dram_ap[c * 128:(c + 1) * 128, p0:p0 + 256]), 16)

        lds2[0] = issue2(0)
        for i in range(len(pieces)):
            c, p0 = pieces[i]
            if i + 1 < len(pieces):
                lds2[i + 1] = issue2(i + 1)
            sl = i % 2
            so = sl * 256
            W("V", lds2[i], ck_thr,
              *(yt_prev if i < 2 else (ytd.get(i - 2),)))
            v.tensor_scalar(out=run_sb[:], in0=wqf[:, so:so + 256],
                            scalar1=thr_sb[:, 2 * col + 1:2 * col + 2],
                            scalar2=None, op0=OP.is_lt)
            v.drain()
            v.tensor_scalar(out=tq_sb[:, so:so + 256], in0=wqf[:, so:so + 256],
                            scalar1=thr_sb[:, 2 * col:2 * col + 1],
                            scalar2=None, op0=OP.is_gt)
            v.drain()
            sb2 = v.tensor_sub(out=tq_sb[:, so:so + 256],
                               in0=tq_sb[:, so:so + 256], in1=run_sb[:])
            v.drain()
            ck_q = inc("V", sb2)
            qdone[i] = ck_q
            W("Y", ck_q)
            dd = y.dma_start(out=dst_dram[c][:, p0:p0 + 256],
                             in_=tq_sb[:, so:so + 256])
            ytd[i] = inc("YT0" if sl == 0 else "YT1", dd, 16)
            yield

    side_gens = []

    def pump_side(n=1):
        for _ in range(n):
            while side_gens:
                try:
                    next(side_gens[0])
                    break
                except StopIteration:
                    side_gens.pop(0)
            else:
                return

    def drain_side():
        while side_gens:
            pump_side()


    def emit_attention(L, ck_ag):
        cc = ccout[L]
        st = {"v0": ck("V")}
        prev_tail = None
        for hp in range(HP):
            W("Y", ck_ag, prev_tail, ck("T") if hp else None,
              ("YS0", cnt["YS0"]), ("YS1", cnt["YS1"]))
            kld = []
            for r in (0, 1):
                srck = cc[r][0:KE].rearrange("(h p n) -> h p n", h=HP, p=128)[hp]
                dd = y.dma_start(
                    out=katt[:].rearrange("p (j q) -> p j q",
                                          q=512)[:, :, 256 * r:256 * r + 256],
                    in_=srck.rearrange("p (j n) -> p j n", n=256))
                kld.append(inc("YA", dd, 16))
                srcv = cc[r][KE:KE + VE].rearrange("(c p n) -> c p n",
                                                   c=T // 128, p=128)
                for u in range(2):
                    for hd in range(2):
                        off = 260 * r + 130 * u + 65 * hd
                        dd = y.dma_start(
                            out=vatt[:].rearrange(
                                "p (j q) -> p j q", q=520)[:, :, off:off + 64],
                            in_=srcv[u::2, :, hp * 128 + hd * 64:
                                     hp * 128 + hd * 64 + 64].rearrange(
                                "c p n -> p c n"))
                        kld.append(inc("YA", dd, 16))
            dd = y.dma_start(out=qatt[:], in_=qsp[hp])
            kld.append(inc("YA", dd, 16))
            W("V", *kld, prev_tail)
            ckone = inc("V", v.memset(
                vatt[:].rearrange("p (c f) -> p c f", f=130)[:, :, 64:130:65], 1.0))
            pump_side(1)
            for j in range(NB):
                q0 = j * 256
                for bt in range(j + 1):
                    ck_sc = {}
                    for hd in range(2):
                        bko = hd * 1024
                        mm = None
                        for i in range(4):
                            kb = 4 * bt + i
                            W("T", *kld, ckone, st.get(("exp", hd)),
                              st.get("avfree") if bt == 0 else None)
                            mm = t.matmul(
                                ps[:, bko + i * 256:bko + i * 256 + 256],
                                katt[64 * hd:64 * hd + 64, kb * 128:kb * 128 + 128],
                                qatt[64 * hd:64 * hd + 64, q0:q0 + 256],
                                start=True, stop=True, tile_position=(64 * hd, 0))
                        ck_sc[hd] = inc("T", mm)
                    pump_side(1)
                    for hd in range(2):
                        bko = hd * 1024
                        po = hd * 1024
                        if L == 0:
                            W("S", ck_sc[hd], st.get(("pfree", hd)), st["v0"])
                            e2 = s.activation(out=p_sb[:, po:po + 1024],
                                              in_=ps[:, bko:bko + 1024],
                                              func=AF.Exp, scale=0.125,
                                              bias=cbias[:, 2:3])
                            ck_e = inc("S", e2)
                        else:
                            W("S", ck_sc[hd], st.get(("pfree", hd)), st["v0"])
                            s.activation(out=p_sb[:, po:po + 1024],
                                         in_=ps[:, bko:bko + 1024],
                                         func=AF.Exp, scale=0.125)
                            s.drain()
                            e2 = s.activation(out=p_sb[:, po:po + 1024],
                                              in_=p_sb[:, po:po + 1024],
                                              func=AF.Ln, bias=cbias[:, 1:2])
                            ck_ln = inc("S", e2)
                            W("V", ck_ln)
                            sq = v.tensor_mul(out=p_sb[:, po:po + 1024],
                                              in0=p_sb[:, po:po + 1024],
                                              in1=p_sb[:, po:po + 1024])
                            v.drain()
                            ck_e = inc("V", sq)
                        st[("exp", hd)] = ck_e
                        if bt == j:
                            W("V", ck_e, ck_masks)
                            mk = v.tensor_mul(out=p_sb[:, po:po + 1024],
                                              in0=p_sb[:, po:po + 1024],
                                              in1=masks_sb[:])
                            v.drain()
                            ck_e = inc("V", mk)
                            st[("exp", hd)] = ck_e
                        mm = None
                        for i in range(4):
                            kb = 4 * bt + i
                            W("T", st[("exp", hd)],
                              st.get("avfree") if bt == 0 else None)
                            mm = t.matmul(
                                ps[0:65, (4 + hd) * 512:(4 + hd) * 512 + 256],
                                vatt[:, kb * 130 + hd * 65:kb * 130 + hd * 65 + 65],
                                p_sb[:, po + i * 256:po + i * 256 + 256],
                                start=(bt == 0 and i == 0),
                                stop=(bt == j and i == 3))
                        st[("pfree", hd)] = inc("T", mm)
                        pump_side(1)
                ck_av = st[("pfree", 1)]
                W("V", ck_av)
                if L == 1:
                    v.tensor_scalar(out=ROW(R_REC, 0, 256),
                                    in0=ps[64:65, 2048:2304], scalar1=1e-9,
                                    scalar2=None, op0=OP.add)
                    v.drain()
                    cps = v.tensor_scalar(out=ROW(R_REC, 256, 256),
                                          in0=ps[64:65, 2560:2816],
                                          scalar1=1e-9, scalar2=None,
                                          op0=OP.add)
                else:
                    v.tensor_copy(out=ROW(R_REC, 0, 256),
                                  in_=ps[64:65, 2048:2304])
                    v.drain()
                    cps = v.tensor_copy(out=ROW(R_REC, 256, 256),
                                        in_=ps[64:65, 2560:2816])
                v.drain()
                ck_rc = inc("V", cps)
                W("T", ck_rc)
                rb = 32 * (R_REC % 3)
                t.matmul(pstr[0:64, 256:512], ones_r[rb:rb + 1, 0:64],
                         ROW(R_REC, 0, 256), start=True, stop=True,
                         tile_position=(rb, 0))
                mm = t.matmul(pstr[64:128, 256:512], ones_r[rb:rb + 1, 0:64],
                              ROW(R_REC, 256, 256), start=True,
                              stop=True, tile_position=(rb, 64))
                ck_bc = inc("T", mm)
                if L == 0:
                    W("S", ck_av)
                    s.copy(out=attst[0:64, q0:q0 + 256], in_=ps[0:64, 2048:2304])
                    s.drain()
                    cpk = s.copy(out=attst[64:128, q0:q0 + 256],
                                 in_=ps[0:64, 2560:2816])
                    s.drain()
                    ck_cp = inc("S", cpk)
                else:
                    W("V", ck_av)
                    v.tensor_copy(out=attst[0:64, q0:q0 + 256],
                                  in_=ps[0:64, 2048:2304])
                    v.drain()
                    cpk = v.tensor_copy(out=attst[64:128, q0:q0 + 256],
                                        in_=ps[0:64, 2560:2816])
                    v.drain()
                    ck_cp = inc("V", cpk)
                W("V", ck_bc, ck_cp)
                v.reciprocal(out=fsc[:, 512:768], in_=pstr[:, 256:512])
                v.drain()
                v.tensor_mul(out=attst[0:64, q0:q0 + 256],
                             in0=attst[0:64, q0:q0 + 256],
                             in1=fsc[0:64, 512:768])
                v.drain()
                mu = v.tensor_mul(out=attst[64:128, q0:q0 + 256],
                                  in0=attst[64:128, q0:q0 + 256],
                                  in1=fsc[64:128, 512:768])
                v.drain()
                st["avread"] = inc("V", mu)
                st["avfree"] = st["avread"]
                pump_side(1)
            W("Y", st["avread"], ("YB", cnt["YB"]))
            dd = y.dma_start(out=asp[hp], in_=attst[:])
            prev_tail = inc("YP", dd, 16)
        return prev_tail

    def emit_oproj(L, w_dram, dep_att):
        last = None
        for half in range(2):
            W("G", ck("T"))
            ck_w = wload("GA", w_dram, 0, D, half * 512, 512, warena, 0)
            for sl in range(T // 256):
                W("Y", dep_att, ck("T") if (half or sl) else None)
                dd = y.dma_start(
                    out=attb[:].rearrange("p (c n) -> p c n", n=256),
                    in_=asp[:, :, sl * 256:sl * 256 + 256].rearrange(
                        "c p n -> p c n"))
                abl = [inc("YB", dd, 16)]
                for m2 in range(4):
                    mc = half * 4 + m2
                    pb = 5 + ((sl * 4 + m2) % 2)
                    mm = None
                    for c in range(DC):
                        W("T", ck_w, *abl, last)
                        mm = t.matmul(
                            ps[:, pb * 512:pb * 512 + 256],
                            warena[:, c * 512 + m2 * 128:c * 512 + m2 * 128 + 128],
                            attb[:, c * 256:(c + 1) * 256],
                            start=(c == 0), stop=(c == DC - 1))
                    ck_t = inc("T", mm)
                    xsl = x_sb[:, mc * T + sl * 256:mc * T + sl * 256 + 256]
                    W("V", ck_t)
                    if L == 0:
                        ad = v.tensor_add(out=xsl, in0=xsl,
                                          in1=ps[:, pb * 512:pb * 512 + 256])
                    else:
                        ad = v.scalar_tensor_tensor(
                            out=xsl, in0=ps[:, pb * 512:pb * 512 + 256],
                            scalar=bpj_sb[:, mc:mc + 1], in1=xsl,
                            op0=OP.add, op1=OP.add)
                    v.drain()
                    last = inc("V", ad)
                    pump_side(1)
        return last

    cck = [ccin[L][0:KE].rearrange("(h p n) -> h p n", h=HP, p=128)
           for L in (0, 1)]
    ccv = [ccin[L][KE:KE + VE].rearrange("(c p n) -> c p n", c=T // 128, p=128)
           for L in (0, 1)]

    ck_xn = emit_layernorm(ck_x0)
    side_gens.append(wquant_gen(w1_in, D, H1, 0, t1d))
    side_gens.append(wquant_gen(w2_in, H1, D, 1, t2d))
    emit_proj_w(wk_in, 0, ck_xn, lambda a, b: cck[0][a][:, b * 512:(b + 1) * 512])
    emit_proj_v(wv_in, 0, ck_xn, lambda a, b: ccv[0][a][:, b * 512:(b + 1) * 512])
    W("G", ("YS0", cnt["YS0"]), ("YS1", cnt["YS1"]))
    ck_ag0 = inc("CC", g.collective_compute(
        "AllGather", OP.bypass, replica_groups=RG, ins=[ccin[0]], outs=[ccout[0]]))
    emit_proj_w(wq_in, 0, ck_xn, lambda a, b: qsp[a][:, b * 512:(b + 1) * 512])

    fs = {}
    for tt in range(NT0):
        t0 = tt * TT0
        for jh in range(JC0):
            if tt == 0 and jh == 0:
                fs["wg"] = wload("GF", wg_in, 0, D, 0, 128, wff_w, 0)
            mm = None
            for dc in range(DC):
                W("T", fs["wg"], ck_xn, fs.get("h"), fs.get("si"),
                  fs.get("hused"))
                mm = t.matmul(ps[:, 0:512],
                              wff_w[:, dc * 128:dc * 128 + 128],
                              xn_sb[:, dc * T + t0:dc * T + t0 + 512],
                              start=(dc == 0), stop=(dc == DC - 1))
            ck_g = inc("T", mm)
            if tt == 0 and jh == 0:
                fs["wu"] = wload("GU", wu_in, 0, D, 0, 128, wff_w, 1024)
            for dc in range(DC):
                W("T", fs["wu"])
                mm = t.matmul(ps[:, 512:1024],
                              wff_w[:, 1024 + dc * 128:1024 + dc * 128 + 128],
                              xn_sb[:, dc * T + t0:dc * T + t0 + 512],
                              start=(dc == 0), stop=(dc == DC - 1))
            ck_gu = inc("T", mm)
            pump_side(1)
            nj, ntt = (jh + 1, tt) if jh + 1 < JC0 else (0, tt + 1)
            if ntt < NT0:
                W("G", ck_g)
                fs["wg"] = wload("GF", wg_in, 0, D, nj * 128, 128, wff_w, 0)
                W("G", ck_gu)
                fs["wu"] = wload("GU", wu_in, 0, D, nj * 128, 128, wff_w, 1024)
            W("S", ck_gu, fs.get("h"))
            si = s.activation(out=fsc[:, 0:512], in_=ps[:, 0:512],
                              func=AF.Silu, scale=C127)
            ck_si = inc("S", si)
            fs["si"] = ck_si
            W("V", ck_si)
            hh = v.tensor_mul(out=hpool[:, jh * 512:(jh + 1) * 512],
                              in0=fsc[:, 0:512], in1=ps[:, 512:1024])
            v.drain()
            fs["h"] = inc("V", hh)
        ck_h = fs["h"]
        for mh in range(2):
            mm = None
            for jc in range(JC0):
                so = (jc % 2) * 512
                if tt == 0 and mh == 0 and jc == 0:
                    W("G", ("YS0", cnt["YS0"]), ("YS1", cnt["YS1"]))
                    fs["wd0"] = wload("GD0", wd_in, 0, 128, 0, 512, kvst, 0)
                ck_wd = fs["wd0" if jc % 2 == 0 else "wd1"]
                for m2 in range(4):
                    W("T", ck_wd, ck_h, fs.get("xadd"))
                    mm = t.matmul(
                        ps[:, m2 * 512:m2 * 512 + 512],
                        kvst[:, so + m2 * 128:so + m2 * 128 + 128],
                        hpool[:, jc * TT0:jc * TT0 + 512],
                        start=(jc == 0), stop=(jc == JC0 - 1))
                inc("T", mm)
                nj, nmh = (jc + 1, mh) if jc + 1 < JC0 else (0, mh + 1)
                ntt2 = tt if nmh < 2 else tt + 1
                if nmh == 2:
                    nmh = 0
                if ntt2 < NT0:
                    W("G", ck("T"))
                    fs["wd0" if nj % 2 == 0 else "wd1"] = wload(
                        "GD0" if nj % 2 == 0 else "GD1", wd_in, nj * 128, 128,
                        nmh * 512, 512, kvst, (nj % 2) * 512)
            ck_dn = ck("T")
            ad = None
            for m2 in range(4):
                mc = mh * 4 + m2
                xsl = x_sb[:, mc * T + t0:mc * T + t0 + 512]
                W("V", ck_dn)
                ad = v.scalar_tensor_tensor(
                    out=xsl, in0=ps[:, m2 * 512:m2 * 512 + 512],
                    scalar=CD127, in1=xsl, op0=OP.mult, op1=OP.add)
                v.drain()
            fs["xadd"] = inc("V", ad)
        fs["hused"] = fs["xadd"]

    ck_attn0 = emit_attention(0, ck_ag0)
    ck_x1 = emit_oproj(0, wo_in, ck_attn0)

    W("Y", ck_x1)
    x4d = x_sb[:].rearrange("p (c j n) -> p c j n", c=DC, j=NB)
    hai3 = hai.rearrange("(c p) n -> p c n", p=128)
    for u in range(2):
        for dcq in range(DC):
            hx = y.dma_start(
                out=hai3[:, dcq, u * NB:(u + 1) * NB],
                in_=x4d[:, dcq, :, 255 * u],
                allow_slow_non_contiguous=True)
            inc("YH", hx, 16)
    ck_hai = ck("YH")
    W("G", ck_hai)
    ck_ag1a = inc("CC", g.collective_compute(
        "AllGather", OP.bypass, replica_groups=RG, ins=[hai], outs=[hao]))

    ck_xn = emit_layernorm(ck_x1)
    emit_proj_w(wqkv_in, 1024, ck_xn,
                lambda a, b: cck[1][a][:, b * 512:(b + 1) * 512])
    emit_proj_v(wqkv_in, 2048, ck_xn,
                lambda a, b: ccv[1][a][:, b * 512:(b + 1) * 512])
    W("G", ("YS0", cnt["YS0"]), ("YS1", cnt["YS1"]))
    ck_ag1 = inc("CC", g.collective_compute(
        "AllGather", OP.bypass, replica_groups=RG, ins=[ccin[1]], outs=[ccout[1]]))
    emit_proj_w(wqkv_in, 0, ck_xn, lambda a, b: qsp[a][:, b * 512:(b + 1) * 512])

    W("Y", ck_ag1a)
    c0 = y.dma_start(
        out=hcand[:, 0:DC * NB].rearrange("p (c j) -> p c j", c=DC)[:, :, 1:NB],
        in_=hao[1].rearrange("(c p) n -> p c n", p=128)[:, :, NB:2 * NB - 1],
        allow_slow_non_contiguous=True)
    inc("YH", c0, 16)
    c1 = y.dma_start(
        out=hcand[:, DC * NB:2 * DC * NB].rearrange("p (c j) -> p c j", c=DC),
        in_=hao[0].rearrange("(c p) n -> p c n", p=128)[:, :, NB:2 * NB],
        allow_slow_non_contiguous=True)
    ck_hc = inc("YH", c1, 16)
    W("V", ck_hc, ck_consts)
    v.memset(hcand[:, 0:DC * NB].rearrange("p (c j) -> p c j", c=DC)[:, :, 0:1], 0.0)
    v.drain()
    hx3 = hx_sb[:].rearrange("p (c n) -> p c n", n=HN)
    hc3a = hcand[:, 0:DC * NB].rearrange("p (c j) -> p c j", c=DC)
    hc3b = hcand[:, DC * NB:2 * DC * NB].rearrange("p (c j) -> p c j", c=DC)
    v.tensor_scalar(out=hx3[:, :, 0:NB], in0=hc3a,
                    scalar1=hsel_sb[:, 0:1], scalar2=None, op0=OP.mult)
    v.drain()
    hl = v.scalar_tensor_tensor(out=hx3[:, :, 0:NB], in0=hc3b,
                                scalar=hsel_sb[:, 1:2], in1=hx3[:, :, 0:NB],
                                op0=OP.mult, op1=OP.add)
    v.drain()
    ck_hl = inc("V", hl)
    W("Y", ck_hl)
    c0 = y.dma_start(
        out=hcand[:, 0:DC * NB].rearrange("p (c j) -> p c j", c=DC),
        in_=hao[1].rearrange("(c p) n -> p c n", p=128)[:, :, 0:NB],
        allow_slow_non_contiguous=True)
    inc("YH", c0, 16)
    c1 = y.dma_start(
        out=hcand[:, DC * NB:2 * DC * NB].rearrange(
            "p (c j) -> p c j", c=DC)[:, :, 0:NB - 1],
        in_=hao[0].rearrange("(c p) n -> p c n", p=128)[:, :, 1:NB],
        allow_slow_non_contiguous=True)
    ck_hc2 = inc("YH", c1, 16)
    W("V", ck_hc2)
    v.memset(hcand[:, DC * NB:2 * DC * NB].rearrange(
        "p (c j) -> p c j", c=DC)[:, :, NB - 1:NB], 0.0)
    v.drain()
    v.tensor_scalar(out=hx3[:, :, NB:HN], in0=hc3a,
                    scalar1=hsel_sb[:, 0:1], scalar2=None, op0=OP.mult)
    v.drain()
    hr = v.scalar_tensor_tensor(out=hx3[:, :, NB:HN], in0=hc3b,
                                scalar=hsel_sb[:, 1:2],
                                in1=hx3[:, :, NB:HN],
                                op0=OP.mult, op1=OP.add)
    v.drain()
    ck_hx = inc("V", hr)

    def hx_ap(tens, dc):
        return tens[:, dc * HN:(dc + 1) * HN]

    for side in range(2):
        o = side * NB
        hxs = [(hx_ap(hx_sb, dc)[:, o:o + NB], hx_ap(hcand, dc)[:, o:o + NB])
               for dc in range(DC)]
        ck_hxn = ln_slice(hxs, NB, ck_hx)
    am = None
    for dc in range(DC):
        W("V", ck_hxn)
        if dc == 0:
            v.tensor_copy(out=run_sb[:, 0:HN], in_=hx_ap(hcand, 0))
            v.drain()
            am = v.tensor_copy(out=s2mb[:, 0:HN], in_=hx_ap(hcand, 0))
        else:
            v.tensor_tensor(out=run_sb[:, 0:HN], in0=run_sb[:, 0:HN],
                            in1=hx_ap(hcand, dc), op=OP.max)
            v.drain()
            am = v.tensor_tensor(out=s2mb[:, 0:HN], in0=s2mb[:, 0:HN],
                                 in1=hx_ap(hcand, dc), op=OP.min)
        v.drain()
    v.tensor_scalar(out=s2mb[:, 0:HN], in0=s2mb[:, 0:HN], scalar1=-1.0,
                    scalar2=None, op0=OP.mult)
    v.drain()
    am = v.tensor_tensor(out=run_sb[:, 0:HN], in0=run_sb[:, 0:HN],
                         in1=s2mb[:, 0:HN], op=OP.max)
    v.drain()
    inc("V", am)
    W("V", ck("V"))
    zz = v.memset(run_sb[:, HN:128], 0.0)
    v.drain()
    ck_hfold = inc("V", zz)
    ck_trb = token_scales(run_sb[:, 0:128], thr_sb[:, 4:5], ck_hfold)
    W("V", ck_trb)
    v.tensor_copy(out=ROW(R_HALO, 64, HN), in_=pstr[0:1, 0:HN])
    v.drain()
    hsm = v.tensor_copy(out=ROW(R_HALO, 96, HN),
                        in_=pstr[32:33, 0:HN])
    v.drain()
    ck_hsm = inc("V", hsm)
    ck_hqb = bcast_row(R_HALO, 64, HN, bb1, 0, 5, deps=(ck_hsm,))
    mu = None
    for dc in range(DC):
        W("V", ck_hqb)
        v.tensor_mul(out=i32a[:, dc * HN:(dc + 1) * HN], in0=hx_ap(hcand, dc),
                     in1=bb1[:, 0:HN])
        v.drain()
    mu = v.tensor_copy(out=hq_sb[:, 0:DC * HN], in_=i32a[:, 0:DC * HN])
    v.drain()
    ck_hq = inc("V", mu)

    for sl in range(NSL):
        for dc in range(DC):
            W("V", ck_xn)
            xs_ = xn_sb[:, dc * T + sl * 512:dc * T + sl * 512 + 512]
            if dc == 0:
                v.tensor_copy(out=bb2[:, 0:512], in_=xs_)
                v.drain()
                am = v.tensor_copy(out=bb1[:, 0:512], in_=xs_)
            else:
                v.tensor_tensor(out=bb2[:, 0:512], in0=bb2[:, 0:512],
                                in1=xs_, op=OP.max)
                v.drain()
                am = v.tensor_tensor(out=bb1[:, 0:512], in0=bb1[:, 0:512],
                                     in1=xs_, op=OP.min)
            v.drain()
        v.tensor_scalar(out=bb1[:, 0:512], in0=bb1[:, 0:512], scalar1=-1.0,
                        scalar2=None, op0=OP.mult)
        v.drain()
        am = v.tensor_tensor(out=bb2[:, 0:512], in0=bb2[:, 0:512],
                             in1=bb1[:, 0:512], op=OP.max)
        v.drain()
        ck_fold = inc("V", am)
        for i in range(4):
            ck_trb = token_scales(bb2[:, i * 128:(i + 1) * 128],
                                  thr_sb[:, 4:5], ck_fold)
            W("V", ck_trb)
            v.tensor_copy(out=ROW(R_S1 + sl, i * 128, 128),
                          in_=pstr[0:1, 0:128])
            v.drain()
            cp = v.tensor_copy(out=ROW(R_S1M + sl, i * 128, 128),
                               in_=pstr[32:33, 0:128])
            v.drain()
            inc("V", cp)
    ck_s1 = ck("V")
    drain_side()

    def ffn1_gen():
      l1 = {}
      for tt in range(NB):
        tb = tt * 256
        sl_, off_ = tb // 512, tb % 512
        ck_s1b = bcast_row(R_S1 + sl_, off_, 256, bb1, 0, "P", deps=(ck_s1,))
        ck_s1mb = bcast_row(R_S1M + sl_, off_, 256, s2mb, 0, "P",
                            deps=(ck_s1,))
        xq = None
        for dc in range(DC):
            W("V", ck_s1b, ck_s1mb, ck_hq, l1.get("w1done"))
            v.tensor_mul(out=i32a[:, 0:256],
                         in0=xn_sb[:, dc * T + tb:dc * T + tb + 256],
                         in1=bb1[:, 0:256])
            v.drain()
            v.tensor_mul(out=xq1_sb[:, dc * 258 + 1:dc * 258 + 257],
                         in0=i32a[:, 0:256], in1=s2mb[:])
            v.drain()
        x3 = xq1_sb[:].rearrange("p (c n) -> p c n", n=258)
        h3 = hq_sb[:].rearrange("p (c n) -> p c n", n=HN)
        v.tensor_copy(out=x3[:, :, 0:1], in_=h3[:, :, tt:tt + 1])
        v.drain()
        xq = v.tensor_copy(out=x3[:, :, 257:258], in_=h3[:, :, NB + tt:NB + tt + 1])
        v.drain()
        ck_xq1 = inc("V", xq)
        yield
        w1g, t1l = {}, {}

        def t1load(jc, dep):
            tslot = (jc % 2) * DC * 128
            W("Y", ("YT0", cnt["YT0"]), ("YT1", cnt["YT1"]), dep)
            tl = y.dma_start(
                out=t1buf[:, tslot:tslot + DC * 128].rearrange(
                    "p (c n) -> p c n", c=DC),
                in_=t1d[:, :, jc * 128:(jc + 1) * 128])
            return inc("Y1a" if jc % 2 == 0 else "Y1b", tl, 16)

        t1l[0] = t1load(0, ck("T"))
        for jc in range(JC1):
            tslot = (jc % 2) * DC * 128
            if jc + 1 < JC1:
                t1l[jc + 1] = t1load(
                    jc + 1, w1g.get(jc - 1, ck("T") if jc < 1 else None))
            pb = 6
            mm = None
            for dc in range(DC):
                W("T", t1l[jc], ck_xq1, l1.get(("hev", jc - 1)))
                mm = t.matmul(ps[:, pb * 512:pb * 512 + 258],
                              t1buf[:, tslot + dc * 128:tslot + dc * 128 + 128],
                              xq1_sb[:, dc * 258:(dc + 1) * 258],
                              start=(dc == 0), stop=(dc == DC - 1))
            ck_w1 = inc("T", mm)
            w1g[jc] = ck_w1
            pso = pb * 512
            W("V", ck_w1)
            v.tensor_scalar(out=fsc[:, 0:256], in0=ps[:, pso:pso + 256],
                            scalar1=cvw_sb[:, jc * 3:jc * 3 + 1], scalar2=None,
                            op0=OP.mult)
            v.drain()
            v.scalar_tensor_tensor(out=fsc[:, 0:256], in0=ps[:, pso + 1:pso + 257],
                                   scalar=cvw_sb[:, jc * 3 + 1:jc * 3 + 2],
                                   in1=fsc[:, 0:256], op0=OP.mult, op1=OP.add)
            v.drain()
            cv3 = v.scalar_tensor_tensor(out=fsc[:, 0:256],
                                         in0=ps[:, pso + 2:pso + 258],
                                         scalar=cvw_sb[:, jc * 3 + 2:jc * 3 + 3],
                                         in1=fsc[:, 0:256], op0=OP.mult, op1=OP.add)
            v.drain()
            ck_cv = inc("V", cv3)
            l1[("hev", jc)] = ck_cv
            W("S", ck_cv)
            si = s.activation(out=hpool[:, jc * 256:(jc + 1) * 256],
                              in_=fsc[:, 0:256], func=AF.Silu,
                              bias=cvb_sb[:, jc:jc + 1])
            ck_si = inc("S", si)
            W("V", ck_si)
            if jc == 0:
                v.tensor_copy(out=run_sb[:, 0:256], in_=hpool[:, 0:256])
                v.drain()
                am = v.tensor_copy(out=s2mb[:], in_=hpool[:, 0:256])
            else:
                v.tensor_tensor(out=run_sb[:, 0:256], in0=run_sb[:, 0:256],
                                in1=hpool[:, jc * 256:(jc + 1) * 256], op=OP.max)
                v.drain()
                am = v.tensor_tensor(out=s2mb[:], in0=s2mb[:],
                                     in1=hpool[:, jc * 256:(jc + 1) * 256],
                                     op=OP.min)
            v.drain()
            inc("V", am)
            yield
        W("V", ck("V"))
        v.tensor_scalar(out=s2mb[:], in0=s2mb[:], scalar1=-1.0,
                        scalar2=None, op0=OP.mult)
        v.drain()
        am = v.tensor_tensor(out=run_sb[:], in0=run_sb[:], in1=s2mb[:],
                             op=OP.max)
        v.drain()
        inc("V", am)
        l1["w1done"] = ck("V")
        ck_run = ck("V")
        for half in range(2):
            ck_trb = token_scales(run_sb[:, half * 128:(half + 1) * 128]
                                  if half else run_sb[:, 0:128],
                                  thr_sb[:, 5:6], ck_run)
            W("V", ck_trb)
            v.tensor_copy(out=ROW(R_S2, half * 128, 128),
                          in_=pstr[0:1, 0:128])
            v.drain()
            cp = v.tensor_copy(out=ROW(R_S2M, half * 128, 128),
                               in_=pstr[32:33, 0:128])
            v.drain()
            inc("V", cp)
            yield
        ck_s2 = ck("V")
        ck_s2b = bcast_row(R_S2, 0, 256, run_sb, 0, "P", deps=(ck_s2,))
        ck_s2mb = bcast_row(R_S2M, 0, 256, s2mb, 0, "P", deps=(ck_s2,))
        xq2 = None
        for jc in range(JC1):
            W("V", ck_s2b, ck_s2mb)
            v.tensor_mul(out=i32a[:, 0:256], in0=hpool[:, jc * 256:(jc + 1) * 256],
                         in1=run_sb[:])
            v.drain()
            xq2 = v.tensor_mul(out=hpool[:, jc * 256:(jc + 1) * 256],
                               in0=i32a[:, 0:256], in1=s2mb[:])
            v.drain()
            if jc % 8 == 7 and jc < JC1 - 1:
                yield
        ck_xq2 = inc("V", xq2)
        pb = 6
        qmm, t2l = {}, {}

        def t2load(qi, dep):
            mc, q = qi // 4, qi % 4
            so = (qi % 2) * 1024
            W("Y", ("YT0", cnt["YT0"]), ("YT1", cnt["YT1"]), dep)
            tl = y.dma_start(
                out=t2buf[:, so:so + 1024].rearrange("p (c n) -> p c n", c=8),
                in_=t2d[q * 8:(q + 1) * 8, :, mc * 128:(mc + 1) * 128])
            return inc("Y2" if qi % 2 == 0 else "Y2b", tl, 16)

        t2l[0] = t2load(0, ck("T"))
        for qi in range(4 * DC):
            mc, q = qi // 4, qi % 4
            so = (qi % 2) * 1024
            if qi + 1 < 4 * DC:
                t2l[qi + 1] = t2load(
                    qi + 1, qmm.get(qi - 1, ck("T") if qi < 1 else None))
            mm = None
            for jc2 in range(8):
                jc = q * 8 + jc2
                W("T", t2l[qi], ck_xq2, l1.get(("xadd", mc - 1)))
                mm = t.matmul(ps[:, pb * 512:pb * 512 + 256],
                              t2buf[:, so + jc2 * 128:so + jc2 * 128 + 128],
                              hpool[:, jc * 256:(jc + 1) * 256],
                              start=(jc == 0), stop=(jc == JC1 - 1))
            qmm[qi] = inc("T", mm)
            if q % 2 == 1:
                yield
            if q == 3:
                xsl = x_sb[:, mc * T + tb:mc * T + tb + 256]
                W("V", qmm[qi])
                ad = v.tensor_add(out=xsl, in0=xsl,
                                  in1=ps[:, pb * 512:pb * 512 + 256])
                v.drain()
                l1[("xadd", mc)] = inc("V", ad)
        l1["w1done"] = ck("V")

    side_gens.append(ffn1_gen())

    ck_attn1 = emit_attention(1, ck_ag1)
    drain_side()
    ck_x2 = emit_oproj(1, wpj_in, ck_attn1)

    W("Y", ck_x2)
    for dc in range(DC):
        inc("YO", y.dma_start(out=out_d[dc * 128:(dc + 1) * 128, :],
                              in_=x_sb[:, dc * T:(dc + 1) * T]), 16)
    y.wait_ge(sems["YO"], cnt["YO"])

    blk.tensor(_replay(tapes["T"]))
    blk.vector(_replay(tapes["V"]))
    blk.scalar(_replay(tapes["S"]))
    blk.sync(_replay(tapes["Y"]))
    blk.gpsimd(_replay(tapes["G"]))

    es.close()
    return nc


def _masks(par):
    kk = np.arange(128)[:, None]
    qq = np.arange(256)[None, :]
    tri0 = (kk <= qq).astype(np.float32)
    tri1 = (kk + 128 <= qq).astype(np.float32)
    one = np.ones((128, 256), np.float32)
    zero = np.zeros((128, 256), np.float32)
    ms = [tri0, tri1, zero, zero] if par == 0 else [one, one, tri0, tri1]
    return np.concatenate(ms, axis=1)


def _stage(inputs, S):
    x = np.asarray(inputs["L0_attn_residual"])
    B = x.shape[0]
    NBG = S // 256
    C = lambda a: np.ascontiguousarray(a)

    wq = C(np.asarray(inputs["L0_attn_W_Q"]).reshape(D, D).T)
    wk = C(np.asarray(inputs["L0_attn_W_K"]).reshape(D, D).T)
    wv = C(np.asarray(inputs["L0_attn_W_V"]).reshape(D, D).T)
    wo = C(np.asarray(inputs["L0_attn_W_O"]).reshape(D, D).T)
    wg = C(np.asarray(inputs["L0_ffn_w_gate"]).T.astype(np.int8))
    wu = C(np.asarray(inputs["L0_ffn_w_up"]).T.astype(np.int8))
    wd = C(np.asarray(inputs["L0_ffn_w_down"]).T.astype(np.int8))
    wqkv = C(np.asarray(inputs["L1_attn_w_qkv"]).T)
    wpj = C(np.asarray(inputs["L1_attn_w_proj"]).T)
    bpj = C(np.asarray(inputs["L1_attn_b_proj"]).reshape(D, 1))
    w1 = C(np.asarray(inputs["L1_ffn_w1"]).T)
    w2 = C(np.asarray(inputs["L1_ffn_w2"]).T)
    cw = C(np.asarray(inputs["L1_ffn_conv_w"]).reshape(H1, 3))
    cb = C(np.asarray(inputs["L1_ffn_conv_b"]).reshape(H1, 1))
    ident = np.eye(128, dtype=np.float32)

    in_maps = []
    for core in range(2 * B):
        b, par = core // 2, core % 2
        blocks = x[b].reshape(NBG, 256, D)[par::2]
        xT = C(blocks.reshape(S // 2, D).T)
        hsel = np.zeros((128, 2), np.float32)
        hsel[:, 0 if par == 0 else 1] = 1.0
        in_maps.append({
            "xT": xT, "masks": _masks(par), "ident": ident, "hsel": hsel,
            "wqT": wq, "wkT": wk, "wvT": wv, "woT": wo,
            "wgT": wg, "wuT": wu, "wdT": wd,
            "wqkvT": wqkv, "wprojT": wpj, "bproj": bpj,
            "w1T": w1, "w2T": w2, "convw": cw, "convb": cb,
        })
    return in_maps


def kernel(**inputs):
    x = np.asarray(inputs["L0_attn_residual"])
    B, S, _ = x.shape
    nc = build_nc(S)
    in_maps = _stage(inputs, S)
    res = run_bass_kernel_spmd(nc, in_maps, core_ids=list(range(2 * B)))
    NBG = S // 256
    out = np.empty((B, S, D), np.float32)
    for core in range(2 * B):
        b, par = core // 2, core % 2
        oT = np.asarray(res.results[core]["outT"])
        out[b].reshape(NBG, 256, D)[par::2] = oT.T.reshape(NBG // 2, 256, D)
    return out


# version 14
# speedup vs baseline: 1.4822x; 1.0161x over previous
import numpy as np
import ml_dtypes
from contextlib import ExitStack

import concourse.bass as bass
import concourse.mybir as mybir
from concourse.bass_utils import run_bass_kernel_spmd
from concourse.alu_op_type import AluOpType as OP

F32 = mybir.dt.float32
BF16 = mybir.dt.bfloat16
I32 = mybir.dt.int32
I8 = mybir.dt.int8
AF = mybir.ActivationFunctionType
AX = mybir.AxisListType

D = 1024
DC = 8
HP = 8
H0 = 3072
JC0 = H0 // 128
H1 = 4096
JC1 = H1 // 128
C127 = float(np.float32(1.0 / 127.0))
CD127 = float(np.float32(1.0 / (127.0 * 127.0)))
EXP_SHIFT = -20.0
RG = [[0, 1], [2, 3], [4, 5], [6, 7]]

R_M, R_R, R_VAR, R_SD = 0, 1, 2, 3
R_REC, R_RECO = 0, 1
R_S2, R_S2M = 2, 3
R_S1, R_S1M = 4, 8
R_WM = 2
R_HALO = 1
R_SC = 0


def build_nc(S):
    T = S // 2
    NB = T // 256
    NSL = T // 512
    TT0 = 512
    NT0 = T // TT0
    HN = 2 * NB
    assert HN <= 32

    nc = bass.Bass("TRN2", target_bir_lowering=False)
    DT = nc.dram_tensor

    xT_in = DT("xT", [D, T], F32, kind="ExternalInput").ap()
    masks_in = DT("masks", [128, 1024], F32, kind="ExternalInput").ap()
    ident_in = DT("ident", [128, 128], F32, kind="ExternalInput").ap()
    hsel_in = DT("hsel", [128, 2], F32, kind="ExternalInput").ap()
    wq_in = DT("wqT", [D, D], F32, kind="ExternalInput").ap()
    wk_in = DT("wkT", [D, D], F32, kind="ExternalInput").ap()
    wv_in = DT("wvT", [D, D], F32, kind="ExternalInput").ap()
    wo_in = DT("woT", [D, D], F32, kind="ExternalInput").ap()
    wg_in = DT("wgT", [D, H0], I8, kind="ExternalInput").ap()
    wu_in = DT("wuT", [D, H0], I8, kind="ExternalInput").ap()
    wd_in = DT("wdT", [H0, D], I8, kind="ExternalInput").ap()
    wqkv_in = DT("wqkvT", [D, 3 * D], F32, kind="ExternalInput").ap()
    wpj_in = DT("wprojT", [D, D], F32, kind="ExternalInput").ap()
    bpj_in = DT("bproj", [D, 1], F32, kind="ExternalInput").ap()
    w1_in = DT("w1T", [D, H1], F32, kind="ExternalInput").ap()
    w2_in = DT("w2T", [H1, D], F32, kind="ExternalInput").ap()
    cw_in = DT("convw", [H1, 3], F32, kind="ExternalInput").ap()
    cb_in = DT("convb", [H1, 1], F32, kind="ExternalInput").ap()
    out_d = DT("outT", [D, T], F32, kind="ExternalOutput").ap()

    KE = HP * 128 * T
    VE = (T // 128) * 128 * D
    ccin = [DT(f"cc{L}i", [KE + VE], BF16, kind="Internal").ap() for L in (0, 1)]
    ccout = [DT(f"cc{L}o", [2, KE + VE], BF16, kind="Internal").ap() for L in (0, 1)]
    hai = DT("hai", [D, HN], F32, kind="Internal").ap()
    hao = DT("hao", [2, D, HN], F32, kind="Internal").ap()
    qsp = DT("qsp", [HP, 128, T], BF16, kind="Internal").ap()
    asp = DT("asp", [HP, 128, T], BF16, kind="Internal").ap()
    t1d = DT("t1d", [DC, 128, H1], BF16, kind="Internal").ap()
    t2d = DT("t2d", [JC1, 128, D], BF16, kind="Internal").ap()

    es = ExitStack()
    SB = lambda n, sh, d: es.enter_context(nc.sbuf_tensor(n, sh, d))

    x_sb = SB("x", [128, DC * T], F32)
    xn_sb = SB("xn", [128, DC * T], BF16)
    rows = SB("rows", [128, 2048], F32)
    ROW = lambda i, c0, w: rows[32 * (i % 3):32 * (i % 3) + 1,
                                (i // 3) * 512 + c0:(i // 3) * 512 + c0 + w]
    bb1 = SB("bb1", [128, 512], F32)
    bb2 = SB("bb2", [128, 512], F32)
    fsc = SB("fsc", [128, 1024], F32)
    warena = SB("warena", [128, DC * 512], BF16)
    wff_w = SB("wff_w", [128, 2 * 1024], BF16)
    kvst = SB("kvst", [128, 2 * 512], BF16)
    katt = SB("katt", [128, S], BF16)
    vatt = SB("vatt", [128, (S // 128) * 130], BF16)
    qatt = SB("qatt", [128, T], BF16)
    p_sb = SB("p_sb", [128, 2048], BF16)
    attst = SB("attst", [128, T], BF16)
    attb = SB("attb", [128, DC * 256], BF16)
    masks_sb = SB("masksb", [128, 1024], BF16)
    ident_sb = SB("identb", [128, 128], F32)
    hsel_sb = SB("hselb", [128, 2], F32)
    ones_r = SB("ones_r", [128, 128], F32)
    onesk = SB("onesk", [128, 1], F32)
    mxc = SB("mxc", [128, 64], F32)
    hpool = SB("hpool", [128, JC0 * TT0], BF16)
    xq1_sb = SB("xq1", [128, DC * 258], BF16)
    t1buf = SB("t1buf", [128, 2 * DC * 128], BF16)
    t2buf = SB("t2buf", [128, 16 * 128], BF16)
    wqf = SB("wqf", [128, 512], BF16)
    tq_sb = SB("tq", [128, 512], BF16)
    wabs = SB("wabs", [128, 4], F32)
    thr_sb = SB("thr", [128, 8], F32)
    i32a = SB("i32a", [128, 256], I32)
    s2mb = SB("s2mb", [128, 256], F32)
    run_sb = SB("run_sb", [128, 256], F32)
    cvw_sb = SB("cvw", [128, 3 * JC1], F32)
    cvb_sb = SB("cvb", [128, JC1], F32)
    bpj_sb = SB("bpjb", [128, DC], F32)
    cbias = SB("cbias", [128, 4], F32)
    hx_sb = SB("hx", [128, 2 * DC * NB], F32)
    hcand = SB("hcand", [128, 2 * DC * NB], F32)
    hq_sb = SB("hq", [128, 2 * DC * NB], BF16)

    ps = es.enter_context(nc.psum_tensor("ps_all", [128, 3584], F32))
    pstr = es.enter_context(nc.psum_tensor("ps_tr", [128, 512], F32))

    DMA_FAMS = ["YX", "YC", "YS0", "YS1", "YH", "YW", "YT0", "YT1", "YA",
                "YB", "YP", "YO", "Y1a", "Y1b", "Y2", "Y2b", "GM", "GA", "GF",
                "GU", "GD0", "GD1", "GW0", "GW1", "CC"]
    sems, cnt = {}, {k: 0 for k in (["T", "V", "S"] + DMA_FAMS)}
    for k in cnt:
        sems[k] = es.enter_context(nc.semaphore("sem" + k))
    last_wait = {}

    blk = es.enter_context(nc.Block())

    class _RecInstr:
        __slots__ = ("incs",)

        def __init__(self):
            self.incs = []

        def then_inc(self, sem, n):
            self.incs.append((sem, n))
            return self

    class _Rec:
        def __init__(self, tape):
            self._tape = tape

        def __getattr__(self, m):
            tape = self._tape

            def call(*a, **k):
                ri = _RecInstr()
                tape.append((m, a, k, ri))
                return ri
            return call

    tapes = {k: [] for k in "TVSYG"}
    ENG = {k: _Rec(tapes[k]) for k in "TVSYG"}
    t, v, s, y, g = (ENG[k] for k in "TVSYG")

    import os as _os
    _SIM = bool(_os.environ.get("BASS_SIM_FP64"))

    def _replay(tape):
        def f(eng):
            for m, a, k, ri in tape:
                instr = getattr(eng, m)(*a, **k)
                for sem, n in ri.incs:
                    instr.then_inc(sem, n)
        return f

    def inc(tag, instr, n=1):
        instr.then_inc(sems[tag], n)
        cnt[tag] += n
        return (tag, cnt[tag])

    def ck(tag):
        return (tag, cnt[tag])

    def W(consumer, *cks):
        lw = last_wait.setdefault(consumer, {})
        best = {}
        for c in cks:
            if c is None:
                continue
            tag, val = c
            if val > best.get(tag, 0):
                best[tag] = val
        for tag, val in best.items():
            if val <= 0 or lw.get(tag, 0) >= val:
                continue
            assert val <= cnt[tag], f"forward wait {tag}:{val} > {cnt[tag]}"
            ENG[consumer].wait_ge(sems[tag], val)
            lw[tag] = val

    for dc in range(DC):
        inc("YX", y.dma_start(out=x_sb[:, dc * T:(dc + 1) * T],
                              in_=xT_in[dc * 128:(dc + 1) * 128, :]), 16)
    ck_x0 = ck("YX")
    inc("YC", y.dma_start(out=ident_sb[:], in_=ident_in), 16)
    inc("YC", y.dma_start(out=hsel_sb[:], in_=hsel_in), 16)
    inc("YC", y.dma_start(
        out=cvw_sb[:].rearrange("p (c k) -> p c k", c=JC1),
        in_=cw_in.rearrange("(c p) k -> p c k", p=128),
        allow_slow_non_contiguous=True), 16)
    inc("YC", y.dma_start(
        out=cvb_sb[:].rearrange("p (c o) -> p c o", c=JC1),
        in_=cb_in.rearrange("(c p) o -> p c o", p=128),
        allow_slow_non_contiguous=True), 16)
    inc("YC", y.dma_start(
        out=bpj_sb[:].rearrange("p (c o) -> p c o", c=DC),
        in_=bpj_in.rearrange("(c p) o -> p c o", p=128),
        allow_slow_non_contiguous=True), 16)
    ck_consts = ck("YC")
    ck_masks = inc("GM", g.dma_start(out=masks_sb[:], in_=masks_in), 16)

    W("V", ck_consts)
    v.memset(ones_r[:], 1.0)
    v.memset(onesk[:], 1.0)
    v.memset(cbias[:, 0:1], 1e-5)
    v.memset(cbias[:, 1:2], 1.0)
    ck_ones = inc("V", v.memset(cbias[:, 2:3], EXP_SHIFT))

    def bcast_row(slot, c0, n, dst_sb, dst_off, pbank, deps=(), recip=False):
        b = 32 * (slot % 3)
        pap = (pstr[:, 128:128 + n] if pbank == "P"
               else ps[:, pbank * 512:pbank * 512 + n])
        W("T", ck_ones, ck("V"), *deps)
        mm = t.matmul(pap, ones_r[b:b + 1, 0:128],
                      ROW(slot, c0, n), start=True, stop=True)
        ckt = inc("T", mm)
        W("V", ckt)
        if recip:
            cp = v.reciprocal(out=dst_sb[:, dst_off:dst_off + n], in_=pap)
        else:
            cp = v.tensor_copy(out=dst_sb[:, dst_off:dst_off + n], in_=pap)
        v.drain()
        return inc("V", cp)

    def wload(fam, dram_ap, row0, rows_, col0, ncols, dst, dst_off):
        ins = g.dma_start(
            out=dst[:, dst_off:dst_off + (rows_ // 128) * ncols].rearrange(
                "p (c n) -> p c n", c=rows_ // 128),
            in_=dram_ap[row0:row0 + rows_, col0:col0 + ncols].rearrange(
                "(c p) n -> p c n", p=128))
        return inc(fam, ins, 16)

    def ln_slice(xs_aps, w_, dep):
        mm2 = None
        W("T", ck("V"))
        for dc in range(DC):
            xap = xs_aps[dc][0]
            off = (dc % 2) * 512
            W("V", dep, ck("T"))
            sq = v.tensor_mul(out=fsc[:, off:off + w_], in0=xap, in1=xap)
            v.drain()
            cksq = inc("V", sq)
            W("T", ck_ones, cksq)
            t.matmul(ps[0:1, 3 * 512:3 * 512 + w_], onesk[:], xap,
                     start=(dc == 0), stop=(dc == DC - 1))
            mm2 = t.matmul(ps[0:1, 4 * 512:4 * 512 + w_], onesk[:],
                           fsc[:, off:off + w_],
                           start=(dc == 0), stop=(dc == DC - 1))
            inc("T", mm2)
        ck_st = ck("T")
        W("V", ck_st)
        v.tensor_scalar(out=ROW(R_M, 0, w_),
                        in0=ps[0:1, 3 * 512:3 * 512 + w_],
                        scalar1=1.0 / D, scalar2=None, op0=OP.mult)
        v.drain()
        v.tensor_mul(out=ROW(R_VAR, 0, w_), in0=ROW(R_M, 0, w_),
                     in1=ROW(R_M, 0, w_))
        v.drain()
        va = v.scalar_tensor_tensor(out=ROW(R_VAR, 0, w_),
                                    in0=ps[0:1, 4 * 512:4 * 512 + w_],
                                    scalar=1.0 / D, in1=ROW(R_VAR, 0, w_),
                                    op0=OP.mult, op1=OP.subtract)
        v.drain()
        ck_var = inc("V", va)
        W("S", ck_var)
        sq2 = s.activation(out=ROW(R_SD, 0, w_),
                           in_=ROW(R_VAR, 0, w_), func=AF.Sqrt,
                           bias=cbias[0:1, 0:1])
        ck_sd = inc("S", sq2)
        ck_mb = bcast_row(R_M, 0, w_, bb1, 0, 5, deps=(ck_sd,))
        ck_rb = bcast_row(R_SD, 0, w_, bb2, 0, 6, deps=(ck_sd,), recip=True)
        mu = None
        for dc in range(DC):
            xap, oap = xs_aps[dc]
            W("V", ck_mb, ck_rb)
            v.tensor_sub(out=fsc[:, 0:w_], in0=xap, in1=bb1[:, 0:w_])
            v.drain()
            mu = v.tensor_mul(out=oap, in0=fsc[:, 0:w_], in1=bb2[:, 0:w_])
            v.drain()
        return inc("V", mu)

    def emit_layernorm(dep):
        outs = []
        for sl in range(NSL):
            c0 = sl * 512
            xs = [(x_sb[:, dc * T + c0:dc * T + c0 + 512],
                   xn_sb[:, dc * T + c0:dc * T + c0 + 512]) for dc in range(DC)]
            outs.append(ln_slice(xs, 512, dep))
        return outs[-1]

    proj_state = {"dma": [], "slot": 0, "evac": []}

    def proj_evac(ck_t, pb, dst_ap):
        sl = proj_state["slot"]
        so = (sl % 2) * 512
        W("S", ck_t)
        if sl >= 2:
            W("S", proj_state["dma"][sl - 2])
        cp = s.copy(out=kvst[:, so:so + 512],
                    in_=ps[:, pb * 512:pb * 512 + 512])
        s.drain()
        ck_v = inc("S", cp)
        proj_state["evac"].append(ck_v)
        W("Y", ck_v, ("YA", cnt["YA"]))
        dd = y.dma_start(out=dst_ap, in_=kvst[:, so:so + 512])
        proj_state["dma"].append(inc("YS0" if sl % 2 == 0 else "YS1", dd, 16))
        proj_state["slot"] += 1
        pump_side(2)

    def emit_proj_w(w_dram, wcol0, dep_xn, dst_fn):
        for half in range(2):
            W("G", ck("T"))
            ck_w = wload("GA", w_dram, 0, D, wcol0 + half * 512, 512, warena, 0)
            for a2 in range(4):
                a = half * 4 + a2
                for b in range(NSL):
                    pb = 5 + (proj_state["slot"] % 2)
                    mm = None
                    ev = proj_state["evac"]
                    for dc in range(DC):
                        W("T", ck_w, dep_xn,
                          ev[-2] if len(ev) >= 2 else None)
                        mm = t.matmul(
                            ps[:, pb * 512:pb * 512 + 512],
                            warena[:, dc * 512 + a2 * 128:dc * 512 + a2 * 128 + 128],
                            xn_sb[:, dc * T + b * 512:dc * T + b * 512 + 512],
                            start=(dc == 0), stop=(dc == DC - 1))
                    proj_evac(inc("T", mm), pb, dst_fn(a, b))

    def emit_proj_v(w_dram, wcol0, dep_xn, dst_fn):
        for b in range(2):
            W("G", ck("T"))
            ck_w = wload("GA", w_dram, 0, D, wcol0 + b * 512, 512, warena, 0)
            for a in range(T // 128):
                pb = 5 + (proj_state["slot"] % 2)
                mm = None
                ev = proj_state["evac"]
                for dc in range(DC):
                    W("T", ck_w, dep_xn,
                      ev[-2] if len(ev) >= 2 else None)
                    mm = t.matmul(
                        ps[:, pb * 512:pb * 512 + 512],
                        xn_sb[:, dc * T + a * 128:dc * T + a * 128 + 128],
                        warena[:, dc * 512:dc * 512 + 512],
                        start=(dc == 0), stop=(dc == DC - 1))
                proj_evac(inc("T", mm), pb, dst_fn(a, b))

    def token_scales(fold_ap, mcol_ap, dep):
        W("T", dep, ck_consts, ck("V"))
        tr = t.transpose(pstr[:, 0:128], fold_ap, ident_sb[:])
        ck_tr = inc("T", tr)
        W("V", ck_tr)
        v.reduce_max(out=mxc[:, 34:35], in_=pstr[:, 0:128], axis=AX.X)
        v.drain()
        v.tensor_scalar(out=mxc[:, 35:36], in0=mxc[:, 34:35], scalar1=1e-5,
                        scalar2=None, op0=OP.max)
        v.drain()
        v.reciprocal(out=mxc[:, 36:37], in_=mxc[:, 35:36])
        v.drain()
        v.tensor_scalar(out=mxc[:, 0:1], in0=mxc[:, 36:37], scalar1=127.0,
                        scalar2=None, op0=OP.mult)
        v.drain()
        sm = v.tensor_scalar(out=mxc[:, 32:33], in0=mxc[:, 35:36], scalar1=mcol_ap,
                             scalar2=C127, op0=OP.mult, op1=OP.mult)
        v.drain()
        ck_mx = inc("V", sm)
        W("T", ck_mx)
        tr2 = t.transpose(pstr[0:64, 0:128], mxc[:, :], ident_sb[:])
        return inc("T", tr2)

    def wquant_gen(dram_ap, rows_, ncol, col, dst_dram):
        pieces = [(c, p0) for c in range(rows_ // 128)
                  for p0 in range(0, ncol, 256)]
        vdone, lds = {}, {}

        def issue(i, dep0):
            c, p0 = pieces[i]
            sl = i % 2
            W("G", vdone.get(i - 2, dep0 if i < 2 else None))
            return inc("GW0" if sl == 0 else "GW1", g.dma_start(
                out=wqf[:, sl * 256:sl * 256 + 256],
                in_=dram_ap[c * 128:(c + 1) * 128, p0:p0 + 256]), 16)

        dep0 = ("V", cnt["V"])
        lds[0] = issue(0, dep0)
        first = True
        ck_acc = None
        for i in range(len(pieces)):
            if i + 1 < len(pieces):
                lds[i + 1] = issue(i + 1, dep0)
            sl = i % 2
            W("V", lds[i])
            v.reduce_sum(out=wabs[:, 2:3], in_=wqf[:, sl * 256:sl * 256 + 256],
                         axis=AX.X, apply_absolute_value=True)
            v.drain()
            if first:
                cp = v.tensor_copy(out=wabs[:, 0:1], in_=wabs[:, 2:3])
                first = False
            else:
                cp = v.tensor_add(out=wabs[:, 0:1], in0=wabs[:, 0:1],
                                  in1=wabs[:, 2:3])
            v.drain()
            ck_acc = inc("V", cp)
            vdone[i] = ck_acc
            yield
        W("T", ck_acc, ck_ones)
        mm = t.matmul(pstr[0:1, 0:1], onesk[:], wabs[:, 0:1],
                      start=True, stop=True)
        ck_t = inc("T", mm)
        W("V", ck_t)
        v.tensor_scalar(out=ROW(R_WM, col, 1), in0=pstr[0:1, 0:1],
                        scalar1=1.0 / (rows_ * ncol), scalar2=None, op0=OP.mult)
        v.drain()
        th = v.tensor_scalar(out=ROW(R_WM, col + 2, 1), in0=ROW(R_WM, col, 1),
                             scalar1=0.5, scalar2=None, op0=OP.mult)
        ck_m = inc("V", th)
        W("T", ck_m)
        wb = 32 * (R_WM % 3)
        mm = t.matmul(pstr[:, 0:1], ones_r[wb:wb + 1, 0:128],
                      ROW(R_WM, col + 2, 1), start=True, stop=True)
        ck_t = inc("T", mm)
        W("V", ck_t)
        v.tensor_copy(out=thr_sb[:, 2 * col:2 * col + 1],
                      in_=pstr[:, 0:1])
        v.drain()
        v.tensor_scalar(out=thr_sb[:, 2 * col + 1:2 * col + 2],
                        in0=thr_sb[:, 2 * col:2 * col + 1], scalar1=-1.0,
                        scalar2=None, op0=OP.mult)
        v.drain()
        th2 = v.tensor_scalar(out=thr_sb[:, 4 + col:5 + col],
                              in0=thr_sb[:, 2 * col:2 * col + 1], scalar1=2.0,
                              scalar2=None, op0=OP.mult)
        v.drain()
        ck_thr = inc("V", th2)
        yield
        qdone, ytd, lds2 = {}, {}, {}
        yt_prev = (("YT0", cnt["YT0"]), ("YT1", cnt["YT1"]))

        def issue2(i):
            c, p0 = pieces[i]
            sl = i % 2
            W("G", qdone.get(i - 2, ck_acc if i < 2 else None))
            return inc("GW0" if sl == 0 else "GW1", g.dma_start(
                out=wqf[:, sl * 256:sl * 256 + 256],
                in_=dram_ap[c * 128:(c + 1) * 128, p0:p0 + 256]), 16)

        lds2[0] = issue2(0)
        for i in range(len(pieces)):
            c, p0 = pieces[i]
            if i + 1 < len(pieces):
                lds2[i + 1] = issue2(i + 1)
            sl = i % 2
            so = sl * 256
            W("V", lds2[i], ck_thr,
              *(yt_prev if i < 2 else (ytd.get(i - 2),)))
            v.tensor_scalar(out=run_sb[:], in0=wqf[:, so:so + 256],
                            scalar1=thr_sb[:, 2 * col + 1:2 * col + 2],
                            scalar2=None, op0=OP.is_lt)
            v.drain()
            v.tensor_scalar(out=tq_sb[:, so:so + 256], in0=wqf[:, so:so + 256],
                            scalar1=thr_sb[:, 2 * col:2 * col + 1],
                            scalar2=None, op0=OP.is_gt)
            v.drain()
            sb2 = v.tensor_sub(out=tq_sb[:, so:so + 256],
                               in0=tq_sb[:, so:so + 256], in1=run_sb[:])
            v.drain()
            ck_q = inc("V", sb2)
            qdone[i] = ck_q
            W("Y", ck_q)
            dd = y.dma_start(out=dst_dram[c][:, p0:p0 + 256],
                             in_=tq_sb[:, so:so + 256])
            ytd[i] = inc("YT0" if sl == 0 else "YT1", dd, 16)
            yield

    side_gens = []

    def pump_side(n=1):
        for _ in range(n):
            while side_gens:
                try:
                    next(side_gens[0])
                    break
                except StopIteration:
                    side_gens.pop(0)
            else:
                return

    def drain_side():
        while side_gens:
            pump_side()


    def emit_attention(L, ck_ag):
        cc = ccout[L]
        st = {"v0": ck("V")}
        prev_tail = None
        for hp in range(HP):
            W("Y", ck_ag, prev_tail, ck("T") if hp else None,
              ("YS0", cnt["YS0"]), ("YS1", cnt["YS1"]))
            kld = []
            for r in (0, 1):
                srck = cc[r][0:KE].rearrange("(h p n) -> h p n", h=HP, p=128)[hp]
                dd = y.dma_start(
                    out=katt[:].rearrange("p (j q) -> p j q",
                                          q=512)[:, :, 256 * r:256 * r + 256],
                    in_=srck.rearrange("p (j n) -> p j n", n=256))
                kld.append(inc("YA", dd, 16))
                srcv = cc[r][KE:KE + VE].rearrange("(c p n) -> c p n",
                                                   c=T // 128, p=128)
                for u in range(2):
                    for hd in range(2):
                        off = 260 * r + 130 * u + 65 * hd
                        dd = y.dma_start(
                            out=vatt[:].rearrange(
                                "p (j q) -> p j q", q=520)[:, :, off:off + 64],
                            in_=srcv[u::2, :, hp * 128 + hd * 64:
                                     hp * 128 + hd * 64 + 64].rearrange(
                                "c p n -> p c n"))
                        kld.append(inc("YA", dd, 16))
            dd = y.dma_start(out=qatt[:], in_=qsp[hp])
            kld.append(inc("YA", dd, 16))
            W("V", *kld, prev_tail)
            ckone = inc("V", v.memset(
                vatt[:].rearrange("p (c f) -> p c f", f=130)[:, :, 64:130:65], 1.0))
            pump_side(1)
            for j in range(NB):
                q0 = j * 256
                for bt in range(j + 1):
                    ck_sc = {}
                    for hd in range(2):
                        bko = hd * 1024
                        mm = None
                        for i in range(4):
                            kb = 4 * bt + i
                            W("T", *kld, ckone, st.get(("exp", hd)),
                              st.get("avfree") if bt == 0 else None)
                            mm = t.matmul(
                                ps[:, bko + i * 256:bko + i * 256 + 256],
                                katt[64 * hd:64 * hd + 64, kb * 128:kb * 128 + 128],
                                qatt[64 * hd:64 * hd + 64, q0:q0 + 256],
                                start=True, stop=True, tile_position=(64 * hd, 0))
                        ck_sc[hd] = inc("T", mm)
                    pump_side(1)
                    for hd in range(2):
                        bko = hd * 1024
                        po = hd * 1024
                        if L == 0:
                            W("S", ck_sc[hd], st.get(("pfree", hd)), st["v0"])
                            e2 = s.activation(out=p_sb[:, po:po + 1024],
                                              in_=ps[:, bko:bko + 1024],
                                              func=AF.Exp, scale=0.125,
                                              bias=cbias[:, 2:3])
                            ck_e = inc("S", e2)
                        else:
                            W("S", ck_sc[hd], st.get(("pfree", hd)), st["v0"])
                            s.activation(out=p_sb[:, po:po + 1024],
                                         in_=ps[:, bko:bko + 1024],
                                         func=AF.Exp, scale=0.125)
                            s.drain()
                            e2 = s.activation(out=p_sb[:, po:po + 1024],
                                              in_=p_sb[:, po:po + 1024],
                                              func=AF.Ln, bias=cbias[:, 1:2])
                            ck_ln = inc("S", e2)
                            W("V", ck_ln)
                            sq = v.tensor_mul(out=p_sb[:, po:po + 1024],
                                              in0=p_sb[:, po:po + 1024],
                                              in1=p_sb[:, po:po + 1024])
                            v.drain()
                            ck_e = inc("V", sq)
                        st[("exp", hd)] = ck_e
                        if bt == j:
                            W("V", ck_e, ck_masks)
                            mk = v.tensor_mul(out=p_sb[:, po:po + 1024],
                                              in0=p_sb[:, po:po + 1024],
                                              in1=masks_sb[:])
                            v.drain()
                            ck_e = inc("V", mk)
                            st[("exp", hd)] = ck_e
                        mm = None
                        for i in range(4):
                            kb = 4 * bt + i
                            W("T", st[("exp", hd)],
                              st.get("avfree") if bt == 0 else None)
                            mm = t.matmul(
                                ps[0:65, (4 + hd) * 512:(4 + hd) * 512 + 256],
                                vatt[:, kb * 130 + hd * 65:kb * 130 + hd * 65 + 65],
                                p_sb[:, po + i * 256:po + i * 256 + 256],
                                start=(bt == 0 and i == 0),
                                stop=(bt == j and i == 3))
                        st[("pfree", hd)] = inc("T", mm)
                        pump_side(1)
                ck_av = st[("pfree", 1)]
                W("V", ck_av)
                if L == 1:
                    v.tensor_scalar(out=ROW(R_REC, 0, 256),
                                    in0=ps[64:65, 2048:2304], scalar1=1e-9,
                                    scalar2=None, op0=OP.add)
                    v.drain()
                    cps = v.tensor_scalar(out=ROW(R_REC, 256, 256),
                                          in0=ps[64:65, 2560:2816],
                                          scalar1=1e-9, scalar2=None,
                                          op0=OP.add)
                else:
                    v.tensor_copy(out=ROW(R_REC, 0, 256),
                                  in_=ps[64:65, 2048:2304])
                    v.drain()
                    cps = v.tensor_copy(out=ROW(R_REC, 256, 256),
                                        in_=ps[64:65, 2560:2816])
                v.drain()
                ck_rc = inc("V", cps)
                W("T", ck_rc)
                rb = 32 * (R_REC % 3)
                t.matmul(pstr[0:64, 256:512], ones_r[rb:rb + 1, 0:64],
                         ROW(R_REC, 0, 256), start=True, stop=True,
                         tile_position=(rb, 0))
                mm = t.matmul(pstr[64:128, 256:512], ones_r[rb:rb + 1, 0:64],
                              ROW(R_REC, 256, 256), start=True,
                              stop=True, tile_position=(rb, 64))
                ck_bc = inc("T", mm)
                if L == 0:
                    W("S", ck_av)
                    s.copy(out=attst[0:64, q0:q0 + 256], in_=ps[0:64, 2048:2304])
                    s.drain()
                    cpk = s.copy(out=attst[64:128, q0:q0 + 256],
                                 in_=ps[0:64, 2560:2816])
                    s.drain()
                    ck_cp = inc("S", cpk)
                else:
                    W("V", ck_av)
                    v.tensor_copy(out=attst[0:64, q0:q0 + 256],
                                  in_=ps[0:64, 2048:2304])
                    v.drain()
                    cpk = v.tensor_copy(out=attst[64:128, q0:q0 + 256],
                                        in_=ps[0:64, 2560:2816])
                    v.drain()
                    ck_cp = inc("V", cpk)
                W("V", ck_bc, ck_cp)
                v.reciprocal(out=fsc[:, 512:768], in_=pstr[:, 256:512])
                v.drain()
                v.tensor_mul(out=attst[0:64, q0:q0 + 256],
                             in0=attst[0:64, q0:q0 + 256],
                             in1=fsc[0:64, 512:768])
                v.drain()
                mu = v.tensor_mul(out=attst[64:128, q0:q0 + 256],
                                  in0=attst[64:128, q0:q0 + 256],
                                  in1=fsc[64:128, 512:768])
                v.drain()
                st["avread"] = inc("V", mu)
                st["avfree"] = st["avread"]
                pump_side(1)
            W("Y", st["avread"], ("YB", cnt["YB"]))
            dd = y.dma_start(out=asp[hp], in_=attst[:])
            prev_tail = inc("YP", dd, 16)
        return prev_tail

    def emit_oproj(L, w_dram, dep_att):
        last = None
        for half in range(2):
            W("G", ck("T"))
            ck_w = wload("GA", w_dram, 0, D, half * 512, 512, warena, 0)
            for sl in range(T // 256):
                W("Y", dep_att, ck("T") if (half or sl) else None)
                dd = y.dma_start(
                    out=attb[:].rearrange("p (c n) -> p c n", n=256),
                    in_=asp[:, :, sl * 256:sl * 256 + 256].rearrange(
                        "c p n -> p c n"))
                abl = [inc("YB", dd, 16)]
                for m2 in range(4):
                    mc = half * 4 + m2
                    pb = 5 + ((sl * 4 + m2) % 2)
                    mm = None
                    for c in range(DC):
                        W("T", ck_w, *abl, last)
                        mm = t.matmul(
                            ps[:, pb * 512:pb * 512 + 256],
                            warena[:, c * 512 + m2 * 128:c * 512 + m2 * 128 + 128],
                            attb[:, c * 256:(c + 1) * 256],
                            start=(c == 0), stop=(c == DC - 1))
                    ck_t = inc("T", mm)
                    xsl = x_sb[:, mc * T + sl * 256:mc * T + sl * 256 + 256]
                    W("V", ck_t)
                    if L == 0:
                        ad = v.tensor_add(out=xsl, in0=xsl,
                                          in1=ps[:, pb * 512:pb * 512 + 256])
                    else:
                        ad = v.scalar_tensor_tensor(
                            out=xsl, in0=ps[:, pb * 512:pb * 512 + 256],
                            scalar=bpj_sb[:, mc:mc + 1], in1=xsl,
                            op0=OP.add, op1=OP.add)
                    v.drain()
                    last = inc("V", ad)
                    pump_side(1)
        return last

    cck = [ccin[L][0:KE].rearrange("(h p n) -> h p n", h=HP, p=128)
           for L in (0, 1)]
    ccv = [ccin[L][KE:KE + VE].rearrange("(c p n) -> c p n", c=T // 128, p=128)
           for L in (0, 1)]

    ck_xn = emit_layernorm(ck_x0)
    side_gens.append(wquant_gen(w1_in, D, H1, 0, t1d))
    side_gens.append(wquant_gen(w2_in, H1, D, 1, t2d))
    emit_proj_w(wk_in, 0, ck_xn, lambda a, b: cck[0][a][:, b * 512:(b + 1) * 512])
    emit_proj_v(wv_in, 0, ck_xn, lambda a, b: ccv[0][a][:, b * 512:(b + 1) * 512])
    W("G", ("YS0", cnt["YS0"]), ("YS1", cnt["YS1"]))
    ck_ag0 = inc("CC", g.collective_compute(
        "AllGather", OP.bypass, replica_groups=RG, ins=[ccin[0]], outs=[ccout[0]]))
    emit_proj_w(wq_in, 0, ck_xn, lambda a, b: qsp[a][:, b * 512:(b + 1) * 512])

    fs = {}
    for tt in range(NT0):
        t0 = tt * TT0
        for jh in range(JC0):
            if tt == 0 and jh == 0:
                fs["wg"] = wload("GF", wg_in, 0, D, 0, 128, wff_w, 0)
            mm = None
            for dc in range(DC):
                W("T", fs["wg"], ck_xn, fs.get("h"), fs.get("si"),
                  fs.get("hused"))
                mm = t.matmul(ps[:, 0:512],
                              wff_w[:, dc * 128:dc * 128 + 128],
                              xn_sb[:, dc * T + t0:dc * T + t0 + 512],
                              start=(dc == 0), stop=(dc == DC - 1))
            ck_g = inc("T", mm)
            if tt == 0 and jh == 0:
                fs["wu"] = wload("GU", wu_in, 0, D, 0, 128, wff_w, 1024)
            for dc in range(DC):
                W("T", fs["wu"])
                mm = t.matmul(ps[:, 512:1024],
                              wff_w[:, 1024 + dc * 128:1024 + dc * 128 + 128],
                              xn_sb[:, dc * T + t0:dc * T + t0 + 512],
                              start=(dc == 0), stop=(dc == DC - 1))
            ck_gu = inc("T", mm)
            pump_side(1)
            nj, ntt = (jh + 1, tt) if jh + 1 < JC0 else (0, tt + 1)
            if ntt < NT0:
                W("G", ck_g)
                fs["wg"] = wload("GF", wg_in, 0, D, nj * 128, 128, wff_w, 0)
                W("G", ck_gu)
                fs["wu"] = wload("GU", wu_in, 0, D, nj * 128, 128, wff_w, 1024)
            W("S", ck_gu, fs.get("h"))
            si = s.activation(out=fsc[:, 0:512], in_=ps[:, 0:512],
                              func=AF.Silu, scale=C127)
            ck_si = inc("S", si)
            fs["si"] = ck_si
            W("V", ck_si)
            hh = v.tensor_mul(out=hpool[:, jh * 512:(jh + 1) * 512],
                              in0=fsc[:, 0:512], in1=ps[:, 512:1024])
            v.drain()
            fs["h"] = inc("V", hh)
        ck_h = fs["h"]
        for mh in range(2):
            mm = None
            for jc in range(JC0):
                so = (jc % 2) * 512
                if tt == 0 and mh == 0 and jc == 0:
                    W("G", ("YS0", cnt["YS0"]), ("YS1", cnt["YS1"]))
                    fs["wd0"] = wload("GD0", wd_in, 0, 128, 0, 512, kvst, 0)
                ck_wd = fs["wd0" if jc % 2 == 0 else "wd1"]
                for m2 in range(4):
                    W("T", ck_wd, ck_h, fs.get("xadd"))
                    mm = t.matmul(
                        ps[:, m2 * 512:m2 * 512 + 512],
                        kvst[:, so + m2 * 128:so + m2 * 128 + 128],
                        hpool[:, jc * TT0:jc * TT0 + 512],
                        start=(jc == 0), stop=(jc == JC0 - 1))
                inc("T", mm)
                nj, nmh = (jc + 1, mh) if jc + 1 < JC0 else (0, mh + 1)
                ntt2 = tt if nmh < 2 else tt + 1
                if nmh == 2:
                    nmh = 0
                if ntt2 < NT0:
                    W("G", ck("T"))
                    fs["wd0" if nj % 2 == 0 else "wd1"] = wload(
                        "GD0" if nj % 2 == 0 else "GD1", wd_in, nj * 128, 128,
                        nmh * 512, 512, kvst, (nj % 2) * 512)
            ck_dn = ck("T")
            ad = None
            for m2 in range(4):
                mc = mh * 4 + m2
                xsl = x_sb[:, mc * T + t0:mc * T + t0 + 512]
                W("V", ck_dn)
                ad = v.scalar_tensor_tensor(
                    out=xsl, in0=ps[:, m2 * 512:m2 * 512 + 512],
                    scalar=CD127, in1=xsl, op0=OP.mult, op1=OP.add)
                v.drain()
            fs["xadd"] = inc("V", ad)
        fs["hused"] = fs["xadd"]

    ck_attn0 = emit_attention(0, ck_ag0)
    ck_x1 = emit_oproj(0, wo_in, ck_attn0)

    W("Y", ck_x1)
    x4d = x_sb[:].rearrange("p (c j n) -> p c j n", c=DC, j=NB)
    hai3 = hai.rearrange("(c p) n -> p c n", p=128)
    for u in range(2):
        for dcq in range(DC):
            hx = y.dma_start(
                out=hai3[:, dcq, u * NB:(u + 1) * NB],
                in_=x4d[:, dcq, :, 255 * u],
                allow_slow_non_contiguous=True)
            inc("YH", hx, 16)
    ck_hai = ck("YH")
    W("G", ck_hai)
    ck_ag1a = inc("CC", g.collective_compute(
        "AllGather", OP.bypass, replica_groups=RG, ins=[hai], outs=[hao]))

    ck_xn = emit_layernorm(ck_x1)
    emit_proj_w(wqkv_in, 1024, ck_xn,
                lambda a, b: cck[1][a][:, b * 512:(b + 1) * 512])
    emit_proj_v(wqkv_in, 2048, ck_xn,
                lambda a, b: ccv[1][a][:, b * 512:(b + 1) * 512])
    W("G", ("YS0", cnt["YS0"]), ("YS1", cnt["YS1"]))
    ck_ag1 = inc("CC", g.collective_compute(
        "AllGather", OP.bypass, replica_groups=RG, ins=[ccin[1]], outs=[ccout[1]]))
    emit_proj_w(wqkv_in, 0, ck_xn, lambda a, b: qsp[a][:, b * 512:(b + 1) * 512])

    W("Y", ck_ag1a)
    c0 = y.dma_start(
        out=hcand[:, 0:DC * NB].rearrange("p (c j) -> p c j", c=DC)[:, :, 1:NB],
        in_=hao[1].rearrange("(c p) n -> p c n", p=128)[:, :, NB:2 * NB - 1],
        allow_slow_non_contiguous=True)
    inc("YH", c0, 16)
    c1 = y.dma_start(
        out=hcand[:, DC * NB:2 * DC * NB].rearrange("p (c j) -> p c j", c=DC),
        in_=hao[0].rearrange("(c p) n -> p c n", p=128)[:, :, NB:2 * NB],
        allow_slow_non_contiguous=True)
    ck_hc = inc("YH", c1, 16)
    W("V", ck_hc, ck_consts)
    v.memset(hcand[:, 0:DC * NB].rearrange("p (c j) -> p c j", c=DC)[:, :, 0:1], 0.0)
    v.drain()
    hx3 = hx_sb[:].rearrange("p (c n) -> p c n", n=HN)
    hc3a = hcand[:, 0:DC * NB].rearrange("p (c j) -> p c j", c=DC)
    hc3b = hcand[:, DC * NB:2 * DC * NB].rearrange("p (c j) -> p c j", c=DC)
    v.tensor_scalar(out=hx3[:, :, 0:NB], in0=hc3a,
                    scalar1=hsel_sb[:, 0:1], scalar2=None, op0=OP.mult)
    v.drain()
    hl = v.scalar_tensor_tensor(out=hx3[:, :, 0:NB], in0=hc3b,
                                scalar=hsel_sb[:, 1:2], in1=hx3[:, :, 0:NB],
                                op0=OP.mult, op1=OP.add)
    v.drain()
    ck_hl = inc("V", hl)
    W("Y", ck_hl)
    c0 = y.dma_start(
        out=hcand[:, 0:DC * NB].rearrange("p (c j) -> p c j", c=DC),
        in_=hao[1].rearrange("(c p) n -> p c n", p=128)[:, :, 0:NB],
        allow_slow_non_contiguous=True)
    inc("YH", c0, 16)
    c1 = y.dma_start(
        out=hcand[:, DC * NB:2 * DC * NB].rearrange(
            "p (c j) -> p c j", c=DC)[:, :, 0:NB - 1],
        in_=hao[0].rearrange("(c p) n -> p c n", p=128)[:, :, 1:NB],
        allow_slow_non_contiguous=True)
    ck_hc2 = inc("YH", c1, 16)
    W("V", ck_hc2)
    v.memset(hcand[:, DC * NB:2 * DC * NB].rearrange(
        "p (c j) -> p c j", c=DC)[:, :, NB - 1:NB], 0.0)
    v.drain()
    v.tensor_scalar(out=hx3[:, :, NB:HN], in0=hc3a,
                    scalar1=hsel_sb[:, 0:1], scalar2=None, op0=OP.mult)
    v.drain()
    hr = v.scalar_tensor_tensor(out=hx3[:, :, NB:HN], in0=hc3b,
                                scalar=hsel_sb[:, 1:2],
                                in1=hx3[:, :, NB:HN],
                                op0=OP.mult, op1=OP.add)
    v.drain()
    ck_hx = inc("V", hr)

    def hx_ap(tens, dc):
        return tens[:, dc * HN:(dc + 1) * HN]

    for side in range(2):
        o = side * NB
        hxs = [(hx_ap(hx_sb, dc)[:, o:o + NB], hx_ap(hcand, dc)[:, o:o + NB])
               for dc in range(DC)]
        ck_hxn = ln_slice(hxs, NB, ck_hx)
    am = None
    for dc in range(DC):
        W("V", ck_hxn)
        if dc == 0:
            v.tensor_copy(out=run_sb[:, 0:HN], in_=hx_ap(hcand, 0))
            v.drain()
            am = v.tensor_copy(out=s2mb[:, 0:HN], in_=hx_ap(hcand, 0))
        else:
            v.tensor_tensor(out=run_sb[:, 0:HN], in0=run_sb[:, 0:HN],
                            in1=hx_ap(hcand, dc), op=OP.max)
            v.drain()
            am = v.tensor_tensor(out=s2mb[:, 0:HN], in0=s2mb[:, 0:HN],
                                 in1=hx_ap(hcand, dc), op=OP.min)
        v.drain()
    v.tensor_scalar(out=s2mb[:, 0:HN], in0=s2mb[:, 0:HN], scalar1=-1.0,
                    scalar2=None, op0=OP.mult)
    v.drain()
    am = v.tensor_tensor(out=run_sb[:, 0:HN], in0=run_sb[:, 0:HN],
                         in1=s2mb[:, 0:HN], op=OP.max)
    v.drain()
    inc("V", am)
    W("V", ck("V"))
    zz = v.memset(run_sb[:, HN:128], 0.0)
    v.drain()
    ck_hfold = inc("V", zz)
    ck_trb = token_scales(run_sb[:, 0:128], thr_sb[:, 4:5], ck_hfold)
    W("V", ck_trb)
    v.tensor_copy(out=ROW(R_HALO, 64, HN), in_=pstr[0:1, 0:HN])
    v.drain()
    hsm = v.tensor_copy(out=ROW(R_HALO, 96, HN),
                        in_=pstr[32:33, 0:HN])
    v.drain()
    ck_hsm = inc("V", hsm)
    ck_hqb = bcast_row(R_HALO, 64, HN, bb1, 0, 5, deps=(ck_hsm,))
    mu = None
    for dc in range(DC):
        W("V", ck_hqb)
        v.tensor_mul(out=i32a[:, dc * HN:(dc + 1) * HN], in0=hx_ap(hcand, dc),
                     in1=bb1[:, 0:HN])
        v.drain()
    mu = v.tensor_copy(out=hq_sb[:, 0:DC * HN], in_=i32a[:, 0:DC * HN])
    v.drain()
    ck_hq = inc("V", mu)

    for sl in range(NSL):
        for dc in range(DC):
            W("V", ck_xn)
            xs_ = xn_sb[:, dc * T + sl * 512:dc * T + sl * 512 + 512]
            if dc == 0:
                v.tensor_copy(out=bb2[:, 0:512], in_=xs_)
                v.drain()
                am = v.tensor_copy(out=bb1[:, 0:512], in_=xs_)
            else:
                v.tensor_tensor(out=bb2[:, 0:512], in0=bb2[:, 0:512],
                                in1=xs_, op=OP.max)
                v.drain()
                am = v.tensor_tensor(out=bb1[:, 0:512], in0=bb1[:, 0:512],
                                     in1=xs_, op=OP.min)
            v.drain()
        v.tensor_scalar(out=bb1[:, 0:512], in0=bb1[:, 0:512], scalar1=-1.0,
                        scalar2=None, op0=OP.mult)
        v.drain()
        am = v.tensor_tensor(out=bb2[:, 0:512], in0=bb2[:, 0:512],
                             in1=bb1[:, 0:512], op=OP.max)
        v.drain()
        ck_fold = inc("V", am)
        for i in range(4):
            ck_trb = token_scales(bb2[:, i * 128:(i + 1) * 128],
                                  thr_sb[:, 4:5], ck_fold)
            W("V", ck_trb)
            v.tensor_copy(out=ROW(R_S1 + sl, i * 128, 128),
                          in_=pstr[0:1, 0:128])
            v.drain()
            cp = v.tensor_copy(out=ROW(R_S1M + sl, i * 128, 128),
                               in_=pstr[32:33, 0:128])
            v.drain()
            inc("V", cp)
    ck_s1 = ck("V")
    drain_side()

    def ffn1_gen():
      l1 = {}
      for tt in range(NB):
        tb = tt * 256
        sl_, off_ = tb // 512, tb % 512
        ck_s1b = bcast_row(R_S1 + sl_, off_, 256, bb1, 0, "P", deps=(ck_s1,))
        ck_s1mb = bcast_row(R_S1M + sl_, off_, 256, s2mb, 0, "P",
                            deps=(ck_s1,))
        xq = None
        for dc in range(DC):
            W("V", ck_s1b, ck_s1mb, ck_hq, l1.get("w1done"))
            v.tensor_mul(out=i32a[:, 0:256],
                         in0=xn_sb[:, dc * T + tb:dc * T + tb + 256],
                         in1=bb1[:, 0:256])
            v.drain()
            v.tensor_mul(out=xq1_sb[:, dc * 258 + 1:dc * 258 + 257],
                         in0=i32a[:, 0:256], in1=s2mb[:])
            v.drain()
        x3 = xq1_sb[:].rearrange("p (c n) -> p c n", n=258)
        h3 = hq_sb[:].rearrange("p (c n) -> p c n", n=HN)
        v.tensor_copy(out=x3[:, :, 0:1], in_=h3[:, :, tt:tt + 1])
        v.drain()
        xq = v.tensor_copy(out=x3[:, :, 257:258], in_=h3[:, :, NB + tt:NB + tt + 1])
        v.drain()
        ck_xq1 = inc("V", xq)
        yield
        w1g, t1l = {}, {}

        def t1load(jc, dep):
            tslot = (jc % 2) * DC * 128
            W("Y", ("YT0", cnt["YT0"]), ("YT1", cnt["YT1"]), dep)
            tl = y.dma_start(
                out=t1buf[:, tslot:tslot + DC * 128].rearrange(
                    "p (c n) -> p c n", c=DC),
                in_=t1d[:, :, jc * 128:(jc + 1) * 128])
            return inc("Y1a" if jc % 2 == 0 else "Y1b", tl, 16)

        t1l[0] = t1load(0, ck("T"))
        for jc in range(JC1):
            tslot = (jc % 2) * DC * 128
            if jc + 1 < JC1:
                t1l[jc + 1] = t1load(
                    jc + 1, w1g.get(jc - 1, ck("T") if jc < 1 else None))
            pb = 6
            mm = None
            for dc in range(DC):
                W("T", t1l[jc], ck_xq1, l1.get(("hev", jc - 1)))
                mm = t.matmul(ps[:, pb * 512:pb * 512 + 258],
                              t1buf[:, tslot + dc * 128:tslot + dc * 128 + 128],
                              xq1_sb[:, dc * 258:(dc + 1) * 258],
                              start=(dc == 0), stop=(dc == DC - 1))
            ck_w1 = inc("T", mm)
            w1g[jc] = ck_w1
            pso = pb * 512
            so_c = (jc % 2) * 256

            def fold(k):
                W("V", l1[("si", k)])
                if k == 0:
                    v.tensor_copy(out=run_sb[:, 0:256], in_=hpool[:, 0:256])
                    v.drain()
                    am = v.tensor_copy(out=s2mb[:], in_=hpool[:, 0:256])
                else:
                    v.tensor_tensor(out=run_sb[:, 0:256], in0=run_sb[:, 0:256],
                                    in1=hpool[:, k * 256:(k + 1) * 256],
                                    op=OP.max)
                    v.drain()
                    am = v.tensor_tensor(out=s2mb[:], in0=s2mb[:],
                                         in1=hpool[:, k * 256:(k + 1) * 256],
                                         op=OP.min)
                v.drain()
                inc("V", am)

            W("V", ck_w1, l1.get(("si", jc - 2)))
            v.tensor_scalar(out=fsc[:, so_c:so_c + 256], in0=ps[:, pso:pso + 256],
                            scalar1=cvw_sb[:, jc * 3:jc * 3 + 1], scalar2=None,
                            op0=OP.mult)
            v.drain()
            v.scalar_tensor_tensor(out=fsc[:, so_c:so_c + 256],
                                   in0=ps[:, pso + 1:pso + 257],
                                   scalar=cvw_sb[:, jc * 3 + 1:jc * 3 + 2],
                                   in1=fsc[:, so_c:so_c + 256],
                                   op0=OP.mult, op1=OP.add)
            v.drain()
            cv3 = v.scalar_tensor_tensor(out=fsc[:, so_c:so_c + 256],
                                         in0=ps[:, pso + 2:pso + 258],
                                         scalar=cvw_sb[:, jc * 3 + 2:jc * 3 + 3],
                                         in1=fsc[:, so_c:so_c + 256],
                                         op0=OP.mult, op1=OP.add)
            v.drain()
            ck_cv = inc("V", cv3)
            l1[("hev", jc)] = ck_cv
            W("S", ck_cv)
            si = s.activation(out=hpool[:, jc * 256:(jc + 1) * 256],
                              in_=fsc[:, so_c:so_c + 256], func=AF.Silu,
                              bias=cvb_sb[:, jc:jc + 1])
            l1[("si", jc)] = inc("S", si)
            if jc > 0:
                fold(jc - 1)
            yield
        fold(JC1 - 1)
        W("V", ck("V"))
        v.tensor_scalar(out=s2mb[:], in0=s2mb[:], scalar1=-1.0,
                        scalar2=None, op0=OP.mult)
        v.drain()
        am = v.tensor_tensor(out=run_sb[:], in0=run_sb[:], in1=s2mb[:],
                             op=OP.max)
        v.drain()
        inc("V", am)
        l1["w1done"] = ck("V")
        ck_run = ck("V")
        for half in range(2):
            ck_trb = token_scales(run_sb[:, half * 128:(half + 1) * 128]
                                  if half else run_sb[:, 0:128],
                                  thr_sb[:, 5:6], ck_run)
            W("V", ck_trb)
            v.tensor_copy(out=ROW(R_S2, half * 128, 128),
                          in_=pstr[0:1, 0:128])
            v.drain()
            cp = v.tensor_copy(out=ROW(R_S2M, half * 128, 128),
                               in_=pstr[32:33, 0:128])
            v.drain()
            inc("V", cp)
            yield
        ck_s2 = ck("V")
        ck_s2b = bcast_row(R_S2, 0, 256, run_sb, 0, "P", deps=(ck_s2,))
        ck_s2mb = bcast_row(R_S2M, 0, 256, s2mb, 0, "P", deps=(ck_s2,))
        xq2 = None
        for jc in range(JC1):
            W("V", ck_s2b, ck_s2mb)
            v.tensor_mul(out=i32a[:, 0:256], in0=hpool[:, jc * 256:(jc + 1) * 256],
                         in1=run_sb[:])
            v.drain()
            xq2 = v.tensor_mul(out=hpool[:, jc * 256:(jc + 1) * 256],
                               in0=i32a[:, 0:256], in1=s2mb[:])
            v.drain()
            if jc % 8 == 7 and jc < JC1 - 1:
                yield
        ck_xq2 = inc("V", xq2)
        pb = 6
        qmm, t2l = {}, {}

        def t2load(qi, dep):
            mc, q = qi // 4, qi % 4
            so = (qi % 2) * 1024
            W("Y", ("YT0", cnt["YT0"]), ("YT1", cnt["YT1"]), dep)
            tl = y.dma_start(
                out=t2buf[:, so:so + 1024].rearrange("p (c n) -> p c n", c=8),
                in_=t2d[q * 8:(q + 1) * 8, :, mc * 128:(mc + 1) * 128])
            return inc("Y2" if qi % 2 == 0 else "Y2b", tl, 16)

        t2l[0] = t2load(0, ck("T"))
        for qi in range(4 * DC):
            mc, q = qi // 4, qi % 4
            so = (qi % 2) * 1024
            if qi + 1 < 4 * DC:
                t2l[qi + 1] = t2load(
                    qi + 1, qmm.get(qi - 1, ck("T") if qi < 1 else None))
            mm = None
            for jc2 in range(8):
                jc = q * 8 + jc2
                W("T", t2l[qi], ck_xq2, l1.get(("xadd", mc - 1)))
                mm = t.matmul(ps[:, pb * 512:pb * 512 + 256],
                              t2buf[:, so + jc2 * 128:so + jc2 * 128 + 128],
                              hpool[:, jc * 256:(jc + 1) * 256],
                              start=(jc == 0), stop=(jc == JC1 - 1))
            qmm[qi] = inc("T", mm)
            if q % 2 == 1:
                yield
            if q == 3:
                xsl = x_sb[:, mc * T + tb:mc * T + tb + 256]
                W("V", qmm[qi])
                ad = v.tensor_add(out=xsl, in0=xsl,
                                  in1=ps[:, pb * 512:pb * 512 + 256])
                v.drain()
                l1[("xadd", mc)] = inc("V", ad)
        l1["w1done"] = ck("V")

    side_gens.append(ffn1_gen())

    ck_attn1 = emit_attention(1, ck_ag1)
    drain_side()
    ck_x2 = emit_oproj(1, wpj_in, ck_attn1)

    W("Y", ck_x2)
    for dc in range(DC):
        inc("YO", y.dma_start(out=out_d[dc * 128:(dc + 1) * 128, :],
                              in_=x_sb[:, dc * T:(dc + 1) * T]), 16)
    y.wait_ge(sems["YO"], cnt["YO"])

    blk.tensor(_replay(tapes["T"]))
    blk.vector(_replay(tapes["V"]))
    blk.scalar(_replay(tapes["S"]))
    blk.sync(_replay(tapes["Y"]))
    blk.gpsimd(_replay(tapes["G"]))

    es.close()
    return nc


def _masks(par):
    kk = np.arange(128)[:, None]
    qq = np.arange(256)[None, :]
    tri0 = (kk <= qq).astype(np.float32)
    tri1 = (kk + 128 <= qq).astype(np.float32)
    one = np.ones((128, 256), np.float32)
    zero = np.zeros((128, 256), np.float32)
    ms = [tri0, tri1, zero, zero] if par == 0 else [one, one, tri0, tri1]
    return np.concatenate(ms, axis=1)


def _stage(inputs, S):
    x = np.asarray(inputs["L0_attn_residual"])
    B = x.shape[0]
    NBG = S // 256
    C = lambda a: np.ascontiguousarray(a)

    wq = C(np.asarray(inputs["L0_attn_W_Q"]).reshape(D, D).T)
    wk = C(np.asarray(inputs["L0_attn_W_K"]).reshape(D, D).T)
    wv = C(np.asarray(inputs["L0_attn_W_V"]).reshape(D, D).T)
    wo = C(np.asarray(inputs["L0_attn_W_O"]).reshape(D, D).T)
    wg = C(np.asarray(inputs["L0_ffn_w_gate"]).T.astype(np.int8))
    wu = C(np.asarray(inputs["L0_ffn_w_up"]).T.astype(np.int8))
    wd = C(np.asarray(inputs["L0_ffn_w_down"]).T.astype(np.int8))
    wqkv = C(np.asarray(inputs["L1_attn_w_qkv"]).T)
    wpj = C(np.asarray(inputs["L1_attn_w_proj"]).T)
    bpj = C(np.asarray(inputs["L1_attn_b_proj"]).reshape(D, 1))
    w1 = C(np.asarray(inputs["L1_ffn_w1"]).T)
    w2 = C(np.asarray(inputs["L1_ffn_w2"]).T)
    cw = C(np.asarray(inputs["L1_ffn_conv_w"]).reshape(H1, 3))
    cb = C(np.asarray(inputs["L1_ffn_conv_b"]).reshape(H1, 1))
    ident = np.eye(128, dtype=np.float32)

    in_maps = []
    for core in range(2 * B):
        b, par = core // 2, core % 2
        blocks = x[b].reshape(NBG, 256, D)[par::2]
        xT = C(blocks.reshape(S // 2, D).T)
        hsel = np.zeros((128, 2), np.float32)
        hsel[:, 0 if par == 0 else 1] = 1.0
        in_maps.append({
            "xT": xT, "masks": _masks(par), "ident": ident, "hsel": hsel,
            "wqT": wq, "wkT": wk, "wvT": wv, "woT": wo,
            "wgT": wg, "wuT": wu, "wdT": wd,
            "wqkvT": wqkv, "wprojT": wpj, "bproj": bpj,
            "w1T": w1, "w2T": w2, "convw": cw, "convb": cb,
        })
    return in_maps


def kernel(**inputs):
    x = np.asarray(inputs["L0_attn_residual"])
    B, S, _ = x.shape
    nc = build_nc(S)
    in_maps = _stage(inputs, S)
    res = run_bass_kernel_spmd(nc, in_maps, core_ids=list(range(2 * B)))
    NBG = S // 256
    out = np.empty((B, S, D), np.float32)
    for core in range(2 * B):
        b, par = core // 2, core % 2
        oT = np.asarray(res.results[core]["outT"])
        out[b].reshape(NBG, 256, D)[par::2] = oT.T.reshape(NBG // 2, 256, D)
    return out
